# version 05
# baseline (speedup 1.0000x reference)
import numpy as np
import ml_dtypes
from contextlib import ExitStack
import concourse.bass as bass
import concourse.mybir as mybir
from concourse.bass_utils import run_bass_kernel_spmd

F32 = mybir.dt.float32
BF16 = mybir.dt.bfloat16
U8 = mybir.dt.uint8
AF = mybir.ActivationFunctionType
ALU = mybir.AluOpType
AX = mybir.AxisListType

NCORES = 8
T = 2048
D = 1024
NCD = D // 128
TS = 512
NSTRIP = T // TS
NH, DK, DV = 4, 256, 256

ENG = ("sync", "tensor", "vector", "scalar", "gpsimd")


class Em:
    def __init__(self, nc, st):
        self.nc, self.st = nc, st
        self.prog = {e: [] for e in ENG}
        self.sem_h = {}
        self.sem_c = {}
        self.last_wait = {e: {} for e in ENG}

    def _sem(self, name):
        if name not in self.sem_h:
            self.sem_h[name] = self.st.enter_context(self.nc.semaphore(name))
            self.sem_c[name] = 0
        return self.sem_h[name]

    AUTO_CHAIN = ("vector", "scalar", "gpsimd")

    def _op(self, engine, fn, tok, dma):
        if engine in self.AUTO_CHAIN and not dma:
            name = "c_" + engine
            h = self._sem(name)
            before = self.sem_c[name]
            self.wait(engine, (name, before))
            self.sem_c[name] = before + 1
            self.prog[engine].append(lambda eng: fn(eng).then_inc(h, 1))
            return (name, before + 1)
        if not tok:
            self.prog[engine].append(fn)
            return None
        name = ("d_" if dma else "c_") + engine
        h = self._sem(name)
        inc = 16 if dma else 1
        self.sem_c[name] += inc
        val = self.sem_c[name]
        self.prog[engine].append(lambda eng: fn(eng).then_inc(h, inc))
        return (name, val)

    def t(self, fn, tok=False):
        return self._op("tensor", fn, tok, False)

    def v(self, fn, tok=False):
        return self._op("vector", fn, tok, False)

    def s(self, fn, tok=False):
        return self._op("scalar", fn, tok, False)

    def g(self, fn, tok=False):
        return self._op("gpsimd", fn, tok, False)

    def tfence(self):
        return self._op("tensor", lambda t: t.nop(), True, False)

    def vfence(self):
        return self._op("vector", lambda v: v.engine_nop(), True, False)

    def gfence(self):
        return self._op("gpsimd", lambda g: g.engine_nop(), True, False)

    def dma(self, fn, q="sync", ch=None):
        ch = "dch_" + (ch or q)
        h = self._sem(ch)
        self.sem_c[ch] += 16
        val = self.sem_c[ch]
        self.prog[q].append(lambda eng: fn(eng).then_inc(h, 16))
        return (ch, val)

    def ch_all(self, ch):
        ch = "dch_" + ch
        return (ch, self.sem_c.get(ch, 0))

    def wait(self, engine, *toks):
        for tk in toks:
            if tk is None:
                continue
            name, val = tk
            if val <= self.last_wait[engine].get(name, 0):
                continue
            self.last_wait[engine][name] = val
            h = self.sem_h[name]
            self.prog[engine].append(lambda eng, h=h, v=val: eng.wait_ge(h, v))

    def barrier(self):
        waits = [(nm, cnt) for nm, cnt in self.sem_c.items() if cnt > 0]
        for e in ENG:
            if self.prog[e] or True:
                self.wait(e, *waits)

    def finalize(self):
        with self.nc.Block() as block:
            for e in ENG:
                ops = self.prog[e]
                if ops:
                    getattr(block, e)(lambda eng, ops=ops: [op(eng) for op in ops])


class Banks:

    def __init__(self, em, tiles):
        self.em = em
        self.tiles = tiles
        self.free_tok = [None] * len(tiles)
        self.i = 0

    def take(self):
        i = self.i % len(self.tiles)
        self.i += 1
        return i, self.tiles[i], self.free_tok[i]

    def consumed(self, i, tok):
        self.free_tok[i] = tok


CST = {}
_w = 0
def _c(name, width):
    global _w
    CST[name] = (_w, width)
    _w += width
_c("ones", 1)
_c("ident", 128)
_c("strict_low", 128)
_c("incl_up", 128)
_c("eps_ln", 1)
_c("eps_l2", 1)
CST_W = _w


def build_const_f32():
    c = np.zeros((128, CST_W), dtype=np.float32)
    c[:, CST["ones"][0]] = 1.0
    o, _ = CST["ident"]
    c[:, o:o + 128] = np.eye(128, dtype=np.float32)
    o, _ = CST["strict_low"]
    c[:, o:o + 128] = np.tril(np.ones((128, 128), np.float32), -1)
    o, _ = CST["incl_up"]
    c[:, o:o + 128] = np.triu(np.ones((128, 128), np.float32), 0)
    c[:, CST["eps_ln"][0]] = 1e-5
    c[:, CST["eps_l2"][0]] = 1e-6
    return c


CSTB_W = 32 * 128


def build_const_bf16():
    import ml_dtypes as _md
    c = np.zeros((32, CSTB_W), dtype=_md.bfloat16)
    for r in range(32):
        c[r, r * 128:(r + 1) * 128] = 1.0
    return c


def cst(csb, name):
    o, w = CST[name]
    return csb[:, o:o + w]


def onesrow(csb):
    o, _ = CST["incl_up"]
    return csb[0:1, o:o + 128]


def load_pp(em, dst_sb, dram_ap, ch=None):
    return em.dma(lambda e: e.dma_start(out=dst_sb, in_=dram_ap.rearrange("(c p) -> p c", p=128),
                                        allow_slow_non_contiguous=True), ch=ch)


def load_fm(em, dst_sb, dram_ap, q="sync", ch=None):
    c = dram_ap.shape[0] // 128
    return em.dma(lambda e: e.dma_start(out=dst_sb.rearrange("p (c t) -> p c t", c=c),
                                        in_=dram_ap.rearrange("(c p) t -> p c t", p=128)), q=q, ch=ch)


def store_fm(em, dram_ap, src_sb, q="sync", ch=None):
    c = dram_ap.shape[0] // 128
    return em.dma(lambda e: e.dma_start(out=dram_ap.rearrange("(c p) t -> p c t", p=128),
                                        in_=src_sb.rearrange("p (c t) -> p c t", c=c)), q=q, ch=ch)


def stage_ln(nc, st, em, dram, name_in, name_out, gamma=None, beta=None, also_f32=None,
             in_dt=F32, uniq=None):
    u = uniq or f"ln_{name_in}"
    sb = lambda nm, sh, dt: st.enter_context(nc.sbuf_tensor(f"{u}_{nm}", sh, dt))
    xs = sb("xs", [128, NCD * T], in_dt)
    xnb = sb("xnb", [128, NCD * T], BF16)
    xn32 = sb("xn32", [128, NCD * T], F32) if also_f32 else None
    sq = sb("sq", [128, TS], F32)
    rows = sb("rows", [1, 6 * TS], F32)
    csb = sb("cst", [128, CST_W], F32)
    gsb = sb("g", [128, NCD], F32) if gamma else None
    bsb = sb("b", [128, NCD], F32) if beta else None
    tmp = sb("tmp", [128, TS], F32)

    pm = st.enter_context(nc.psum_tensor(f"{u}_pm", [1, TS], F32))
    pv = st.enter_context(nc.psum_tensor(f"{u}_pv", [1, TS], F32))
    pbm = st.enter_context(nc.psum_tensor(f"{u}_pbm", [128, TS], F32))
    pbr = st.enter_context(nc.psum_tensor(f"{u}_pbr", [128, TS], F32))

    onesb = sb("onesb", [128, 1], BF16) if in_dt == BF16 else None
    load_fm(em, xs[:], dram[name_in], ch=u)
    em.dma(lambda e: e.dma_start(out=csb[:], in_=dram["cst_f32"]), ch=u)
    if gamma:
        load_pp(em, gsb[:], dram[gamma], ch=u)
    if beta:
        load_pp(em, bsb[:], dram[beta], ch=u)
    ld = ldc = ldg = ldb = em.ch_all(u)
    ones_ap = cst(csb, "ones")
    ocast = None
    if onesb is not None:
        em.wait("vector", ld)
        ocast = em.v(lambda v: v.tensor_copy(onesb[:], cst(csb, "ones")))
        ones_ap = onesb[:]

    stat_free = None
    bc_free = None
    for s in range(NSTRIP):
        xstrip = lambda c: xs[:, c * T + s * TS: c * T + (s + 1) * TS]
        em.wait("tensor", ld, ldc, stat_free, ocast)
        for c in range(NCD):
            xc = xstrip(c)
            em.t(lambda t, xc=xc, c=c: t.matmul(pm[:], ones_ap, xc,
                                                start=(c == 0), stop=(c == NCD - 1)))
        prev_mm = None
        for c in range(NCD):
            xc = xstrip(c)
            em.wait("scalar", ld, stat_free, prev_mm)
            sqt = em.s(lambda sc, xc=xc: sc.activation(out=sq[:], in_=xc, func=AF.Square), tok=True)
            em.wait("tensor", sqt)
            prev_mm = em.t(lambda t, c=c: t.matmul(pv[:], cst(csb, "ones"), sq[:],
                                                   start=(c == 0), stop=(c == NCD - 1)), tok=True)
        tok_mm = prev_mm
        em.wait("vector", tok_mm)
        em.v(lambda v: v.tensor_scalar_mul(rows[:, 0:TS], pm[:], 1.0 / D))
        em.v(lambda v: v.tensor_scalar_mul(rows[:, TS:2 * TS], pv[:], 1.0 / D))
        em.v(lambda v: v.tensor_mul(rows[:, 2 * TS:3 * TS], rows[:, 0:TS], rows[:, 0:TS]))
        stat_free = em.v(lambda v: v.tensor_sub(rows[:, 3 * TS:4 * TS], rows[:, TS:2 * TS], rows[:, 2 * TS:3 * TS]), tok=True)
        em.wait("scalar", stat_free)
        epsap = csb[0:1, CST["eps_ln"][0]:CST["eps_ln"][0] + 1]
        sqt2 = em.s(lambda sc, epsap=epsap: sc.activation(out=rows[:, 4 * TS:5 * TS], in_=rows[:, 3 * TS:4 * TS],
                                             func=AF.Sqrt, bias=epsap), tok=True)
        em.wait("vector", sqt2)
        rt = em.v(lambda v: v.reciprocal(rows[:, 5 * TS:6 * TS], rows[:, 4 * TS:5 * TS]), tok=True)
        em.wait("tensor", rt, bc_free)
        em.t(lambda t: t.matmul(pbm[:], onesrow(csb), rows[0:1, 0:TS], start=True, stop=True))
        bt = em.t(lambda t: t.matmul(pbr[:], onesrow(csb), rows[0:1, 5 * TS:6 * TS], start=True, stop=True), tok=True)
        em.wait("vector", bt, ldg, ldb)
        for c in range(NCD):
            xc = xstrip(c)
            oc = xnb[:, c * T + s * TS: c * T + (s + 1) * TS]
            oc32 = xn32[:, c * T + s * TS: c * T + (s + 1) * TS] if also_f32 else None
            em.v(lambda v, xc=xc: v.tensor_sub(tmp[:], xc, pbm[:]))
            if gamma:
                em.v(lambda v: v.tensor_mul(tmp[:], tmp[:], pbr[:]))
                gcol = gsb[:, c:c + 1]
                bcol = bsb[:, c:c + 1]
                em.v(lambda v, oc=oc, gcol=gcol, bcol=bcol: v.tensor_scalar(oc, tmp[:], gcol, bcol, ALU.mult, ALU.add))
                if also_f32:
                    em.v(lambda v, oc32=oc32, gcol=gcol, bcol=bcol: v.tensor_scalar(oc32, tmp[:], gcol, bcol, ALU.mult, ALU.add))
            else:
                em.v(lambda v, oc=oc: v.tensor_mul(oc, tmp[:], pbr[:]))
                if also_f32:
                    em.v(lambda v, oc32=oc32: v.tensor_mul(oc32, tmp[:], pbr[:]))
        bc_free = em.vfence()
    em.wait("sync", bc_free)
    store_fm(em, dram[name_out], xnb[:], ch=u + "st")
    if also_f32:
        store_fm(em, dram[also_f32], xn32[:], ch=u + "st")
    em.wait("sync", em.ch_all(u + "st"))


def stage_l0qkv(nc, st, em, dram, parts=('qkv', 'beta')):
    sb = lambda nm, sh, dt: st.enter_context(nc.sbuf_tensor(f"l0qkv_{nm}", sh, dt))
    xnb = sb("xn", [128, NCD * T], BF16)
    wsb = sb("w", [128, NCD * 1024], BF16)
    wbb = sb("wb", [128, NCD * 4], BF16)
    cwsb = sb("cw", [128, NCD * 4 * 3], F32)
    pre = sb("pre", [128, 3 + T], F32)
    conv = sb("conv", [128, T], F32)
    outb = sb("out", [128, NCD * T], BF16)
    beta_sb = sb("beta", [128, 16 * 4], F32)

    pb = [st.enter_context(nc.psum_tensor(f"l0qkv_p{i}", [128, TS], F32)) for i in range(4)]
    pbeta = st.enter_context(nc.psum_tensor("l0qkv_pbeta", [128, 4], F32))

    load_fm(em, xnb[:], dram["xn"], ch="l0qkv")
    for i, nm in enumerate(["L0_attn_conv_q_w", "L0_attn_conv_k_w", "L0_attn_conv_v_w"]):
        em.dma(lambda e, nm=nm, i=i: e.dma_start(
            out=cwsb[:, i * NCD * 4:(i + 1) * NCD * 4].rearrange("p (c j) -> p c j", c=NCD),
            in_=dram[nm].rearrange("(c p) j -> p c j", p=128)), ch="l0qkv")
    em.dma(lambda e: e.dma_start(out=wbb[:].rearrange("p (c m) -> p c m", c=NCD),
                                 in_=dram["L0_attn_Wb"].rearrange("(c p) m -> p c m", p=128)), q="gpsimd", ch="l0qkvb")
    ldx = em.ch_all("l0qkv")
    ldcw = [ldx, ldx, ldx]
    ldwb = em.ch_all("l0qkvb")

    banks = Banks(em, pb)
    zpad = em.v(lambda v: v.memset(pre[:, 0:3], 0.0), tok=True)
    last_mm = None
    tensors = [("L0_attn_Wq", "q_s"), ("L0_attn_Wk", "k_s"), ("L0_attn_Wv", "v_s")] if 'qkv' in parts else []
    for ti, (wname, oname) in enumerate(tensors):
        em.wait("gpsimd", last_mm)
        em.dma(lambda e, wname=wname: e.dma_start(
            out=wsb[:].rearrange("p (c m) -> p c m", c=NCD),
            in_=dram[wname].rearrange("(c p) m -> p c m", p=128)), q="gpsimd", ch=f"l0qkv_w{ti}")
        ldw = em.ch_all(f"l0qkv_w{ti}")
        for mc in range(NCD):
            for s in range(NSTRIP):
                bi, pt, ftok = banks.take()
                em.wait("tensor", ldx, ldw, ftok)
                mt = None
                for kc in range(NCD):
                    wk = wsb[:, kc * 1024 + mc * 128: kc * 1024 + (mc + 1) * 128]
                    xk = xnb[:, kc * T + s * TS: kc * T + (s + 1) * TS]
                    mt = em.t(lambda t, wk=wk, xk=xk, kc=kc, pt=pt: t.matmul(
                        pt[:], wk, xk, start=(kc == 0), stop=(kc == NCD - 1)), tok=(kc == NCD - 1))
                last_mm = mt
                em.wait("vector", mt, zpad)
                et = em.v(lambda v, s=s, pt=pt: v.tensor_copy(pre[:, 3 + s * TS: 3 + (s + 1) * TS], pt[:]), tok=True)
                banks.consumed(bi, et)
            em.wait("vector", ldcw[ti])
            cw = [cwsb[:, ti * NCD * 4 + mc * 4 + j: ti * NCD * 4 + mc * 4 + j + 1] for j in range(4)]
            em.v(lambda v, cw0=cw[0]: v.tensor_scalar_mul(conv[:], pre[:, 0:T], cw0))
            for j in range(1, 4):
                em.v(lambda v, j=j, cwj=cw[j]: v.scalar_tensor_tensor(conv[:], pre[:, j:j + T], cwj, conv[:], ALU.mult, ALU.add))
            cvt = em.vfence()
            em.wait("scalar", cvt)
            oc = outb[:, mc * T:(mc + 1) * T]
            stok = em.s(lambda sc, oc=oc: sc.activation(out=oc, in_=conv[:], func=AF.Silu), tok=True)
            em.wait("vector", stok)
        em.wait("sync", stok)
        store_fm(em, dram[oname], outb[:], ch="l0qkvst")
        em.wait("scalar", em.ch_all("l0qkvst"))
    if 'beta' not in parts:
        return
    em.wait("tensor", ldwb, ldx)
    sg = None
    for tt in range(16):
        em.wait("tensor", sg)
        mt = None
        for kc in range(NCD):
            xk = xnb[:, kc * T + tt * 128: kc * T + (tt + 1) * 128]
            wk = wbb[:, kc * 4:(kc + 1) * 4]
            mt = em.t(lambda t, xk=xk, wk=wk, kc=kc: t.matmul(
                pbeta[:], xk, wk, start=(kc == 0), stop=(kc == NCD - 1)), tok=(kc == NCD - 1))
        em.wait("scalar", mt)
        ob = beta_sb[:, tt * 4:(tt + 1) * 4]
        sg = em.s(lambda sc, ob=ob: sc.activation(out=ob, in_=pbeta[:], func=AF.Sigmoid), tok=True)
    em.wait("sync", sg)
    em.dma(lambda e: e.dma_start(out=dram["beta_tm"].rearrange("(tt p) h -> p tt h", p=128),
                                 in_=beta_sb[:].rearrange("p (tt h) -> p tt h", tt=16)), ch="l0qkvst")
    em.wait("sync", em.ch_all("l0qkvst"))


def np_dt(dt):
    return {F32: np.float32, BF16: ml_dtypes.bfloat16, U8: np.uint8}[dt]


def run_stages(stage_fns, inputs_per_core, input_specs, output_specs, trace=False):
    nc = bass.Bass("TRN2", target_bir_lowering=False, debug=False)
    dram = {}
    for nm, (sh, dt) in input_specs.items():
        dram[nm] = nc.dram_tensor(nm, list(sh), dt, kind="ExternalInput").ap()
    for nm, (sh, dt) in output_specs.items():
        dram[nm] = nc.dram_tensor(nm, list(sh), dt, kind="ExternalOutput").ap()
    with ExitStack() as st:
        em = Em(nc, st)
        for f in stage_fns:
            f(nc, st, em, dram)
        em.finalize()
    res = run_bass_kernel_spmd(nc, inputs_per_core, core_ids=list(range(len(inputs_per_core))), trace=trace)
    return res





def sim_stages(stage_fns, inputs, input_specs, output_specs):
    from concourse.bass_interp import CoreSim
    nc = bass.Bass("TRN2", target_bir_lowering=False, debug=False)
    dram = {}
    for nm, (sh, dt) in input_specs.items():
        dram[nm] = nc.dram_tensor(nm, list(sh), dt, kind="ExternalInput").ap()
    for nm, (sh, dt) in output_specs.items():
        dram[nm] = nc.dram_tensor(nm, list(sh), dt, kind="ExternalOutput").ap()
    with ExitStack() as st:
        em = Em(nc, st)
        for f in stage_fns:
            f(nc, st, em, dram)
        em.finalize()
    sim = CoreSim(nc, trace=False)
    for nm in input_specs:
        sim.tensor(nm)[:] = inputs[nm]
    sim.simulate(check_with_hw=False)
    return {nm: np.array(sim.tensor(nm)) for nm in output_specs}


def stage_l0prep(nc, st, em, dram):
    sb = lambda nm, sh, dt: st.enter_context(nc.sbuf_tensor(f"l0pr_{nm}", sh, dt))
    qs = sb("qs", [128, NCD * T], BF16)
    ks = sb("ks", [128, NCD * T], BF16)
    vs = sb("vs", [128, NCD * T], BF16)
    csb = sb("cst", [128, CST_W], F32)
    idb = sb("idb", [128, 128], BF16)
    sq = sb("sq", [128, TS], F32)
    row = sb("row", [1, 2 * TS], F32)
    stg = [sb(f"stg{i}", [128, 256], BF16) for i in range(2)]

    prow = st.enter_context(nc.psum_tensor("l0pr_prow", [1, TS], F32))
    pbr = st.enter_context(nc.psum_tensor("l0pr_pbr", [128, TS], F32))
    ptr = [st.enter_context(nc.psum_tensor(f"l0pr_ptr{i}", [128, 256], BF16)) for i in range(2)]

    load_fm(em, qs[:], dram["q_s"], ch="l0pr")
    load_fm(em, ks[:], dram["k_s"], ch="l0pr")
    load_fm(em, vs[:], dram["v_s"], ch="l0pr")
    em.dma(lambda e: e.dma_start(out=csb[:], in_=dram["cst_f32"]), ch="l0pr")
    ld = em.ch_all("l0pr")
    em.wait("vector", ld)
    idt = em.v(lambda v: v.tensor_copy(idb[:], cst(csb, "ident")))

    eps2 = csb[0:1, CST["eps_l2"][0]:CST["eps_l2"][0] + 1]
    last_norm = None
    prow_free = None
    pbr_free = None
    for ti, src_sb in enumerate([qs, ks]):
        for h in range(NH):
            for s in range(NSTRIP):
                mmt = None
                for i, c in enumerate([2 * h, 2 * h + 1]):
                    xc = src_sb[:, c * T + s * TS: c * T + (s + 1) * TS]
                    em.wait("scalar", ld, prow_free, mmt)
                    sqt = em.s(lambda sc, xc=xc: sc.activation(out=sq[:], in_=xc, func=AF.Square))
                    em.wait("tensor", sqt, prow_free)
                    mmt = em.t(lambda t, i=i: t.matmul(prow[:], cst(csb, "ones"), sq[:],
                                                       start=(i == 0), stop=(i == 1)), tok=True)
                em.wait("scalar", mmt)
                rsq = em.s(lambda sc: sc.activation(out=row[:, 0:TS], in_=prow[:], func=AF.Sqrt, bias=eps2))
                prow_free = rsq
                em.wait("vector", rsq)
                rec = em.v(lambda v: v.reciprocal(row[:, TS:2 * TS], row[:, 0:TS]))
                em.wait("tensor", rec, pbr_free)
                bct = em.t(lambda t: t.matmul(pbr[:], onesrow(csb), row[0:1, TS:2 * TS], start=True, stop=True), tok=True)
                em.wait("vector", bct)
                for c in (2 * h, 2 * h + 1):
                    xc = src_sb[:, c * T + s * TS: c * T + (s + 1) * TS]
                    last_norm = em.v(lambda v, xc=xc: v.tensor_mul(xc, xc, pbr[:]))
                pbr_free = last_norm
        em.wait("sync", last_norm)
        store_fm(em, dram["q_n" if ti == 0 else "k_n"], src_sb[:], ch="l0prst")
    ev_tok = {0: None, 1: None}
    mm_free = {0: None, 1: None}
    for src_sb, oname in [(ks, "k_tm"), (vs, "v_tm")]:
        for h in range(NH):
            for tt in range(16):
                i = tt % 2
                em.wait("tensor", idt, last_norm, mm_free[i])
                trt = None
                for dc in range(2):
                    xc = src_sb[:, (2 * h + dc) * T + tt * 128: (2 * h + dc) * T + (tt + 1) * 128]
                    trt = em.t(lambda t, xc=xc, dc=dc, i=i: t.transpose(ptr[i][:, dc * 128:(dc + 1) * 128], xc, idb[:]),
                               tok=(dc == 1))
                em.wait("vector", trt, em.ch_all(f"l0prtt{i}"))
                ev = em.v(lambda v, i=i: v.tensor_copy(stg[i][:], ptr[i][:]))
                mm_free[i] = ev
                em.wait("sync", ev)
                dst = dram[oname][h * T + tt * 128: h * T + (tt + 1) * 128, :]
                em.dma(lambda e, dst=dst, i=i: e.dma_start(out=dst, in_=stg[i][:]), ch=f"l0prtt{i}")
    em.wait("sync", em.ch_all("l0prtt0"), em.ch_all("l0prtt1"), em.ch_all("l0prst"))


def stage_delta(nc, st, em, dram):
    C = 128
    NCHUNK = T // C
    sb = lambda nm, sh, dt: st.enter_context(nc.sbuf_tensor(f"dl_{nm}", sh, dt))
    qn = sb("qn", [128, NCD * T], BF16)
    kn = sb("kn", [128, NCD * T], BF16)
    bt = sb("bt", [128, 16 * 4], F32)
    csb = sb("cst", [128, CST_W], F32)
    idb = sb("idb", [128, 128], BF16)
    ktm = [sb(f"ktm{i}", [128, 256], BF16) for i in range(2)]
    vtm = [sb(f"vtm{i}", [128, 256], BF16) for i in range(2)]
    Ab = sb("A", [128, 128], BF16)
    BT = sb("BT", [128, 128], BF16)
    Xb = sb("X", [128, 512], BF16)
    wfm = sb("wfm", [128, 256], BF16)
    u2 = sb("u2", [128, 256], BF16)
    al = sb("al", [128, 128], BF16)
    s32 = sb("s32", [128, 512], F32)
    sbf = sb("sbf", [128, 512], BF16)
    dstg = [sb(f"dstg{i}", [128, 256], BF16) for i in range(2)]

    pG = st.enter_context(nc.psum_tensor("dl_pG", [128, 128], F32))
    pX = st.enter_context(nc.psum_tensor("dl_pX", [128, 512], F32))
    pB = st.enter_context(nc.psum_tensor("dl_pB", [128, 128], F32))
    pT = st.enter_context(nc.psum_tensor("dl_pT", [128, 256], BF16))
    pt1 = st.enter_context(nc.psum_tensor("dl_pt1", [128, 256], F32))
    pS = st.enter_context(nc.psum_tensor("dl_pS", [128, 512], F32))
    pO = st.enter_context(nc.psum_tensor("dl_pO", [128, 256], F32))

    load_fm(em, qn[:], dram["q_n"], ch="dl")
    load_fm(em, kn[:], dram["k_n"], ch="dl")
    em.dma(lambda e: e.dma_start(out=bt[:].rearrange("p (tt h) -> p tt h", tt=16),
                                 in_=dram["beta_tm"].rearrange("(tt p) h -> p tt h", p=128)), ch="dl")
    em.dma(lambda e: e.dma_start(out=csb[:], in_=dram["cst_f32"]), ch="dl")
    ld = em.ch_all("dl")
    em.wait("vector", ld)
    idt = em.v(lambda v: v.tensor_copy(idb[:], cst(csb, "ident")))

    def ldtile(h, n, wait_tok=None):
        i = n % 2
        em.wait("sync", wait_tok)
        em.dma(lambda e, h=h, n=n, i=i: e.dma_start(out=ktm[i][:], in_=dram["k_tm"][h * T + n * C: h * T + (n + 1) * C, :]), ch=f"dlt{i}")
        em.dma(lambda e, h=h, n=n, i=i: e.dma_start(out=vtm[i][:], in_=dram["v_tm"][h * T + n * C: h * T + (n + 1) * C, :]), ch=f"dlt{i}")
        return em.ch_all(f"dlt{i}")

    tok = ldtile(0, 0)
    tile_toks = {(0, 0): tok}
    last_store = {0: None, 1: None}
    for h in range(NH):
        zs = em.v(lambda v: v.memset(s32[:], 0.0))
        zb = em.v(lambda v: v.memset(sbf[:], 0.0))
        for n in range(NCHUNK):
            i = n % 2
            ltok = tile_toks.pop((h, n))
            em.wait("tensor", zb)
            nh, nn = (h, n + 1) if n + 1 < NCHUNK else (h + 1, 0)
            if nh < NH:
                tile_toks[(nh, nn)] = ldtile(nh, nn, wait_tok=zb)
            bcol = bt[:, n * 4 + h: n * 4 + h + 1]
            em.wait("tensor", ld)
            gt = None
            for dc in range(2):
                kc = kn[:, (2 * h + dc) * T + n * C: (2 * h + dc) * T + (n + 1) * C]
                gt = em.t(lambda t, kc=kc, dc=dc: t.matmul(pG[:], kc, kc, start=(dc == 0), stop=(dc == 1)),
                          tok=(dc == 1))
            em.wait("vector", gt, ltok)
            a1 = em.v(lambda v: v.tensor_mul(Ab[:], pG[:], cst(csb, "strict_low")))
            a2 = em.v(lambda v, bcol=bcol: v.tensor_scalar(Ab[:], Ab[:], bcol, -1.0, ALU.mult, ALU.mult))
            x1 = em.v(lambda v, i=i, bcol=bcol: v.tensor_scalar_mul(Xb[:, 0:256], vtm[i][:], bcol))
            x2 = em.v(lambda v, i=i, bcol=bcol: v.tensor_scalar_mul(Xb[:, 256:512], ktm[i][:], bcol))
            prevBT = None
            prevX = x2
            prevB = a2
            for p in range(7):
                em.wait("tensor", prevB, prevX, idt, prevBT)
                trt = em.t(lambda t: t.transpose(pT[:, 0:128], Ab[:], idb[:]), tok=True)
                em.wait("vector", trt)
                prevBT = em.v(lambda v: v.tensor_copy(BT[:], pT[:, 0:128]))
                em.wait("tensor", prevBT, prevX)
                em.t(lambda t: t.matmul(pX[:], idb[:], Xb[:], start=True, stop=False))
                xt = em.t(lambda t: t.matmul(pX[:], BT[:], Xb[:], start=False, stop=True), tok=True)
                if p < 6:
                    bsq = em.t(lambda t: t.matmul(pB[:], BT[:], Ab[:], start=True, stop=True), tok=True)
                em.wait("vector", xt)
                prevX = em.v(lambda v: v.tensor_copy(Xb[:], pX[:]))
                if p < 6:
                    em.wait("vector", bsq)
                    prevB = em.v(lambda v: v.tensor_copy(Ab[:], pB[:]))
            em.wait("tensor", prevX, prevBT)
            trt = None
            for dc in range(2):
                trt = em.t(lambda t, dc=dc: t.transpose(pT[:, 128 * dc:128 * (dc + 1)], Xb[:, 256 + 128 * dc: 256 + 128 * (dc + 1)], idb[:]), tok=(dc == 1))
            em.wait("vector", trt)
            wv = em.v(lambda v: v.tensor_copy(wfm[:], pT[:]))
            em.wait("tensor", wv, zb)
            tt1 = None
            for dc in range(2):
                tt1 = em.t(lambda t, dc=dc: t.matmul(pt1[:], wfm[:, 128 * dc:128 * (dc + 1)], sbf[:, 256 * dc:256 * (dc + 1)],
                                                     start=(dc == 0), stop=(dc == 1)), tok=(dc == 1))
            em.wait("vector", tt1)
            ut = em.v(lambda v: v.tensor_sub(u2[:], Xb[:, 0:256], pt1[:]))
            altt = None
            for dc in range(2):
                kc = kn[:, (2 * h + dc) * T + n * C: (2 * h + dc) * T + (n + 1) * C]
                qc = qn[:, (2 * h + dc) * T + n * C: (2 * h + dc) * T + (n + 1) * C]
                altt = em.t(lambda t, kc=kc, qc=qc, dc=dc: t.matmul(pG[:], kc, qc, start=(dc == 0), stop=(dc == 1)),
                            tok=(dc == 1))
            em.wait("vector", altt)
            alv = em.v(lambda v: v.tensor_mul(al[:], pG[:], cst(csb, "incl_up")))
            em.wait("tensor", alv, ut)
            for dc in range(2):
                qc = qn[:, (2 * h + dc) * T + n * C: (2 * h + dc) * T + (n + 1) * C]
                em.t(lambda t, qc=qc, dc=dc: t.matmul(pO[:], qc, sbf[:, 256 * dc:256 * (dc + 1)], start=(dc == 0), stop=False))
            ot = em.t(lambda t: t.matmul(pO[:], al[:], u2[:], start=False, stop=True), tok=True)
            stt = None
            for dc in range(2):
                stt = em.t(lambda t, dc=dc, i=i: t.matmul(pS[:, 256 * dc:256 * (dc + 1)], ktm[i][:, 128 * dc:128 * (dc + 1)], u2[:],
                                                          start=True, stop=True), tok=(dc == 1))
            em.wait("vector", ot)
            j = n % 2
            em.wait("vector", em.ch_all(f"dlst{j}"))
            ov = em.v(lambda v, j=j: v.tensor_copy(dstg[j][:], pO[:]))
            em.wait("sync", ov)
            dst = dram["delta_tm"][h * T + n * C: h * T + (n + 1) * C, :]
            em.dma(lambda e, dst=dst, j=j: e.dma_start(out=dst, in_=dstg[j][:]), ch=f"dlst{j}")
            em.wait("vector", stt)
            sv = em.v(lambda v: v.tensor_add(s32[:], s32[:], pS[:]))
            zb = em.v(lambda v: v.tensor_copy(sbf[:], s32[:]))
    em.wait("sync", em.ch_all("dlst0"), em.ch_all("dlst1"))


PAD = 63
CW = PAD + T


def stage_l0fir(nc, st, em, dram):
    sb = lambda nm, sh, dt: st.enter_context(nc.sbuf_tensor(f"l0f_{nm}", sh, dt))
    vpad = sb("vpad", [128, NCD * CW], BF16)
    lsb = sb("ls", [128, NCD * T], BF16)
    llb = sb("ll", [128, NCD * T], BF16)
    wsw = sb("wsw", [128, NCD * 5], F32)
    wlw = sb("wlw", [128, NCD * 64], F32)
    csb = sb("cst", [128, CST_W], F32)
    idb = sb("idb", [128, 128], BF16)
    accd = sb("accd", [128, T], F32)
    accp = sb("accp", [128, T], F32)
    tile = [sb(f"tile{i}", [128, 256], BF16) for i in range(2)]
    sqb = sb("sqb", [128, 256], F32)
    raw = sb("raw", [128, 12], F32)
    st16 = sb("st16", [128, 16], BF16)
    mtmp = sb("mtmp", [128, 4], F32)
    sstg = [sb(f"sstg{i}", [16, 128], BF16) for i in range(2)]

    ptr = st.enter_context(nc.psum_tensor("l0f_ptr", [128, 256], BF16))
    pst = st.enter_context(nc.psum_tensor("l0f_pst", [16, 128], BF16))

    zp = em.v(lambda v: v.memset(vpad[:], 0.0))
    em.wait("sync", zp)
    em.dma(lambda e: e.dma_start(
        out=vpad[:].rearrange("p (c t) -> p c t", c=NCD)[:, :, PAD:CW],
        in_=dram["v_s"].rearrange("(c p) t -> p c t", p=128)), ch="l0f")
    em.dma(lambda e: e.dma_start(out=wsw[:].rearrange("p (c j) -> p c j", c=NCD),
                                 in_=dram["L0_attn_fir_short_w"].rearrange("(c p) j -> p c j", p=128)), ch="l0f")
    em.dma(lambda e: e.dma_start(out=wlw[:].rearrange("p (c j) -> p c j", c=NCD),
                                 in_=dram["L0_attn_fir_long_w"].rearrange("(c p) j -> p c j", p=128)), ch="l0f")
    em.dma(lambda e: e.dma_start(out=csb[:], in_=dram["cst_f32"]), ch="l0f")
    ld = em.ch_all("l0f")
    em.wait("vector", ld)
    idt = em.v(lambda v: v.tensor_copy(idb[:], cst(csb, "ident")))

    em.wait("gpsimd", ld)
    for c in range(NCD):
        base = c * CW
        for j in range(5):
            wj = wsw[:, c * 5 + j: c * 5 + j + 1]
            xs_ = vpad[:, base + 59 + j: base + 59 + j + T]
            if j == 0:
                em.v(lambda v, xs_=xs_, wj=wj: v.tensor_scalar_mul(accd[:], xs_, wj))
            elif j < 4:
                em.v(lambda v, xs_=xs_, wj=wj: v.scalar_tensor_tensor(accd[:], xs_, wj, accd[:], ALU.mult, ALU.add))
            else:
                em.v(lambda v, xs_=xs_, wj=wj, c=c: v.scalar_tensor_tensor(
                    lsb[:, c * T:(c + 1) * T], xs_, wj, accd[:], ALU.mult, ALU.add))
        eng = em.v if c < 8 else em.g
        acc = accd if c < 8 else accp
        for j in range(64):
            wj = wlw[:, c * 64 + j: c * 64 + j + 1]
            xs_ = vpad[:, base + j: base + j + T]
            if j == 0:
                eng(lambda e, xs_=xs_, wj=wj, acc=acc: e.tensor_scalar_mul(acc[:], xs_, wj))
            elif j < 63:
                eng(lambda e, xs_=xs_, wj=wj, acc=acc: e.scalar_tensor_tensor(acc[:], xs_, wj, acc[:], ALU.mult, ALU.add))
            else:
                eng(lambda e, xs_=xs_, wj=wj, acc=acc, c=c: e.scalar_tensor_tensor(
                    llb[:, c * T:(c + 1) * T], xs_, wj, acc[:], ALU.mult, ALU.mult if False else ALU.add))
    fv = em.v(lambda v: v.engine_nop())
    fg = em.g(lambda g: g.engine_nop())

    ptr_free = None
    pst_free = None
    tile_free = [None, None]
    for h in range(NH):
        for tt in range(16):
            ib = tt % 2
            for q in range(4):
                if q < 2:
                    srcb = lsb if q == 0 else llb
                    em.wait("tensor", idt, fv, fg, ptr_free)
                    trt = None
                    for dc in range(2):
                        xc = srcb[:, (2 * h + dc) * T + tt * 128: (2 * h + dc) * T + (tt + 1) * 128]
                        trt = em.t(lambda t, xc=xc, dc=dc: t.transpose(ptr[:, dc * 128:(dc + 1) * 128], xc, idb[:]),
                                   tok=(dc == 1))
                    em.wait("vector", trt, tile_free[ib] if q == 0 else None)
                    tv = em.v(lambda v, ib=ib: v.tensor_copy(tile[ib][:], ptr[:]))
                    ptr_free = tv
                    ltok = tv
                else:
                    nm = "delta_tm" if q == 2 else "v_tm"
                    em.wait("sync", tile_free[ib])
                    em.dma(lambda e, nm=nm, h=h, tt=tt, ib=ib: e.dma_start(
                        out=tile[ib][:], in_=dram[nm][h * T + tt * 128: h * T + (tt + 1) * 128, :]), ch=f"l0ft{ib}")
                    ltok = em.ch_all(f"l0ft{ib}")
                    em.wait("vector", ltok)
                em.v(lambda v, ib=ib, q=q: v.reduce_sum(raw[:, 3 * q: 3 * q + 1], tile[ib][:], AX.X))
                em.wait("scalar", ltok)
                sqt = em.s(lambda sc, ib=ib: sc.activation(out=sqb[:], in_=tile[ib][:], func=AF.Square))
                em.wait("vector", sqt)
                em.v(lambda v, q=q: v.reduce_sum(raw[:, 3 * q + 1: 3 * q + 2], sqb[:], AX.X))
                lastr = em.v(lambda v, ib=ib, q=q: v.reduce_sum(raw[:, 3 * q + 2: 3 * q + 3], tile[ib][:], AX.X,
                                                               apply_absolute_value=True))
                tile_free[ib] = lastr
            s16v = lambda i: st16[:].rearrange("p (q s) -> p q s", s=4)[:, :, i:i + 1]
            rawv = lambda i: raw[:].rearrange("p (q s) -> p q s", s=3)[:, :, i:i + 1]
            em.v(lambda v: v.tensor_scalar_mul(s16v(0), rawv(0), 1.0 / 256))
            em.v(lambda v: v.tensor_scalar_mul(mtmp[:].rearrange("p (q s) -> p q s", s=1), rawv(0), 1.0 / 256))
            em.v(lambda v: v.tensor_mul(mtmp[:], mtmp[:], mtmp[:]))
            em.v(lambda v: v.scalar_tensor_tensor(s16v(1), rawv(1), 1.0 / 256,
                                                  mtmp[:].rearrange("p (q s) -> p q s", s=1), ALU.mult, ALU.subtract))
            fin = em.v(lambda v: v.tensor_scalar_mul(s16v(2), rawv(2), 1.0 / 256))
            em.wait("scalar", lastr, fin)
            sqr = em.s(lambda sc: sc.activation(out=s16v(3), in_=rawv(1), func=AF.Sqrt))
            em.wait("tensor", sqr, fin, pst_free)
            tst = em.t(lambda t: t.transpose(pst[:], st16[:], idb[:]), tok=True)
            jb = tt % 2
            em.wait("vector", tst, em.ch_all(f"l0fs{jb}"))
            sv = em.v(lambda v, jb=jb: v.tensor_copy(sstg[jb][:], pst[:]))
            pst_free = sv
            em.wait("sync", sv)
            dst = dram["stats_f"][0:16, h * T + tt * 128: h * T + (tt + 1) * 128]
            em.dma(lambda e, dst=dst, jb=jb: e.dma_start(out=dst, in_=sstg[jb][:]), ch=f"l0fs{jb}")
    st1 = store_fm(em, dram["ls_fm"], lsb[:], ch="l0fst")
    st2 = store_fm(em, dram["ll_fm"], llb[:], ch="l0fst")
    em.wait("sync", em.ch_all("l0fst"), em.ch_all("l0fs0"), em.ch_all("l0fs1"))


def stage_l0gate(nc, st, em, dram):
    sb = lambda nm, sh, dt: st.enter_context(nc.sbuf_tensor(f"l0g_{nm}", sh, dt))
    xnb = sb("xn", [128, NCD * T], BF16)
    w1a = sb("w1a", [128, NCD * 1024], BF16)
    w1b = sb("w1b", [16, 1024], BF16)
    b1 = sb("b1", [128, NCD], F32)
    w2 = sb("w2", [128, NCD * 4], BF16)
    b2r = sb("b2r", [1, 4], F32)
    ltr = sb("ltr", [1, 4], F32)
    rtr = sb("rtr", [1, 4], F32)
    stf = sb("stf", [16, NH * T], BF16)
    csb = sb("cst", [128, CST_W], F32)
    idb = sb("idb", [128, 128], BF16)
    h1s = sb("h1s", [128, NCD * TS], BF16)
    b2bc = sb("b2bc", [128, 4], F32)
    rtc = sb("rtc", [128, 4], F32)
    tsb = sb("tsb", [128, 4], F32)
    nmax = sb("nmax", [128, 1], F32)
    zsum = sb("zsum", [128, 1], F32)
    rz = sb("rz", [128, 1], F32)
    eb = sb("eb", [128, 4], F32)
    prt = sb("prt", [128, 4], BF16)
    pstg = [sb(f"pstg{i}", [4, 128], BF16) for i in range(2)]

    ph1 = [st.enter_context(nc.psum_tensor(f"l0g_ph1_{i}", [128, TS], F32)) for i in range(2)]
    plog = st.enter_context(nc.psum_tensor("l0g_plog", [128, 4], F32))
    pbc = st.enter_context(nc.psum_tensor("l0g_pbc", [128, 4], F32))
    ppr = st.enter_context(nc.psum_tensor("l0g_ppr", [4, 128], BF16))

    load_fm(em, xnb[:], dram["xn"], ch="l0g")
    em.dma(lambda e: e.dma_start(out=csb[:], in_=dram["cst_f32"]), ch="l0g")
    em.dma(lambda e: e.dma_start(out=stf[:], in_=dram["stats_f"]), ch="l0g")
    load_pp(em, b1[:], dram["L0_attn_mlp_b1"], ch="l0g")
    em.dma(lambda e: e.dma_start(out=b2r[:], in_=dram["L0_attn_mlp_b2"].rearrange("(o f) -> o f", o=1)), ch="l0g")
    em.dma(lambda e: e.dma_start(out=ltr[:], in_=dram["L0_attn_log_temp"].rearrange("(a h) one -> a (h one)", a=1)), ch="l0g")
    em.dma(lambda e: e.dma_start(out=w1a[:].rearrange("p (c m) -> p c m", c=NCD),
                                 in_=dram["L0_attn_mlp_w1"][0:1024, :].rearrange("(c p) m -> p c m", p=128)),
           q="gpsimd", ch="l0gw")
    em.dma(lambda e: e.dma_start(out=w1b[:], in_=dram["L0_attn_mlp_w1"][1024:1040, :]), q="gpsimd", ch="l0gw")
    em.dma(lambda e: e.dma_start(out=w2[:].rearrange("p (c m) -> p c m", c=NCD),
                                 in_=dram["L0_attn_mlp_w2"].rearrange("(c p) m -> p c m", p=128)),
           q="gpsimd", ch="l0gw")
    ld = em.ch_all("l0g")
    ldw = em.ch_all("l0gw")
    em.wait("vector", ld)
    idt = em.v(lambda v: v.tensor_copy(idb[:], cst(csb, "ident")))
    em.wait("scalar", ld)
    ext = em.s(lambda sc: sc.activation(out=rtr[:], in_=ltr[:], func=AF.Exp))
    em.wait("vector", ext)
    rct = em.v(lambda v: v.reciprocal(rtr[:], rtr[:]))
    em.wait("tensor", rct)
    bt1 = em.t(lambda t: t.matmul(pbc[:], onesrow(csb), b2r[:], start=True, stop=True), tok=True)
    em.wait("vector", bt1)
    bv1 = em.v(lambda v: v.tensor_copy(b2bc[:], pbc[:]))
    bv2 = bv1
    for h in range(NH):
        em.wait("tensor", bv2)
        bt2 = em.t(lambda t, h=h: t.matmul(pbc[:, 0:1], onesrow(csb), rtr[0:1, h:h + 1], start=True, stop=True), tok=True)
        em.wait("vector", bt2)
        bv2 = em.v(lambda v, h=h: v.tensor_copy(rtc[:, h:h + 1], pbc[:, 0:1]))

    ph_free = [None, None]
    plog_free = None
    ppr_free = None
    for h in range(NH):
        for s in range(NSTRIP):
            for mc in range(NCD):
                ib = mc % 2
                em.wait("tensor", ld, ldw, ph_free[ib])
                for kc in range(NCD):
                    wk = w1a[:, kc * 1024 + mc * 128: kc * 1024 + (mc + 1) * 128]
                    xk = xnb[:, kc * T + s * TS: kc * T + (s + 1) * TS]
                    em.t(lambda t, wk=wk, xk=xk, kc=kc, ib=ib: t.matmul(
                        ph1[ib][:], wk, xk, start=(kc == 0), stop=False))
                sk = stf[0:16, h * T + s * TS: h * T + (s + 1) * TS]
                w1bk = w1b[0:16, mc * 128: (mc + 1) * 128]
                mt = em.t(lambda t, sk=sk, w1bk=w1bk, ib=ib: t.matmul(
                    ph1[ib][:], w1bk, sk, start=False, stop=True), tok=True)
                em.wait("scalar", mt, ldw)
                b1c = b1[:, mc: mc + 1]
                gl = em.s(lambda sc, ib=ib, b1c=b1c, mc=mc: sc.activation(
                    out=h1s[:, mc * TS:(mc + 1) * TS], in_=ph1[ib][:], func=AF.Gelu, bias=b1c))
                ph_free[ib] = gl
            for tt in range(4):
                gtt = s * 4 + tt
                jb = gtt % 2
                em.wait("tensor", gl, plog_free)
                lt = None
                for mc in range(NCD):
                    hk = h1s[:, mc * TS + tt * 128: mc * TS + (tt + 1) * 128]
                    wk2 = w2[:, mc * 4: (mc + 1) * 4]
                    lt = em.t(lambda t, hk=hk, wk2=wk2, mc=mc: t.matmul(
                        plog[:], hk, wk2, start=(mc == 0), stop=(mc == NCD - 1)), tok=(mc == NCD - 1))
                em.wait("vector", lt, bv1, bv2)
                em.v(lambda v: v.tensor_add(tsb[:], plog[:], b2bc[:]))
                em.v(lambda v, h=h: v.tensor_scalar_mul(tsb[:], tsb[:], rtc[:, h:h + 1]))
                mx = em.v(lambda v: v.reduce_max(nmax[:], tsb[:], AX.X, negate=True))
                em.wait("scalar", mx)
                et = em.s(lambda sc: sc.activation(out=eb[:], in_=tsb[:], func=AF.Exp, bias=nmax[:, 0:1],
                                                   accum_out=zsum[:, 0:1]))
                em.wait("vector", et)
                plog_free = em.v(lambda v: v.reciprocal(rz[:], zsum[:]))
                em.v(lambda v: v.tensor_scalar_mul(rz[:], rz[:], 0.8))
                pv = em.v(lambda v: v.tensor_scalar(prt[:], eb[:], rz[:, 0:1], 0.05, ALU.mult, ALU.add))
                em.wait("tensor", pv, idt, ppr_free)
                ptk = em.t(lambda t: t.transpose(ppr[:], prt[:], idb[:]), tok=True)
                em.wait("vector", ptk, em.ch_all(f"l0gp{jb}"))
                sv = em.v(lambda v, jb=jb: v.tensor_copy(pstg[jb][:], ppr[:]))
                ppr_free = sv
                em.wait("sync", sv)
                dst = dram["probs_rows"][0:4, h * T + gtt * 128: h * T + (gtt + 1) * 128]
                em.dma(lambda e, dst=dst, jb=jb: e.dma_start(out=dst, in_=pstg[jb][:]), ch=f"l0gp{jb}")
    em.wait("sync", em.ch_all("l0gp0"), em.ch_all("l0gp1"))


def stage_l0mix(nc, st, em, dram):
    sb = lambda nm, sh, dt: st.enter_context(nc.sbuf_tensor(f"l0m_{nm}", sh, dt))
    lsb = sb("ls", [128, NCD * T], BF16)
    llb = sb("ll", [128, NCD * T], BF16)
    vsb = sb("vs", [128, NCD * T], BF16)
    onb = sb("on", [128, NCD * T], BF16)
    wob = sb("wo", [128, NCD * 1024], BF16)
    prw = sb("prw", [4, NH * T], BF16)
    selb = sb("sel", [32, CSTB_W], BF16)
    csb = sb("cst", [128, CST_W], F32)
    idb = sb("idb", [128, 128], BF16)
    onw = sb("onw", [128, 2], F32)
    dtile = [sb(f"dt{i}", [128, 4 * 256], BF16) for i in range(2)]
    dstg = sb("dstg", [128, 2 * TS], BF16)
    accb = sb("acc", [128, 2 * TS], F32)
    tmpb = sb("tmp", [128, TS], F32)
    sqb = sb("sq", [128, TS], F32)
    rowb = sb("row", [1, 2 * TS], F32)
    ystg = [sb(f"ystg{i}", [128, TS], F32) for i in range(2)]

    pbc = [st.enter_context(nc.psum_tensor(f"l0m_pbc{i}", [128, TS], F32)) for i in range(4)]
    pd = st.enter_context(nc.psum_tensor("l0m_pd", [128, TS], BF16))
    prow = st.enter_context(nc.psum_tensor("l0m_prow", [1, TS], F32))
    pbr = st.enter_context(nc.psum_tensor("l0m_pbr", [128, TS], F32))
    pwo = st.enter_context(nc.psum_tensor("l0m_pwo", [128, TS], F32))

    load_fm(em, lsb[:], dram["ls_fm"], ch="l0m")
    load_fm(em, llb[:], dram["ll_fm"], ch="l0m")
    load_fm(em, vsb[:], dram["v_s"], ch="l0m")
    em.dma(lambda e: e.dma_start(out=prw[:], in_=dram["probs_rows"]), ch="l0m")
    em.dma(lambda e: e.dma_start(out=selb[:], in_=dram["cst_bf16"]), ch="l0m")
    em.dma(lambda e: e.dma_start(out=csb[:], in_=dram["cst_f32"]), ch="l0m")
    load_pp(em, onw[:], dram["L0_attn_o_norm_w"], ch="l0m")
    em.dma(lambda e: e.dma_start(out=wob[:].rearrange("p (c m) -> p c m", c=NCD),
                                 in_=dram["L0_attn_Wo"].rearrange("(c p) m -> p c m", p=128)), q="gpsimd", ch="l0mw")
    ld = em.ch_all("l0m")
    ldw = em.ch_all("l0mw")
    em.wait("vector", ld)
    idt = em.v(lambda v: v.tensor_copy(idb[:], cst(csb, "ident")))

    eps1 = csb[0:1, CST["eps_ln"][0]:CST["eps_ln"][0] + 1]
    pd_free = None
    pbc_free = [None] * 4
    prow_free = None
    pbr_free = None
    acc_free = None
    dt_free = [None, None]
    for h in range(NH):
        for s in range(NSTRIP):
            ib = (h * NSTRIP + s) % 2
            em.wait("sync", dt_free[ib])
            em.dma(lambda e, h=h, s=s, ib=ib: e.dma_start(
                out=dtile[ib][:].rearrange("p (j d) -> p j d", j=4),
                in_=dram["delta_tm"][h * T + s * TS: h * T + (s + 1) * TS, :].rearrange("(j p) d -> p j d", p=128)),
                ch=f"l0md{ib}")
            dl = em.ch_all(f"l0md{ib}")
            bts = []
            for i in range(4):
                em.wait("tensor", ld, pbc_free[i])
                pr = prw[0:4, h * T + s * TS: h * T + (s + 1) * TS]
                sl = selb[0:4, i * 128: (i + 1) * 128]
                bts.append(em.t(lambda t, pr=pr, sl=sl, i=i: t.matmul(pbc[i][:], sl, pr, start=True, stop=True), tok=True))
            for dc in range(2):
                em.wait("tensor", idt, dl, pd_free)
                trt = None
                for j in range(4):
                    xc = dtile[ib][:, j * 256 + dc * 128: j * 256 + (dc + 1) * 128]
                    trt = em.t(lambda t, xc=xc, j=j: t.transpose(pd[:, j * 128:(j + 1) * 128], xc, idb[:]),
                               tok=(j == 3))
                dt_free[ib] = trt
                em.wait("vector", trt)
                pd_free = em.v(lambda v, dc=dc: v.tensor_copy(dstg[:, dc * TS:(dc + 1) * TS], pd[:]))
                c = 2 * h + dc
                em.wait("vector", bts[0], bts[1], bts[2], bts[3], acc_free)
                adst = accb[:, dc * TS:(dc + 1) * TS]
                em.v(lambda v, c=c, s=s, adst=adst: v.tensor_mul(
                    adst, lsb[:, c * T + s * TS: c * T + (s + 1) * TS], pbc[0][:]))
                for i, srcb in ((1, llb), (3, vsb)):
                    em.v(lambda v, c=c, s=s, srcb=srcb, i=i: v.tensor_mul(
                        tmpb[:], srcb[:, c * T + s * TS: c * T + (s + 1) * TS], pbc[i][:]))
                    em.v(lambda v, adst=adst: v.tensor_add(adst, adst, tmpb[:]))
                em.v(lambda v, dc=dc: v.tensor_mul(tmpb[:], dstg[:, dc * TS:(dc + 1) * TS], pbc[2][:]))
                mfin = em.v(lambda v, adst=adst: v.tensor_add(adst, adst, tmpb[:]))
                for i in range(4):
                    pbc_free[i] = mfin
            mmt = None
            for dc in range(2):
                em.wait("scalar", mfin, prow_free, mmt)
                sqt = em.s(lambda sc, dc=dc: sc.activation(out=sqb[:], in_=accb[:, dc * TS:(dc + 1) * TS], func=AF.Square))
                em.wait("tensor", sqt, prow_free)
                mmt = em.t(lambda t, dc=dc: t.matmul(prow[:], cst(csb, "ones"), sqb[:],
                                                     start=(dc == 0), stop=(dc == 1)), tok=True)
            em.wait("vector", mmt)
            em.v(lambda v: v.tensor_scalar_mul(rowb[:, 0:TS], prow[:], 1.0 / 256))
            rv = em.v(lambda v: v.engine_nop())
            prow_free = rv
            em.wait("scalar", rv)
            sq2 = em.s(lambda sc: sc.activation(out=rowb[:, TS:2 * TS], in_=rowb[:, 0:TS], func=AF.Sqrt, bias=eps1))
            em.wait("vector", sq2)
            rc = em.v(lambda v: v.reciprocal(rowb[:, 0:TS], rowb[:, TS:2 * TS]))
            em.wait("tensor", rc, pbr_free)
            bct = em.t(lambda t: t.matmul(pbr[:], onesrow(csb), rowb[0:1, 0:TS], start=True, stop=True), tok=True)
            em.wait("vector", bct)
            af = None
            for dc in range(2):
                c = 2 * h + dc
                oc = onb[:, c * T + s * TS: c * T + (s + 1) * TS]
                wcol = onw[:, dc: dc + 1]
                af = em.v(lambda v, oc=oc, wcol=wcol, dc=dc: v.scalar_tensor_tensor(
                    oc, accb[:, dc * TS:(dc + 1) * TS], wcol, pbr[:], ALU.mult, ALU.mult))
            pbr_free = af
            acc_free = af
    onfin = af
    pwo_free = None
    for mc in range(NCD):
        for s in range(NSTRIP):
            jb = (mc * NSTRIP + s) % 2
            em.wait("tensor", onfin, ldw, pwo_free)
            mt = None
            for kc in range(NCD):
                wk = wob[:, kc * 1024 + mc * 128: kc * 1024 + (mc + 1) * 128]
                ok = onb[:, kc * T + s * TS: kc * T + (s + 1) * TS]
                mt = em.t(lambda t, wk=wk, ok=ok, kc=kc: t.matmul(pwo[:], wk, ok,
                                                                  start=(kc == 0), stop=(kc == NCD - 1)),
                          tok=(kc == NCD - 1))
            em.wait("vector", mt, em.ch_all(f"l0my{jb}"))
            yv = em.v(lambda v, jb=jb: v.tensor_copy(ystg[jb][:], pwo[:]))
            pwo_free = yv
            em.wait("sync", yv)
            dst = dram["y_l0attn"][mc * 128:(mc + 1) * 128, s * TS:(s + 1) * TS]
            em.dma(lambda e, dst=dst, jb=jb: e.dma_start(out=dst, in_=ystg[jb][:]), ch=f"l0my{jb}")
    em.wait("sync", em.ch_all("l0my0"), em.ch_all("l0my1"))


def stage_l0moe(nc, st, em, dram, E=32):
    NI = 6
    sb = lambda nm, sh, dt: st.enter_context(nc.sbuf_tensor(f"l0e_{nm}", sh, dt))
    xnb = sb("xn", [128, NCD * T], BF16)
    acc = sb("acc", [128, NCD * T], F32)
    gwf = sb("gwf", [32, 1024], F32)
    gwT = sb("gwT", [128, NCD * 32], BF16)
    csb = sb("cst", [128, CST_W], F32)
    selb = sb("sel", [32, CSTB_W], BF16)
    idb = sb("idb", [128, 128], BF16)
    ebr = sb("ebr", [1, 32], F32)
    ebbc = sb("ebbc", [128, 32], F32)
    scb = sb("scb", [128, 32], F32)
    s0b = sb("s0b", [128, 32], F32)
    spb = sb("spb", [128, 32], F32)
    mcol = sb("mcol", [128, 1], F32)
    msku = sb("msku", [128, 32], U8)
    mskf = sb("mskf", [128, 32], F32)
    negb = sb("negb", [128, 32], F32)
    rzc = sb("rzc", [128, 1], F32)
    cmbt = sb("cmbt", [128, 32], BF16)
    cmbr = sb("cmbr", [32, T], BF16)
    w13 = [sb(f"w13_{i}", [128, NCD * 768 * 2], BF16) for i in range(2)]
    w2b = sb("w2b", [128, NI * 1024], BF16)
    silu_t = sb("silu", [128, TS], F32)
    t2 = sb("t2", [128, TS], F32)
    asb = sb("asb", [128, NI * TS], BF16)
    ystg = sb("ystg", [128, TS], F32)

    pr = st.enter_context(nc.psum_tensor("l0e_pr", [128, 32], F32))
    prt = st.enter_context(nc.psum_tensor("l0e_prt", [32, 128], BF16))
    peb = st.enter_context(nc.psum_tensor("l0e_peb", [128, 32], F32))
    pg = st.enter_context(nc.psum_tensor("l0e_pg", [128, TS], F32))
    pu = st.enter_context(nc.psum_tensor("l0e_pu", [128, TS], F32))
    pcmb = st.enter_context(nc.psum_tensor("l0e_pcmb", [128, TS], F32))
    py = [st.enter_context(nc.psum_tensor(f"l0e_py{i}", [128, TS], F32)) for i in range(2)]

    load_fm(em, xnb[:], dram["xn"], ch="l0e")
    em.dma(lambda e: e.dma_start(out=gwf[:], in_=dram["L0_ffn_gate_w"]), ch="l0e")
    em.dma(lambda e: e.dma_start(out=csb[:], in_=dram["cst_f32"]), ch="l0e")
    em.dma(lambda e: e.dma_start(out=selb[:], in_=dram["cst_bf16"]), ch="l0e")
    em.dma(lambda e: e.dma_start(out=ebr[:], in_=dram["L0_ffn_e_bias"].rearrange("(o f) -> o f", o=1)), ch="l0e")
    ld = em.ch_all("l0e")
    em.wait("vector", ld)
    idt = em.v(lambda v: v.tensor_copy(idb[:], cst(csb, "ident")))
    zacc = em.v(lambda v: v.memset(acc[:], 0.0))
    zneg = em.v(lambda v: v.memset(negb[:], -1e30))
    gt_free = None
    for c in range(NCD):
        em.wait("tensor", ld, gt_free)
        tg = em.t(lambda t, c=c: t.transpose(pr[:], gwf[0:32, c * 128:(c + 1) * 128],
                                             csb[0:32, CST["ident"][0]:CST["ident"][0] + 32]), tok=True)
        em.wait("vector", tg)
        gt_free = em.v(lambda v, c=c: v.tensor_copy(gwT[:, c * 32:(c + 1) * 32], pr[:, 0:32]))
    em.wait("tensor", gt_free)
    bt = em.t(lambda t: t.matmul(peb[:], onesrow(csb), ebr[:], start=True, stop=True), tok=True)
    em.wait("vector", bt)
    ebv = em.v(lambda v: v.tensor_copy(ebbc[:], peb[:]))
    pr_free = gt_free
    prt_free = None
    cm_fin = None
    for tt in range(16):
        em.wait("tensor", pr_free)
        mt = None
        for kc in range(NCD):
            xk = xnb[:, kc * T + tt * 128: kc * T + (tt + 1) * 128]
            gk = gwT[:, kc * 32:(kc + 1) * 32]
            mt = em.t(lambda t, xk=xk, gk=gk, kc=kc: t.matmul(pr[:], xk, gk,
                                                              start=(kc == 0), stop=(kc == NCD - 1)), tok=(kc == NCD - 1))
        em.wait("scalar", mt)
        sg = em.s(lambda sc: sc.activation(out=scb[:], in_=pr[:], func=AF.Sigmoid))
        pr_free = sg
        em.wait("vector", sg, ebv, zneg)
        em.v(lambda v: v.tensor_add(s0b[:], scb[:], ebbc[:]))
        em.v(lambda v: v.tensor_copy(spb[:], s0b[:]))
        for r in range(4):
            em.v(lambda v: v.reduce_max(mcol[:], spb[:], AX.X))
            if r < 3:
                em.v(lambda v: v.tensor_single_scalar(mskf[:], spb[:], mcol[:, 0:1], ALU.is_ge))
                em.v(lambda v: v.scalar_tensor_tensor(spb[:], mskf[:], -1e30, spb[:], ALU.mult, ALU.add))
        em.v(lambda v: v.tensor_single_scalar(mskf[:], s0b[:], mcol[:, 0:1], ALU.is_ge))
        em.v(lambda v: v.tensor_mul(mskf[:], mskf[:], scb[:]))
        em.v(lambda v: v.reduce_sum(rzc[:], mskf[:], AX.X))
        em.v(lambda v: v.reciprocal(rzc[:], rzc[:]))
        cmv = em.v(lambda v: v.tensor_scalar_mul(cmbt[:], mskf[:], rzc[:, 0:1]))
        em.wait("tensor", cmv, idt, prt_free)
        ct = em.t(lambda t: t.transpose(prt[:], cmbt[:], idb[:]), tok=True)
        em.wait("vector", ct)
        cm_fin = em.v(lambda v, tt=tt: v.tensor_copy(cmbr[:, tt * 128:(tt + 1) * 128], prt[:]))
        prt_free = cm_fin
    def ldw13(e):
        i = e % 2
        em.dma(lambda ee, e=e, i=i: ee.dma_start(
            out=w13[i][:, 0:NCD * 768].rearrange("p (c m) -> p c m", c=NCD),
            in_=dram["L0_ffn_w1"][e].rearrange("(c p) m -> p c m", p=128)), q="gpsimd", ch=f"l0ew{i}")
        em.dma(lambda ee, e=e, i=i: ee.dma_start(
            out=w13[i][:, NCD * 768:].rearrange("p (c m) -> p c m", c=NCD),
            in_=dram["L0_ffn_w3"][e].rearrange("(c p) m -> p c m", p=128)), q="gpsimd", ch=f"l0ew{i}")
        return em.ch_all(f"l0ew{i}")

    wtok = {0: ldw13(0)}
    last_w13_user = [None, None]
    w2_user = None
    pg_free, pu_free, pcmb_free = None, None, None
    py_free = [None, None]
    afin = None
    for e in range(E):
        i = e % 2
        if e + 1 < E:
            em.wait("gpsimd", last_w13_user[(e + 1) % 2])
            wtok[e + 1] = ldw13(e + 1)
        em.wait("gpsimd", w2_user)
        em.dma(lambda ee, e=e: ee.dma_start(
            out=w2b[:].rearrange("p (c m) -> p c m", c=NI),
            in_=dram["L0_ffn_w2"][e].rearrange("(c p) m -> p c m", p=128)), q="gpsimd", ch="l0ew2")
        w2tok = em.ch_all("l0ew2")
        wt = wtok.pop(e)
        for s in range(NSTRIP):
            em.wait("tensor", cm_fin, pcmb_free)
            sl = selb[0:32, e * 128:(e + 1) * 128]
            cr = cmbr[0:32, s * TS:(s + 1) * TS]
            cbt = em.t(lambda t, sl=sl, cr=cr: t.matmul(pcmb[:], sl, cr, start=True, stop=True), tok=True)
            for ic in range(NI):
                em.wait("tensor", wt, ld, pg_free, pu_free)
                for kc in range(NCD):
                    w1k = w13[i][:, kc * 768 + ic * 128: kc * 768 + (ic + 1) * 128]
                    xk = xnb[:, kc * T + s * TS: kc * T + (s + 1) * TS]
                    em.t(lambda t, w1k=w1k, xk=xk, kc=kc: t.matmul(pg[:], w1k, xk,
                                                                   start=(kc == 0), stop=(kc == NCD - 1)))
                gmt = None
                for kc in range(NCD):
                    w3k = w13[i][:, NCD * 768 + kc * 768 + ic * 128: NCD * 768 + kc * 768 + (ic + 1) * 128]
                    xk = xnb[:, kc * T + s * TS: kc * T + (s + 1) * TS]
                    gmt = em.t(lambda t, w3k=w3k, xk=xk, kc=kc: t.matmul(pu[:], w3k, xk,
                                                                         start=(kc == 0), stop=(kc == NCD - 1)),
                               tok=(kc == NCD - 1))
                em.wait("scalar", gmt)
                sil = em.s(lambda sc: sc.activation(out=silu_t[:], in_=pg[:], func=AF.Silu))
                pg_free = sil
                em.wait("vector", sil, cbt)
                pu_free = em.v(lambda v: v.tensor_mul(t2[:], silu_t[:], pu[:]))
                av = em.v(lambda v, ic=ic: v.tensor_mul(asb[:, ic * TS:(ic + 1) * TS], t2[:], pcmb[:]))
            pcmb_free = av
            for oc in range(NCD):
                jb = oc % 2
                em.wait("tensor", av, w2tok, py_free[jb])
                ymt = None
                for ic in range(NI):
                    w2k = w2b[:, ic * 1024 + oc * 128: ic * 1024 + (oc + 1) * 128]
                    ak = asb[:, ic * TS:(ic + 1) * TS]
                    ymt = em.t(lambda t, w2k=w2k, ak=ak, ic=ic, jb=jb: t.matmul(py[jb][:], w2k, ak,
                                                                                start=(ic == 0), stop=(ic == NI - 1)),
                               tok=(ic == NI - 1))
                w2_user = ymt
                last_w13_user[i] = ymt
                em.wait("vector", ymt, zacc)
                ac = acc[:, oc * T + s * TS: oc * T + (s + 1) * TS]
                afin = em.v(lambda v, ac=ac, jb=jb: v.tensor_add(ac, ac, py[jb][:]))
                py_free[jb] = afin
    em.wait("sync", afin)
    store_fm(em, dram["y_l0ffn"], acc[:], ch="l0est")
    em.wait("sync", em.ch_all("l0est"))


def stage_add3(nc, st, em, dram, na, nb, ncc, nout, uniq, dta=F32):
    sb = lambda nm, sh, dt: st.enter_context(nc.sbuf_tensor(f"{uniq}_{nm}", sh, dt))
    a = sb("a", [128, NCD * T], F32)
    b = sb("b", [128, NCD * T], F32)
    load_fm(em, a[:], dram[na], ch=uniq)
    load_fm(em, b[:], dram[nb], ch=uniq)
    ld = em.ch_all(uniq)
    em.wait("vector", ld)
    fin = None
    for i in range(2 * NCD):
        sl = slice(i * 1024, (i + 1) * 1024)
        fin = em.v(lambda v, sl=sl: v.tensor_add(a[:, sl], a[:, sl], b[:, sl]))
    em.wait("sync", fin)
    load_fm(em, b[:], dram[ncc], ch=uniq + "c")
    ld2 = em.ch_all(uniq + "c")
    em.wait("vector", ld2)
    for i in range(2 * NCD):
        sl = slice(i * 1024, (i + 1) * 1024)
        fin = em.v(lambda v, sl=sl: v.tensor_add(a[:, sl], a[:, sl], b[:, sl]))
    em.wait("sync", fin)
    store_fm(em, dram[nout], a[:], ch=uniq + "st")
    em.wait("sync", em.ch_all(uniq + "st"))


def stage_mm(nc, st, em, dram, name_in, wname, bname, name_out, Mout, uniq, out_dt=BF16):
    NMO = Mout // 128
    sb = lambda nm, sh, dt: st.enter_context(nc.sbuf_tensor(f"{uniq}_{nm}", sh, dt))
    x = sb("x", [128, NCD * T], BF16)
    w = sb("w", [128, NCD * Mout], BF16)
    bb = sb("b", [128, NMO], F32)
    stg = [sb(f"stg{i}", [128, TS], out_dt) for i in range(2)]
    ps = [st.enter_context(nc.psum_tensor(f"{uniq}_p{i}", [128, TS], F32)) for i in range(2)]
    load_fm(em, x[:], dram[name_in], ch=uniq)
    load_pp(em, bb[:], dram[bname], ch=uniq)
    em.dma(lambda e: e.dma_start(out=w[:].rearrange("p (c m) -> p c m", c=NCD),
                                 in_=dram[wname].rearrange("(c p) m -> p c m", p=128)), q="gpsimd", ch=uniq + "w")
    ld = em.ch_all(uniq)
    ldw = em.ch_all(uniq + "w")
    ps_free = [None, None]
    for mc in range(NMO):
        for s in range(NSTRIP):
            ib = (mc * NSTRIP + s) % 2
            em.wait("tensor", ld, ldw, ps_free[ib])
            mt = None
            for kc in range(NCD):
                wk = w[:, kc * Mout + mc * 128: kc * Mout + (mc + 1) * 128]
                xk = x[:, kc * T + s * TS: kc * T + (s + 1) * TS]
                mt = em.t(lambda t, wk=wk, xk=xk, kc=kc, ib=ib: t.matmul(
                    ps[ib][:], wk, xk, start=(kc == 0), stop=(kc == NCD - 1)), tok=(kc == NCD - 1))
            em.wait("scalar", mt, em.ch_all(f"{uniq}s{ib}"))
            bcol = bb[:, mc: mc + 1]
            ev = em.s(lambda sc, ib=ib, bcol=bcol: sc.add(stg[ib][:], ps[ib][:], bcol))
            ps_free[ib] = ev
            em.wait("sync", ev)
            dst = dram[name_out][mc * 128:(mc + 1) * 128, s * TS:(s + 1) * TS]
            em.dma(lambda e, dst=dst, ib=ib: e.dma_start(out=dst, in_=stg[ib][:]), ch=f"{uniq}s{ib}")
    em.wait("sync", em.ch_all(f"{uniq}s0"), em.ch_all(f"{uniq}s1"))


TPAD = 4


def stage_l1attn(nc, st, em, dram):
    NG = 4
    sb = lambda nm, sh, dt: st.enter_context(nc.sbuf_tensor(f"l1a_{nm}", sh, dt))
    xnb = sb("xn", [128, NCD * T], BF16)
    wq = sb("wq", [128, NCD * 768], BF16)
    bq = sb("bq", [128, 24], F32)
    CWD = T + 2 * TPAD
    qkv = sb("qkv", [128, 6 * CWD], BF16)
    bias = sb("biasb", [128, 16 * 128], F32)
    csb = sb("cst", [128, CST_W], F32)
    idb = sb("idb", [128, 128], BF16)
    s2 = sb("s2", [128, 128], F32)
    nmax = sb("nmax", [128, 1], F32)
    zs = sb("zs", [128, 1], F32)
    rz = sb("rz", [128, 1], F32)
    eb = sb("eb", [128, 128], BF16)
    ebt = sb("ebt", [128, 128], BF16)
    vt = sb("vt", [128, 64], BF16)
    obuf = sb("obuf", [128, NCD * CWD], BF16)
    wp = sb("wp", [128, NCD * 1024], BF16)
    bp = sb("bp", [128, NCD], F32)
    ystg = [sb(f"ystg{i}", [128, TS], F32) for i in range(2)]

    pS = st.enter_context(nc.psum_tensor("l1a_pS", [128, 128], F32))
    pt1 = st.enter_context(nc.psum_tensor("l1a_pt1", [128, 128], BF16))
    pt2 = st.enter_context(nc.psum_tensor("l1a_pt2", [128, 64], BF16))
    pO = st.enter_context(nc.psum_tensor("l1a_pO", [64, 128], F32))
    pY = [st.enter_context(nc.psum_tensor(f"l1a_pY{i}", [128, TS], F32)) for i in range(2)]

    load_fm(em, xnb[:], dram["xn1"], ch="l1a")
    em.dma(lambda e: e.dma_start(out=csb[:], in_=dram["cst_f32"]), ch="l1a")
    em.dma(lambda e: e.dma_start(out=bias[:].rearrange("p (h j) -> p h j", h=16),
                                 in_=dram["l1a_bias"].rearrange("h p j -> p h j")), ch="l1a")
    em.dma(lambda e: e.dma_start(out=wp[:].rearrange("p (c m) -> p c m", c=NCD),
                                 in_=dram["L1_attn_w_proj"].rearrange("(c p) m -> p c m", p=128)),
           q="gpsimd", ch="l1apw")
    load_pp(em, bp[:], dram["L1_attn_b_proj"], ch="l1a")
    load_pp(em, bq[:], dram["L1_attn_b_qkv"], ch="l1a")
    ld = em.ch_all("l1a")
    ldpw = em.ch_all("l1apw")
    em.wait("vector", ld)
    idt = em.v(lambda v: v.tensor_copy(idb[:], cst(csb, "ident")))

    pq_free = None
    o_done = None
    for g in range(NG):
        em.wait("gpsimd", pq_free)
        for j, off in enumerate([g * 256, 1024 + g * 256, 2048 + g * 256]):
            em.dma(lambda e, j=j, off=off: e.dma_start(
                out=wq[:, j * NCD * 256:(j + 1) * NCD * 256].rearrange("p (c m) -> p c m", c=NCD),
                in_=dram["L1_attn_w_qkv"][:, off:off + 256].rearrange("(c p) m -> p c m", p=128)),
                q="gpsimd", ch=f"l1aw{g % 2}")
        ldw = em.ch_all(f"l1aw{g % 2}")
        ldwb = ld
        mt = None
        for mc in range(6):
            for s in range(NSTRIP):
                jb = (mc * NSTRIP + s) % 2
                em.wait("tensor", ld, ldw, pq_free)
                em.wait("scalar", ldwb)
                for kc in range(NCD):
                    wk = wq[:, kc * 256 + (mc // 2) * NCD * 256 + (mc % 2) * 128:
                            kc * 256 + (mc // 2) * NCD * 256 + (mc % 2) * 128 + 128]
                    xk = xnb[:, kc * T + s * TS: kc * T + (s + 1) * TS]
                    mt = em.t(lambda t, wk=wk, xk=xk, kc=kc, jb=jb: t.matmul(
                        pY[jb][:], wk, xk, start=(kc == 0), stop=(kc == NCD - 1)), tok=(kc == NCD - 1))
                em.wait("scalar", mt)
                bcol = bq[:, (mc // 2) * 8 + g * 2 + (mc % 2): (mc // 2) * 8 + g * 2 + (mc % 2) + 1]
                dstq = qkv[:, mc * CWD + TPAD + s * TS: mc * CWD + TPAD + (s + 1) * TS]
                ev = em.s(lambda sc, jb=jb, bcol=bcol, dstq=dstq: sc.add(dstq, pY[jb][:], bcol))
                pq_free = ev
        wr = None
        em.wait("vector", ev)
        for mc in range(6):
            wr = em.v(lambda v, mc=mc: v.tensor_copy(
                qkv[:, mc * CWD + T + TPAD: mc * CWD + T + 2 * TPAD],
                qkv[:, mc * CWD + TPAD: mc * CWD + 2 * TPAD]))
        pS_free = None
        pt1_free = None
        pt2_free = None
        pO_free = None
        for hh in range(4):
            h = 4 * g + hh
            qoff = (hh // 2) * CWD + 0 * NCD * CWD
            koff = 2 * CWD + (hh // 2) * CWD
            voff = 4 * CWD + (hh // 2) * CWD
            pb = 64 * (hh % 2)
            for tt in range(16):
                em.wait("tensor", ev, wr, pS_free)
                ro = 2 * TPAD + 128 * tt
                qs_ = qkv[pb:pb + 64, qoff + ro: qoff + ro + 128]
                ks_ = qkv[pb:pb + 64, koff + ro: koff + ro + 128]
                st_ = em.t(lambda t, qs_=qs_, ks_=ks_: t.matmul(pS[:], qs_, ks_, start=True, stop=True), tok=True)
                em.wait("vector", st_)
                bsl = bias[:, h * 128:(h + 1) * 128]
                em.v(lambda v, bsl=bsl: v.scalar_tensor_tensor(s2[:], pS[:], 0.125, bsl, ALU.mult, ALU.add))
                mx = em.v(lambda v: v.reduce_max(nmax[:], s2[:], AX.X, negate=True))
                em.wait("scalar", mx)
                et = em.s(lambda sc: sc.activation(out=eb[:], in_=s2[:], func=AF.Exp, bias=nmax[:, 0:1],
                                                   accum_out=zs[:, 0:1]))
                em.wait("vector", et)
                em.v(lambda v: v.reciprocal(rz[:], zs[:]))
                pv = em.v(lambda v: v.tensor_scalar_mul(eb[:], eb[:], rz[:, 0:1]))
                pS_free = pv
                em.wait("tensor", pv, idt, pt1_free)
                t1 = em.t(lambda t: t.transpose(pt1[:], eb[:], idb[:]), tok=True)
                vs_ = qkv[pb:pb + 64, voff + ro: voff + ro + 128]
                em.wait("tensor", pt2_free)
                t2_ = em.t(lambda t, vs_=vs_, pb=pb: t.transpose(pt2[:], vs_, idb[pb:pb + 64, pb:pb + 64]), tok=True)
                em.wait("vector", t1, t2_)
                e1 = em.v(lambda v: v.tensor_copy(ebt[:], pt1[:]))
                pt1_free = e1
                e2 = em.v(lambda v: v.tensor_copy(vt[:], pt2[:, 0:64]))
                pt2_free = e2
                em.wait("tensor", e1, e2, pO_free)
                ot = em.t(lambda t: t.matmul(pO[:], vt[:], ebt[:], start=True, stop=True), tok=True)
                em.wait("vector", ot)
                dsto = obuf[pb:pb + 64, (hh // 2 + 2 * g) * CWD + ro: (hh // 2 + 2 * g) * CWD + ro + 128]
                pO_free = em.v(lambda v, dsto=dsto: v.tensor_copy(dsto, pO[:]))
            o_done = pO_free
    uw = None
    for mc in range(NCD):
        em.wait("vector", o_done)
        uw = em.v(lambda v, mc=mc: v.tensor_copy(obuf[:, mc * CWD + TPAD: mc * CWD + 2 * TPAD],
                                                 obuf[:, mc * CWD + T + TPAD: mc * CWD + T + 2 * TPAD]))
    pY_free = [None, None]
    for mc in range(NCD):
        for s in range(NSTRIP):
            jb = (mc * NSTRIP + s) % 2
            em.wait("tensor", uw, ldpw, pY_free[jb])
            mt = None
            for kc in range(NCD):
                wk = wp[:, kc * 1024 + mc * 128: kc * 1024 + (mc + 1) * 128]
                ok = obuf[:, kc * CWD + TPAD + s * TS: kc * CWD + TPAD + (s + 1) * TS]
                mt = em.t(lambda t, wk=wk, ok=ok, kc=kc, jb=jb: t.matmul(
                    pY[jb][:], wk, ok, start=(kc == 0), stop=(kc == NCD - 1)), tok=(kc == NCD - 1))
            em.wait("scalar", mt, em.ch_all(f"l1ay{jb}"))
            bcol = bp[:, mc: mc + 1]
            ev = em.s(lambda sc, jb=jb, bcol=bcol: sc.add(ystg[jb][:], pY[jb][:], bcol))
            pY_free[jb] = ev
            em.wait("sync", ev)
            dst = dram["y_l1attn"][mc * 128:(mc + 1) * 128, s * TS:(s + 1) * TS]
            em.dma(lambda e, dst=dst, jb=jb: e.dma_start(out=dst, in_=ystg[jb][:]), ch=f"l1ay{jb}")
    em.wait("sync", em.ch_all("l1ay0"), em.ch_all("l1ay1"))


def build_l1a_bias(bias_table):
    WS = 8
    coords = np.stack(np.meshgrid(np.arange(WS), np.arange(WS), indexing="ij"))
    cf = coords.reshape(2, -1)
    rel = cf[:, :, None] - cf[:, None, :]
    rel = rel.transpose(1, 2, 0).astype(np.int64)
    rel[..., 0] += WS - 1
    rel[..., 1] += WS - 1
    rel[..., 0] *= 2 * WS - 1
    idx = rel.sum(-1)[:WS, :WS]
    bias = np.asarray(bias_table)[idx.reshape(-1)].reshape(WS, WS, 16).transpose(2, 0, 1)
    out = np.full((16, 128, 128), -1e30, dtype=np.float32)
    for w in range(16):
        out[:, 8 * w:8 * w + 8, 8 * w:8 * w + 8] = bias
    return out


def stage_l1fattn(nc, st, em, dram):
    NHE, DH = 16, 64
    sb = lambda nm, sh, dt: st.enter_context(nc.sbuf_tensor(f"l1f_{nm}", sh, dt))
    qh = [sb(f"qh{i}", [64, T], BF16) for i in range(2)]
    kh = [sb(f"kh{i}", [64, T], BF16) for i in range(2)]
    vh = [sb(f"vh{i}", [64, T], BF16) for i in range(2)]
    vt = sb("vt", [128, 16 * DH], BF16)
    Eb = sb("E", [128, T], BF16)
    ET = sb("ET", [128, T], BF16)
    mx4 = sb("mx4", [128, 4], F32)
    nmax = sb("nmax", [128, 1], F32)
    zp = sb("zp", [128, 4], F32)
    zs = sb("zs", [128, 1], F32)
    rz = sb("rz", [128, 1], F32)
    ctx = sb("ctx", [128, NCD * T], BF16)
    csb = sb("cst", [128, CST_W], F32)
    idb = sb("idb", [128, 128], BF16)

    pS = [st.enter_context(nc.psum_tensor(f"l1f_pS{i}", [128, TS], F32)) for i in range(4)]
    pT = st.enter_context(nc.psum_tensor("l1f_pT", [128, TS], BF16))
    pO = st.enter_context(nc.psum_tensor("l1f_pO", [64, 128], F32))

    em.dma(lambda e: e.dma_start(out=csb[:], in_=dram["cst_f32"]), ch="l1f")
    ld = em.ch_all("l1f")
    em.wait("vector", ld)
    idt = em.v(lambda v: v.tensor_copy(idb[:], cst(csb, "ident")))

    def ldhead(h):
        i = h % 2
        for nm, buf, row in (("q", qh, 0), ("k", kh, 1024), ("v", vh, 2048)):
            em.dma(lambda e, buf=buf, row=row, h=h, i=i: e.dma_start(
                out=buf[i][:], in_=dram["qkv2"][row + h * DH: row + (h + 1) * DH, :]), ch=f"l1fh{i}")
        return em.ch_all(f"l1fh{i}")

    htok = {0: ldhead(0)}
    buf_user = [None, None]
    pT_free = None
    pS_free = [None] * 4
    pO_free = None
    for h in range(NHE):
        i = h % 2
        ltok = htok.pop(h)
        if h + 1 < NHE:
            em.wait("sync", buf_user[(h + 1) % 2])
            htok[h + 1] = ldhead(h + 1)
        vtv = None
        for c in range(16):
            em.wait("tensor", ltok, idt, pT_free)
            tv = em.t(lambda t, c=c, i=i: t.transpose(pT[:, (c % 8) * 64:(c % 8) * 64 + 64],
                                                      vh[i][0:64, c * 128:(c + 1) * 128],
                                                      idb[0:64, 0:64]), tok=((c % 8) == 7))
            if (c % 8) == 7:
                em.wait("vector", tv)
                half = c // 8
                vtv = em.v(lambda v, half=half: v.tensor_copy(vt[:, half * 512:(half + 1) * 512], pT[:]))
                pT_free = vtv
        for qt in range(16):
            smt = [None] * 4
            for ks in range(4):
                em.wait("tensor", ltok, pS_free[ks])
                qs_ = qh[i][0:64, qt * 128:(qt + 1) * 128]
                ks_ = kh[i][0:64, ks * TS:(ks + 1) * TS]
                smt[ks] = em.t(lambda t, qs_=qs_, ks_=ks_, ks2=ks: t.matmul(pS[ks2][:], qs_, ks_,
                                                                            start=True, stop=True), tok=True)
            em.wait("vector", *smt)
            for ks in range(4):
                em.v(lambda v, ks=ks: v.reduce_max(mx4[:, ks:ks + 1], pS[ks][:], AX.X))
            mxt = em.v(lambda v: v.reduce_max(nmax[:], mx4[:], AX.X, negate=True))
            em.wait("scalar", mxt)
            ets = None
            for ks in range(4):
                ets = em.s(lambda sc, ks=ks: sc.activation(out=Eb[:, ks * TS:(ks + 1) * TS], in_=pS[ks][:],
                                                           func=AF.Exp, scale=0.125, bias=nmax[:, 0:1],
                                                           accum_out=zp[:, ks:ks + 1]))
                pS_free[ks] = ets
            em.wait("vector", ets)
            em.v(lambda v: v.reduce_sum(zs[:], zp[:], AX.X))
            rzt = em.v(lambda v: v.reciprocal(rz[:], zs[:]))
            nrm = None
            for ks in range(4):
                nrm = em.v(lambda v, ks=ks: v.tensor_scalar_mul(Eb[:, ks * TS:(ks + 1) * TS],
                                                                Eb[:, ks * TS:(ks + 1) * TS], rz[:, 0:1]))
            for grp in range(4):
                em.wait("tensor", nrm, pT_free)
                trt = None
                for j in range(4):
                    c = grp * 4 + j
                    trt = em.t(lambda t, c=c, j=j: t.transpose(pT[:, j * 128:(j + 1) * 128],
                                                               Eb[:, c * 128:(c + 1) * 128], idb[:]),
                               tok=(j == 3))
                em.wait("vector", trt)
                pT_free = em.v(lambda v, grp=grp: v.tensor_copy(ET[:, grp * TS:(grp + 1) * TS], pT[:]))
            em.wait("tensor", pT_free, pO_free)
            omt = None
            for c in range(16):
                vk = vt[:, c * DH:(c + 1) * DH]
                ek = ET[:, c * 128:(c + 1) * 128]
                omt = em.t(lambda t, vk=vk, ek=ek, c=c: t.matmul(pO[:], vk, ek,
                                                                 start=(c == 0), stop=(c == 15)), tok=(c == 15))
            buf_user[i] = omt
            em.wait("vector", omt)
            dstc = ctx[64 * (h % 2):64 * (h % 2) + 64, (h // 2) * T + qt * 128: (h // 2) * T + (qt + 1) * 128]
            pO_free = em.v(lambda v, dstc=dstc: v.tensor_copy(dstc, pO[:]))
    em.wait("sync", pO_free)
    store_fm(em, dram["ctx"], ctx[:], ch="l1fst")
    em.wait("sync", em.ch_all("l1fst"))


def stage_l1h(nc, st, em, dram):
    sb = lambda nm, sh, dt: st.enter_context(nc.sbuf_tensor(f"l1h_{nm}", sh, dt))
    cb = sb("ctx", [128, NCD * T], BF16)
    xb = sb("x", [128, NCD * T], F32)
    wb = sb("w", [128, NCD * 1024], BF16)
    bo = sb("bo", [128, NCD], F32)
    stg = [sb(f"stg{i}", [128, TS], F32) for i in range(2)]
    ps = [st.enter_context(nc.psum_tensor(f"l1h_p{i}", [128, TS], F32)) for i in range(2)]
    load_fm(em, cb[:], dram["ctx"], ch="l1h")
    load_fm(em, xb[:], dram["xn1_32"], ch="l1h")
    load_pp(em, bo[:], dram["L1_ffn_b_o"], ch="l1h")
    em.dma(lambda e: e.dma_start(out=wb[:].rearrange("p (c m) -> p c m", c=NCD),
                                 in_=dram["L1_ffn_w_o"].rearrange("(c p) m -> p c m", p=128)), q="gpsimd", ch="l1hw")
    ld = em.ch_all("l1h")
    ldw = em.ch_all("l1hw")
    ps_free = [None, None]
    for mc in range(NCD):
        for s in range(NSTRIP):
            ib = (mc * NSTRIP + s) % 2
            em.wait("tensor", ld, ldw, ps_free[ib])
            mt = None
            for kc in range(NCD):
                wk = wb[:, kc * 1024 + mc * 128: kc * 1024 + (mc + 1) * 128]
                ck = cb[:, kc * T + s * TS: kc * T + (s + 1) * TS]
                mt = em.t(lambda t, wk=wk, ck=ck, kc=kc, ib=ib: t.matmul(
                    ps[ib][:], wk, ck, start=(kc == 0), stop=(kc == NCD - 1)), tok=(kc == NCD - 1))
            em.wait("vector", mt, ld, em.ch_all(f"l1hs{ib}"))
            bcol = bo[:, mc: mc + 1]
            xk = xb[:, mc * T + s * TS: mc * T + (s + 1) * TS]
            ev = em.v(lambda v, ib=ib, bcol=bcol, xk=xk: v.scalar_tensor_tensor(
                stg[ib][:], ps[ib][:], bcol, xk, ALU.add, ALU.add))
            ps_free[ib] = ev
            em.wait("sync", ev)
            dst = dram["h_ffn"][mc * 128:(mc + 1) * 128, s * TS:(s + 1) * TS]
            em.dma(lambda e, dst=dst, ib=ib: e.dma_start(out=dst, in_=stg[ib][:]), ch=f"l1hs{ib}")
    em.wait("sync", em.ch_all("l1hs0"), em.ch_all("l1hs1"))


def stage_l1moe(nc, st, em, dram, E=8):
    NI2 = 32
    sb = lambda nm, sh, dt: st.enter_context(nc.sbuf_tensor(f"l1m_{nm}", sh, dt))
    hbuf = sb("hb", [128, NCD * T], F32)
    xs = sb("xs", [128, NCD * TS], BF16)
    hx = sb("hx", [128, NI2 * TS], BF16)
    wbuf = [sb(f"wb{i}", [128, 16 * 1024], BF16) for i in range(2)]
    gwb = sb("gw", [128, NCD * 8], BF16)
    b1b = sb("b1", [128, E * NI2], F32)
    b2b = sb("b2", [128, E * NCD], F32)
    web = sb("web", [128, TS], BF16)
    wer = sb("wer", [8, T], BF16)
    selb = sb("sel", [32, CSTB_W], BF16)
    csb = sb("cst", [128, CST_W], F32)
    idb = sb("idb", [128, 128], BF16)
    prb = sb("prb", [128, 8], F32)
    spb = sb("spb", [128, 8], F32)
    mcol = sb("mcol", [128, 1], F32)
    mskf = sb("mskf", [128, 8], F32)
    nmax = sb("nmax", [128, 1], F32)
    zs = sb("zs", [128, 1], F32)
    rz = sb("rz", [128, 1], F32)
    wet = sb("wet", [128, 8], F32)
    tb = sb("tb", [128, TS], F32)

    pp = [st.enter_context(nc.psum_tensor(f"l1m_p{i}", [128, TS], F32)) for i in range(8)]

    load_fm(em, hbuf[:], dram["h_ffn"], ch="l1m")
    em.dma(lambda e: e.dma_start(out=gwb[:].rearrange("p (c m) -> p c m", c=NCD),
                                 in_=dram["L1_ffn_w_gate"].rearrange("(c p) m -> p c m", p=128)), q="gpsimd", ch="l1mg")
    em.dma(lambda e: e.dma_start(out=selb[:], in_=dram["cst_bf16"]), ch="l1m")
    em.dma(lambda e: e.dma_start(out=csb[:], in_=dram["cst_f32"]), ch="l1m")
    em.dma(lambda e: e.dma_start(out=b1b[:].rearrange("p (e c) -> p e c", e=E),
                                 in_=dram["L1_ffn_b1"].rearrange("e (c p) -> p e c", p=128),
                                 allow_slow_non_contiguous=True), ch="l1m")
    em.dma(lambda e: e.dma_start(out=b2b[:].rearrange("p (e c) -> p e c", e=E),
                                 in_=dram["L1_ffn_b2"].rearrange("e (c p) -> p e c", p=128),
                                 allow_slow_non_contiguous=True), ch="l1m")
    ld = em.ch_all("l1m")
    ldg = em.ch_all("l1mg")
    em.wait("vector", ld)
    idt = em.v(lambda v: v.tensor_copy(idb[:], cst(csb, "ident")))

    def ldw(name, e, hf, slot, guard):
        em.wait("gpsimd", guard)
        if name == "L1_ffn_w1":
            srcap = dram[name][e][:, hf * 2048:(hf + 1) * 2048].rearrange("(c p) m -> p c m", p=128)
            dstap = wbuf[slot][:].rearrange("p (c m) -> p c m", c=NCD)
        else:
            srcap = dram[name][e][hf * 2048:(hf + 1) * 2048, :].rearrange("(c p) m -> p c m", p=128)
            dstap = wbuf[slot][:].rearrange("p (c m) -> p c m", c=16)
        em.dma(lambda ee, srcap=srcap, dstap=dstap: ee.dma_start(out=dstap, in_=srcap),
               q="gpsimd", ch=f"l1mw{slot}")
        return em.ch_all(f"l1mw{slot}")

    pr_free = None
    prt_free = None
    xl_user = None
    acc_fin = None
    wslot_user = [None, None]
    for s in range(NSTRIP):
        em.wait("sync", xl_user)
        em.dma(lambda e, s=s: e.dma_start(
            out=xs[:].rearrange("p (c t) -> p c t", c=NCD),
            in_=dram["xln2"].rearrange("(c p) t -> p c t", p=128)[:, :, s * TS:(s + 1) * TS]), ch="l1mx")
        lx = em.ch_all("l1mx")
        wrt = None
        for tt in range(4):
            em.wait("tensor", lx, ldg, pr_free)
            mt = None
            for kc in range(NCD):
                xk = xs[:, kc * TS + tt * 128: kc * TS + (tt + 1) * 128]
                gk = gwb[:, kc * 8:(kc + 1) * 8]
                mt = em.t(lambda t, xk=xk, gk=gk, kc=kc: t.matmul(pp[0][:, 0:8], xk, gk,
                                                                  start=(kc == 0), stop=(kc == NCD - 1)),
                          tok=(kc == NCD - 1))
            em.wait("vector", mt)
            em.v(lambda v: v.reduce_max(nmax[:], pp[0][:, 0:8], AX.X, negate=True))
            mxt = em.v(lambda v: v.engine_nop())
            em.wait("scalar", mxt)
            et = em.s(lambda sc: sc.activation(out=prb[:], in_=pp[0][:, 0:8], func=AF.Exp, bias=nmax[:, 0:1],
                                               accum_out=zs[:, 0:1]))
            pr_free = et
            em.wait("vector", et)
            em.v(lambda v: v.reciprocal(rz[:], zs[:]))
            em.v(lambda v: v.tensor_scalar_mul(prb[:], prb[:], rz[:, 0:1]))
            em.v(lambda v: v.tensor_copy(spb[:], prb[:]))
            em.v(lambda v: v.reduce_max(mcol[:], spb[:], AX.X))
            em.v(lambda v: v.tensor_single_scalar(mskf[:], spb[:], mcol[:, 0:1], ALU.is_ge))
            em.v(lambda v: v.scalar_tensor_tensor(spb[:], mskf[:], -1e30, spb[:], ALU.mult, ALU.add))
            em.v(lambda v: v.reduce_max(mcol[:], spb[:], AX.X))
            em.v(lambda v: v.tensor_single_scalar(mskf[:], prb[:], mcol[:, 0:1], ALU.is_ge))
            em.v(lambda v: v.tensor_mul(mskf[:], mskf[:], prb[:]))
            em.v(lambda v: v.reduce_sum(zs[:], mskf[:], AX.X))
            em.v(lambda v: v.reciprocal(rz[:], zs[:]))
            wev = em.v(lambda v: v.tensor_scalar_mul(wet[:], mskf[:], rz[:, 0:1]))
            em.wait("tensor", wev, idt, prt_free)
            trt = em.t(lambda t: t.transpose(pp[1][0:8, 0:128], wet[:, 0:8], cst(csb, "ident")), tok=True)
            em.wait("vector", trt)
            wrt = em.v(lambda v, s=s, tt=tt: v.tensor_copy(wer[0:8, s * TS + tt * 128: s * TS + (tt + 1) * 128],
                                                           pp[1][0:8, 0:128]))
            prt_free = wrt
        for e in range(E):
            w1t0 = ldw("L1_ffn_w1", e, 0, 0, wslot_user[0])
            w1t1 = ldw("L1_ffn_w1", e, 1, 1, wslot_user[1])
            gt = None
            gel_user = [None, None]
            for hf in range(2):
                wt = w1t0 if hf == 0 else w1t1
                for ic in range(16):
                    gic = hf * 16 + ic
                    jb = gic % 2
                    em.wait("tensor", lx, wt, wrt, gel_user[jb], pr_free if gic == 0 else None)
                    mt = None
                    for kc in range(NCD):
                        wk = wbuf[hf][:, kc * 2048 + ic * 128: kc * 2048 + (ic + 1) * 128]
                        xk = xs[:, kc * TS:(kc + 1) * TS]
                        mt = em.t(lambda t, wk=wk, xk=xk, kc=kc, jb=jb: t.matmul(
                            pp[jb][:], wk, xk, start=(kc == 0), stop=(kc == NCD - 1)), tok=(kc == NCD - 1))
                    em.wait("scalar", mt, ld)
                    bcol = b1b[:, e * NI2 // 4 + gic // 4: e * NI2 // 4 + gic // 4 + 1] if False else \
                        b1b[:, e * 8 + gic // 4: e * 8 + gic // 4 + 1]
                    gt = em.s(lambda sc, jb=jb, bcol=bcol, gic=gic: sc.activation(
                        out=hx[:, gic * TS:(gic + 1) * TS], in_=pp[jb][:], func=AF.Gelu_apprx_tanh, bias=bcol))
                    gel_user[jb] = gt
                wslot_user[hf] = gt
            em.wait("tensor", wrt, gt)
            sl = selb[0:8, e * 128:(e + 1) * 128]
            cr = wer[0:8, s * TS:(s + 1) * TS]
            cbt = em.t(lambda t, sl=sl, cr=cr: t.matmul(pp[0][:], sl, cr, start=True, stop=True), tok=True)
            em.wait("vector", cbt)
            webv = em.v(lambda v: v.tensor_copy(web[:], pp[0][:]))
            w2t0 = ldw("L1_ffn_w2", e, 0, 0, wslot_user[0])
            ymt = None
            for hf in range(2):
                w2t = w2t0 if hf == 0 else ldw("L1_ffn_w2", e, 1, 1, wslot_user[1])
                for oc in range(NCD):
                    em.wait("tensor", w2t, gt, webv if hf == 0 else None)
                    for ic in range(16):
                        gic = hf * 16 + ic
                        wk = wbuf[hf][:, ic * 1024 + oc * 128: ic * 1024 + (oc + 1) * 128]
                        hk = hx[:, gic * TS:(gic + 1) * TS]
                        ymt = em.t(lambda t, wk=wk, hk=hk, hf=hf, ic=ic, oc=oc: t.matmul(
                            pp[oc][:], wk, hk, start=(hf == 0 and ic == 0), stop=(hf == 1 and ic == 15)),
                            tok=(ic == 15))
                wslot_user[hf] = ymt
            afin = None
            for oc in range(NCD):
                em.wait("vector", ymt, webv, ld)
                b2c = b2b[:, e * NCD + oc: e * NCD + oc + 1]
                em.v(lambda v, oc=oc, b2c=b2c: v.scalar_tensor_tensor(tb[:], pp[oc][:], b2c, web[:],
                                                                      ALU.add, ALU.mult))
                hk = hbuf[:, oc * T + s * TS: oc * T + (s + 1) * TS]
                afin = em.v(lambda v, hk=hk: v.tensor_add(hk, hk, tb[:]))
            acc_fin = afin
            pr_free = afin
            prt_free = afin
        xl_user = acc_fin
    em.wait("sync", acc_fin)
    store_fm(em, dram["y_l1ffn"], hbuf[:], ch="l1mst")
    em.wait("sync", em.ch_all("l1mst"))


SCR = {
    "xn": ([D, T], BF16), "q_s": ([D, T], BF16), "k_s": ([D, T], BF16), "v_s": ([D, T], BF16),
    "beta_tm": ([T, 4], F32), "q_n": ([D, T], BF16), "k_n": ([D, T], BF16),
    "k_tm": ([NH * T, DK], BF16), "v_tm": ([NH * T, DV], BF16), "delta_tm": ([NH * T, DV], BF16),
    "ls_fm": ([D, T], BF16), "ll_fm": ([D, T], BF16), "stats_f": ([16, NH * T], BF16),
    "probs_rows": ([4, NH * T], BF16), "y_l0attn": ([D, T], F32), "y_l0ffn": ([D, T], F32),
    "x1": ([D, T], F32), "xn1": ([D, T], BF16), "xn1_32": ([D, T], F32),
    "y_l1attn": ([D, T], F32), "xln1": ([D, T], BF16), "qkv2": ([3 * D, T], BF16),
    "ctx": ([D, T], BF16), "h_ffn": ([D, T], F32), "xln2": ([D, T], BF16),
    "y_l1ffn": ([D, T], F32), "x2out": ([D, T], F32),
}
CONSTS = {"cst_f32": ([128, CST_W], F32), "cst_bf16": ([32, CSTB_W], BF16)}
MODIN = {
    "xcore": ([D, T], F32),
    "L0_attn_Wq": ([D, D], F32), "L0_attn_Wk": ([D, D], F32), "L0_attn_Wv": ([D, D], F32),
    "L0_attn_Wb": ([D, 4], F32),
    "L0_attn_conv_q_w": ([D, 4], F32), "L0_attn_conv_k_w": ([D, 4], F32), "L0_attn_conv_v_w": ([D, 4], F32),
    "L0_attn_fir_short_w": ([D, 5], F32), "L0_attn_fir_long_w": ([D, 64], F32),
    "L0_attn_mlp_w1": ([1040, 1024], F32), "L0_attn_mlp_b1": ([1024], F32),
    "L0_attn_mlp_w2": ([1024, 4], F32), "L0_attn_mlp_b2": ([4], F32),
    "L0_attn_log_temp": ([4, 1], F32), "L0_attn_o_norm_w": ([256], F32), "L0_attn_Wo": ([D, D], F32),
    "L0_ffn_gate_w": ([32, D], F32), "L0_ffn_w1": ([32, D, 768], F32), "L0_ffn_w3": ([32, D, 768], F32),
    "L0_ffn_w2": ([32, 768, D], F32), "L0_ffn_e_bias": ([32], F32),
    "l1a_bias": ([16, 128, 128], F32),
    "L1_attn_w_qkv": ([D, 3 * D], F32), "L1_attn_b_qkv": ([3 * D], F32),
    "L1_attn_w_proj": ([D, D], F32), "L1_attn_b_proj": ([D], F32),
    "L1_ffn_ln1_w": ([D], F32), "L1_ffn_ln1_b": ([D], F32),
    "L1_ffn_ln2_w": ([D], F32), "L1_ffn_ln2_b": ([D], F32),
    "L1_ffn_w_qkv": ([D, 3 * D], F32), "L1_ffn_b_qkv": ([3 * D], F32),
    "L1_ffn_w_o": ([D, D], F32), "L1_ffn_b_o": ([D], F32),
    "L1_ffn_w_gate": ([D, 8], F32), "L1_ffn_w1": ([8, D, 4096], F32), "L1_ffn_b1": ([8, 4096], F32),
    "L1_ffn_w2": ([8, 4096, D], F32), "L1_ffn_b2": ([8, D], F32),
}

STAGES = [
    ("ln0", lambda nc, st, em, dram: stage_ln(nc, st, em, dram, "xcore", "xn", uniq="ln0"),
     ["xcore"], ["xn"]),
    ("l0qkv", stage_l0qkv,
     ["xn", "L0_attn_Wq", "L0_attn_Wk", "L0_attn_Wv", "L0_attn_Wb",
      "L0_attn_conv_q_w", "L0_attn_conv_k_w", "L0_attn_conv_v_w"],
     ["q_s", "k_s", "v_s", "beta_tm"]),
    ("l0prep", stage_l0prep, ["q_s", "k_s", "v_s"], ["q_n", "k_n", "k_tm", "v_tm"]),
    ("delta", stage_delta, ["q_n", "k_n", "k_tm", "v_tm", "beta_tm"], ["delta_tm"]),
    ("l0fir", stage_l0fir,
     ["v_s", "delta_tm", "v_tm", "L0_attn_fir_short_w", "L0_attn_fir_long_w"],
     ["ls_fm", "ll_fm", "stats_f"]),
    ("l0gate", stage_l0gate,
     ["xn", "stats_f", "L0_attn_mlp_w1", "L0_attn_mlp_b1", "L0_attn_mlp_w2",
      "L0_attn_mlp_b2", "L0_attn_log_temp"], ["probs_rows"]),
    ("l0mix", stage_l0mix,
     ["ls_fm", "ll_fm", "v_s", "delta_tm", "probs_rows", "L0_attn_o_norm_w", "L0_attn_Wo"],
     ["y_l0attn"]),
    ("l0moe", stage_l0moe,
     ["xn", "L0_ffn_gate_w", "L0_ffn_e_bias", "L0_ffn_w1", "L0_ffn_w3", "L0_ffn_w2"],
     ["y_l0ffn"]),
    ("mid", lambda nc, st, em, dram: stage_add3(nc, st, em, dram, "xcore", "y_l0attn", "y_l0ffn", "x1", "m1"),
     ["xcore", "y_l0attn", "y_l0ffn"], ["x1"]),
    ("ln1", lambda nc, st, em, dram: stage_ln(nc, st, em, dram, "x1", "xn1", also_f32="xn1_32", uniq="ln1"),
     ["x1"], ["xn1", "xn1_32"]),
    ("l1attn", stage_l1attn,
     ["xn1", "l1a_bias", "L1_attn_w_qkv", "L1_attn_b_qkv", "L1_attn_w_proj", "L1_attn_b_proj"],
     ["y_l1attn"]),
    ("lnf1", lambda nc, st, em, dram: stage_ln(nc, st, em, dram, "xn1", "xln1", gamma="L1_ffn_ln1_w",
                                              beta="L1_ffn_ln1_b", in_dt=BF16, uniq="lnf1"),
     ["xn1", "L1_ffn_ln1_w", "L1_ffn_ln1_b"], ["xln1"]),
    ("qkv2", lambda nc, st, em, dram: stage_mm(nc, st, em, dram, "xln1", "L1_ffn_w_qkv", "L1_ffn_b_qkv",
                                              "qkv2", 3 * D, "qk2"),
     ["xln1", "L1_ffn_w_qkv", "L1_ffn_b_qkv"], ["qkv2"]),
    ("l1fattn", stage_l1fattn, ["qkv2"], ["ctx"]),
    ("l1h", stage_l1h, ["ctx", "xn1_32", "L1_ffn_w_o", "L1_ffn_b_o"], ["h_ffn"]),
    ("lnf2", lambda nc, st, em, dram: stage_ln(nc, st, em, dram, "h_ffn", "xln2", gamma="L1_ffn_ln2_w",
                                              beta="L1_ffn_ln2_b", uniq="lnf2"),
     ["h_ffn", "L1_ffn_ln2_w", "L1_ffn_ln2_b"], ["xln2"]),
    ("l1moe", stage_l1moe,
     ["h_ffn", "xln2", "L1_ffn_w_gate", "L1_ffn_w1", "L1_ffn_b1", "L1_ffn_w2", "L1_ffn_b2"],
     ["y_l1ffn"]),
    ("final", lambda nc, st, em, dram: stage_add3(nc, st, em, dram, "x1", "y_l1attn", "y_l1ffn", "x2out", "fin"),
     ["x1", "y_l1attn", "y_l1ffn"], ["x2out"]),
]


def prep_host_inputs(inputs):
    hs = inputs["L0_attn_hidden_states"]
    shared = {}
    for nm in MODIN:
        if nm in ("xcore", "l1a_bias"):
            continue
        if nm == "L0_attn_fir_short_w":
            shared[nm] = np.ascontiguousarray(inputs[nm].reshape(1024, 5))
        elif nm == "L0_attn_fir_long_w":
            shared[nm] = np.ascontiguousarray(inputs[nm].reshape(1024, 64))
        else:
            shared[nm] = np.ascontiguousarray(inputs[nm])
    shared["l1a_bias"] = build_l1a_bias(inputs["L1_attn_bias_table"])
    shared["cst_f32"] = build_const_f32()
    shared["cst_bf16"] = build_const_bf16()
    xcores = [np.ascontiguousarray(hs[c].T).astype(np.float32, copy=False) for c in range(NCORES)]
    return xcores, shared


def kernel(**inputs):
    xcores, shared = prep_host_inputs(inputs)
    per_core = [dict(xcore=xcores[c]) for c in range(NCORES)]
    for name, fn, ins_, outs_ in STAGES:
        isp = {}
        for nm in ins_ + ["cst_f32", "cst_bf16"]:
            if nm in SCR:
                isp[nm] = SCR[nm]
            elif nm in MODIN:
                isp[nm] = MODIN[nm]
            else:
                isp[nm] = CONSTS[nm]
        osp = {nm: SCR[nm] for nm in outs_}
        in_maps = []
        for c in range(NCORES):
            m = {}
            for nm in isp:
                if nm in shared:
                    m[nm] = shared[nm]
                else:
                    m[nm] = per_core[c][nm]
            in_maps.append(m)
        res = run_stages([fn], in_maps, isp, osp)
        for c in range(NCORES):
            for nm in outs_:
                per_core[c][nm] = np.asarray(res.results[c][nm])
    out = np.stack([per_core[c]["x2out"].T for c in range(NCORES)], axis=0)
    return np.ascontiguousarray(out.astype(np.float32, copy=False))


def kernel_unfused(**inputs):
    xcores, shared = prep_host_inputs(inputs)
    per_core = [dict(xcore=xcores[c]) for c in range(NCORES)]
    for name, fn, ins_, outs_ in STAGES:
        isp = {}
        for nm in ins_ + ["cst_f32", "cst_bf16"]:
            isp[nm] = SCR.get(nm) or MODIN.get(nm) or CONSTS.get(nm)
        osp = {nm: SCR[nm] for nm in outs_}
        in_maps = []
        for c in range(NCORES):
            in_maps.append({nm: (shared[nm] if nm in shared else per_core[c][nm]) for nm in isp})
        res = run_stages([fn], in_maps, isp, osp)
        for c in range(NCORES):
            for nm in outs_:
                per_core[c][nm] = np.asarray(res.results[c][nm])
    out = np.stack([per_core[c]["x2out"].T for c in range(NCORES)], axis=0)
    return np.ascontiguousarray(out.astype(np.float32, copy=False))


def build_fused():
    nc = bass.Bass("TRN2", target_bir_lowering=False, debug=False)
    dram = {}
    for nm, (sh, dt) in {**MODIN, **CONSTS}.items():
        dram[nm] = nc.dram_tensor(nm, list(sh), dt, kind="ExternalInput").ap()
    for nm, (sh, dt) in SCR.items():
        kind = "ExternalOutput" if nm == "x2out" else "Internal"
        dram[nm] = nc.dram_tensor(nm, list(sh), dt, kind=kind).ap()
    with ExitStack() as st_outer:
        em = Em(nc, st_outer)
        for name, fn, ins_, outs_ in STAGES:
            with ExitStack() as st_stage:
                fn(nc, st_stage, em, dram)
            em.barrier()
        em.finalize()
    return nc


def kernel(**inputs):
    xcores, shared = prep_host_inputs(inputs)
    nc = build_fused()
    in_maps = [{**shared, "xcore": xcores[c]} for c in range(NCORES)]
    res = run_bass_kernel_spmd(nc, in_maps, core_ids=list(range(NCORES)))
    out = np.stack([np.asarray(res.results[c]["x2out"]).T for c in range(NCORES)], axis=0)
    return np.ascontiguousarray(out.astype(np.float32, copy=False))


# version 06
# speedup vs baseline: 1.1640x; 1.1640x over previous
import numpy as np
import ml_dtypes
from contextlib import ExitStack
import concourse.bass as bass
import concourse.mybir as mybir
from concourse.bass_utils import run_bass_kernel_spmd

F32 = mybir.dt.float32
BF16 = mybir.dt.bfloat16
U8 = mybir.dt.uint8
AF = mybir.ActivationFunctionType
ALU = mybir.AluOpType
AX = mybir.AxisListType

NCORES = 8
T = 2048
D = 1024
NCD = D // 128
TS = 512
NSTRIP = T // TS
NH, DK, DV = 4, 256, 256

ENG = ("sync", "tensor", "vector", "scalar", "gpsimd")


class Em:
    def __init__(self, nc, st):
        self.nc, self.st = nc, st
        self.prog = {e: [] for e in ENG}
        self.sem_h = {}
        self.sem_c = {}
        self.last_wait = {e: {} for e in ENG}

    def _sem(self, name):
        if name not in self.sem_h:
            self.sem_h[name] = self.st.enter_context(self.nc.semaphore(name))
            self.sem_c[name] = 0
        return self.sem_h[name]

    AUTO_CHAIN = ("vector", "scalar", "gpsimd")

    def _op(self, engine, fn, tok, dma):
        if engine in self.AUTO_CHAIN and not dma:
            name = "c_" + engine
            h = self._sem(name)
            before = self.sem_c[name]
            self.wait(engine, (name, before))
            self.sem_c[name] = before + 1
            self.prog[engine].append(lambda eng: fn(eng).then_inc(h, 1))
            return (name, before + 1)
        if not tok:
            self.prog[engine].append(fn)
            return None
        name = ("d_" if dma else "c_") + engine
        h = self._sem(name)
        inc = 16 if dma else 1
        self.sem_c[name] += inc
        val = self.sem_c[name]
        self.prog[engine].append(lambda eng: fn(eng).then_inc(h, inc))
        return (name, val)

    def t(self, fn, tok=False):
        return self._op("tensor", fn, tok, False)

    def v(self, fn, tok=False):
        return self._op("vector", fn, tok, False)

    def s(self, fn, tok=False):
        return self._op("scalar", fn, tok, False)

    def g(self, fn, tok=False):
        return self._op("gpsimd", fn, tok, False)

    def tfence(self):
        return self._op("tensor", lambda t: t.nop(), True, False)

    def vfence(self):
        return self._op("vector", lambda v: v.engine_nop(), True, False)

    def gfence(self):
        return self._op("gpsimd", lambda g: g.engine_nop(), True, False)

    def dma(self, fn, q="sync", ch=None):
        ch = "dch_" + (ch or q)
        h = self._sem(ch)
        self.sem_c[ch] += 16
        val = self.sem_c[ch]
        self.prog[q].append(lambda eng: fn(eng).then_inc(h, 16))
        return (ch, val)

    def ch_all(self, ch):
        ch = "dch_" + ch
        return (ch, self.sem_c.get(ch, 0))

    def wait(self, engine, *toks):
        for tk in toks:
            if tk is None:
                continue
            name, val = tk
            if val <= self.last_wait[engine].get(name, 0):
                continue
            self.last_wait[engine][name] = val
            h = self.sem_h[name]
            self.prog[engine].append(lambda eng, h=h, v=val: eng.wait_ge(h, v))

    def barrier(self):
        waits = [(nm, cnt) for nm, cnt in self.sem_c.items() if cnt > 0]
        for e in ENG:
            if self.prog[e] or True:
                self.wait(e, *waits)

    def finalize(self):
        with self.nc.Block() as block:
            for e in ENG:
                ops = self.prog[e]
                if ops:
                    getattr(block, e)(lambda eng, ops=ops: [op(eng) for op in ops])


class Banks:

    def __init__(self, em, tiles):
        self.em = em
        self.tiles = tiles
        self.free_tok = [None] * len(tiles)
        self.i = 0

    def take(self):
        i = self.i % len(self.tiles)
        self.i += 1
        return i, self.tiles[i], self.free_tok[i]

    def consumed(self, i, tok):
        self.free_tok[i] = tok


CST = {}
_w = 0
def _c(name, width):
    global _w
    CST[name] = (_w, width)
    _w += width
_c("ones", 1)
_c("ident", 128)
_c("strict_low", 128)
_c("incl_up", 128)
_c("eps_ln", 1)
_c("eps_l2", 1)
CST_W = _w


def build_const_f32():
    c = np.zeros((128, CST_W), dtype=np.float32)
    c[:, CST["ones"][0]] = 1.0
    o, _ = CST["ident"]
    c[:, o:o + 128] = np.eye(128, dtype=np.float32)
    o, _ = CST["strict_low"]
    c[:, o:o + 128] = np.tril(np.ones((128, 128), np.float32), -1)
    o, _ = CST["incl_up"]
    c[:, o:o + 128] = np.triu(np.ones((128, 128), np.float32), 0)
    c[:, CST["eps_ln"][0]] = 1e-5
    c[:, CST["eps_l2"][0]] = 1e-6
    return c


CSTB_W = 32 * 128


def build_const_bf16():
    import ml_dtypes as _md
    c = np.zeros((32, CSTB_W), dtype=_md.bfloat16)
    for r in range(32):
        c[r, r * 128:(r + 1) * 128] = 1.0
    return c


def cst(csb, name):
    o, w = CST[name]
    return csb[:, o:o + w]


def onesrow(csb):
    o, _ = CST["incl_up"]
    return csb[0:1, o:o + 128]


def load_pp(em, dst_sb, dram_ap, ch=None):
    return em.dma(lambda e: e.dma_start(out=dst_sb, in_=dram_ap.rearrange("(c p) -> p c", p=128),
                                        allow_slow_non_contiguous=True), ch=ch)


def load_fm(em, dst_sb, dram_ap, q="sync", ch=None):
    c = dram_ap.shape[0] // 128
    return em.dma(lambda e: e.dma_start(out=dst_sb.rearrange("p (c t) -> p c t", c=c),
                                        in_=dram_ap.rearrange("(c p) t -> p c t", p=128)), q=q, ch=ch)


def store_fm(em, dram_ap, src_sb, q="sync", ch=None):
    c = dram_ap.shape[0] // 128
    return em.dma(lambda e: e.dma_start(out=dram_ap.rearrange("(c p) t -> p c t", p=128),
                                        in_=src_sb.rearrange("p (c t) -> p c t", c=c)), q=q, ch=ch)


def stage_ln(nc, st, em, dram, name_in, name_out, gamma=None, beta=None, also_f32=None,
             in_dt=F32, uniq=None):
    u = uniq or f"ln_{name_in}"
    sb = lambda nm, sh, dt: st.enter_context(nc.sbuf_tensor(f"{u}_{nm}", sh, dt))
    xs = sb("xs", [128, NCD * T], in_dt)
    xnb = sb("xnb", [128, NCD * T], BF16)
    xn32 = sb("xn32", [128, NCD * T], F32) if also_f32 else None
    sq = sb("sq", [128, TS], F32)
    rows = sb("rows", [1, 6 * TS], F32)
    csb = sb("cst", [128, CST_W], F32)
    gsb = sb("g", [128, NCD], F32) if gamma else None
    bsb = sb("b", [128, NCD], F32) if beta else None
    tmp = sb("tmp", [128, TS], F32)

    pm = st.enter_context(nc.psum_tensor(f"{u}_pm", [1, TS], F32))
    pv = st.enter_context(nc.psum_tensor(f"{u}_pv", [1, TS], F32))
    pbm = st.enter_context(nc.psum_tensor(f"{u}_pbm", [128, TS], F32))
    pbr = st.enter_context(nc.psum_tensor(f"{u}_pbr", [128, TS], F32))

    onesb = sb("onesb", [128, 1], BF16) if in_dt == BF16 else None
    load_fm(em, xs[:], dram[name_in], ch=u)
    em.dma(lambda e: e.dma_start(out=csb[:], in_=dram["cst_f32"]), ch=u)
    if gamma:
        load_pp(em, gsb[:], dram[gamma], ch=u)
    if beta:
        load_pp(em, bsb[:], dram[beta], ch=u)
    ld = ldc = ldg = ldb = em.ch_all(u)
    ones_ap = cst(csb, "ones")
    ocast = None
    if onesb is not None:
        em.wait("vector", ld)
        ocast = em.v(lambda v: v.tensor_copy(onesb[:], cst(csb, "ones")))
        ones_ap = onesb[:]

    stat_free = None
    bc_free = None
    for s in range(NSTRIP):
        xstrip = lambda c: xs[:, c * T + s * TS: c * T + (s + 1) * TS]
        em.wait("tensor", ld, ldc, stat_free, ocast)
        for c in range(NCD):
            xc = xstrip(c)
            em.t(lambda t, xc=xc, c=c: t.matmul(pm[:], ones_ap, xc,
                                                start=(c == 0), stop=(c == NCD - 1)))
        prev_mm = None
        for c in range(NCD):
            xc = xstrip(c)
            em.wait("scalar", ld, stat_free, prev_mm)
            sqt = em.s(lambda sc, xc=xc: sc.activation(out=sq[:], in_=xc, func=AF.Square), tok=True)
            em.wait("tensor", sqt)
            prev_mm = em.t(lambda t, c=c: t.matmul(pv[:], cst(csb, "ones"), sq[:],
                                                   start=(c == 0), stop=(c == NCD - 1)), tok=True)
        tok_mm = prev_mm
        em.wait("vector", tok_mm)
        em.v(lambda v: v.tensor_scalar_mul(rows[:, 0:TS], pm[:], 1.0 / D))
        em.v(lambda v: v.tensor_scalar_mul(rows[:, TS:2 * TS], pv[:], 1.0 / D))
        em.v(lambda v: v.tensor_mul(rows[:, 2 * TS:3 * TS], rows[:, 0:TS], rows[:, 0:TS]))
        stat_free = em.v(lambda v: v.tensor_sub(rows[:, 3 * TS:4 * TS], rows[:, TS:2 * TS], rows[:, 2 * TS:3 * TS]), tok=True)
        em.wait("scalar", stat_free)
        epsap = csb[0:1, CST["eps_ln"][0]:CST["eps_ln"][0] + 1]
        sqt2 = em.s(lambda sc, epsap=epsap: sc.activation(out=rows[:, 4 * TS:5 * TS], in_=rows[:, 3 * TS:4 * TS],
                                             func=AF.Sqrt, bias=epsap), tok=True)
        em.wait("vector", sqt2)
        rt = em.v(lambda v: v.reciprocal(rows[:, 5 * TS:6 * TS], rows[:, 4 * TS:5 * TS]), tok=True)
        em.wait("tensor", rt, bc_free)
        em.t(lambda t: t.matmul(pbm[:], onesrow(csb), rows[0:1, 0:TS], start=True, stop=True))
        bt = em.t(lambda t: t.matmul(pbr[:], onesrow(csb), rows[0:1, 5 * TS:6 * TS], start=True, stop=True), tok=True)
        em.wait("vector", bt, ldg, ldb)
        for c in range(NCD):
            xc = xstrip(c)
            oc = xnb[:, c * T + s * TS: c * T + (s + 1) * TS]
            oc32 = xn32[:, c * T + s * TS: c * T + (s + 1) * TS] if also_f32 else None
            em.v(lambda v, xc=xc: v.tensor_sub(tmp[:], xc, pbm[:]))
            if gamma:
                em.v(lambda v: v.tensor_mul(tmp[:], tmp[:], pbr[:]))
                gcol = gsb[:, c:c + 1]
                bcol = bsb[:, c:c + 1]
                em.v(lambda v, oc=oc, gcol=gcol, bcol=bcol: v.tensor_scalar(oc, tmp[:], gcol, bcol, ALU.mult, ALU.add))
                if also_f32:
                    em.v(lambda v, oc32=oc32, gcol=gcol, bcol=bcol: v.tensor_scalar(oc32, tmp[:], gcol, bcol, ALU.mult, ALU.add))
            else:
                em.v(lambda v, oc=oc: v.tensor_mul(oc, tmp[:], pbr[:]))
                if also_f32:
                    em.v(lambda v, oc32=oc32: v.tensor_mul(oc32, tmp[:], pbr[:]))
        bc_free = em.vfence()
    em.wait("sync", bc_free)
    store_fm(em, dram[name_out], xnb[:], ch=u + "st")
    if also_f32:
        store_fm(em, dram[also_f32], xn32[:], ch=u + "st")
    em.wait("sync", em.ch_all(u + "st"))


def stage_l0qkv(nc, st, em, dram, parts=('qkv', 'beta')):
    sb = lambda nm, sh, dt: st.enter_context(nc.sbuf_tensor(f"l0qkv_{nm}", sh, dt))
    xnb = sb("xn", [128, NCD * T], BF16)
    wsb = sb("w", [128, NCD * 1024], BF16)
    wbb = sb("wb", [128, NCD * 4], BF16)
    cwsb = sb("cw", [128, NCD * 4 * 3], F32)
    pre = sb("pre", [128, 3 + T], F32)
    conv = sb("conv", [128, T], F32)
    outb = sb("out", [128, NCD * T], BF16)
    beta_sb = sb("beta", [128, 16 * 4], F32)

    pb = [st.enter_context(nc.psum_tensor(f"l0qkv_p{i}", [128, TS], F32)) for i in range(4)]
    pbeta = st.enter_context(nc.psum_tensor("l0qkv_pbeta", [128, 4], F32))

    load_fm(em, xnb[:], dram["xn"], ch="l0qkv")
    for i, nm in enumerate(["L0_attn_conv_q_w", "L0_attn_conv_k_w", "L0_attn_conv_v_w"]):
        em.dma(lambda e, nm=nm, i=i: e.dma_start(
            out=cwsb[:, i * NCD * 4:(i + 1) * NCD * 4].rearrange("p (c j) -> p c j", c=NCD),
            in_=dram[nm].rearrange("(c p) j -> p c j", p=128)), ch="l0qkv")
    em.dma(lambda e: e.dma_start(out=wbb[:].rearrange("p (c m) -> p c m", c=NCD),
                                 in_=dram["L0_attn_Wb"].rearrange("(c p) m -> p c m", p=128)), q="gpsimd", ch="l0qkvb")
    ldx = em.ch_all("l0qkv")
    ldcw = [ldx, ldx, ldx]
    ldwb = em.ch_all("l0qkvb")

    banks = Banks(em, pb)
    zpad = em.v(lambda v: v.memset(pre[:, 0:3], 0.0), tok=True)
    last_mm = None
    tensors = [("L0_attn_Wq", "q_s"), ("L0_attn_Wk", "k_s"), ("L0_attn_Wv", "v_s")] if 'qkv' in parts else []
    for ti, (wname, oname) in enumerate(tensors):
        em.wait("gpsimd", last_mm)
        em.dma(lambda e, wname=wname: e.dma_start(
            out=wsb[:].rearrange("p (c m) -> p c m", c=NCD),
            in_=dram[wname].rearrange("(c p) m -> p c m", p=128)), q="gpsimd", ch=f"l0qkv_w{ti}")
        ldw = em.ch_all(f"l0qkv_w{ti}")
        for mc in range(NCD):
            for s in range(NSTRIP):
                bi, pt, ftok = banks.take()
                em.wait("tensor", ldx, ldw, ftok)
                mt = None
                for kc in range(NCD):
                    wk = wsb[:, kc * 1024 + mc * 128: kc * 1024 + (mc + 1) * 128]
                    xk = xnb[:, kc * T + s * TS: kc * T + (s + 1) * TS]
                    mt = em.t(lambda t, wk=wk, xk=xk, kc=kc, pt=pt: t.matmul(
                        pt[:], wk, xk, start=(kc == 0), stop=(kc == NCD - 1)), tok=(kc == NCD - 1))
                last_mm = mt
                em.wait("vector", mt, zpad)
                et = em.v(lambda v, s=s, pt=pt: v.tensor_copy(pre[:, 3 + s * TS: 3 + (s + 1) * TS], pt[:]), tok=True)
                banks.consumed(bi, et)
            em.wait("vector", ldcw[ti])
            cw = [cwsb[:, ti * NCD * 4 + mc * 4 + j: ti * NCD * 4 + mc * 4 + j + 1] for j in range(4)]
            em.v(lambda v, cw0=cw[0]: v.tensor_scalar_mul(conv[:], pre[:, 0:T], cw0))
            for j in range(1, 4):
                em.v(lambda v, j=j, cwj=cw[j]: v.scalar_tensor_tensor(conv[:], pre[:, j:j + T], cwj, conv[:], ALU.mult, ALU.add))
            cvt = em.vfence()
            em.wait("scalar", cvt)
            oc = outb[:, mc * T:(mc + 1) * T]
            stok = em.s(lambda sc, oc=oc: sc.activation(out=oc, in_=conv[:], func=AF.Silu), tok=True)
            em.wait("vector", stok)
        em.wait("sync", stok)
        store_fm(em, dram[oname], outb[:], ch="l0qkvst")
        em.wait("scalar", em.ch_all("l0qkvst"))
    if 'beta' not in parts:
        return
    em.wait("tensor", ldwb, ldx)
    sg = None
    for tt in range(16):
        em.wait("tensor", sg)
        mt = None
        for kc in range(NCD):
            xk = xnb[:, kc * T + tt * 128: kc * T + (tt + 1) * 128]
            wk = wbb[:, kc * 4:(kc + 1) * 4]
            mt = em.t(lambda t, xk=xk, wk=wk, kc=kc: t.matmul(
                pbeta[:], xk, wk, start=(kc == 0), stop=(kc == NCD - 1)), tok=(kc == NCD - 1))
        em.wait("scalar", mt)
        ob = beta_sb[:, tt * 4:(tt + 1) * 4]
        sg = em.s(lambda sc, ob=ob: sc.activation(out=ob, in_=pbeta[:], func=AF.Sigmoid), tok=True)
    em.wait("sync", sg)
    em.dma(lambda e: e.dma_start(out=dram["beta_tm"].rearrange("(tt p) h -> p tt h", p=128),
                                 in_=beta_sb[:].rearrange("p (tt h) -> p tt h", tt=16)), ch="l0qkvst")
    em.wait("sync", em.ch_all("l0qkvst"))


def np_dt(dt):
    return {F32: np.float32, BF16: ml_dtypes.bfloat16, U8: np.uint8}[dt]


def run_stages(stage_fns, inputs_per_core, input_specs, output_specs, trace=False):
    nc = bass.Bass("TRN2", target_bir_lowering=False, debug=False)
    dram = {}
    for nm, (sh, dt) in input_specs.items():
        dram[nm] = nc.dram_tensor(nm, list(sh), dt, kind="ExternalInput").ap()
    for nm, (sh, dt) in output_specs.items():
        dram[nm] = nc.dram_tensor(nm, list(sh), dt, kind="ExternalOutput").ap()
    with ExitStack() as st:
        em = Em(nc, st)
        for f in stage_fns:
            f(nc, st, em, dram)
        em.finalize()
    res = run_bass_kernel_spmd(nc, inputs_per_core, core_ids=list(range(len(inputs_per_core))), trace=trace)
    return res





def sim_stages(stage_fns, inputs, input_specs, output_specs):
    from concourse.bass_interp import CoreSim
    nc = bass.Bass("TRN2", target_bir_lowering=False, debug=False)
    dram = {}
    for nm, (sh, dt) in input_specs.items():
        dram[nm] = nc.dram_tensor(nm, list(sh), dt, kind="ExternalInput").ap()
    for nm, (sh, dt) in output_specs.items():
        dram[nm] = nc.dram_tensor(nm, list(sh), dt, kind="ExternalOutput").ap()
    with ExitStack() as st:
        em = Em(nc, st)
        for f in stage_fns:
            f(nc, st, em, dram)
        em.finalize()
    sim = CoreSim(nc, trace=False)
    for nm in input_specs:
        sim.tensor(nm)[:] = inputs[nm]
    sim.simulate(check_with_hw=False)
    return {nm: np.array(sim.tensor(nm)) for nm in output_specs}


def stage_l0prep(nc, st, em, dram):
    sb = lambda nm, sh, dt: st.enter_context(nc.sbuf_tensor(f"l0pr_{nm}", sh, dt))
    qs = sb("qs", [128, NCD * T], BF16)
    ks = sb("ks", [128, NCD * T], BF16)
    vs = sb("vs", [128, NCD * T], BF16)
    csb = sb("cst", [128, CST_W], F32)
    idb = sb("idb", [128, 128], BF16)
    sq = sb("sq", [128, TS], F32)
    row = sb("row", [1, 2 * TS], F32)
    stg = [sb(f"stg{i}", [128, 256], BF16) for i in range(2)]

    prow = st.enter_context(nc.psum_tensor("l0pr_prow", [1, TS], F32))
    pbr = st.enter_context(nc.psum_tensor("l0pr_pbr", [128, TS], F32))
    ptr = [st.enter_context(nc.psum_tensor(f"l0pr_ptr{i}", [128, 256], BF16)) for i in range(2)]

    load_fm(em, qs[:], dram["q_s"], ch="l0pr")
    load_fm(em, ks[:], dram["k_s"], ch="l0pr")
    load_fm(em, vs[:], dram["v_s"], ch="l0pr")
    em.dma(lambda e: e.dma_start(out=csb[:], in_=dram["cst_f32"]), ch="l0pr")
    ld = em.ch_all("l0pr")
    em.wait("vector", ld)
    idt = em.v(lambda v: v.tensor_copy(idb[:], cst(csb, "ident")))

    eps2 = csb[0:1, CST["eps_l2"][0]:CST["eps_l2"][0] + 1]
    last_norm = None
    prow_free = None
    pbr_free = None
    for ti, src_sb in enumerate([qs, ks]):
        for h in range(NH):
            for s in range(NSTRIP):
                mmt = None
                for i, c in enumerate([2 * h, 2 * h + 1]):
                    xc = src_sb[:, c * T + s * TS: c * T + (s + 1) * TS]
                    em.wait("scalar", ld, prow_free, mmt)
                    sqt = em.s(lambda sc, xc=xc: sc.activation(out=sq[:], in_=xc, func=AF.Square))
                    em.wait("tensor", sqt, prow_free)
                    mmt = em.t(lambda t, i=i: t.matmul(prow[:], cst(csb, "ones"), sq[:],
                                                       start=(i == 0), stop=(i == 1)), tok=True)
                em.wait("scalar", mmt)
                rsq = em.s(lambda sc: sc.activation(out=row[:, 0:TS], in_=prow[:], func=AF.Sqrt, bias=eps2))
                prow_free = rsq
                em.wait("vector", rsq)
                rec = em.v(lambda v: v.reciprocal(row[:, TS:2 * TS], row[:, 0:TS]))
                em.wait("tensor", rec, pbr_free)
                bct = em.t(lambda t: t.matmul(pbr[:], onesrow(csb), row[0:1, TS:2 * TS], start=True, stop=True), tok=True)
                em.wait("vector", bct)
                for c in (2 * h, 2 * h + 1):
                    xc = src_sb[:, c * T + s * TS: c * T + (s + 1) * TS]
                    last_norm = em.v(lambda v, xc=xc: v.tensor_mul(xc, xc, pbr[:]))
                pbr_free = last_norm
        em.wait("sync", last_norm)
        store_fm(em, dram["q_n" if ti == 0 else "k_n"], src_sb[:], ch="l0prst")
    ev_tok = {0: None, 1: None}
    mm_free = {0: None, 1: None}
    for src_sb, oname in [(ks, "k_tm"), (vs, "v_tm")]:
        for h in range(NH):
            for tt in range(16):
                i = tt % 2
                em.wait("tensor", idt, last_norm, mm_free[i])
                trt = None
                for dc in range(2):
                    xc = src_sb[:, (2 * h + dc) * T + tt * 128: (2 * h + dc) * T + (tt + 1) * 128]
                    trt = em.t(lambda t, xc=xc, dc=dc, i=i: t.transpose(ptr[i][:, dc * 128:(dc + 1) * 128], xc, idb[:]),
                               tok=(dc == 1))
                em.wait("vector", trt, em.ch_all(f"l0prtt{i}"))
                ev = em.v(lambda v, i=i: v.tensor_copy(stg[i][:], ptr[i][:]))
                mm_free[i] = ev
                em.wait("sync", ev)
                dst = dram[oname][h * T + tt * 128: h * T + (tt + 1) * 128, :]
                em.dma(lambda e, dst=dst, i=i: e.dma_start(out=dst, in_=stg[i][:]), ch=f"l0prtt{i}")
    em.wait("sync", em.ch_all("l0prtt0"), em.ch_all("l0prtt1"), em.ch_all("l0prst"))


def stage_delta(nc, st, em, dram):
    C = 128
    NCHUNK = T // C
    sb = lambda nm, sh, dt: st.enter_context(nc.sbuf_tensor(f"dl_{nm}", sh, dt))
    qn = sb("qn", [128, NCD * T], BF16)
    kn = sb("kn", [128, NCD * T], BF16)
    bt = sb("bt", [128, 16 * 4], F32)
    csb = sb("cst", [128, CST_W], F32)
    idb = sb("idb", [128, 128], BF16)
    ktm = [sb(f"ktm{i}", [128, 256], BF16) for i in range(2)]
    vtm = [sb(f"vtm{i}", [128, 256], BF16) for i in range(2)]
    Ab = sb("A", [128, 128], BF16)
    BT = sb("BT", [128, 128], BF16)
    Xb = sb("X", [128, 512], BF16)
    wfm = sb("wfm", [128, 256], BF16)
    u2 = sb("u2", [128, 256], BF16)
    al = sb("al", [128, 128], BF16)
    s32 = sb("s32", [128, 512], F32)
    sbf = sb("sbf", [128, 512], BF16)
    dstg = [sb(f"dstg{i}", [128, 256], BF16) for i in range(2)]

    pG = st.enter_context(nc.psum_tensor("dl_pG", [128, 128], F32))
    pX = st.enter_context(nc.psum_tensor("dl_pX", [128, 512], F32))
    pB = st.enter_context(nc.psum_tensor("dl_pB", [128, 128], F32))
    pT = st.enter_context(nc.psum_tensor("dl_pT", [128, 256], BF16))
    pt1 = st.enter_context(nc.psum_tensor("dl_pt1", [128, 256], F32))
    pS = st.enter_context(nc.psum_tensor("dl_pS", [128, 512], F32))
    pO = st.enter_context(nc.psum_tensor("dl_pO", [128, 256], F32))

    load_fm(em, qn[:], dram["q_n"], ch="dl")
    load_fm(em, kn[:], dram["k_n"], ch="dl")
    em.dma(lambda e: e.dma_start(out=bt[:].rearrange("p (tt h) -> p tt h", tt=16),
                                 in_=dram["beta_tm"].rearrange("(tt p) h -> p tt h", p=128)), ch="dl")
    em.dma(lambda e: e.dma_start(out=csb[:], in_=dram["cst_f32"]), ch="dl")
    ld = em.ch_all("dl")
    em.wait("vector", ld)
    idt = em.v(lambda v: v.tensor_copy(idb[:], cst(csb, "ident")))

    def ldtile(h, n, wait_tok=None):
        i = n % 2
        em.wait("sync", wait_tok)
        em.dma(lambda e, h=h, n=n, i=i: e.dma_start(out=ktm[i][:], in_=dram["k_tm"][h * T + n * C: h * T + (n + 1) * C, :]), ch=f"dlt{i}")
        em.dma(lambda e, h=h, n=n, i=i: e.dma_start(out=vtm[i][:], in_=dram["v_tm"][h * T + n * C: h * T + (n + 1) * C, :]), ch=f"dlt{i}")
        return em.ch_all(f"dlt{i}")

    tok = ldtile(0, 0)
    tile_toks = {(0, 0): tok}
    last_store = {0: None, 1: None}
    for h in range(NH):
        zs = em.v(lambda v: v.memset(s32[:], 0.0))
        zb = em.v(lambda v: v.memset(sbf[:], 0.0))
        for n in range(NCHUNK):
            i = n % 2
            ltok = tile_toks.pop((h, n))
            em.wait("tensor", zb)
            nh, nn = (h, n + 1) if n + 1 < NCHUNK else (h + 1, 0)
            if nh < NH:
                tile_toks[(nh, nn)] = ldtile(nh, nn, wait_tok=zb)
            bcol = bt[:, n * 4 + h: n * 4 + h + 1]
            em.wait("tensor", ld)
            gt = None
            for dc in range(2):
                kc = kn[:, (2 * h + dc) * T + n * C: (2 * h + dc) * T + (n + 1) * C]
                gt = em.t(lambda t, kc=kc, dc=dc: t.matmul(pG[:], kc, kc, start=(dc == 0), stop=(dc == 1)),
                          tok=(dc == 1))
            em.wait("vector", gt, ltok)
            a1 = em.v(lambda v: v.tensor_mul(Ab[:], pG[:], cst(csb, "strict_low")))
            a2 = em.v(lambda v, bcol=bcol: v.tensor_scalar(Ab[:], Ab[:], bcol, -1.0, ALU.mult, ALU.mult))
            x1 = em.v(lambda v, i=i, bcol=bcol: v.tensor_scalar_mul(Xb[:, 0:256], vtm[i][:], bcol))
            x2 = em.v(lambda v, i=i, bcol=bcol: v.tensor_scalar_mul(Xb[:, 256:512], ktm[i][:], bcol))
            prevBT = None
            prevX = x2
            prevB = a2
            for p in range(7):
                em.wait("tensor", prevB, prevX, idt, prevBT)
                trt = em.t(lambda t: t.transpose(pT[:, 0:128], Ab[:], idb[:]), tok=True)
                em.wait("vector", trt)
                prevBT = em.v(lambda v: v.tensor_copy(BT[:], pT[:, 0:128]))
                em.wait("tensor", prevBT, prevX)
                em.t(lambda t: t.matmul(pX[:], idb[:], Xb[:], start=True, stop=False))
                xt = em.t(lambda t: t.matmul(pX[:], BT[:], Xb[:], start=False, stop=True), tok=True)
                if p < 6:
                    bsq = em.t(lambda t: t.matmul(pB[:], BT[:], Ab[:], start=True, stop=True), tok=True)
                em.wait("vector", xt)
                prevX = em.v(lambda v: v.tensor_copy(Xb[:], pX[:]))
                if p < 6:
                    em.wait("vector", bsq)
                    prevB = em.v(lambda v: v.tensor_copy(Ab[:], pB[:]))
            em.wait("tensor", prevX, prevBT)
            trt = None
            for dc in range(2):
                trt = em.t(lambda t, dc=dc: t.transpose(pT[:, 128 * dc:128 * (dc + 1)], Xb[:, 256 + 128 * dc: 256 + 128 * (dc + 1)], idb[:]), tok=(dc == 1))
            em.wait("vector", trt)
            wv = em.v(lambda v: v.tensor_copy(wfm[:], pT[:]))
            em.wait("tensor", wv, zb)
            tt1 = None
            for dc in range(2):
                tt1 = em.t(lambda t, dc=dc: t.matmul(pt1[:], wfm[:, 128 * dc:128 * (dc + 1)], sbf[:, 256 * dc:256 * (dc + 1)],
                                                     start=(dc == 0), stop=(dc == 1)), tok=(dc == 1))
            em.wait("vector", tt1)
            ut = em.v(lambda v: v.tensor_sub(u2[:], Xb[:, 0:256], pt1[:]))
            altt = None
            for dc in range(2):
                kc = kn[:, (2 * h + dc) * T + n * C: (2 * h + dc) * T + (n + 1) * C]
                qc = qn[:, (2 * h + dc) * T + n * C: (2 * h + dc) * T + (n + 1) * C]
                altt = em.t(lambda t, kc=kc, qc=qc, dc=dc: t.matmul(pG[:], kc, qc, start=(dc == 0), stop=(dc == 1)),
                            tok=(dc == 1))
            em.wait("vector", altt)
            alv = em.v(lambda v: v.tensor_mul(al[:], pG[:], cst(csb, "incl_up")))
            em.wait("tensor", alv, ut)
            for dc in range(2):
                qc = qn[:, (2 * h + dc) * T + n * C: (2 * h + dc) * T + (n + 1) * C]
                em.t(lambda t, qc=qc, dc=dc: t.matmul(pO[:], qc, sbf[:, 256 * dc:256 * (dc + 1)], start=(dc == 0), stop=False))
            ot = em.t(lambda t: t.matmul(pO[:], al[:], u2[:], start=False, stop=True), tok=True)
            stt = None
            for dc in range(2):
                stt = em.t(lambda t, dc=dc, i=i: t.matmul(pS[:, 256 * dc:256 * (dc + 1)], ktm[i][:, 128 * dc:128 * (dc + 1)], u2[:],
                                                          start=True, stop=True), tok=(dc == 1))
            em.wait("vector", ot)
            j = n % 2
            em.wait("vector", em.ch_all(f"dlst{j}"))
            ov = em.v(lambda v, j=j: v.tensor_copy(dstg[j][:], pO[:]))
            em.wait("sync", ov)
            dst = dram["delta_tm"][h * T + n * C: h * T + (n + 1) * C, :]
            em.dma(lambda e, dst=dst, j=j: e.dma_start(out=dst, in_=dstg[j][:]), ch=f"dlst{j}")
            em.wait("vector", stt)
            sv = em.v(lambda v: v.tensor_add(s32[:], s32[:], pS[:]))
            zb = em.v(lambda v: v.tensor_copy(sbf[:], s32[:]))
    em.wait("sync", em.ch_all("dlst0"), em.ch_all("dlst1"))


PAD = 63
CW = PAD + T


def stage_l0fir(nc, st, em, dram):
    sb = lambda nm, sh, dt: st.enter_context(nc.sbuf_tensor(f"l0f_{nm}", sh, dt))
    vpad = sb("vpad", [128, NCD * CW], BF16)
    lsb = sb("ls", [128, NCD * T], BF16)
    llb = sb("ll", [128, NCD * T], BF16)
    wsw = sb("wsw", [128, NCD * 5], F32)
    wlw = sb("wlw", [128, NCD * 64], F32)
    csb = sb("cst", [128, CST_W], F32)
    idb = sb("idb", [128, 128], BF16)
    accd = sb("accd", [128, T], F32)
    accp = sb("accp", [128, T], F32)
    tile = [sb(f"tile{i}", [128, 256], BF16) for i in range(2)]
    sqb = sb("sqb", [128, 256], F32)
    raw = sb("raw", [128, 12], F32)
    st16 = sb("st16", [128, 16], BF16)
    mtmp = sb("mtmp", [128, 4], F32)
    sstg = [sb(f"sstg{i}", [16, 128], BF16) for i in range(2)]

    ptr = st.enter_context(nc.psum_tensor("l0f_ptr", [128, 256], BF16))
    pst = st.enter_context(nc.psum_tensor("l0f_pst", [16, 128], BF16))

    zp = em.v(lambda v: v.memset(vpad[:], 0.0))
    em.wait("sync", zp)
    em.dma(lambda e: e.dma_start(
        out=vpad[:].rearrange("p (c t) -> p c t", c=NCD)[:, :, PAD:CW],
        in_=dram["v_s"].rearrange("(c p) t -> p c t", p=128)), ch="l0f")
    em.dma(lambda e: e.dma_start(out=wsw[:].rearrange("p (c j) -> p c j", c=NCD),
                                 in_=dram["L0_attn_fir_short_w"].rearrange("(c p) j -> p c j", p=128)), ch="l0f")
    em.dma(lambda e: e.dma_start(out=wlw[:].rearrange("p (c j) -> p c j", c=NCD),
                                 in_=dram["L0_attn_fir_long_w"].rearrange("(c p) j -> p c j", p=128)), ch="l0f")
    em.dma(lambda e: e.dma_start(out=csb[:], in_=dram["cst_f32"]), ch="l0f")
    ld = em.ch_all("l0f")
    em.wait("vector", ld)
    idt = em.v(lambda v: v.tensor_copy(idb[:], cst(csb, "ident")))

    em.wait("gpsimd", ld)
    for c in range(NCD):
        base = c * CW
        for j in range(5):
            wj = wsw[:, c * 5 + j: c * 5 + j + 1]
            xs_ = vpad[:, base + 59 + j: base + 59 + j + T]
            if j == 0:
                em.v(lambda v, xs_=xs_, wj=wj: v.tensor_scalar_mul(accd[:], xs_, wj))
            elif j < 4:
                em.v(lambda v, xs_=xs_, wj=wj: v.scalar_tensor_tensor(accd[:], xs_, wj, accd[:], ALU.mult, ALU.add))
            else:
                em.v(lambda v, xs_=xs_, wj=wj, c=c: v.scalar_tensor_tensor(
                    lsb[:, c * T:(c + 1) * T], xs_, wj, accd[:], ALU.mult, ALU.add))
        eng = em.v if c < 8 else em.g
        acc = accd if c < 8 else accp
        for j in range(64):
            wj = wlw[:, c * 64 + j: c * 64 + j + 1]
            xs_ = vpad[:, base + j: base + j + T]
            if j == 0:
                eng(lambda e, xs_=xs_, wj=wj, acc=acc: e.tensor_scalar_mul(acc[:], xs_, wj))
            elif j < 63:
                eng(lambda e, xs_=xs_, wj=wj, acc=acc: e.scalar_tensor_tensor(acc[:], xs_, wj, acc[:], ALU.mult, ALU.add))
            else:
                eng(lambda e, xs_=xs_, wj=wj, acc=acc, c=c: e.scalar_tensor_tensor(
                    llb[:, c * T:(c + 1) * T], xs_, wj, acc[:], ALU.mult, ALU.mult if False else ALU.add))
    fv = em.v(lambda v: v.engine_nop())
    fg = em.g(lambda g: g.engine_nop())

    ptr_free = None
    pst_free = None
    tile_free = [None, None]
    for h in range(NH):
        for tt in range(16):
            ib = tt % 2
            for q in range(4):
                if q < 2:
                    srcb = lsb if q == 0 else llb
                    em.wait("tensor", idt, fv, fg, ptr_free)
                    trt = None
                    for dc in range(2):
                        xc = srcb[:, (2 * h + dc) * T + tt * 128: (2 * h + dc) * T + (tt + 1) * 128]
                        trt = em.t(lambda t, xc=xc, dc=dc: t.transpose(ptr[:, dc * 128:(dc + 1) * 128], xc, idb[:]),
                                   tok=(dc == 1))
                    em.wait("vector", trt, tile_free[ib] if q == 0 else None)
                    tv = em.v(lambda v, ib=ib: v.tensor_copy(tile[ib][:], ptr[:]))
                    ptr_free = tv
                    ltok = tv
                else:
                    nm = "delta_tm" if q == 2 else "v_tm"
                    em.wait("sync", tile_free[ib])
                    em.dma(lambda e, nm=nm, h=h, tt=tt, ib=ib: e.dma_start(
                        out=tile[ib][:], in_=dram[nm][h * T + tt * 128: h * T + (tt + 1) * 128, :]), ch=f"l0ft{ib}")
                    ltok = em.ch_all(f"l0ft{ib}")
                    em.wait("vector", ltok)
                em.v(lambda v, ib=ib, q=q: v.reduce_sum(raw[:, 3 * q: 3 * q + 1], tile[ib][:], AX.X))
                em.wait("scalar", ltok)
                sqt = em.s(lambda sc, ib=ib: sc.activation(out=sqb[:], in_=tile[ib][:], func=AF.Square))
                em.wait("vector", sqt)
                em.v(lambda v, q=q: v.reduce_sum(raw[:, 3 * q + 1: 3 * q + 2], sqb[:], AX.X))
                lastr = em.v(lambda v, ib=ib, q=q: v.reduce_sum(raw[:, 3 * q + 2: 3 * q + 3], tile[ib][:], AX.X,
                                                               apply_absolute_value=True))
                tile_free[ib] = lastr
            s16v = lambda i: st16[:].rearrange("p (q s) -> p q s", s=4)[:, :, i:i + 1]
            rawv = lambda i: raw[:].rearrange("p (q s) -> p q s", s=3)[:, :, i:i + 1]
            em.v(lambda v: v.tensor_scalar_mul(s16v(0), rawv(0), 1.0 / 256))
            em.v(lambda v: v.tensor_scalar_mul(mtmp[:].rearrange("p (q s) -> p q s", s=1), rawv(0), 1.0 / 256))
            em.v(lambda v: v.tensor_mul(mtmp[:], mtmp[:], mtmp[:]))
            em.v(lambda v: v.scalar_tensor_tensor(s16v(1), rawv(1), 1.0 / 256,
                                                  mtmp[:].rearrange("p (q s) -> p q s", s=1), ALU.mult, ALU.subtract))
            fin = em.v(lambda v: v.tensor_scalar_mul(s16v(2), rawv(2), 1.0 / 256))
            em.wait("scalar", lastr, fin)
            sqr = em.s(lambda sc: sc.activation(out=s16v(3), in_=rawv(1), func=AF.Sqrt))
            em.wait("tensor", sqr, fin, pst_free)
            tst = em.t(lambda t: t.transpose(pst[:], st16[:], idb[:]), tok=True)
            jb = tt % 2
            em.wait("vector", tst, em.ch_all(f"l0fs{jb}"))
            sv = em.v(lambda v, jb=jb: v.tensor_copy(sstg[jb][:], pst[:]))
            pst_free = sv
            em.wait("sync", sv)
            dst = dram["stats_f"][0:16, h * T + tt * 128: h * T + (tt + 1) * 128]
            em.dma(lambda e, dst=dst, jb=jb: e.dma_start(out=dst, in_=sstg[jb][:]), ch=f"l0fs{jb}")
    st1 = store_fm(em, dram["ls_fm"], lsb[:], ch="l0fst")
    st2 = store_fm(em, dram["ll_fm"], llb[:], ch="l0fst")
    em.wait("sync", em.ch_all("l0fst"), em.ch_all("l0fs0"), em.ch_all("l0fs1"))


def stage_l0gate(nc, st, em, dram):
    sb = lambda nm, sh, dt: st.enter_context(nc.sbuf_tensor(f"l0g_{nm}", sh, dt))
    xnb = sb("xn", [128, NCD * T], BF16)
    w1a = sb("w1a", [128, NCD * 1024], BF16)
    w1b = sb("w1b", [16, 1024], BF16)
    b1 = sb("b1", [128, NCD], F32)
    w2 = sb("w2", [128, NCD * 4], BF16)
    b2r = sb("b2r", [1, 4], F32)
    ltr = sb("ltr", [1, 4], F32)
    rtr = sb("rtr", [1, 4], F32)
    stf = sb("stf", [16, NH * T], BF16)
    csb = sb("cst", [128, CST_W], F32)
    idb = sb("idb", [128, 128], BF16)
    h1s = sb("h1s", [128, NCD * TS], BF16)
    b2bc = sb("b2bc", [128, 4], F32)
    rtc = sb("rtc", [128, 4], F32)
    tsb = sb("tsb", [128, 4], F32)
    nmax = sb("nmax", [128, 1], F32)
    zsum = sb("zsum", [128, 1], F32)
    rz = sb("rz", [128, 1], F32)
    eb = sb("eb", [128, 4], F32)
    prt = sb("prt", [128, 4], BF16)
    pstg = [sb(f"pstg{i}", [4, 128], BF16) for i in range(2)]

    ph1 = [st.enter_context(nc.psum_tensor(f"l0g_ph1_{i}", [128, TS], F32)) for i in range(2)]
    plog = st.enter_context(nc.psum_tensor("l0g_plog", [128, 4], F32))
    pbc = st.enter_context(nc.psum_tensor("l0g_pbc", [128, 4], F32))
    ppr = st.enter_context(nc.psum_tensor("l0g_ppr", [4, 128], BF16))

    load_fm(em, xnb[:], dram["xn"], ch="l0g")
    em.dma(lambda e: e.dma_start(out=csb[:], in_=dram["cst_f32"]), ch="l0g")
    em.dma(lambda e: e.dma_start(out=stf[:], in_=dram["stats_f"]), ch="l0g")
    load_pp(em, b1[:], dram["L0_attn_mlp_b1"], ch="l0g")
    em.dma(lambda e: e.dma_start(out=b2r[:], in_=dram["L0_attn_mlp_b2"].rearrange("(o f) -> o f", o=1)), ch="l0g")
    em.dma(lambda e: e.dma_start(out=ltr[:], in_=dram["L0_attn_log_temp"].rearrange("(a h) one -> a (h one)", a=1)), ch="l0g")
    em.dma(lambda e: e.dma_start(out=w1a[:].rearrange("p (c m) -> p c m", c=NCD),
                                 in_=dram["L0_attn_mlp_w1"][0:1024, :].rearrange("(c p) m -> p c m", p=128)),
           q="gpsimd", ch="l0gw")
    em.dma(lambda e: e.dma_start(out=w1b[:], in_=dram["L0_attn_mlp_w1"][1024:1040, :]), q="gpsimd", ch="l0gw")
    em.dma(lambda e: e.dma_start(out=w2[:].rearrange("p (c m) -> p c m", c=NCD),
                                 in_=dram["L0_attn_mlp_w2"].rearrange("(c p) m -> p c m", p=128)),
           q="gpsimd", ch="l0gw")
    ld = em.ch_all("l0g")
    ldw = em.ch_all("l0gw")
    em.wait("vector", ld)
    idt = em.v(lambda v: v.tensor_copy(idb[:], cst(csb, "ident")))
    em.wait("scalar", ld)
    ext = em.s(lambda sc: sc.activation(out=rtr[:], in_=ltr[:], func=AF.Exp))
    em.wait("vector", ext)
    rct = em.v(lambda v: v.reciprocal(rtr[:], rtr[:]))
    em.wait("tensor", rct)
    bt1 = em.t(lambda t: t.matmul(pbc[:], onesrow(csb), b2r[:], start=True, stop=True), tok=True)
    em.wait("vector", bt1)
    bv1 = em.v(lambda v: v.tensor_copy(b2bc[:], pbc[:]))
    bv2 = bv1
    for h in range(NH):
        em.wait("tensor", bv2)
        bt2 = em.t(lambda t, h=h: t.matmul(pbc[:, 0:1], onesrow(csb), rtr[0:1, h:h + 1], start=True, stop=True), tok=True)
        em.wait("vector", bt2)
        bv2 = em.v(lambda v, h=h: v.tensor_copy(rtc[:, h:h + 1], pbc[:, 0:1]))

    ph_free = [None, None]
    plog_free = None
    ppr_free = None
    for h in range(NH):
        for s in range(NSTRIP):
            for mc in range(NCD):
                ib = mc % 2
                em.wait("tensor", ld, ldw, ph_free[ib])
                for kc in range(NCD):
                    wk = w1a[:, kc * 1024 + mc * 128: kc * 1024 + (mc + 1) * 128]
                    xk = xnb[:, kc * T + s * TS: kc * T + (s + 1) * TS]
                    em.t(lambda t, wk=wk, xk=xk, kc=kc, ib=ib: t.matmul(
                        ph1[ib][:], wk, xk, start=(kc == 0), stop=False))
                sk = stf[0:16, h * T + s * TS: h * T + (s + 1) * TS]
                w1bk = w1b[0:16, mc * 128: (mc + 1) * 128]
                mt = em.t(lambda t, sk=sk, w1bk=w1bk, ib=ib: t.matmul(
                    ph1[ib][:], w1bk, sk, start=False, stop=True), tok=True)
                em.wait("scalar", mt, ldw)
                b1c = b1[:, mc: mc + 1]
                gl = em.s(lambda sc, ib=ib, b1c=b1c, mc=mc: sc.activation(
                    out=h1s[:, mc * TS:(mc + 1) * TS], in_=ph1[ib][:], func=AF.Gelu, bias=b1c))
                ph_free[ib] = gl
            for tt in range(4):
                gtt = s * 4 + tt
                jb = gtt % 2
                em.wait("tensor", gl, plog_free)
                lt = None
                for mc in range(NCD):
                    hk = h1s[:, mc * TS + tt * 128: mc * TS + (tt + 1) * 128]
                    wk2 = w2[:, mc * 4: (mc + 1) * 4]
                    lt = em.t(lambda t, hk=hk, wk2=wk2, mc=mc: t.matmul(
                        plog[:], hk, wk2, start=(mc == 0), stop=(mc == NCD - 1)), tok=(mc == NCD - 1))
                em.wait("vector", lt, bv1, bv2)
                em.v(lambda v: v.tensor_add(tsb[:], plog[:], b2bc[:]))
                em.v(lambda v, h=h: v.tensor_scalar_mul(tsb[:], tsb[:], rtc[:, h:h + 1]))
                mx = em.v(lambda v: v.reduce_max(nmax[:], tsb[:], AX.X, negate=True))
                em.wait("scalar", mx)
                et = em.s(lambda sc: sc.activation(out=eb[:], in_=tsb[:], func=AF.Exp, bias=nmax[:, 0:1],
                                                   accum_out=zsum[:, 0:1]))
                em.wait("vector", et)
                plog_free = em.v(lambda v: v.reciprocal(rz[:], zsum[:]))
                em.v(lambda v: v.tensor_scalar_mul(rz[:], rz[:], 0.8))
                pv = em.v(lambda v: v.tensor_scalar(prt[:], eb[:], rz[:, 0:1], 0.05, ALU.mult, ALU.add))
                em.wait("tensor", pv, idt, ppr_free)
                ptk = em.t(lambda t: t.transpose(ppr[:], prt[:], idb[:]), tok=True)
                em.wait("vector", ptk, em.ch_all(f"l0gp{jb}"))
                sv = em.v(lambda v, jb=jb: v.tensor_copy(pstg[jb][:], ppr[:]))
                ppr_free = sv
                em.wait("sync", sv)
                dst = dram["probs_rows"][0:4, h * T + gtt * 128: h * T + (gtt + 1) * 128]
                em.dma(lambda e, dst=dst, jb=jb: e.dma_start(out=dst, in_=pstg[jb][:]), ch=f"l0gp{jb}")
    em.wait("sync", em.ch_all("l0gp0"), em.ch_all("l0gp1"))


def stage_l0mix(nc, st, em, dram):
    sb = lambda nm, sh, dt: st.enter_context(nc.sbuf_tensor(f"l0m_{nm}", sh, dt))
    lsb = sb("ls", [128, NCD * T], BF16)
    llb = sb("ll", [128, NCD * T], BF16)
    vsb = sb("vs", [128, NCD * T], BF16)
    onb = sb("on", [128, NCD * T], BF16)
    wob = sb("wo", [128, NCD * 1024], BF16)
    prw = sb("prw", [4, NH * T], BF16)
    selb = sb("sel", [32, CSTB_W], BF16)
    csb = sb("cst", [128, CST_W], F32)
    idb = sb("idb", [128, 128], BF16)
    onw = sb("onw", [128, 2], F32)
    dtile = [sb(f"dt{i}", [128, 4 * 256], BF16) for i in range(2)]
    dstg = sb("dstg", [128, 2 * TS], BF16)
    accb = sb("acc", [128, 2 * TS], F32)
    tmpb = sb("tmp", [128, TS], F32)
    sqb = sb("sq", [128, TS], F32)
    rowb = sb("row", [1, 2 * TS], F32)
    ystg = [sb(f"ystg{i}", [128, TS], F32) for i in range(2)]

    pbc = [st.enter_context(nc.psum_tensor(f"l0m_pbc{i}", [128, TS], F32)) for i in range(4)]
    pd = st.enter_context(nc.psum_tensor("l0m_pd", [128, TS], BF16))
    prow = st.enter_context(nc.psum_tensor("l0m_prow", [1, TS], F32))
    pbr = st.enter_context(nc.psum_tensor("l0m_pbr", [128, TS], F32))
    pwo = st.enter_context(nc.psum_tensor("l0m_pwo", [128, TS], F32))

    load_fm(em, lsb[:], dram["ls_fm"], ch="l0m")
    load_fm(em, llb[:], dram["ll_fm"], ch="l0m")
    load_fm(em, vsb[:], dram["v_s"], ch="l0m")
    em.dma(lambda e: e.dma_start(out=prw[:], in_=dram["probs_rows"]), ch="l0m")
    em.dma(lambda e: e.dma_start(out=selb[:], in_=dram["cst_bf16"]), ch="l0m")
    em.dma(lambda e: e.dma_start(out=csb[:], in_=dram["cst_f32"]), ch="l0m")
    load_pp(em, onw[:], dram["L0_attn_o_norm_w"], ch="l0m")
    em.dma(lambda e: e.dma_start(out=wob[:].rearrange("p (c m) -> p c m", c=NCD),
                                 in_=dram["L0_attn_Wo"].rearrange("(c p) m -> p c m", p=128)), q="gpsimd", ch="l0mw")
    ld = em.ch_all("l0m")
    ldw = em.ch_all("l0mw")
    em.wait("vector", ld)
    idt = em.v(lambda v: v.tensor_copy(idb[:], cst(csb, "ident")))

    eps1 = csb[0:1, CST["eps_ln"][0]:CST["eps_ln"][0] + 1]
    pd_free = None
    pbc_free = [None] * 4
    prow_free = None
    pbr_free = None
    acc_free = None
    dt_free = [None, None]
    for h in range(NH):
        for s in range(NSTRIP):
            ib = (h * NSTRIP + s) % 2
            em.wait("sync", dt_free[ib])
            em.dma(lambda e, h=h, s=s, ib=ib: e.dma_start(
                out=dtile[ib][:].rearrange("p (j d) -> p j d", j=4),
                in_=dram["delta_tm"][h * T + s * TS: h * T + (s + 1) * TS, :].rearrange("(j p) d -> p j d", p=128)),
                ch=f"l0md{ib}")
            dl = em.ch_all(f"l0md{ib}")
            bts = []
            for i in range(4):
                em.wait("tensor", ld, pbc_free[i])
                pr = prw[0:4, h * T + s * TS: h * T + (s + 1) * TS]
                sl = selb[0:4, i * 128: (i + 1) * 128]
                bts.append(em.t(lambda t, pr=pr, sl=sl, i=i: t.matmul(pbc[i][:], sl, pr, start=True, stop=True), tok=True))
            for dc in range(2):
                em.wait("tensor", idt, dl, pd_free)
                trt = None
                for j in range(4):
                    xc = dtile[ib][:, j * 256 + dc * 128: j * 256 + (dc + 1) * 128]
                    trt = em.t(lambda t, xc=xc, j=j: t.transpose(pd[:, j * 128:(j + 1) * 128], xc, idb[:]),
                               tok=(j == 3))
                dt_free[ib] = trt
                em.wait("vector", trt)
                pd_free = em.v(lambda v, dc=dc: v.tensor_copy(dstg[:, dc * TS:(dc + 1) * TS], pd[:]))
                c = 2 * h + dc
                em.wait("vector", bts[0], bts[1], bts[2], bts[3], acc_free)
                adst = accb[:, dc * TS:(dc + 1) * TS]
                em.v(lambda v, c=c, s=s, adst=adst: v.tensor_mul(
                    adst, lsb[:, c * T + s * TS: c * T + (s + 1) * TS], pbc[0][:]))
                for i, srcb in ((1, llb), (3, vsb)):
                    em.v(lambda v, c=c, s=s, srcb=srcb, i=i: v.tensor_mul(
                        tmpb[:], srcb[:, c * T + s * TS: c * T + (s + 1) * TS], pbc[i][:]))
                    em.v(lambda v, adst=adst: v.tensor_add(adst, adst, tmpb[:]))
                em.v(lambda v, dc=dc: v.tensor_mul(tmpb[:], dstg[:, dc * TS:(dc + 1) * TS], pbc[2][:]))
                mfin = em.v(lambda v, adst=adst: v.tensor_add(adst, adst, tmpb[:]))
                for i in range(4):
                    pbc_free[i] = mfin
            mmt = None
            for dc in range(2):
                em.wait("scalar", mfin, prow_free, mmt)
                sqt = em.s(lambda sc, dc=dc: sc.activation(out=sqb[:], in_=accb[:, dc * TS:(dc + 1) * TS], func=AF.Square))
                em.wait("tensor", sqt, prow_free)
                mmt = em.t(lambda t, dc=dc: t.matmul(prow[:], cst(csb, "ones"), sqb[:],
                                                     start=(dc == 0), stop=(dc == 1)), tok=True)
            em.wait("vector", mmt)
            em.v(lambda v: v.tensor_scalar_mul(rowb[:, 0:TS], prow[:], 1.0 / 256))
            rv = em.v(lambda v: v.engine_nop())
            prow_free = rv
            em.wait("scalar", rv)
            sq2 = em.s(lambda sc: sc.activation(out=rowb[:, TS:2 * TS], in_=rowb[:, 0:TS], func=AF.Sqrt, bias=eps1))
            em.wait("vector", sq2)
            rc = em.v(lambda v: v.reciprocal(rowb[:, 0:TS], rowb[:, TS:2 * TS]))
            em.wait("tensor", rc, pbr_free)
            bct = em.t(lambda t: t.matmul(pbr[:], onesrow(csb), rowb[0:1, 0:TS], start=True, stop=True), tok=True)
            em.wait("vector", bct)
            af = None
            for dc in range(2):
                c = 2 * h + dc
                oc = onb[:, c * T + s * TS: c * T + (s + 1) * TS]
                wcol = onw[:, dc: dc + 1]
                af = em.v(lambda v, oc=oc, wcol=wcol, dc=dc: v.scalar_tensor_tensor(
                    oc, accb[:, dc * TS:(dc + 1) * TS], wcol, pbr[:], ALU.mult, ALU.mult))
            pbr_free = af
            acc_free = af
    onfin = af
    pwo_free = None
    for mc in range(NCD):
        for s in range(NSTRIP):
            jb = (mc * NSTRIP + s) % 2
            em.wait("tensor", onfin, ldw, pwo_free)
            mt = None
            for kc in range(NCD):
                wk = wob[:, kc * 1024 + mc * 128: kc * 1024 + (mc + 1) * 128]
                ok = onb[:, kc * T + s * TS: kc * T + (s + 1) * TS]
                mt = em.t(lambda t, wk=wk, ok=ok, kc=kc: t.matmul(pwo[:], wk, ok,
                                                                  start=(kc == 0), stop=(kc == NCD - 1)),
                          tok=(kc == NCD - 1))
            em.wait("vector", mt, em.ch_all(f"l0my{jb}"))
            yv = em.v(lambda v, jb=jb: v.tensor_copy(ystg[jb][:], pwo[:]))
            pwo_free = yv
            em.wait("sync", yv)
            dst = dram["y_l0attn"][mc * 128:(mc + 1) * 128, s * TS:(s + 1) * TS]
            em.dma(lambda e, dst=dst, jb=jb: e.dma_start(out=dst, in_=ystg[jb][:]), ch=f"l0my{jb}")
    em.wait("sync", em.ch_all("l0my0"), em.ch_all("l0my1"))


def stage_l0moe(nc, st, em, dram, E=32):
    NI = 6
    sb = lambda nm, sh, dt: st.enter_context(nc.sbuf_tensor(f"l0e_{nm}", sh, dt))
    xnb = sb("xn", [128, NCD * T], BF16)
    acc = sb("acc", [128, NCD * T], F32)
    gwf = sb("gwf", [32, 1024], F32)
    gwT = sb("gwT", [128, NCD * 32], BF16)
    csb = sb("cst", [128, CST_W], F32)
    selb = sb("sel", [32, CSTB_W], BF16)
    idb = sb("idb", [128, 128], BF16)
    ebr = sb("ebr", [1, 32], F32)
    ebbc = sb("ebbc", [128, 32], F32)
    scb = sb("scb", [128, 32], F32)
    s0b = sb("s0b", [128, 32], F32)
    spb = sb("spb", [128, 32], F32)
    mcol = sb("mcol", [128, 1], F32)
    msku = sb("msku", [128, 32], U8)
    mskf = sb("mskf", [128, 32], F32)
    negb = sb("negb", [128, 32], F32)
    rzc = sb("rzc", [128, 1], F32)
    cmbt = sb("cmbt", [128, 32], BF16)
    cmbr = sb("cmbr", [32, T], BF16)
    w13 = [sb(f"w13_{i}", [128, NCD * 768 * 2], BF16) for i in range(2)]
    w2b = sb("w2b", [128, NI * 1024], BF16)
    silu_t = sb("silu", [128, TS], F32)
    t2 = sb("t2", [128, TS], F32)
    asb = sb("asb", [128, NI * TS], BF16)
    ystg = sb("ystg", [128, TS], F32)

    pr = st.enter_context(nc.psum_tensor("l0e_pr", [128, 32], F32))
    prt = st.enter_context(nc.psum_tensor("l0e_prt", [32, 128], BF16))
    peb = st.enter_context(nc.psum_tensor("l0e_peb", [128, 32], F32))
    pg = st.enter_context(nc.psum_tensor("l0e_pg", [128, TS], F32))
    pu = st.enter_context(nc.psum_tensor("l0e_pu", [128, TS], F32))
    pcmb = st.enter_context(nc.psum_tensor("l0e_pcmb", [128, TS], F32))
    py = [st.enter_context(nc.psum_tensor(f"l0e_py{i}", [128, TS], F32)) for i in range(2)]

    load_fm(em, xnb[:], dram["xn"], ch="l0e")
    em.dma(lambda e: e.dma_start(out=gwf[:], in_=dram["L0_ffn_gate_w"]), ch="l0e")
    em.dma(lambda e: e.dma_start(out=csb[:], in_=dram["cst_f32"]), ch="l0e")
    em.dma(lambda e: e.dma_start(out=selb[:], in_=dram["cst_bf16"]), ch="l0e")
    em.dma(lambda e: e.dma_start(out=ebr[:], in_=dram["L0_ffn_e_bias"].rearrange("(o f) -> o f", o=1)), ch="l0e")
    ld = em.ch_all("l0e")
    em.wait("vector", ld)
    idt = em.v(lambda v: v.tensor_copy(idb[:], cst(csb, "ident")))
    zacc = em.v(lambda v: v.memset(acc[:], 0.0))
    zneg = em.v(lambda v: v.memset(negb[:], -1e30))
    gt_free = None
    for c in range(NCD):
        em.wait("tensor", ld, gt_free)
        tg = em.t(lambda t, c=c: t.transpose(pr[:], gwf[0:32, c * 128:(c + 1) * 128],
                                             csb[0:32, CST["ident"][0]:CST["ident"][0] + 32]), tok=True)
        em.wait("vector", tg)
        gt_free = em.v(lambda v, c=c: v.tensor_copy(gwT[:, c * 32:(c + 1) * 32], pr[:, 0:32]))
    em.wait("tensor", gt_free)
    bt = em.t(lambda t: t.matmul(peb[:], onesrow(csb), ebr[:], start=True, stop=True), tok=True)
    em.wait("vector", bt)
    ebv = em.v(lambda v: v.tensor_copy(ebbc[:], peb[:]))
    pr_free = gt_free
    prt_free = None
    cm_fin = None
    for tt in range(16):
        em.wait("tensor", pr_free)
        mt = None
        for kc in range(NCD):
            xk = xnb[:, kc * T + tt * 128: kc * T + (tt + 1) * 128]
            gk = gwT[:, kc * 32:(kc + 1) * 32]
            mt = em.t(lambda t, xk=xk, gk=gk, kc=kc: t.matmul(pr[:], xk, gk,
                                                              start=(kc == 0), stop=(kc == NCD - 1)), tok=(kc == NCD - 1))
        em.wait("scalar", mt)
        sg = em.s(lambda sc: sc.activation(out=scb[:], in_=pr[:], func=AF.Sigmoid))
        pr_free = sg
        em.wait("vector", sg, ebv, zneg)
        em.v(lambda v: v.tensor_add(s0b[:], scb[:], ebbc[:]))
        em.v(lambda v: v.tensor_copy(spb[:], s0b[:]))
        for r in range(4):
            em.v(lambda v: v.reduce_max(mcol[:], spb[:], AX.X))
            if r < 3:
                em.v(lambda v: v.tensor_single_scalar(mskf[:], spb[:], mcol[:, 0:1], ALU.is_ge))
                em.v(lambda v: v.scalar_tensor_tensor(spb[:], mskf[:], -1e30, spb[:], ALU.mult, ALU.add))
        em.v(lambda v: v.tensor_single_scalar(mskf[:], s0b[:], mcol[:, 0:1], ALU.is_ge))
        em.v(lambda v: v.tensor_mul(mskf[:], mskf[:], scb[:]))
        em.v(lambda v: v.reduce_sum(rzc[:], mskf[:], AX.X))
        em.v(lambda v: v.reciprocal(rzc[:], rzc[:]))
        cmv = em.v(lambda v: v.tensor_scalar_mul(cmbt[:], mskf[:], rzc[:, 0:1]))
        em.wait("tensor", cmv, idt, prt_free)
        ct = em.t(lambda t: t.transpose(prt[:], cmbt[:], idb[:]), tok=True)
        em.wait("vector", ct)
        cm_fin = em.v(lambda v, tt=tt: v.tensor_copy(cmbr[:, tt * 128:(tt + 1) * 128], prt[:]))
        prt_free = cm_fin
    def ldw13(e):
        i = e % 2
        em.dma(lambda ee, e=e, i=i: ee.dma_start(
            out=w13[i][:, 0:NCD * 768].rearrange("p (c m) -> p c m", c=NCD),
            in_=dram["L0_ffn_w1"][e].rearrange("(c p) m -> p c m", p=128)), q="gpsimd", ch=f"l0ew{i}")
        em.dma(lambda ee, e=e, i=i: ee.dma_start(
            out=w13[i][:, NCD * 768:].rearrange("p (c m) -> p c m", c=NCD),
            in_=dram["L0_ffn_w3"][e].rearrange("(c p) m -> p c m", p=128)), q="gpsimd", ch=f"l0ew{i}")
        return em.ch_all(f"l0ew{i}")

    wtok = {0: ldw13(0)}
    last_w13_user = [None, None]
    w2_user = None
    pg_free, pu_free, pcmb_free = None, None, None
    py_free = [None, None]
    afin = None
    for e in range(E):
        i = e % 2
        if e + 1 < E:
            em.wait("gpsimd", last_w13_user[(e + 1) % 2])
            wtok[e + 1] = ldw13(e + 1)
        em.wait("gpsimd", w2_user)
        em.dma(lambda ee, e=e: ee.dma_start(
            out=w2b[:].rearrange("p (c m) -> p c m", c=NI),
            in_=dram["L0_ffn_w2"][e].rearrange("(c p) m -> p c m", p=128)), q="gpsimd", ch="l0ew2")
        w2tok = em.ch_all("l0ew2")
        wt = wtok.pop(e)
        for s in range(NSTRIP):
            em.wait("tensor", cm_fin, pcmb_free)
            sl = selb[0:32, e * 128:(e + 1) * 128]
            cr = cmbr[0:32, s * TS:(s + 1) * TS]
            cbt = em.t(lambda t, sl=sl, cr=cr: t.matmul(pcmb[:], sl, cr, start=True, stop=True), tok=True)
            for ic in range(NI):
                em.wait("tensor", wt, ld, pg_free, pu_free)
                for kc in range(NCD):
                    w1k = w13[i][:, kc * 768 + ic * 128: kc * 768 + (ic + 1) * 128]
                    xk = xnb[:, kc * T + s * TS: kc * T + (s + 1) * TS]
                    em.t(lambda t, w1k=w1k, xk=xk, kc=kc: t.matmul(pg[:], w1k, xk,
                                                                   start=(kc == 0), stop=(kc == NCD - 1)))
                gmt = None
                for kc in range(NCD):
                    w3k = w13[i][:, NCD * 768 + kc * 768 + ic * 128: NCD * 768 + kc * 768 + (ic + 1) * 128]
                    xk = xnb[:, kc * T + s * TS: kc * T + (s + 1) * TS]
                    gmt = em.t(lambda t, w3k=w3k, xk=xk, kc=kc: t.matmul(pu[:], w3k, xk,
                                                                         start=(kc == 0), stop=(kc == NCD - 1)),
                               tok=(kc == NCD - 1))
                em.wait("scalar", gmt)
                sil = em.s(lambda sc: sc.activation(out=silu_t[:], in_=pg[:], func=AF.Silu))
                pg_free = sil
                em.wait("vector", sil, cbt)
                pu_free = em.v(lambda v: v.tensor_mul(t2[:], silu_t[:], pu[:]))
                av = em.v(lambda v, ic=ic: v.tensor_mul(asb[:, ic * TS:(ic + 1) * TS], t2[:], pcmb[:]))
            pcmb_free = av
            for oc in range(NCD):
                jb = oc % 2
                em.wait("tensor", av, w2tok, py_free[jb])
                ymt = None
                for ic in range(NI):
                    w2k = w2b[:, ic * 1024 + oc * 128: ic * 1024 + (oc + 1) * 128]
                    ak = asb[:, ic * TS:(ic + 1) * TS]
                    ymt = em.t(lambda t, w2k=w2k, ak=ak, ic=ic, jb=jb: t.matmul(py[jb][:], w2k, ak,
                                                                                start=(ic == 0), stop=(ic == NI - 1)),
                               tok=(ic == NI - 1))
                w2_user = ymt
                last_w13_user[i] = ymt
                em.wait("vector", ymt, zacc)
                ac = acc[:, oc * T + s * TS: oc * T + (s + 1) * TS]
                afin = em.v(lambda v, ac=ac, jb=jb: v.tensor_add(ac, ac, py[jb][:]))
                py_free[jb] = afin
    em.wait("sync", afin)
    store_fm(em, dram["y_l0ffn"], acc[:], ch="l0est")
    em.wait("sync", em.ch_all("l0est"))


def stage_add3(nc, st, em, dram, na, nb, ncc, nout, uniq, dta=F32):
    sb = lambda nm, sh, dt: st.enter_context(nc.sbuf_tensor(f"{uniq}_{nm}", sh, dt))
    a = sb("a", [128, NCD * T], F32)
    b = sb("b", [128, NCD * T], F32)
    load_fm(em, a[:], dram[na], ch=uniq)
    load_fm(em, b[:], dram[nb], ch=uniq)
    ld = em.ch_all(uniq)
    em.wait("vector", ld)
    fin = None
    for i in range(2 * NCD):
        sl = slice(i * 1024, (i + 1) * 1024)
        fin = em.v(lambda v, sl=sl: v.tensor_add(a[:, sl], a[:, sl], b[:, sl]))
    em.wait("sync", fin)
    load_fm(em, b[:], dram[ncc], ch=uniq + "c")
    ld2 = em.ch_all(uniq + "c")
    em.wait("vector", ld2)
    for i in range(2 * NCD):
        sl = slice(i * 1024, (i + 1) * 1024)
        fin = em.v(lambda v, sl=sl: v.tensor_add(a[:, sl], a[:, sl], b[:, sl]))
    em.wait("sync", fin)
    store_fm(em, dram[nout], a[:], ch=uniq + "st")
    em.wait("sync", em.ch_all(uniq + "st"))


def stage_mm(nc, st, em, dram, name_in, wname, bname, name_out, Mout, uniq, out_dt=BF16):
    NMO = Mout // 128
    sb = lambda nm, sh, dt: st.enter_context(nc.sbuf_tensor(f"{uniq}_{nm}", sh, dt))
    x = sb("x", [128, NCD * T], BF16)
    w = sb("w", [128, NCD * Mout], BF16)
    bb = sb("b", [128, NMO], F32)
    stg = [sb(f"stg{i}", [128, TS], out_dt) for i in range(2)]
    ps = [st.enter_context(nc.psum_tensor(f"{uniq}_p{i}", [128, TS], F32)) for i in range(2)]
    load_fm(em, x[:], dram[name_in], ch=uniq)
    load_pp(em, bb[:], dram[bname], ch=uniq)
    em.dma(lambda e: e.dma_start(out=w[:].rearrange("p (c m) -> p c m", c=NCD),
                                 in_=dram[wname].rearrange("(c p) m -> p c m", p=128)), q="gpsimd", ch=uniq + "w")
    ld = em.ch_all(uniq)
    ldw = em.ch_all(uniq + "w")
    ps_free = [None, None]
    for mc in range(NMO):
        for s in range(NSTRIP):
            ib = (mc * NSTRIP + s) % 2
            em.wait("tensor", ld, ldw, ps_free[ib])
            mt = None
            for kc in range(NCD):
                wk = w[:, kc * Mout + mc * 128: kc * Mout + (mc + 1) * 128]
                xk = x[:, kc * T + s * TS: kc * T + (s + 1) * TS]
                mt = em.t(lambda t, wk=wk, xk=xk, kc=kc, ib=ib: t.matmul(
                    ps[ib][:], wk, xk, start=(kc == 0), stop=(kc == NCD - 1)), tok=(kc == NCD - 1))
            em.wait("scalar", mt, em.ch_all(f"{uniq}s{ib}"))
            bcol = bb[:, mc: mc + 1]
            ev = em.s(lambda sc, ib=ib, bcol=bcol: sc.add(stg[ib][:], ps[ib][:], bcol))
            ps_free[ib] = ev
            em.wait("sync", ev)
            dst = dram[name_out][mc * 128:(mc + 1) * 128, s * TS:(s + 1) * TS]
            em.dma(lambda e, dst=dst, ib=ib: e.dma_start(out=dst, in_=stg[ib][:]), ch=f"{uniq}s{ib}")
    em.wait("sync", em.ch_all(f"{uniq}s0"), em.ch_all(f"{uniq}s1"))


TPAD = 4


def stage_l1attn(nc, st, em, dram):
    NG = 4
    sb = lambda nm, sh, dt: st.enter_context(nc.sbuf_tensor(f"l1a_{nm}", sh, dt))
    xnb = sb("xn", [128, NCD * T], BF16)
    wq = sb("wq", [128, NCD * 768], BF16)
    bq = sb("bq", [128, 24], F32)
    CWD = T + 2 * TPAD
    qkv = sb("qkv", [128, 6 * CWD], BF16)
    bias = sb("biasb", [128, 16 * 128], F32)
    csb = sb("cst", [128, CST_W], F32)
    idb = sb("idb", [128, 128], BF16)
    s2 = sb("s2", [128, 128], F32)
    nmax = sb("nmax", [128, 1], F32)
    zs = sb("zs", [128, 1], F32)
    rz = sb("rz", [128, 1], F32)
    eb = sb("eb", [128, 128], BF16)
    ebt = sb("ebt", [128, 128], BF16)
    vt = sb("vt", [128, 64], BF16)
    obuf = sb("obuf", [128, NCD * CWD], BF16)
    wp = sb("wp", [128, NCD * 1024], BF16)
    bp = sb("bp", [128, NCD], F32)
    ystg = [sb(f"ystg{i}", [128, TS], F32) for i in range(2)]

    pS = st.enter_context(nc.psum_tensor("l1a_pS", [128, 128], F32))
    pt1 = st.enter_context(nc.psum_tensor("l1a_pt1", [128, 128], BF16))
    pt2 = st.enter_context(nc.psum_tensor("l1a_pt2", [128, 64], BF16))
    pO = st.enter_context(nc.psum_tensor("l1a_pO", [64, 128], F32))
    pY = [st.enter_context(nc.psum_tensor(f"l1a_pY{i}", [128, TS], F32)) for i in range(2)]

    load_fm(em, xnb[:], dram["xn1"], ch="l1a")
    em.dma(lambda e: e.dma_start(out=csb[:], in_=dram["cst_f32"]), ch="l1a")
    em.dma(lambda e: e.dma_start(out=bias[:].rearrange("p (h j) -> p h j", h=16),
                                 in_=dram["l1a_bias"].rearrange("h p j -> p h j")), ch="l1a")
    em.dma(lambda e: e.dma_start(out=wp[:].rearrange("p (c m) -> p c m", c=NCD),
                                 in_=dram["L1_attn_w_proj"].rearrange("(c p) m -> p c m", p=128)),
           q="gpsimd", ch="l1apw")
    load_pp(em, bp[:], dram["L1_attn_b_proj"], ch="l1a")
    load_pp(em, bq[:], dram["L1_attn_b_qkv"], ch="l1a")
    ld = em.ch_all("l1a")
    ldpw = em.ch_all("l1apw")
    em.wait("vector", ld)
    idt = em.v(lambda v: v.tensor_copy(idb[:], cst(csb, "ident")))

    pq_free = None
    o_done = None
    for g in range(NG):
        em.wait("gpsimd", pq_free)
        for j, off in enumerate([g * 256, 1024 + g * 256, 2048 + g * 256]):
            em.dma(lambda e, j=j, off=off: e.dma_start(
                out=wq[:, j * NCD * 256:(j + 1) * NCD * 256].rearrange("p (c m) -> p c m", c=NCD),
                in_=dram["L1_attn_w_qkv"][:, off:off + 256].rearrange("(c p) m -> p c m", p=128)),
                q="gpsimd", ch=f"l1aw{g % 2}")
        ldw = em.ch_all(f"l1aw{g % 2}")
        ldwb = ld
        mt = None
        for mc in range(6):
            for s in range(NSTRIP):
                jb = (mc * NSTRIP + s) % 2
                em.wait("tensor", ld, ldw, pq_free)
                em.wait("scalar", ldwb)
                for kc in range(NCD):
                    wk = wq[:, kc * 256 + (mc // 2) * NCD * 256 + (mc % 2) * 128:
                            kc * 256 + (mc // 2) * NCD * 256 + (mc % 2) * 128 + 128]
                    xk = xnb[:, kc * T + s * TS: kc * T + (s + 1) * TS]
                    mt = em.t(lambda t, wk=wk, xk=xk, kc=kc, jb=jb: t.matmul(
                        pY[jb][:], wk, xk, start=(kc == 0), stop=(kc == NCD - 1)), tok=(kc == NCD - 1))
                em.wait("scalar", mt)
                bcol = bq[:, (mc // 2) * 8 + g * 2 + (mc % 2): (mc // 2) * 8 + g * 2 + (mc % 2) + 1]
                dstq = qkv[:, mc * CWD + TPAD + s * TS: mc * CWD + TPAD + (s + 1) * TS]
                ev = em.s(lambda sc, jb=jb, bcol=bcol, dstq=dstq: sc.add(dstq, pY[jb][:], bcol))
                pq_free = ev
        wr = None
        em.wait("vector", ev)
        for mc in range(6):
            wr = em.v(lambda v, mc=mc: v.tensor_copy(
                qkv[:, mc * CWD + T + TPAD: mc * CWD + T + 2 * TPAD],
                qkv[:, mc * CWD + TPAD: mc * CWD + 2 * TPAD]))
        pS_free = None
        pt1_free = None
        pt2_free = None
        pO_free = None
        for hh in range(4):
            h = 4 * g + hh
            qoff = (hh // 2) * CWD + 0 * NCD * CWD
            koff = 2 * CWD + (hh // 2) * CWD
            voff = 4 * CWD + (hh // 2) * CWD
            pb = 64 * (hh % 2)
            for tt in range(16):
                em.wait("tensor", ev, wr, pS_free)
                ro = 2 * TPAD + 128 * tt
                qs_ = qkv[pb:pb + 64, qoff + ro: qoff + ro + 128]
                ks_ = qkv[pb:pb + 64, koff + ro: koff + ro + 128]
                st_ = em.t(lambda t, qs_=qs_, ks_=ks_: t.matmul(pS[:], qs_, ks_, start=True, stop=True), tok=True)
                em.wait("vector", st_)
                bsl = bias[:, h * 128:(h + 1) * 128]
                em.v(lambda v, bsl=bsl: v.scalar_tensor_tensor(s2[:], pS[:], 0.125, bsl, ALU.mult, ALU.add))
                mx = em.v(lambda v: v.reduce_max(nmax[:], s2[:], AX.X, negate=True))
                em.wait("scalar", mx)
                et = em.s(lambda sc: sc.activation(out=eb[:], in_=s2[:], func=AF.Exp, bias=nmax[:, 0:1],
                                                   accum_out=zs[:, 0:1]))
                em.wait("vector", et)
                em.v(lambda v: v.reciprocal(rz[:], zs[:]))
                pv = em.v(lambda v: v.tensor_scalar_mul(eb[:], eb[:], rz[:, 0:1]))
                pS_free = pv
                em.wait("tensor", pv, idt, pt1_free)
                t1 = em.t(lambda t: t.transpose(pt1[:], eb[:], idb[:]), tok=True)
                vs_ = qkv[pb:pb + 64, voff + ro: voff + ro + 128]
                em.wait("tensor", pt2_free)
                t2_ = em.t(lambda t, vs_=vs_, pb=pb: t.transpose(pt2[:], vs_, idb[pb:pb + 64, pb:pb + 64]), tok=True)
                em.wait("vector", t1, t2_)
                e1 = em.v(lambda v: v.tensor_copy(ebt[:], pt1[:]))
                pt1_free = e1
                e2 = em.v(lambda v: v.tensor_copy(vt[:], pt2[:, 0:64]))
                pt2_free = e2
                em.wait("tensor", e1, e2, pO_free)
                ot = em.t(lambda t: t.matmul(pO[:], vt[:], ebt[:], start=True, stop=True), tok=True)
                em.wait("vector", ot)
                dsto = obuf[pb:pb + 64, (hh // 2 + 2 * g) * CWD + ro: (hh // 2 + 2 * g) * CWD + ro + 128]
                pO_free = em.v(lambda v, dsto=dsto: v.tensor_copy(dsto, pO[:]))
            o_done = pO_free
    uw = None
    for mc in range(NCD):
        em.wait("vector", o_done)
        uw = em.v(lambda v, mc=mc: v.tensor_copy(obuf[:, mc * CWD + TPAD: mc * CWD + 2 * TPAD],
                                                 obuf[:, mc * CWD + T + TPAD: mc * CWD + T + 2 * TPAD]))
    pY_free = [None, None]
    for mc in range(NCD):
        for s in range(NSTRIP):
            jb = (mc * NSTRIP + s) % 2
            em.wait("tensor", uw, ldpw, pY_free[jb])
            mt = None
            for kc in range(NCD):
                wk = wp[:, kc * 1024 + mc * 128: kc * 1024 + (mc + 1) * 128]
                ok = obuf[:, kc * CWD + TPAD + s * TS: kc * CWD + TPAD + (s + 1) * TS]
                mt = em.t(lambda t, wk=wk, ok=ok, kc=kc, jb=jb: t.matmul(
                    pY[jb][:], wk, ok, start=(kc == 0), stop=(kc == NCD - 1)), tok=(kc == NCD - 1))
            em.wait("scalar", mt, em.ch_all(f"l1ay{jb}"))
            bcol = bp[:, mc: mc + 1]
            ev = em.s(lambda sc, jb=jb, bcol=bcol: sc.add(ystg[jb][:], pY[jb][:], bcol))
            pY_free[jb] = ev
            em.wait("sync", ev)
            dst = dram["y_l1attn"][mc * 128:(mc + 1) * 128, s * TS:(s + 1) * TS]
            em.dma(lambda e, dst=dst, jb=jb: e.dma_start(out=dst, in_=ystg[jb][:]), ch=f"l1ay{jb}")
    em.wait("sync", em.ch_all("l1ay0"), em.ch_all("l1ay1"))


def build_l1a_bias(bias_table):
    WS = 8
    coords = np.stack(np.meshgrid(np.arange(WS), np.arange(WS), indexing="ij"))
    cf = coords.reshape(2, -1)
    rel = cf[:, :, None] - cf[:, None, :]
    rel = rel.transpose(1, 2, 0).astype(np.int64)
    rel[..., 0] += WS - 1
    rel[..., 1] += WS - 1
    rel[..., 0] *= 2 * WS - 1
    idx = rel.sum(-1)[:WS, :WS]
    bias = np.asarray(bias_table)[idx.reshape(-1)].reshape(WS, WS, 16).transpose(2, 0, 1)
    out = np.full((16, 128, 128), -1e30, dtype=np.float32)
    for w in range(16):
        out[:, 8 * w:8 * w + 8, 8 * w:8 * w + 8] = bias
    return out


def stage_l1fattn(nc, st, em, dram):
    NHE, DH = 16, 64
    QB = 512
    NQB = T // QB
    sb = lambda nm, sh, dt: st.enter_context(nc.sbuf_tensor(f"l1f_{nm}", sh, dt))
    qh = [sb(f"qh{i}", [64, T], BF16) for i in range(2)]
    kh = [sb(f"kh{i}", [64, T], BF16) for i in range(2)]
    vh = [sb(f"vh{i}", [64, T], BF16) for i in range(2)]
    vt = [sb(f"vt{i}", [128, 16 * 65], BF16) for i in range(2)]
    ET = [sb(f"ET{i}", [128, 16 * QB], BF16) for i in range(2)]
    rz = [sb(f"rz{i}", [1, QB], F32) for i in range(2)]
    rzb = [sb(f"rzb{i}", [64, QB], F32) for i in range(2)]
    negc = sb("negc", [128, 1], F32)
    ctx = sb("ctx", [128, NCD * T], BF16)
    csb = sb("cst", [128, CST_W], F32)
    idb = sb("idb", [128, 128], BF16)

    pS = [st.enter_context(nc.psum_tensor(f"l1f_pS{i}", [128, QB], F32)) for i in range(2)]
    pO = [st.enter_context(nc.psum_tensor(f"l1f_pO{i}", [65, QB], F32)) for i in range(2)]
    pB = [st.enter_context(nc.psum_tensor(f"l1f_pB{i}", [64, QB], F32)) for i in range(2)]
    pT = st.enter_context(nc.psum_tensor("l1f_pT", [128, 128], BF16))

    em.dma(lambda e: e.dma_start(out=csb[:], in_=dram["cst_f32"]), ch="l1f")
    ld = em.ch_all("l1f")
    em.wait("vector", ld)
    idt = em.v(lambda v: v.tensor_copy(idb[:], cst(csb, "ident")))
    em.v(lambda v: v.memset(negc[:], -20.0))
    for i in range(2):
        em.v(lambda v, i=i: v.memset(vt[i][:], 1.0))

    def ldhead(h):
        i = h % 2
        for buf, row in ((qh, 0), (kh, 1024), (vh, 2048)):
            em.dma(lambda e, buf=buf, row=row, h=h, i=i: e.dma_start(
                out=buf[i][:], in_=dram["qkv2"][row + h * DH: row + (h + 1) * DH, :]), ch=f"l1fh{i}")
        return em.ch_all(f"l1fh{i}")

    htok = {0: ldhead(0)}
    qk_user = [None, None]
    v_user = [None, None]
    vt_user = [None, None]
    et_read = [None, None]
    pS_user = [None, None]
    pO_user = [None, None]
    pB_user = [None, None]
    rz_user = [None, None]
    pT_free = None
    pend = None
    gi = 0
    for h in range(NHE):
        i = h % 2
        ltok = htok.pop(h)
        if h + 1 < NHE:
            em.wait("sync", qk_user[(h + 1) % 2], v_user[(h + 1) % 2])
            htok[h + 1] = ldhead(h + 1)
        vt_copy = None
        for c2 in range(8):
            trt = None
            for j in range(2):
                c = 2 * c2 + j
                src = vh[i][0:64, c * 128:(c + 1) * 128]
                em.wait("tensor", ltok, idt, pT_free)
                trt = em.t(lambda t, src=src, j=j: t.transpose(pT[:, j * 64:(j + 1) * 64],
                                                               src, idb[0:64, 0:64]), tok=(j == 1))
            v_user[i] = trt
            em.wait("vector", trt, vt_user[i])
            dstv = vt[i][:].rearrange("p (c w) -> p c w", w=65)[:, 2 * c2:2 * c2 + 2, 0:64]
            srcv = pT[:].rearrange("p (c w) -> p c w", w=64)
            vt_copy = em.v(lambda v, dstv=dstv, srcv=srcv: v.tensor_copy(dstv, srcv))
            pT_free = vt_copy
        for qb in range(NQB):
            jb = gi % 2
            last_exp = None
            for kt in range(16):
                ip = kt % 2
                em.wait("tensor", ltok, pS_user[ip])
                ks_ = kh[i][0:64, kt * 128:(kt + 1) * 128]
                qs_ = qh[i][0:64, qb * QB:(qb + 1) * QB]
                mm = em.t(lambda t, ks_=ks_, qs_=qs_, ip=ip: t.matmul(pS[ip][:], ks_, qs_,
                                                                      start=True, stop=True), tok=True)
                qk_user[i] = mm
                em.wait("scalar", mm, et_read[jb] if kt == 0 else None)
                et = em.s(lambda sc, ip=ip, jb=jb, kt=kt: sc.activation(
                    out=ET[jb][:, kt * QB:(kt + 1) * QB], in_=pS[ip][:],
                    func=AF.Exp, scale=0.125, bias=negc[:, 0:1]))
                pS_user[ip] = et
                last_exp = et
            if pend is not None:
                pend()
            cm = None
            for kt in range(16):
                em.wait("tensor", last_exp if kt == 0 else None,
                        pO_user[jb] if kt == 0 else None,
                        vt_copy if kt == 0 else None)
                vk = vt[i][:, kt * 65:(kt + 1) * 65]
                ek = ET[jb][:, kt * QB:(kt + 1) * QB]
                cm = em.t(lambda t, vk=vk, ek=ek, kt=kt, jb=jb: t.matmul(
                    pO[jb][:], vk, ek, start=(kt == 0), stop=(kt == 15)), tok=(kt == 15))
            vt_user[i] = cm
            et_read[jb] = cm
            em.wait("vector", cm, rz_user[jb])
            rc = em.v(lambda v, jb=jb: v.reciprocal(rz[jb][:], pO[jb][64:65, :]))

            def mk_evac(h=h, qb=qb, jb=jb, rc=rc):
                def go():
                    em.wait("tensor", rc, pB_user[jb])
                    bm = em.t(lambda t, jb=jb: t.matmul(pB[jb][:], onesrow(csb)[0:1, 0:64],
                                                        rz[jb][:], start=True, stop=True), tok=True)
                    rz_user[jb] = bm
                    em.wait("vector", bm)
                    cp = em.v(lambda v, jb=jb: v.tensor_copy(rzb[jb][:], pB[jb][:]))
                    pB_user[jb] = cp
                    dstc = ctx[64 * (h % 2):64 * (h % 2) + 64,
                               (h // 2) * T + qb * QB: (h // 2) * T + (qb + 1) * QB]
                    mu = em.v(lambda v, jb=jb, dstc=dstc: v.tensor_mul(dstc, pO[jb][0:64, :], rzb[jb][:]))
                    pO_user[jb] = mu
                return go
            pend = mk_evac()
            gi += 1
    pend()
    em.wait("sync", pO_user[0], pO_user[1])
    store_fm(em, dram["ctx"], ctx[:], ch="l1fst")
    em.wait("sync", em.ch_all("l1fst"))


def stage_l1h(nc, st, em, dram):
    sb = lambda nm, sh, dt: st.enter_context(nc.sbuf_tensor(f"l1h_{nm}", sh, dt))
    cb = sb("ctx", [128, NCD * T], BF16)
    xb = sb("x", [128, NCD * T], F32)
    wb = sb("w", [128, NCD * 1024], BF16)
    bo = sb("bo", [128, NCD], F32)
    stg = [sb(f"stg{i}", [128, TS], F32) for i in range(2)]
    ps = [st.enter_context(nc.psum_tensor(f"l1h_p{i}", [128, TS], F32)) for i in range(2)]
    load_fm(em, cb[:], dram["ctx"], ch="l1h")
    load_fm(em, xb[:], dram["xn1_32"], ch="l1h")
    load_pp(em, bo[:], dram["L1_ffn_b_o"], ch="l1h")
    em.dma(lambda e: e.dma_start(out=wb[:].rearrange("p (c m) -> p c m", c=NCD),
                                 in_=dram["L1_ffn_w_o"].rearrange("(c p) m -> p c m", p=128)), q="gpsimd", ch="l1hw")
    ld = em.ch_all("l1h")
    ldw = em.ch_all("l1hw")
    ps_free = [None, None]
    for mc in range(NCD):
        for s in range(NSTRIP):
            ib = (mc * NSTRIP + s) % 2
            em.wait("tensor", ld, ldw, ps_free[ib])
            mt = None
            for kc in range(NCD):
                wk = wb[:, kc * 1024 + mc * 128: kc * 1024 + (mc + 1) * 128]
                ck = cb[:, kc * T + s * TS: kc * T + (s + 1) * TS]
                mt = em.t(lambda t, wk=wk, ck=ck, kc=kc, ib=ib: t.matmul(
                    ps[ib][:], wk, ck, start=(kc == 0), stop=(kc == NCD - 1)), tok=(kc == NCD - 1))
            em.wait("vector", mt, ld, em.ch_all(f"l1hs{ib}"))
            bcol = bo[:, mc: mc + 1]
            xk = xb[:, mc * T + s * TS: mc * T + (s + 1) * TS]
            ev = em.v(lambda v, ib=ib, bcol=bcol, xk=xk: v.scalar_tensor_tensor(
                stg[ib][:], ps[ib][:], bcol, xk, ALU.add, ALU.add))
            ps_free[ib] = ev
            em.wait("sync", ev)
            dst = dram["h_ffn"][mc * 128:(mc + 1) * 128, s * TS:(s + 1) * TS]
            em.dma(lambda e, dst=dst, ib=ib: e.dma_start(out=dst, in_=stg[ib][:]), ch=f"l1hs{ib}")
    em.wait("sync", em.ch_all("l1hs0"), em.ch_all("l1hs1"))


def stage_l1moe(nc, st, em, dram, E=8):
    NI2 = 32
    sb = lambda nm, sh, dt: st.enter_context(nc.sbuf_tensor(f"l1m_{nm}", sh, dt))
    hbuf = sb("hb", [128, NCD * T], F32)
    xs = sb("xs", [128, NCD * TS], BF16)
    hx = sb("hx", [128, NI2 * TS], BF16)
    wbuf = [sb(f"wb{i}", [128, 16 * 1024], BF16) for i in range(2)]
    gwb = sb("gw", [128, NCD * 8], BF16)
    b1b = sb("b1", [128, E * NI2], F32)
    b2b = sb("b2", [128, E * NCD], F32)
    web = sb("web", [128, TS], BF16)
    wer = sb("wer", [8, T], BF16)
    selb = sb("sel", [32, CSTB_W], BF16)
    csb = sb("cst", [128, CST_W], F32)
    idb = sb("idb", [128, 128], BF16)
    prb = sb("prb", [128, 8], F32)
    spb = sb("spb", [128, 8], F32)
    mcol = sb("mcol", [128, 1], F32)
    mskf = sb("mskf", [128, 8], F32)
    nmax = sb("nmax", [128, 1], F32)
    zs = sb("zs", [128, 1], F32)
    rz = sb("rz", [128, 1], F32)
    wet = sb("wet", [128, 8], F32)
    tb = sb("tb", [128, TS], F32)

    pp = [st.enter_context(nc.psum_tensor(f"l1m_p{i}", [128, TS], F32)) for i in range(8)]

    load_fm(em, hbuf[:], dram["h_ffn"], ch="l1m")
    em.dma(lambda e: e.dma_start(out=gwb[:].rearrange("p (c m) -> p c m", c=NCD),
                                 in_=dram["L1_ffn_w_gate"].rearrange("(c p) m -> p c m", p=128)), q="gpsimd", ch="l1mg")
    em.dma(lambda e: e.dma_start(out=selb[:], in_=dram["cst_bf16"]), ch="l1m")
    em.dma(lambda e: e.dma_start(out=csb[:], in_=dram["cst_f32"]), ch="l1m")
    em.dma(lambda e: e.dma_start(out=b1b[:].rearrange("p (e c) -> p e c", e=E),
                                 in_=dram["L1_ffn_b1"].rearrange("e (c p) -> p e c", p=128),
                                 allow_slow_non_contiguous=True), ch="l1m")
    em.dma(lambda e: e.dma_start(out=b2b[:].rearrange("p (e c) -> p e c", e=E),
                                 in_=dram["L1_ffn_b2"].rearrange("e (c p) -> p e c", p=128),
                                 allow_slow_non_contiguous=True), ch="l1m")
    ld = em.ch_all("l1m")
    ldg = em.ch_all("l1mg")
    em.wait("vector", ld)
    idt = em.v(lambda v: v.tensor_copy(idb[:], cst(csb, "ident")))

    def ldw(name, e, hf, slot, guard):
        em.wait("gpsimd", guard)
        if name == "L1_ffn_w1":
            srcap = dram[name][e][:, hf * 2048:(hf + 1) * 2048].rearrange("(c p) m -> p c m", p=128)
            dstap = wbuf[slot][:].rearrange("p (c m) -> p c m", c=NCD)
        else:
            srcap = dram[name][e][hf * 2048:(hf + 1) * 2048, :].rearrange("(c p) m -> p c m", p=128)
            dstap = wbuf[slot][:].rearrange("p (c m) -> p c m", c=16)
        em.dma(lambda ee, srcap=srcap, dstap=dstap: ee.dma_start(out=dstap, in_=srcap),
               q="gpsimd", ch=f"l1mw{slot}")
        return em.ch_all(f"l1mw{slot}")

    pr_free = None
    prt_free = None
    xl_user = None
    acc_fin = None
    wslot_user = [None, None]
    for s in range(NSTRIP):
        em.wait("sync", xl_user)
        em.dma(lambda e, s=s: e.dma_start(
            out=xs[:].rearrange("p (c t) -> p c t", c=NCD),
            in_=dram["xln2"].rearrange("(c p) t -> p c t", p=128)[:, :, s * TS:(s + 1) * TS]), ch="l1mx")
        lx = em.ch_all("l1mx")
        wrt = None
        for tt in range(4):
            em.wait("tensor", lx, ldg, pr_free)
            mt = None
            for kc in range(NCD):
                xk = xs[:, kc * TS + tt * 128: kc * TS + (tt + 1) * 128]
                gk = gwb[:, kc * 8:(kc + 1) * 8]
                mt = em.t(lambda t, xk=xk, gk=gk, kc=kc: t.matmul(pp[0][:, 0:8], xk, gk,
                                                                  start=(kc == 0), stop=(kc == NCD - 1)),
                          tok=(kc == NCD - 1))
            em.wait("vector", mt)
            em.v(lambda v: v.reduce_max(nmax[:], pp[0][:, 0:8], AX.X, negate=True))
            mxt = em.v(lambda v: v.engine_nop())
            em.wait("scalar", mxt)
            et = em.s(lambda sc: sc.activation(out=prb[:], in_=pp[0][:, 0:8], func=AF.Exp, bias=nmax[:, 0:1],
                                               accum_out=zs[:, 0:1]))
            pr_free = et
            em.wait("vector", et)
            em.v(lambda v: v.reciprocal(rz[:], zs[:]))
            em.v(lambda v: v.tensor_scalar_mul(prb[:], prb[:], rz[:, 0:1]))
            em.v(lambda v: v.tensor_copy(spb[:], prb[:]))
            em.v(lambda v: v.reduce_max(mcol[:], spb[:], AX.X))
            em.v(lambda v: v.tensor_single_scalar(mskf[:], spb[:], mcol[:, 0:1], ALU.is_ge))
            em.v(lambda v: v.scalar_tensor_tensor(spb[:], mskf[:], -1e30, spb[:], ALU.mult, ALU.add))
            em.v(lambda v: v.reduce_max(mcol[:], spb[:], AX.X))
            em.v(lambda v: v.tensor_single_scalar(mskf[:], prb[:], mcol[:, 0:1], ALU.is_ge))
            em.v(lambda v: v.tensor_mul(mskf[:], mskf[:], prb[:]))
            em.v(lambda v: v.reduce_sum(zs[:], mskf[:], AX.X))
            em.v(lambda v: v.reciprocal(rz[:], zs[:]))
            wev = em.v(lambda v: v.tensor_scalar_mul(wet[:], mskf[:], rz[:, 0:1]))
            em.wait("tensor", wev, idt, prt_free)
            trt = em.t(lambda t: t.transpose(pp[1][0:8, 0:128], wet[:, 0:8], cst(csb, "ident")), tok=True)
            em.wait("vector", trt)
            wrt = em.v(lambda v, s=s, tt=tt: v.tensor_copy(wer[0:8, s * TS + tt * 128: s * TS + (tt + 1) * 128],
                                                           pp[1][0:8, 0:128]))
            prt_free = wrt
        for e in range(E):
            w1t0 = ldw("L1_ffn_w1", e, 0, 0, wslot_user[0])
            w1t1 = ldw("L1_ffn_w1", e, 1, 1, wslot_user[1])
            gt = None
            gel_user = [None, None]
            for hf in range(2):
                wt = w1t0 if hf == 0 else w1t1
                for ic in range(16):
                    gic = hf * 16 + ic
                    jb = gic % 2
                    em.wait("tensor", lx, wt, wrt, gel_user[jb], pr_free if gic == 0 else None)
                    mt = None
                    for kc in range(NCD):
                        wk = wbuf[hf][:, kc * 2048 + ic * 128: kc * 2048 + (ic + 1) * 128]
                        xk = xs[:, kc * TS:(kc + 1) * TS]
                        mt = em.t(lambda t, wk=wk, xk=xk, kc=kc, jb=jb: t.matmul(
                            pp[jb][:], wk, xk, start=(kc == 0), stop=(kc == NCD - 1)), tok=(kc == NCD - 1))
                    em.wait("scalar", mt, ld)
                    bcol = b1b[:, e * NI2 // 4 + gic // 4: e * NI2 // 4 + gic // 4 + 1] if False else \
                        b1b[:, e * 8 + gic // 4: e * 8 + gic // 4 + 1]
                    gt = em.s(lambda sc, jb=jb, bcol=bcol, gic=gic: sc.activation(
                        out=hx[:, gic * TS:(gic + 1) * TS], in_=pp[jb][:], func=AF.Gelu_apprx_tanh, bias=bcol))
                    gel_user[jb] = gt
                wslot_user[hf] = gt
            em.wait("tensor", wrt, gt)
            sl = selb[0:8, e * 128:(e + 1) * 128]
            cr = wer[0:8, s * TS:(s + 1) * TS]
            cbt = em.t(lambda t, sl=sl, cr=cr: t.matmul(pp[0][:], sl, cr, start=True, stop=True), tok=True)
            em.wait("vector", cbt)
            webv = em.v(lambda v: v.tensor_copy(web[:], pp[0][:]))
            w2t0 = ldw("L1_ffn_w2", e, 0, 0, wslot_user[0])
            ymt = None
            for hf in range(2):
                w2t = w2t0 if hf == 0 else ldw("L1_ffn_w2", e, 1, 1, wslot_user[1])
                for oc in range(NCD):
                    em.wait("tensor", w2t, gt, webv if hf == 0 else None)
                    for ic in range(16):
                        gic = hf * 16 + ic
                        wk = wbuf[hf][:, ic * 1024 + oc * 128: ic * 1024 + (oc + 1) * 128]
                        hk = hx[:, gic * TS:(gic + 1) * TS]
                        ymt = em.t(lambda t, wk=wk, hk=hk, hf=hf, ic=ic, oc=oc: t.matmul(
                            pp[oc][:], wk, hk, start=(hf == 0 and ic == 0), stop=(hf == 1 and ic == 15)),
                            tok=(ic == 15))
                wslot_user[hf] = ymt
            afin = None
            for oc in range(NCD):
                em.wait("vector", ymt, webv, ld)
                b2c = b2b[:, e * NCD + oc: e * NCD + oc + 1]
                em.v(lambda v, oc=oc, b2c=b2c: v.scalar_tensor_tensor(tb[:], pp[oc][:], b2c, web[:],
                                                                      ALU.add, ALU.mult))
                hk = hbuf[:, oc * T + s * TS: oc * T + (s + 1) * TS]
                afin = em.v(lambda v, hk=hk: v.tensor_add(hk, hk, tb[:]))
            acc_fin = afin
            pr_free = afin
            prt_free = afin
        xl_user = acc_fin
    em.wait("sync", acc_fin)
    store_fm(em, dram["y_l1ffn"], hbuf[:], ch="l1mst")
    em.wait("sync", em.ch_all("l1mst"))


SCR = {
    "xn": ([D, T], BF16), "q_s": ([D, T], BF16), "k_s": ([D, T], BF16), "v_s": ([D, T], BF16),
    "beta_tm": ([T, 4], F32), "q_n": ([D, T], BF16), "k_n": ([D, T], BF16),
    "k_tm": ([NH * T, DK], BF16), "v_tm": ([NH * T, DV], BF16), "delta_tm": ([NH * T, DV], BF16),
    "ls_fm": ([D, T], BF16), "ll_fm": ([D, T], BF16), "stats_f": ([16, NH * T], BF16),
    "probs_rows": ([4, NH * T], BF16), "y_l0attn": ([D, T], F32), "y_l0ffn": ([D, T], F32),
    "x1": ([D, T], F32), "xn1": ([D, T], BF16), "xn1_32": ([D, T], F32),
    "y_l1attn": ([D, T], F32), "xln1": ([D, T], BF16), "qkv2": ([3 * D, T], BF16),
    "ctx": ([D, T], BF16), "h_ffn": ([D, T], F32), "xln2": ([D, T], BF16),
    "y_l1ffn": ([D, T], F32), "x2out": ([D, T], F32),
}
CONSTS = {"cst_f32": ([128, CST_W], F32), "cst_bf16": ([32, CSTB_W], BF16)}
MODIN = {
    "xcore": ([D, T], F32),
    "L0_attn_Wq": ([D, D], F32), "L0_attn_Wk": ([D, D], F32), "L0_attn_Wv": ([D, D], F32),
    "L0_attn_Wb": ([D, 4], F32),
    "L0_attn_conv_q_w": ([D, 4], F32), "L0_attn_conv_k_w": ([D, 4], F32), "L0_attn_conv_v_w": ([D, 4], F32),
    "L0_attn_fir_short_w": ([D, 5], F32), "L0_attn_fir_long_w": ([D, 64], F32),
    "L0_attn_mlp_w1": ([1040, 1024], F32), "L0_attn_mlp_b1": ([1024], F32),
    "L0_attn_mlp_w2": ([1024, 4], F32), "L0_attn_mlp_b2": ([4], F32),
    "L0_attn_log_temp": ([4, 1], F32), "L0_attn_o_norm_w": ([256], F32), "L0_attn_Wo": ([D, D], F32),
    "L0_ffn_gate_w": ([32, D], F32), "L0_ffn_w1": ([32, D, 768], F32), "L0_ffn_w3": ([32, D, 768], F32),
    "L0_ffn_w2": ([32, 768, D], F32), "L0_ffn_e_bias": ([32], F32),
    "l1a_bias": ([16, 128, 128], F32),
    "L1_attn_w_qkv": ([D, 3 * D], F32), "L1_attn_b_qkv": ([3 * D], F32),
    "L1_attn_w_proj": ([D, D], F32), "L1_attn_b_proj": ([D], F32),
    "L1_ffn_ln1_w": ([D], F32), "L1_ffn_ln1_b": ([D], F32),
    "L1_ffn_ln2_w": ([D], F32), "L1_ffn_ln2_b": ([D], F32),
    "L1_ffn_w_qkv": ([D, 3 * D], F32), "L1_ffn_b_qkv": ([3 * D], F32),
    "L1_ffn_w_o": ([D, D], F32), "L1_ffn_b_o": ([D], F32),
    "L1_ffn_w_gate": ([D, 8], F32), "L1_ffn_w1": ([8, D, 4096], F32), "L1_ffn_b1": ([8, 4096], F32),
    "L1_ffn_w2": ([8, 4096, D], F32), "L1_ffn_b2": ([8, D], F32),
}

STAGES = [
    ("ln0", lambda nc, st, em, dram: stage_ln(nc, st, em, dram, "xcore", "xn", uniq="ln0"),
     ["xcore"], ["xn"]),
    ("l0qkv", stage_l0qkv,
     ["xn", "L0_attn_Wq", "L0_attn_Wk", "L0_attn_Wv", "L0_attn_Wb",
      "L0_attn_conv_q_w", "L0_attn_conv_k_w", "L0_attn_conv_v_w"],
     ["q_s", "k_s", "v_s", "beta_tm"]),
    ("l0prep", stage_l0prep, ["q_s", "k_s", "v_s"], ["q_n", "k_n", "k_tm", "v_tm"]),
    ("delta", stage_delta, ["q_n", "k_n", "k_tm", "v_tm", "beta_tm"], ["delta_tm"]),
    ("l0fir", stage_l0fir,
     ["v_s", "delta_tm", "v_tm", "L0_attn_fir_short_w", "L0_attn_fir_long_w"],
     ["ls_fm", "ll_fm", "stats_f"]),
    ("l0gate", stage_l0gate,
     ["xn", "stats_f", "L0_attn_mlp_w1", "L0_attn_mlp_b1", "L0_attn_mlp_w2",
      "L0_attn_mlp_b2", "L0_attn_log_temp"], ["probs_rows"]),
    ("l0mix", stage_l0mix,
     ["ls_fm", "ll_fm", "v_s", "delta_tm", "probs_rows", "L0_attn_o_norm_w", "L0_attn_Wo"],
     ["y_l0attn"]),
    ("l0moe", stage_l0moe,
     ["xn", "L0_ffn_gate_w", "L0_ffn_e_bias", "L0_ffn_w1", "L0_ffn_w3", "L0_ffn_w2"],
     ["y_l0ffn"]),
    ("mid", lambda nc, st, em, dram: stage_add3(nc, st, em, dram, "xcore", "y_l0attn", "y_l0ffn", "x1", "m1"),
     ["xcore", "y_l0attn", "y_l0ffn"], ["x1"]),
    ("ln1", lambda nc, st, em, dram: stage_ln(nc, st, em, dram, "x1", "xn1", also_f32="xn1_32", uniq="ln1"),
     ["x1"], ["xn1", "xn1_32"]),
    ("l1attn", stage_l1attn,
     ["xn1", "l1a_bias", "L1_attn_w_qkv", "L1_attn_b_qkv", "L1_attn_w_proj", "L1_attn_b_proj"],
     ["y_l1attn"]),
    ("lnf1", lambda nc, st, em, dram: stage_ln(nc, st, em, dram, "xn1", "xln1", gamma="L1_ffn_ln1_w",
                                              beta="L1_ffn_ln1_b", in_dt=BF16, uniq="lnf1"),
     ["xn1", "L1_ffn_ln1_w", "L1_ffn_ln1_b"], ["xln1"]),
    ("qkv2", lambda nc, st, em, dram: stage_mm(nc, st, em, dram, "xln1", "L1_ffn_w_qkv", "L1_ffn_b_qkv",
                                              "qkv2", 3 * D, "qk2"),
     ["xln1", "L1_ffn_w_qkv", "L1_ffn_b_qkv"], ["qkv2"]),
    ("l1fattn", stage_l1fattn, ["qkv2"], ["ctx"]),
    ("l1h", stage_l1h, ["ctx", "xn1_32", "L1_ffn_w_o", "L1_ffn_b_o"], ["h_ffn"]),
    ("lnf2", lambda nc, st, em, dram: stage_ln(nc, st, em, dram, "h_ffn", "xln2", gamma="L1_ffn_ln2_w",
                                              beta="L1_ffn_ln2_b", uniq="lnf2"),
     ["h_ffn", "L1_ffn_ln2_w", "L1_ffn_ln2_b"], ["xln2"]),
    ("l1moe", stage_l1moe,
     ["h_ffn", "xln2", "L1_ffn_w_gate", "L1_ffn_w1", "L1_ffn_b1", "L1_ffn_w2", "L1_ffn_b2"],
     ["y_l1ffn"]),
    ("final", lambda nc, st, em, dram: stage_add3(nc, st, em, dram, "x1", "y_l1attn", "y_l1ffn", "x2out", "fin"),
     ["x1", "y_l1attn", "y_l1ffn"], ["x2out"]),
]


def prep_host_inputs(inputs):
    hs = inputs["L0_attn_hidden_states"]
    shared = {}
    for nm in MODIN:
        if nm in ("xcore", "l1a_bias"):
            continue
        if nm == "L0_attn_fir_short_w":
            shared[nm] = np.ascontiguousarray(inputs[nm].reshape(1024, 5))
        elif nm == "L0_attn_fir_long_w":
            shared[nm] = np.ascontiguousarray(inputs[nm].reshape(1024, 64))
        else:
            shared[nm] = np.ascontiguousarray(inputs[nm])
    shared["l1a_bias"] = build_l1a_bias(inputs["L1_attn_bias_table"])
    shared["cst_f32"] = build_const_f32()
    shared["cst_bf16"] = build_const_bf16()
    xcores = [np.ascontiguousarray(hs[c].T).astype(np.float32, copy=False) for c in range(NCORES)]
    return xcores, shared


def kernel(**inputs):
    xcores, shared = prep_host_inputs(inputs)
    per_core = [dict(xcore=xcores[c]) for c in range(NCORES)]
    for name, fn, ins_, outs_ in STAGES:
        isp = {}
        for nm in ins_ + ["cst_f32", "cst_bf16"]:
            if nm in SCR:
                isp[nm] = SCR[nm]
            elif nm in MODIN:
                isp[nm] = MODIN[nm]
            else:
                isp[nm] = CONSTS[nm]
        osp = {nm: SCR[nm] for nm in outs_}
        in_maps = []
        for c in range(NCORES):
            m = {}
            for nm in isp:
                if nm in shared:
                    m[nm] = shared[nm]
                else:
                    m[nm] = per_core[c][nm]
            in_maps.append(m)
        res = run_stages([fn], in_maps, isp, osp)
        for c in range(NCORES):
            for nm in outs_:
                per_core[c][nm] = np.asarray(res.results[c][nm])
    out = np.stack([per_core[c]["x2out"].T for c in range(NCORES)], axis=0)
    return np.ascontiguousarray(out.astype(np.float32, copy=False))


def kernel_unfused(**inputs):
    xcores, shared = prep_host_inputs(inputs)
    per_core = [dict(xcore=xcores[c]) for c in range(NCORES)]
    for name, fn, ins_, outs_ in STAGES:
        isp = {}
        for nm in ins_ + ["cst_f32", "cst_bf16"]:
            isp[nm] = SCR.get(nm) or MODIN.get(nm) or CONSTS.get(nm)
        osp = {nm: SCR[nm] for nm in outs_}
        in_maps = []
        for c in range(NCORES):
            in_maps.append({nm: (shared[nm] if nm in shared else per_core[c][nm]) for nm in isp})
        res = run_stages([fn], in_maps, isp, osp)
        for c in range(NCORES):
            for nm in outs_:
                per_core[c][nm] = np.asarray(res.results[c][nm])
    out = np.stack([per_core[c]["x2out"].T for c in range(NCORES)], axis=0)
    return np.ascontiguousarray(out.astype(np.float32, copy=False))


def build_fused():
    nc = bass.Bass("TRN2", target_bir_lowering=False, debug=False)
    dram = {}
    for nm, (sh, dt) in {**MODIN, **CONSTS}.items():
        dram[nm] = nc.dram_tensor(nm, list(sh), dt, kind="ExternalInput").ap()
    for nm, (sh, dt) in SCR.items():
        kind = "ExternalOutput" if nm == "x2out" else "Internal"
        dram[nm] = nc.dram_tensor(nm, list(sh), dt, kind=kind).ap()
    with ExitStack() as st_outer:
        em = Em(nc, st_outer)
        for name, fn, ins_, outs_ in STAGES:
            with ExitStack() as st_stage:
                fn(nc, st_stage, em, dram)
            em.barrier()
        em.finalize()
    return nc


def kernel(**inputs):
    xcores, shared = prep_host_inputs(inputs)
    nc = build_fused()
    in_maps = [{**shared, "xcore": xcores[c]} for c in range(NCORES)]
    res = run_bass_kernel_spmd(nc, in_maps, core_ids=list(range(NCORES)))
    out = np.stack([np.asarray(res.results[c]["x2out"]).T for c in range(NCORES)], axis=0)
    return np.ascontiguousarray(out.astype(np.float32, copy=False))


# version 10
# speedup vs baseline: 1.2057x; 1.0358x over previous
import numpy as np
import ml_dtypes
from contextlib import ExitStack
import concourse.bass as bass
import concourse.mybir as mybir
from concourse.bass_utils import run_bass_kernel_spmd

F32 = mybir.dt.float32
BF16 = mybir.dt.bfloat16
U8 = mybir.dt.uint8
AF = mybir.ActivationFunctionType
ALU = mybir.AluOpType
AX = mybir.AxisListType

NCORES = 8
T = 2048
D = 1024
NCD = D // 128
TS = 512
NSTRIP = T // TS
NH, DK, DV = 4, 256, 256

ENG = ("sync", "tensor", "vector", "scalar", "gpsimd")


class Em:
    def __init__(self, nc, st):
        self.nc, self.st = nc, st
        self.prog = {e: [] for e in ENG}
        self.sem_h = {}
        self.sem_c = {}
        self.last_wait = {e: {} for e in ENG}

    def _sem(self, name):
        if name not in self.sem_h:
            self.sem_h[name] = self.st.enter_context(self.nc.semaphore(name))
            self.sem_c[name] = 0
        return self.sem_h[name]

    AUTO_CHAIN = ("vector", "scalar", "gpsimd")

    def _op(self, engine, fn, tok, dma):
        if engine in self.AUTO_CHAIN and not dma:
            name = "c_" + engine
            h = self._sem(name)
            before = self.sem_c[name]
            self.wait(engine, (name, before))
            self.sem_c[name] = before + 1
            self.prog[engine].append(lambda eng: fn(eng).then_inc(h, 1))
            return (name, before + 1)
        if not tok:
            self.prog[engine].append(fn)
            return None
        name = ("d_" if dma else "c_") + engine
        h = self._sem(name)
        inc = 16 if dma else 1
        self.sem_c[name] += inc
        val = self.sem_c[name]
        self.prog[engine].append(lambda eng: fn(eng).then_inc(h, inc))
        return (name, val)

    def t(self, fn, tok=False):
        return self._op("tensor", fn, tok, False)

    def v(self, fn, tok=False):
        return self._op("vector", fn, tok, False)

    def s(self, fn, tok=False):
        return self._op("scalar", fn, tok, False)

    def g(self, fn, tok=False):
        return self._op("gpsimd", fn, tok, False)

    def tfence(self):
        return self._op("tensor", lambda t: t.nop(), True, False)

    def vfence(self):
        return self._op("vector", lambda v: v.engine_nop(), True, False)

    def gfence(self):
        return self._op("gpsimd", lambda g: g.engine_nop(), True, False)

    def dma(self, fn, q="sync", ch=None):
        ch = "dch_" + (ch or q)
        h = self._sem(ch)
        self.sem_c[ch] += 16
        val = self.sem_c[ch]
        self.prog[q].append(lambda eng: fn(eng).then_inc(h, 16))
        return (ch, val)

    def ch_all(self, ch):
        ch = "dch_" + ch
        return (ch, self.sem_c.get(ch, 0))

    def wait(self, engine, *toks):
        for tk in toks:
            if tk is None:
                continue
            name, val = tk
            if val <= self.last_wait[engine].get(name, 0):
                continue
            self.last_wait[engine][name] = val
            h = self.sem_h[name]
            self.prog[engine].append(lambda eng, h=h, v=val: eng.wait_ge(h, v))

    def barrier(self):
        waits = [(nm, cnt) for nm, cnt in self.sem_c.items() if cnt > 0]
        for e in ENG:
            if self.prog[e] or True:
                self.wait(e, *waits)

    def finalize(self):
        with self.nc.Block() as block:
            for e in ENG:
                ops = self.prog[e]
                if ops:
                    getattr(block, e)(lambda eng, ops=ops: [op(eng) for op in ops])


class Banks:

    def __init__(self, em, tiles):
        self.em = em
        self.tiles = tiles
        self.free_tok = [None] * len(tiles)
        self.i = 0

    def take(self):
        i = self.i % len(self.tiles)
        self.i += 1
        return i, self.tiles[i], self.free_tok[i]

    def consumed(self, i, tok):
        self.free_tok[i] = tok


CST = {}
_w = 0
def _c(name, width):
    global _w
    CST[name] = (_w, width)
    _w += width
_c("ones", 1)
_c("ident", 128)
_c("strict_low", 128)
_c("incl_up", 128)
_c("eps_ln", 1)
_c("eps_l2", 1)
CST_W = _w


def build_const_f32():
    c = np.zeros((128, CST_W), dtype=np.float32)
    c[:, CST["ones"][0]] = 1.0
    o, _ = CST["ident"]
    c[:, o:o + 128] = np.eye(128, dtype=np.float32)
    o, _ = CST["strict_low"]
    c[:, o:o + 128] = np.tril(np.ones((128, 128), np.float32), -1)
    o, _ = CST["incl_up"]
    c[:, o:o + 128] = np.triu(np.ones((128, 128), np.float32), 0)
    c[:, CST["eps_ln"][0]] = 1e-5
    c[:, CST["eps_l2"][0]] = 1e-6
    return c


CSTB_W = 32 * 128


def build_const_bf16():
    import ml_dtypes as _md
    c = np.zeros((32, CSTB_W), dtype=_md.bfloat16)
    for r in range(32):
        c[r, r * 128:(r + 1) * 128] = 1.0
    return c


def cst(csb, name):
    o, w = CST[name]
    return csb[:, o:o + w]


def onesrow(csb):
    o, _ = CST["incl_up"]
    return csb[0:1, o:o + 128]


def load_pp(em, dst_sb, dram_ap, ch=None):
    return em.dma(lambda e: e.dma_start(out=dst_sb, in_=dram_ap.rearrange("(c p) -> p c", p=128),
                                        allow_slow_non_contiguous=True), ch=ch)


def load_fm(em, dst_sb, dram_ap, q="sync", ch=None):
    c = dram_ap.shape[0] // 128
    return em.dma(lambda e: e.dma_start(out=dst_sb.rearrange("p (c t) -> p c t", c=c),
                                        in_=dram_ap.rearrange("(c p) t -> p c t", p=128)), q=q, ch=ch)


def store_fm(em, dram_ap, src_sb, q="sync", ch=None):
    c = dram_ap.shape[0] // 128
    return em.dma(lambda e: e.dma_start(out=dram_ap.rearrange("(c p) t -> p c t", p=128),
                                        in_=src_sb.rearrange("p (c t) -> p c t", c=c)), q=q, ch=ch)


def stage_ln(nc, st, em, dram, name_in, name_out, gamma=None, beta=None, also_f32=None,
             in_dt=F32, uniq=None):
    u = uniq or f"ln_{name_in}"
    sb = lambda nm, sh, dt: st.enter_context(nc.sbuf_tensor(f"{u}_{nm}", sh, dt))
    xs = sb("xs", [128, NCD * T], in_dt)
    xnb = sb("xnb", [128, NCD * T], BF16)
    xn32 = sb("xn32", [128, NCD * T], F32) if also_f32 else None
    sq = sb("sq", [128, TS], F32)
    rows = sb("rows", [1, 6 * TS], F32)
    csb = sb("cst", [128, CST_W], F32)
    gsb = sb("g", [128, NCD], F32) if gamma else None
    bsb = sb("b", [128, NCD], F32) if beta else None
    tmp = sb("tmp", [128, TS], F32)

    pm = st.enter_context(nc.psum_tensor(f"{u}_pm", [1, TS], F32))
    pv = st.enter_context(nc.psum_tensor(f"{u}_pv", [1, TS], F32))
    pbm = st.enter_context(nc.psum_tensor(f"{u}_pbm", [128, TS], F32))
    pbr = st.enter_context(nc.psum_tensor(f"{u}_pbr", [128, TS], F32))

    onesb = sb("onesb", [128, 1], BF16) if in_dt == BF16 else None
    load_fm(em, xs[:], dram[name_in], ch=u)
    em.dma(lambda e: e.dma_start(out=csb[:], in_=dram["cst_f32"]), ch=u)
    if gamma:
        load_pp(em, gsb[:], dram[gamma], ch=u)
    if beta:
        load_pp(em, bsb[:], dram[beta], ch=u)
    ld = ldc = ldg = ldb = em.ch_all(u)
    ones_ap = cst(csb, "ones")
    ocast = None
    if onesb is not None:
        em.wait("vector", ld)
        ocast = em.v(lambda v: v.tensor_copy(onesb[:], cst(csb, "ones")))
        ones_ap = onesb[:]

    stat_free = None
    bc_free = None
    for s in range(NSTRIP):
        xstrip = lambda c: xs[:, c * T + s * TS: c * T + (s + 1) * TS]
        em.wait("tensor", ld, ldc, stat_free, ocast)
        for c in range(NCD):
            xc = xstrip(c)
            em.t(lambda t, xc=xc, c=c: t.matmul(pm[:], ones_ap, xc,
                                                start=(c == 0), stop=(c == NCD - 1)))
        prev_mm = None
        for c in range(NCD):
            xc = xstrip(c)
            em.wait("scalar", ld, stat_free, prev_mm)
            sqt = em.s(lambda sc, xc=xc: sc.activation(out=sq[:], in_=xc, func=AF.Square), tok=True)
            em.wait("tensor", sqt)
            prev_mm = em.t(lambda t, c=c: t.matmul(pv[:], cst(csb, "ones"), sq[:],
                                                   start=(c == 0), stop=(c == NCD - 1)), tok=True)
        tok_mm = prev_mm
        em.wait("vector", tok_mm)
        em.v(lambda v: v.tensor_scalar_mul(rows[:, 0:TS], pm[:], 1.0 / D))
        em.v(lambda v: v.tensor_scalar_mul(rows[:, TS:2 * TS], pv[:], 1.0 / D))
        em.v(lambda v: v.tensor_mul(rows[:, 2 * TS:3 * TS], rows[:, 0:TS], rows[:, 0:TS]))
        stat_free = em.v(lambda v: v.tensor_sub(rows[:, 3 * TS:4 * TS], rows[:, TS:2 * TS], rows[:, 2 * TS:3 * TS]), tok=True)
        em.wait("scalar", stat_free)
        epsap = csb[0:1, CST["eps_ln"][0]:CST["eps_ln"][0] + 1]
        sqt2 = em.s(lambda sc, epsap=epsap: sc.activation(out=rows[:, 4 * TS:5 * TS], in_=rows[:, 3 * TS:4 * TS],
                                             func=AF.Sqrt, bias=epsap), tok=True)
        em.wait("vector", sqt2)
        rt = em.v(lambda v: v.reciprocal(rows[:, 5 * TS:6 * TS], rows[:, 4 * TS:5 * TS]), tok=True)
        em.wait("tensor", rt, bc_free)
        em.t(lambda t: t.matmul(pbm[:], onesrow(csb), rows[0:1, 0:TS], start=True, stop=True))
        bt = em.t(lambda t: t.matmul(pbr[:], onesrow(csb), rows[0:1, 5 * TS:6 * TS], start=True, stop=True), tok=True)
        em.wait("vector", bt, ldg, ldb)
        for c in range(NCD):
            xc = xstrip(c)
            oc = xnb[:, c * T + s * TS: c * T + (s + 1) * TS]
            oc32 = xn32[:, c * T + s * TS: c * T + (s + 1) * TS] if also_f32 else None
            em.v(lambda v, xc=xc: v.tensor_sub(tmp[:], xc, pbm[:]))
            if gamma:
                em.v(lambda v: v.tensor_mul(tmp[:], tmp[:], pbr[:]))
                gcol = gsb[:, c:c + 1]
                bcol = bsb[:, c:c + 1]
                em.v(lambda v, oc=oc, gcol=gcol, bcol=bcol: v.tensor_scalar(oc, tmp[:], gcol, bcol, ALU.mult, ALU.add))
                if also_f32:
                    em.v(lambda v, oc32=oc32, gcol=gcol, bcol=bcol: v.tensor_scalar(oc32, tmp[:], gcol, bcol, ALU.mult, ALU.add))
            else:
                em.v(lambda v, oc=oc: v.tensor_mul(oc, tmp[:], pbr[:]))
                if also_f32:
                    em.v(lambda v, oc32=oc32: v.tensor_mul(oc32, tmp[:], pbr[:]))
        bc_free = em.vfence()
    em.wait("sync", bc_free)
    store_fm(em, dram[name_out], xnb[:], ch=u + "st")
    if also_f32:
        store_fm(em, dram[also_f32], xn32[:], ch=u + "st")
    em.wait("sync", em.ch_all(u + "st"))


def stage_l0qkv(nc, st, em, dram, parts=('qkv', 'beta')):
    sb = lambda nm, sh, dt: st.enter_context(nc.sbuf_tensor(f"l0qkv_{nm}", sh, dt))
    xnb = sb("xn", [128, NCD * T], BF16)
    wsb = sb("w", [128, NCD * 1024], BF16)
    wbb = sb("wb", [128, NCD * 4], BF16)
    cwsb = sb("cw", [128, NCD * 4 * 3], F32)
    pre = sb("pre", [128, 3 + T], F32)
    conv = sb("conv", [128, T], F32)
    outb = sb("out", [128, NCD * T], BF16)
    beta_sb = sb("beta", [128, 16 * 4], F32)

    pb = [st.enter_context(nc.psum_tensor(f"l0qkv_p{i}", [128, TS], F32)) for i in range(4)]
    pbeta = st.enter_context(nc.psum_tensor("l0qkv_pbeta", [128, 4], F32))

    load_fm(em, xnb[:], dram["xn"], ch="l0qkv")
    for i, nm in enumerate(["L0_attn_conv_q_w", "L0_attn_conv_k_w", "L0_attn_conv_v_w"]):
        em.dma(lambda e, nm=nm, i=i: e.dma_start(
            out=cwsb[:, i * NCD * 4:(i + 1) * NCD * 4].rearrange("p (c j) -> p c j", c=NCD),
            in_=dram[nm].rearrange("(c p) j -> p c j", p=128)), ch="l0qkv")
    em.dma(lambda e: e.dma_start(out=wbb[:].rearrange("p (c m) -> p c m", c=NCD),
                                 in_=dram["L0_attn_Wb"].rearrange("(c p) m -> p c m", p=128)), q="gpsimd", ch="l0qkvb")
    ldx = em.ch_all("l0qkv")
    ldcw = [ldx, ldx, ldx]
    ldwb = em.ch_all("l0qkvb")

    banks = Banks(em, pb)
    zpad = em.v(lambda v: v.memset(pre[:, 0:3], 0.0), tok=True)
    last_mm = None
    tensors = [("L0_attn_Wq", "q_s"), ("L0_attn_Wk", "k_s"), ("L0_attn_Wv", "v_s")] if 'qkv' in parts else []
    for ti, (wname, oname) in enumerate(tensors):
        em.wait("gpsimd", last_mm)
        em.dma(lambda e, wname=wname: e.dma_start(
            out=wsb[:].rearrange("p (c m) -> p c m", c=NCD),
            in_=dram[wname].rearrange("(c p) m -> p c m", p=128)), q="gpsimd", ch=f"l0qkv_w{ti}")
        ldw = em.ch_all(f"l0qkv_w{ti}")
        for mc in range(NCD):
            for s in range(NSTRIP):
                bi, pt, ftok = banks.take()
                em.wait("tensor", ldx, ldw, ftok)
                mt = None
                for kc in range(NCD):
                    wk = wsb[:, kc * 1024 + mc * 128: kc * 1024 + (mc + 1) * 128]
                    xk = xnb[:, kc * T + s * TS: kc * T + (s + 1) * TS]
                    mt = em.t(lambda t, wk=wk, xk=xk, kc=kc, pt=pt: t.matmul(
                        pt[:], wk, xk, start=(kc == 0), stop=(kc == NCD - 1)), tok=(kc == NCD - 1))
                last_mm = mt
                em.wait("vector", mt, zpad)
                et = em.v(lambda v, s=s, pt=pt: v.tensor_copy(pre[:, 3 + s * TS: 3 + (s + 1) * TS], pt[:]), tok=True)
                banks.consumed(bi, et)
            em.wait("vector", ldcw[ti])
            cw = [cwsb[:, ti * NCD * 4 + mc * 4 + j: ti * NCD * 4 + mc * 4 + j + 1] for j in range(4)]
            em.v(lambda v, cw0=cw[0]: v.tensor_scalar_mul(conv[:], pre[:, 0:T], cw0))
            for j in range(1, 4):
                em.v(lambda v, j=j, cwj=cw[j]: v.scalar_tensor_tensor(conv[:], pre[:, j:j + T], cwj, conv[:], ALU.mult, ALU.add))
            cvt = em.vfence()
            em.wait("scalar", cvt)
            oc = outb[:, mc * T:(mc + 1) * T]
            stok = em.s(lambda sc, oc=oc: sc.activation(out=oc, in_=conv[:], func=AF.Silu), tok=True)
            em.wait("vector", stok)
        em.wait("sync", stok)
        store_fm(em, dram[oname], outb[:], ch="l0qkvst")
        em.wait("scalar", em.ch_all("l0qkvst"))
    if 'beta' not in parts:
        return
    em.wait("tensor", ldwb, ldx)
    sg = None
    for tt in range(16):
        em.wait("tensor", sg)
        mt = None
        for kc in range(NCD):
            xk = xnb[:, kc * T + tt * 128: kc * T + (tt + 1) * 128]
            wk = wbb[:, kc * 4:(kc + 1) * 4]
            mt = em.t(lambda t, xk=xk, wk=wk, kc=kc: t.matmul(
                pbeta[:], xk, wk, start=(kc == 0), stop=(kc == NCD - 1)), tok=(kc == NCD - 1))
        em.wait("scalar", mt)
        ob = beta_sb[:, tt * 4:(tt + 1) * 4]
        sg = em.s(lambda sc, ob=ob: sc.activation(out=ob, in_=pbeta[:], func=AF.Sigmoid), tok=True)
    em.wait("sync", sg)
    em.dma(lambda e: e.dma_start(out=dram["beta_tm"].rearrange("(tt p) h -> p tt h", p=128),
                                 in_=beta_sb[:].rearrange("p (tt h) -> p tt h", tt=16)), ch="l0qkvst")
    em.wait("sync", em.ch_all("l0qkvst"))


def np_dt(dt):
    return {F32: np.float32, BF16: ml_dtypes.bfloat16, U8: np.uint8}[dt]


def run_stages(stage_fns, inputs_per_core, input_specs, output_specs, trace=False):
    nc = bass.Bass("TRN2", target_bir_lowering=False, debug=False)
    dram = {}
    for nm, (sh, dt) in input_specs.items():
        dram[nm] = nc.dram_tensor(nm, list(sh), dt, kind="ExternalInput").ap()
    for nm, (sh, dt) in output_specs.items():
        dram[nm] = nc.dram_tensor(nm, list(sh), dt, kind="ExternalOutput").ap()
    with ExitStack() as st:
        em = Em(nc, st)
        for f in stage_fns:
            f(nc, st, em, dram)
        em.finalize()
    res = run_bass_kernel_spmd(nc, inputs_per_core, core_ids=list(range(len(inputs_per_core))), trace=trace)
    return res





def sim_stages(stage_fns, inputs, input_specs, output_specs):
    from concourse.bass_interp import CoreSim
    nc = bass.Bass("TRN2", target_bir_lowering=False, debug=False)
    dram = {}
    for nm, (sh, dt) in input_specs.items():
        dram[nm] = nc.dram_tensor(nm, list(sh), dt, kind="ExternalInput").ap()
    for nm, (sh, dt) in output_specs.items():
        dram[nm] = nc.dram_tensor(nm, list(sh), dt, kind="ExternalOutput").ap()
    with ExitStack() as st:
        em = Em(nc, st)
        for f in stage_fns:
            f(nc, st, em, dram)
        em.finalize()
    sim = CoreSim(nc, trace=False)
    for nm in input_specs:
        sim.tensor(nm)[:] = inputs[nm]
    sim.simulate(check_with_hw=False)
    return {nm: np.array(sim.tensor(nm)) for nm in output_specs}


def stage_l0prep(nc, st, em, dram):
    sb = lambda nm, sh, dt: st.enter_context(nc.sbuf_tensor(f"l0pr_{nm}", sh, dt))
    qs = sb("qs", [128, NCD * T], BF16)
    ks = sb("ks", [128, NCD * T], BF16)
    vs = sb("vs", [128, NCD * T], BF16)
    csb = sb("cst", [128, CST_W], F32)
    idb = sb("idb", [128, 128], BF16)
    sq = sb("sq", [128, TS], F32)
    row = sb("row", [1, 2 * TS], F32)
    stg = [sb(f"stg{i}", [128, 256], BF16) for i in range(2)]

    prow = st.enter_context(nc.psum_tensor("l0pr_prow", [1, TS], F32))
    pbr = st.enter_context(nc.psum_tensor("l0pr_pbr", [128, TS], F32))
    ptr = [st.enter_context(nc.psum_tensor(f"l0pr_ptr{i}", [128, 256], BF16)) for i in range(2)]

    load_fm(em, qs[:], dram["q_s"], ch="l0pr")
    load_fm(em, ks[:], dram["k_s"], ch="l0pr")
    load_fm(em, vs[:], dram["v_s"], ch="l0pr")
    em.dma(lambda e: e.dma_start(out=csb[:], in_=dram["cst_f32"]), ch="l0pr")
    ld = em.ch_all("l0pr")
    em.wait("vector", ld)
    idt = em.v(lambda v: v.tensor_copy(idb[:], cst(csb, "ident")))

    eps2 = csb[0:1, CST["eps_l2"][0]:CST["eps_l2"][0] + 1]
    last_norm = None
    prow_free = None
    pbr_free = None
    for ti, src_sb in enumerate([qs, ks]):
        for h in range(NH):
            for s in range(NSTRIP):
                mmt = None
                for i, c in enumerate([2 * h, 2 * h + 1]):
                    xc = src_sb[:, c * T + s * TS: c * T + (s + 1) * TS]
                    em.wait("scalar", ld, prow_free, mmt)
                    sqt = em.s(lambda sc, xc=xc: sc.activation(out=sq[:], in_=xc, func=AF.Square))
                    em.wait("tensor", sqt, prow_free)
                    mmt = em.t(lambda t, i=i: t.matmul(prow[:], cst(csb, "ones"), sq[:],
                                                       start=(i == 0), stop=(i == 1)), tok=True)
                em.wait("scalar", mmt)
                rsq = em.s(lambda sc: sc.activation(out=row[:, 0:TS], in_=prow[:], func=AF.Sqrt, bias=eps2))
                prow_free = rsq
                em.wait("vector", rsq)
                rec = em.v(lambda v: v.reciprocal(row[:, TS:2 * TS], row[:, 0:TS]))
                em.wait("tensor", rec, pbr_free)
                bct = em.t(lambda t: t.matmul(pbr[:], onesrow(csb), row[0:1, TS:2 * TS], start=True, stop=True), tok=True)
                em.wait("vector", bct)
                for c in (2 * h, 2 * h + 1):
                    xc = src_sb[:, c * T + s * TS: c * T + (s + 1) * TS]
                    last_norm = em.v(lambda v, xc=xc: v.tensor_mul(xc, xc, pbr[:]))
                pbr_free = last_norm
        em.wait("sync", last_norm)
        store_fm(em, dram["q_n" if ti == 0 else "k_n"], src_sb[:], ch="l0prst")
    ev_tok = {0: None, 1: None}
    mm_free = {0: None, 1: None}
    for src_sb, oname in [(ks, "k_tm"), (vs, "v_tm")]:
        for h in range(NH):
            for tt in range(16):
                i = tt % 2
                em.wait("tensor", idt, last_norm, mm_free[i])
                trt = None
                for dc in range(2):
                    xc = src_sb[:, (2 * h + dc) * T + tt * 128: (2 * h + dc) * T + (tt + 1) * 128]
                    trt = em.t(lambda t, xc=xc, dc=dc, i=i: t.transpose(ptr[i][:, dc * 128:(dc + 1) * 128], xc, idb[:]),
                               tok=(dc == 1))
                em.wait("vector", trt, em.ch_all(f"l0prtt{i}"))
                ev = em.v(lambda v, i=i: v.tensor_copy(stg[i][:], ptr[i][:]))
                mm_free[i] = ev
                em.wait("sync", ev)
                dst = dram[oname][h * T + tt * 128: h * T + (tt + 1) * 128, :]
                em.dma(lambda e, dst=dst, i=i: e.dma_start(out=dst, in_=stg[i][:]), ch=f"l0prtt{i}")
    em.wait("sync", em.ch_all("l0prtt0"), em.ch_all("l0prtt1"), em.ch_all("l0prst"))


def stage_delta(nc, st, em, dram):
    C = 128
    NCHUNK = T // C
    sb = lambda nm, sh, dt: st.enter_context(nc.sbuf_tensor(f"dl_{nm}", sh, dt))
    qn = sb("qn", [128, NCD * T], BF16)
    kn = sb("kn", [128, NCD * T], BF16)
    bt = sb("bt", [128, 16 * 4], F32)
    csb = sb("cst", [128, CST_W], F32)
    idb = sb("idb", [128, 128], BF16)
    ktm = [sb(f"ktm{i}", [128, 256], BF16) for i in range(2)]
    vtm = [sb(f"vtm{i}", [128, 256], BF16) for i in range(2)]
    Ab = sb("A", [128, 128], BF16)
    BT = sb("BT", [128, 128], BF16)
    Xb = sb("X", [128, 512], BF16)
    wfm = sb("wfm", [128, 256], BF16)
    u2 = sb("u2", [128, 256], BF16)
    al = sb("al", [128, 128], BF16)
    s32 = sb("s32", [128, 512], F32)
    sbf = sb("sbf", [128, 512], BF16)
    dstg = [sb(f"dstg{i}", [128, 256], BF16) for i in range(2)]

    pG = st.enter_context(nc.psum_tensor("dl_pG", [128, 128], F32))
    pX = st.enter_context(nc.psum_tensor("dl_pX", [128, 512], F32))
    pB = st.enter_context(nc.psum_tensor("dl_pB", [128, 128], F32))
    pT = st.enter_context(nc.psum_tensor("dl_pT", [128, 256], BF16))
    pt1 = st.enter_context(nc.psum_tensor("dl_pt1", [128, 256], F32))
    pS = st.enter_context(nc.psum_tensor("dl_pS", [128, 512], F32))
    pO = st.enter_context(nc.psum_tensor("dl_pO", [128, 256], F32))

    load_fm(em, qn[:], dram["q_n"], ch="dl")
    load_fm(em, kn[:], dram["k_n"], ch="dl")
    em.dma(lambda e: e.dma_start(out=bt[:].rearrange("p (tt h) -> p tt h", tt=16),
                                 in_=dram["beta_tm"].rearrange("(tt p) h -> p tt h", p=128)), ch="dl")
    em.dma(lambda e: e.dma_start(out=csb[:], in_=dram["cst_f32"]), ch="dl")
    ld = em.ch_all("dl")
    em.wait("vector", ld)
    idt = em.v(lambda v: v.tensor_copy(idb[:], cst(csb, "ident")))

    def ldtile(h, n, wait_tok=None):
        i = n % 2
        em.wait("sync", wait_tok)
        em.dma(lambda e, h=h, n=n, i=i: e.dma_start(out=ktm[i][:], in_=dram["k_tm"][h * T + n * C: h * T + (n + 1) * C, :]), ch=f"dlt{i}")
        em.dma(lambda e, h=h, n=n, i=i: e.dma_start(out=vtm[i][:], in_=dram["v_tm"][h * T + n * C: h * T + (n + 1) * C, :]), ch=f"dlt{i}")
        return em.ch_all(f"dlt{i}")

    tok = ldtile(0, 0)
    tile_toks = {(0, 0): tok}
    last_store = {0: None, 1: None}
    for h in range(NH):
        zs = em.v(lambda v: v.memset(s32[:], 0.0))
        zb = em.v(lambda v: v.memset(sbf[:], 0.0))
        for n in range(NCHUNK):
            i = n % 2
            ltok = tile_toks.pop((h, n))
            em.wait("tensor", zb)
            nh, nn = (h, n + 1) if n + 1 < NCHUNK else (h + 1, 0)
            if nh < NH:
                tile_toks[(nh, nn)] = ldtile(nh, nn, wait_tok=zb)
            bcol = bt[:, n * 4 + h: n * 4 + h + 1]
            em.wait("tensor", ld)
            gt = None
            for dc in range(2):
                kc = kn[:, (2 * h + dc) * T + n * C: (2 * h + dc) * T + (n + 1) * C]
                gt = em.t(lambda t, kc=kc, dc=dc: t.matmul(pG[:], kc, kc, start=(dc == 0), stop=(dc == 1)),
                          tok=(dc == 1))
            em.wait("vector", gt, ltok)
            a1 = em.v(lambda v: v.tensor_mul(Ab[:], pG[:], cst(csb, "strict_low")))
            a2 = em.v(lambda v, bcol=bcol: v.tensor_scalar(Ab[:], Ab[:], bcol, -1.0, ALU.mult, ALU.mult))
            x1 = em.v(lambda v, i=i, bcol=bcol: v.tensor_scalar_mul(Xb[:, 0:256], vtm[i][:], bcol))
            x2 = em.v(lambda v, i=i, bcol=bcol: v.tensor_scalar_mul(Xb[:, 256:512], ktm[i][:], bcol))
            prevBT = None
            prevX = x2
            prevB = a2
            for p in range(7):
                em.wait("tensor", prevB, prevX, idt, prevBT)
                trt = em.t(lambda t: t.transpose(pT[:, 0:128], Ab[:], idb[:]), tok=True)
                em.wait("vector", trt)
                prevBT = em.v(lambda v: v.tensor_copy(BT[:], pT[:, 0:128]))
                em.wait("tensor", prevBT, prevX)
                em.t(lambda t: t.matmul(pX[:], idb[:], Xb[:], start=True, stop=False))
                xt = em.t(lambda t: t.matmul(pX[:], BT[:], Xb[:], start=False, stop=True), tok=True)
                if p < 6:
                    bsq = em.t(lambda t: t.matmul(pB[:], BT[:], Ab[:], start=True, stop=True), tok=True)
                em.wait("vector", xt)
                prevX = em.v(lambda v: v.tensor_copy(Xb[:], pX[:]))
                if p < 6:
                    em.wait("vector", bsq)
                    prevB = em.v(lambda v: v.tensor_copy(Ab[:], pB[:]))
            em.wait("tensor", prevX, prevBT)
            trt = None
            for dc in range(2):
                trt = em.t(lambda t, dc=dc: t.transpose(pT[:, 128 * dc:128 * (dc + 1)], Xb[:, 256 + 128 * dc: 256 + 128 * (dc + 1)], idb[:]), tok=(dc == 1))
            em.wait("vector", trt)
            wv = em.v(lambda v: v.tensor_copy(wfm[:], pT[:]))
            em.wait("tensor", wv, zb)
            tt1 = None
            for dc in range(2):
                tt1 = em.t(lambda t, dc=dc: t.matmul(pt1[:], wfm[:, 128 * dc:128 * (dc + 1)], sbf[:, 256 * dc:256 * (dc + 1)],
                                                     start=(dc == 0), stop=(dc == 1)), tok=(dc == 1))
            em.wait("vector", tt1)
            ut = em.v(lambda v: v.tensor_sub(u2[:], Xb[:, 0:256], pt1[:]))
            altt = None
            for dc in range(2):
                kc = kn[:, (2 * h + dc) * T + n * C: (2 * h + dc) * T + (n + 1) * C]
                qc = qn[:, (2 * h + dc) * T + n * C: (2 * h + dc) * T + (n + 1) * C]
                altt = em.t(lambda t, kc=kc, qc=qc, dc=dc: t.matmul(pG[:], kc, qc, start=(dc == 0), stop=(dc == 1)),
                            tok=(dc == 1))
            em.wait("vector", altt)
            alv = em.v(lambda v: v.tensor_mul(al[:], pG[:], cst(csb, "incl_up")))
            em.wait("tensor", alv, ut)
            for dc in range(2):
                qc = qn[:, (2 * h + dc) * T + n * C: (2 * h + dc) * T + (n + 1) * C]
                em.t(lambda t, qc=qc, dc=dc: t.matmul(pO[:], qc, sbf[:, 256 * dc:256 * (dc + 1)], start=(dc == 0), stop=False))
            ot = em.t(lambda t: t.matmul(pO[:], al[:], u2[:], start=False, stop=True), tok=True)
            stt = None
            for dc in range(2):
                stt = em.t(lambda t, dc=dc, i=i: t.matmul(pS[:, 256 * dc:256 * (dc + 1)], ktm[i][:, 128 * dc:128 * (dc + 1)], u2[:],
                                                          start=True, stop=True), tok=(dc == 1))
            em.wait("vector", ot)
            j = n % 2
            em.wait("vector", em.ch_all(f"dlst{j}"))
            ov = em.v(lambda v, j=j: v.tensor_copy(dstg[j][:], pO[:]))
            em.wait("sync", ov)
            dst = dram["delta_tm"][h * T + n * C: h * T + (n + 1) * C, :]
            em.dma(lambda e, dst=dst, j=j: e.dma_start(out=dst, in_=dstg[j][:]), ch=f"dlst{j}")
            em.wait("vector", stt)
            sv = em.v(lambda v: v.tensor_add(s32[:], s32[:], pS[:]))
            zb = em.v(lambda v: v.tensor_copy(sbf[:], s32[:]))
    em.wait("sync", em.ch_all("dlst0"), em.ch_all("dlst1"))


PAD = 63
CW = PAD + T


def stage_l0fir(nc, st, em, dram):
    HW = 512
    sb = lambda nm, sh, dt: st.enter_context(nc.sbuf_tensor(f"l0f_{nm}", sh, dt))
    vpad = sb("vpad", [128, NCD * CW], BF16)
    lsb = sb("ls", [128, NCD * T], BF16)
    llb = sb("ll", [128, NCD * T], BF16)
    wsw = sb("wsw", [128, NCD * 5], F32)
    wlw = sb("wlw", [128, NCD * 64], F32)
    csb = sb("cst", [128, CST_W], F32)
    idb = sb("idb", [128, 128], BF16)
    diag = [sb(f"diag{i}", [128, 69 * 128], BF16) for i in range(2)]
    tile = [sb(f"tile{i}", [128, 256], BF16) for i in range(2)]
    sqb = sb("sqb", [128, 256], F32)
    raw = sb("raw", [128, 12], F32)
    st16 = sb("st16", [128, 16], BF16)
    mtmp = sb("mtmp", [128, 4], F32)
    sstg = [sb(f"sstg{i}", [16, 128], BF16) for i in range(2)]

    pcv = [st.enter_context(nc.psum_tensor(f"l0f_pcv{i}", [128, HW], F32)) for i in range(2)]
    ptr = st.enter_context(nc.psum_tensor("l0f_ptr", [128, 256], BF16))
    pst = st.enter_context(nc.psum_tensor("l0f_pst", [16, 128], BF16))

    zp = em.v(lambda v: v.memset(vpad[:], 0.0))
    em.wait("sync", zp)
    em.dma(lambda e: e.dma_start(
        out=vpad[:].rearrange("p (c t) -> p c t", c=NCD)[:, :, PAD:CW],
        in_=dram["v_s"].rearrange("(c p) t -> p c t", p=128)), ch="l0f")
    em.dma(lambda e: e.dma_start(out=wsw[:].rearrange("p (c j) -> p c j", c=NCD),
                                 in_=dram["L0_attn_fir_short_w"].rearrange("(c p) j -> p c j", p=128)), ch="l0f")
    em.dma(lambda e: e.dma_start(out=wlw[:].rearrange("p (c j) -> p c j", c=NCD),
                                 in_=dram["L0_attn_fir_long_w"].rearrange("(c p) j -> p c j", p=128)), ch="l0f")
    em.dma(lambda e: e.dma_start(out=csb[:], in_=dram["cst_f32"]), ch="l0f")
    ld = em.ch_all("l0f")
    em.wait("vector", ld)
    idt = em.v(lambda v: v.tensor_copy(idb[:], cst(csb, "ident")))

    diag_user = [None, None]
    pcv_user = [None, None]
    evac_tok = {}
    rr = 0
    for c in range(NCD):
        i = c % 2
        em.wait("vector", diag_user[i], ld)
        dbuild = None
        for j in range(64):
            wj = wlw[:, c * 64 + j: c * 64 + j + 1]
            dbuild = em.v(lambda v, j=j, wj=wj, i=i: v.tensor_scalar_mul(
                diag[i][:, j * 128:(j + 1) * 128], idb[:], wj))
        for j in range(5):
            wj = wsw[:, c * 5 + j: c * 5 + j + 1]
            dbuild = em.v(lambda v, j=j, wj=wj, i=i: v.tensor_scalar_mul(
                diag[i][:, (64 + j) * 128:(64 + j + 1) * 128], idb[:], wj))
        base = c * CW
        for hh in range(4):
            for kind in ("L", "S"):
                pp = rr % 2
                rr += 1
                em.wait("tensor", dbuild, ld, pcv_user[pp])
                mm = None
                taps = range(64) if kind == "L" else range(5)
                for jj, j in enumerate(taps):
                    off = j if kind == "L" else 59 + j
                    dslot = j if kind == "L" else 64 + j
                    xs_ = vpad[:, base + off + hh * HW: base + off + hh * HW + HW]
                    dg = diag[i][:, dslot * 128:(dslot + 1) * 128]
                    last = (jj == (63 if kind == "L" else 4))
                    mm = em.t(lambda t, dg=dg, xs_=xs_, jj=jj, pp=pp, last=last: t.matmul(
                        pcv[pp][:], dg, xs_, start=(jj == 0), stop=last), tok=last)
                diag_user[i] = mm
                em.wait("scalar", mm)
                dstb = (llb if kind == "L" else lsb)[:, c * T + hh * HW: c * T + hh * HW + HW]
                ev = em.s(lambda sc, dstb=dstb, pp=pp: sc.activation(out=dstb, in_=pcv[pp][:], func=AF.Copy))
                pcv_user[pp] = ev
                evac_tok[(kind, c, hh)] = ev

    ptr_free = None
    pst_free = None
    sqb_user = None
    tile_free = [None, None]
    for h in range(NH):
        cvdone = [evac_tok[(k, 2 * h + dc, hh)] for k in ("L", "S") for dc in (0, 1) for hh in range(4)]
        for tt in range(16):
            ib = tt % 2
            for q in range(4):
                if q < 2:
                    srcb = lsb if q == 0 else llb
                    em.wait("tensor", idt, ptr_free, *cvdone)
                    trt = None
                    for dc in range(2):
                        xc = srcb[:, (2 * h + dc) * T + tt * 128: (2 * h + dc) * T + (tt + 1) * 128]
                        trt = em.t(lambda t, xc=xc, dc=dc: t.transpose(ptr[:, dc * 128:(dc + 1) * 128], xc, idb[:]),
                                   tok=(dc == 1))
                    em.wait("vector", trt, tile_free[ib] if q == 0 else None)
                    tv = em.v(lambda v, ib=ib: v.tensor_copy(tile[ib][:], ptr[:]))
                    ptr_free = tv
                    src_ap = tile[ib][:]
                    ltok = tv
                else:
                    nm = "delta_tm" if q == 2 else "v_tm"
                    em.wait("sync", tile_free[ib])
                    em.dma(lambda e, nm=nm, h=h, tt=tt, ib=ib: e.dma_start(
                        out=tile[ib][:], in_=dram[nm][h * T + tt * 128: h * T + (tt + 1) * 128, :]), ch=f"l0ft{ib}")
                    ltok = em.ch_all(f"l0ft{ib}")
                    em.wait("vector", ltok)
                    src_ap = tile[ib][:]
                em.v(lambda v, src_ap=src_ap, q=q: v.reduce_sum(raw[:, 3 * q: 3 * q + 1], src_ap, AX.X))
                em.wait("scalar", ltok, sqb_user)
                sqt = em.s(lambda sc, src_ap=src_ap: sc.activation(out=sqb[:], in_=src_ap, func=AF.Square))
                em.wait("vector", sqt)
                sqb_user = em.v(lambda v, q=q: v.reduce_sum(raw[:, 3 * q + 1: 3 * q + 2], sqb[:], AX.X))
                lastr = em.v(lambda v, src_ap=src_ap, q=q: v.reduce_sum(raw[:, 3 * q + 2: 3 * q + 3], src_ap, AX.X,
                                                                       apply_absolute_value=True))
                tile_free[ib] = lastr
            s16v = lambda i: st16[:].rearrange("p (q s) -> p q s", s=4)[:, :, i:i + 1]
            rawv = lambda i: raw[:].rearrange("p (q s) -> p q s", s=3)[:, :, i:i + 1]
            em.v(lambda v: v.tensor_scalar_mul(s16v(0), rawv(0), 1.0 / 256))
            em.v(lambda v: v.tensor_scalar_mul(mtmp[:].rearrange("p (q s) -> p q s", s=1), rawv(0), 1.0 / 256))
            em.v(lambda v: v.tensor_mul(mtmp[:], mtmp[:], mtmp[:]))
            em.v(lambda v: v.scalar_tensor_tensor(s16v(1), rawv(1), 1.0 / 256,
                                                  mtmp[:].rearrange("p (q s) -> p q s", s=1), ALU.mult, ALU.subtract))
            fin = em.v(lambda v: v.tensor_scalar_mul(s16v(2), rawv(2), 1.0 / 256))
            em.wait("scalar", lastr, fin)
            sqr = em.s(lambda sc: sc.activation(out=s16v(3), in_=rawv(1), func=AF.Sqrt))
            em.wait("tensor", sqr, fin, pst_free)
            tst = em.t(lambda t: t.transpose(pst[:], st16[:], idb[:]), tok=True)
            jb = tt % 2
            em.wait("vector", tst, em.ch_all(f"l0fs{jb}"))
            sv = em.v(lambda v, jb=jb: v.tensor_copy(sstg[jb][:], pst[:]))
            pst_free = sv
            em.wait("sync", sv)
            dst = dram["stats_f"][0:16, h * T + tt * 128: h * T + (tt + 1) * 128]
            em.dma(lambda e, dst=dst, jb=jb: e.dma_start(out=dst, in_=sstg[jb][:]), ch=f"l0fs{jb}")
    st1 = store_fm(em, dram["ls_fm"], lsb[:], ch="l0fst")
    st2 = store_fm(em, dram["ll_fm"], llb[:], ch="l0fst")
    em.wait("sync", em.ch_all("l0fst"), em.ch_all("l0fs0"), em.ch_all("l0fs1"))


def stage_l0gate(nc, st, em, dram):
    sb = lambda nm, sh, dt: st.enter_context(nc.sbuf_tensor(f"l0g_{nm}", sh, dt))
    xnb = sb("xn", [128, NCD * T], BF16)
    w1a = sb("w1a", [128, NCD * 1024], BF16)
    w1b = sb("w1b", [16, 1024], BF16)
    b1 = sb("b1", [128, NCD], F32)
    w2 = sb("w2", [128, NCD * 4], BF16)
    b2r = sb("b2r", [1, 4], F32)
    ltr = sb("ltr", [1, 4], F32)
    rtr = sb("rtr", [1, 4], F32)
    stf = sb("stf", [16, NH * T], BF16)
    csb = sb("cst", [128, CST_W], F32)
    idb = sb("idb", [128, 128], BF16)
    h1s = sb("h1s", [128, NCD * TS], BF16)
    b2bc = sb("b2bc", [128, 4], F32)
    rtc = sb("rtc", [128, 4], F32)
    tsb = sb("tsb", [128, 4], F32)
    nmax = sb("nmax", [128, 1], F32)
    zsum = sb("zsum", [128, 1], F32)
    rz = sb("rz", [128, 1], F32)
    eb = sb("eb", [128, 4], F32)
    prt = sb("prt", [128, 4], BF16)
    pstg = [sb(f"pstg{i}", [4, 128], BF16) for i in range(2)]

    ph1 = [st.enter_context(nc.psum_tensor(f"l0g_ph1_{i}", [128, TS], F32)) for i in range(2)]
    plog = st.enter_context(nc.psum_tensor("l0g_plog", [128, 4], F32))
    pbc = st.enter_context(nc.psum_tensor("l0g_pbc", [128, 4], F32))
    ppr = st.enter_context(nc.psum_tensor("l0g_ppr", [4, 128], BF16))

    load_fm(em, xnb[:], dram["xn"], ch="l0g")
    em.dma(lambda e: e.dma_start(out=csb[:], in_=dram["cst_f32"]), ch="l0g")
    em.dma(lambda e: e.dma_start(out=stf[:], in_=dram["stats_f"]), ch="l0g")
    load_pp(em, b1[:], dram["L0_attn_mlp_b1"], ch="l0g")
    em.dma(lambda e: e.dma_start(out=b2r[:], in_=dram["L0_attn_mlp_b2"].rearrange("(o f) -> o f", o=1)), ch="l0g")
    em.dma(lambda e: e.dma_start(out=ltr[:], in_=dram["L0_attn_log_temp"].rearrange("(a h) one -> a (h one)", a=1)), ch="l0g")
    em.dma(lambda e: e.dma_start(out=w1a[:].rearrange("p (c m) -> p c m", c=NCD),
                                 in_=dram["L0_attn_mlp_w1"][0:1024, :].rearrange("(c p) m -> p c m", p=128)),
           q="gpsimd", ch="l0gw")
    em.dma(lambda e: e.dma_start(out=w1b[:], in_=dram["L0_attn_mlp_w1"][1024:1040, :]), q="gpsimd", ch="l0gw")
    em.dma(lambda e: e.dma_start(out=w2[:].rearrange("p (c m) -> p c m", c=NCD),
                                 in_=dram["L0_attn_mlp_w2"].rearrange("(c p) m -> p c m", p=128)),
           q="gpsimd", ch="l0gw")
    ld = em.ch_all("l0g")
    ldw = em.ch_all("l0gw")
    em.wait("vector", ld)
    idt = em.v(lambda v: v.tensor_copy(idb[:], cst(csb, "ident")))
    em.wait("scalar", ld)
    ext = em.s(lambda sc: sc.activation(out=rtr[:], in_=ltr[:], func=AF.Exp))
    em.wait("vector", ext)
    rct = em.v(lambda v: v.reciprocal(rtr[:], rtr[:]))
    em.wait("tensor", rct)
    bt1 = em.t(lambda t: t.matmul(pbc[:], onesrow(csb), b2r[:], start=True, stop=True), tok=True)
    em.wait("vector", bt1)
    bv1 = em.v(lambda v: v.tensor_copy(b2bc[:], pbc[:]))
    bv2 = bv1
    for h in range(NH):
        em.wait("tensor", bv2)
        bt2 = em.t(lambda t, h=h: t.matmul(pbc[:, 0:1], onesrow(csb), rtr[0:1, h:h + 1], start=True, stop=True), tok=True)
        em.wait("vector", bt2)
        bv2 = em.v(lambda v, h=h: v.tensor_copy(rtc[:, h:h + 1], pbc[:, 0:1]))

    ph_free = [None, None]
    plog_free = None
    ppr_free = None
    for h in range(NH):
        for s in range(NSTRIP):
            for mc in range(NCD):
                ib = mc % 2
                em.wait("tensor", ld, ldw, ph_free[ib])
                for kc in range(NCD):
                    wk = w1a[:, kc * 1024 + mc * 128: kc * 1024 + (mc + 1) * 128]
                    xk = xnb[:, kc * T + s * TS: kc * T + (s + 1) * TS]
                    em.t(lambda t, wk=wk, xk=xk, kc=kc, ib=ib: t.matmul(
                        ph1[ib][:], wk, xk, start=(kc == 0), stop=False))
                sk = stf[0:16, h * T + s * TS: h * T + (s + 1) * TS]
                w1bk = w1b[0:16, mc * 128: (mc + 1) * 128]
                mt = em.t(lambda t, sk=sk, w1bk=w1bk, ib=ib: t.matmul(
                    ph1[ib][:], w1bk, sk, start=False, stop=True), tok=True)
                em.wait("scalar", mt, ldw)
                b1c = b1[:, mc: mc + 1]
                gl = em.s(lambda sc, ib=ib, b1c=b1c, mc=mc: sc.activation(
                    out=h1s[:, mc * TS:(mc + 1) * TS], in_=ph1[ib][:], func=AF.Gelu, bias=b1c))
                ph_free[ib] = gl
            for tt in range(4):
                gtt = s * 4 + tt
                jb = gtt % 2
                em.wait("tensor", gl, plog_free)
                lt = None
                for mc in range(NCD):
                    hk = h1s[:, mc * TS + tt * 128: mc * TS + (tt + 1) * 128]
                    wk2 = w2[:, mc * 4: (mc + 1) * 4]
                    lt = em.t(lambda t, hk=hk, wk2=wk2, mc=mc: t.matmul(
                        plog[:], hk, wk2, start=(mc == 0), stop=(mc == NCD - 1)), tok=(mc == NCD - 1))
                em.wait("vector", lt, bv1, bv2)
                em.v(lambda v: v.tensor_add(tsb[:], plog[:], b2bc[:]))
                em.v(lambda v, h=h: v.tensor_scalar_mul(tsb[:], tsb[:], rtc[:, h:h + 1]))
                mx = em.v(lambda v: v.reduce_max(nmax[:], tsb[:], AX.X, negate=True))
                em.wait("scalar", mx)
                et = em.s(lambda sc: sc.activation(out=eb[:], in_=tsb[:], func=AF.Exp, bias=nmax[:, 0:1],
                                                   accum_out=zsum[:, 0:1]))
                em.wait("vector", et)
                plog_free = em.v(lambda v: v.reciprocal(rz[:], zsum[:]))
                em.v(lambda v: v.tensor_scalar_mul(rz[:], rz[:], 0.8))
                pv = em.v(lambda v: v.tensor_scalar(prt[:], eb[:], rz[:, 0:1], 0.05, ALU.mult, ALU.add))
                em.wait("tensor", pv, idt, ppr_free)
                ptk = em.t(lambda t: t.transpose(ppr[:], prt[:], idb[:]), tok=True)
                em.wait("vector", ptk, em.ch_all(f"l0gp{jb}"))
                sv = em.v(lambda v, jb=jb: v.tensor_copy(pstg[jb][:], ppr[:]))
                ppr_free = sv
                em.wait("sync", sv)
                dst = dram["probs_rows"][0:4, h * T + gtt * 128: h * T + (gtt + 1) * 128]
                em.dma(lambda e, dst=dst, jb=jb: e.dma_start(out=dst, in_=pstg[jb][:]), ch=f"l0gp{jb}")
    em.wait("sync", em.ch_all("l0gp0"), em.ch_all("l0gp1"))


def stage_l0mix(nc, st, em, dram):
    sb = lambda nm, sh, dt: st.enter_context(nc.sbuf_tensor(f"l0m_{nm}", sh, dt))
    lsb = sb("ls", [128, NCD * T], BF16)
    llb = sb("ll", [128, NCD * T], BF16)
    vsb = sb("vs", [128, NCD * T], BF16)
    onb = sb("on", [128, NCD * T], BF16)
    wob = sb("wo", [128, NCD * 1024], BF16)
    prw = sb("prw", [4, NH * T], BF16)
    selb = sb("sel", [32, CSTB_W], BF16)
    csb = sb("cst", [128, CST_W], F32)
    idb = sb("idb", [128, 128], BF16)
    onw = sb("onw", [128, 2], F32)
    dtile = [sb(f"dt{i}", [128, 4 * 256], BF16) for i in range(2)]
    dstg = sb("dstg", [128, 2 * TS], BF16)
    accb = sb("acc", [128, 2 * TS], F32)
    tmpb = sb("tmp", [128, TS], F32)
    sqb = sb("sq", [128, TS], F32)
    rowb = sb("row", [1, 2 * TS], F32)
    ystg = [sb(f"ystg{i}", [128, TS], F32) for i in range(2)]

    pbc = [st.enter_context(nc.psum_tensor(f"l0m_pbc{i}", [128, TS], F32)) for i in range(4)]
    pd = st.enter_context(nc.psum_tensor("l0m_pd", [128, TS], BF16))
    prow = st.enter_context(nc.psum_tensor("l0m_prow", [1, TS], F32))
    pbr = st.enter_context(nc.psum_tensor("l0m_pbr", [128, TS], F32))
    pwo = st.enter_context(nc.psum_tensor("l0m_pwo", [128, TS], F32))

    load_fm(em, lsb[:], dram["ls_fm"], ch="l0m")
    load_fm(em, llb[:], dram["ll_fm"], ch="l0m")
    load_fm(em, vsb[:], dram["v_s"], ch="l0m")
    em.dma(lambda e: e.dma_start(out=prw[:], in_=dram["probs_rows"]), ch="l0m")
    em.dma(lambda e: e.dma_start(out=selb[:], in_=dram["cst_bf16"]), ch="l0m")
    em.dma(lambda e: e.dma_start(out=csb[:], in_=dram["cst_f32"]), ch="l0m")
    load_pp(em, onw[:], dram["L0_attn_o_norm_w"], ch="l0m")
    em.dma(lambda e: e.dma_start(out=wob[:].rearrange("p (c m) -> p c m", c=NCD),
                                 in_=dram["L0_attn_Wo"].rearrange("(c p) m -> p c m", p=128)), q="gpsimd", ch="l0mw")
    ld = em.ch_all("l0m")
    ldw = em.ch_all("l0mw")
    em.wait("vector", ld)
    idt = em.v(lambda v: v.tensor_copy(idb[:], cst(csb, "ident")))

    eps1 = csb[0:1, CST["eps_ln"][0]:CST["eps_ln"][0] + 1]
    pd_free = None
    pbc_free = [None] * 4
    prow_free = None
    pbr_free = None
    acc_free = None
    dt_free = [None, None]
    for h in range(NH):
        for s in range(NSTRIP):
            ib = (h * NSTRIP + s) % 2
            em.wait("sync", dt_free[ib])
            em.dma(lambda e, h=h, s=s, ib=ib: e.dma_start(
                out=dtile[ib][:].rearrange("p (j d) -> p j d", j=4),
                in_=dram["delta_tm"][h * T + s * TS: h * T + (s + 1) * TS, :].rearrange("(j p) d -> p j d", p=128)),
                ch=f"l0md{ib}")
            dl = em.ch_all(f"l0md{ib}")
            bts = []
            for i in range(4):
                em.wait("tensor", ld, pbc_free[i])
                pr = prw[0:4, h * T + s * TS: h * T + (s + 1) * TS]
                sl = selb[0:4, i * 128: (i + 1) * 128]
                bts.append(em.t(lambda t, pr=pr, sl=sl, i=i: t.matmul(pbc[i][:], sl, pr, start=True, stop=True), tok=True))
            for dc in range(2):
                em.wait("tensor", idt, dl, pd_free)
                trt = None
                for j in range(4):
                    xc = dtile[ib][:, j * 256 + dc * 128: j * 256 + (dc + 1) * 128]
                    trt = em.t(lambda t, xc=xc, j=j: t.transpose(pd[:, j * 128:(j + 1) * 128], xc, idb[:]),
                               tok=(j == 3))
                dt_free[ib] = trt
                em.wait("vector", trt)
                pd_free = em.v(lambda v, dc=dc: v.tensor_copy(dstg[:, dc * TS:(dc + 1) * TS], pd[:]))
                c = 2 * h + dc
                em.wait("vector", bts[0], bts[1], bts[2], bts[3], acc_free)
                adst = accb[:, dc * TS:(dc + 1) * TS]
                em.v(lambda v, c=c, s=s, adst=adst: v.tensor_mul(
                    adst, lsb[:, c * T + s * TS: c * T + (s + 1) * TS], pbc[0][:]))
                for i, srcb in ((1, llb), (3, vsb)):
                    em.v(lambda v, c=c, s=s, srcb=srcb, i=i: v.tensor_mul(
                        tmpb[:], srcb[:, c * T + s * TS: c * T + (s + 1) * TS], pbc[i][:]))
                    em.v(lambda v, adst=adst: v.tensor_add(adst, adst, tmpb[:]))
                em.v(lambda v, dc=dc: v.tensor_mul(tmpb[:], dstg[:, dc * TS:(dc + 1) * TS], pbc[2][:]))
                mfin = em.v(lambda v, adst=adst: v.tensor_add(adst, adst, tmpb[:]))
                for i in range(4):
                    pbc_free[i] = mfin
            mmt = None
            for dc in range(2):
                em.wait("scalar", mfin, prow_free, mmt)
                sqt = em.s(lambda sc, dc=dc: sc.activation(out=sqb[:], in_=accb[:, dc * TS:(dc + 1) * TS], func=AF.Square))
                em.wait("tensor", sqt, prow_free)
                mmt = em.t(lambda t, dc=dc: t.matmul(prow[:], cst(csb, "ones"), sqb[:],
                                                     start=(dc == 0), stop=(dc == 1)), tok=True)
            em.wait("vector", mmt)
            em.v(lambda v: v.tensor_scalar_mul(rowb[:, 0:TS], prow[:], 1.0 / 256))
            rv = em.v(lambda v: v.engine_nop())
            prow_free = rv
            em.wait("scalar", rv)
            sq2 = em.s(lambda sc: sc.activation(out=rowb[:, TS:2 * TS], in_=rowb[:, 0:TS], func=AF.Sqrt, bias=eps1))
            em.wait("vector", sq2)
            rc = em.v(lambda v: v.reciprocal(rowb[:, 0:TS], rowb[:, TS:2 * TS]))
            em.wait("tensor", rc, pbr_free)
            bct = em.t(lambda t: t.matmul(pbr[:], onesrow(csb), rowb[0:1, 0:TS], start=True, stop=True), tok=True)
            em.wait("vector", bct)
            af = None
            for dc in range(2):
                c = 2 * h + dc
                oc = onb[:, c * T + s * TS: c * T + (s + 1) * TS]
                wcol = onw[:, dc: dc + 1]
                af = em.v(lambda v, oc=oc, wcol=wcol, dc=dc: v.scalar_tensor_tensor(
                    oc, accb[:, dc * TS:(dc + 1) * TS], wcol, pbr[:], ALU.mult, ALU.mult))
            pbr_free = af
            acc_free = af
    onfin = af
    pwo_free = None
    for mc in range(NCD):
        for s in range(NSTRIP):
            jb = (mc * NSTRIP + s) % 2
            em.wait("tensor", onfin, ldw, pwo_free)
            mt = None
            for kc in range(NCD):
                wk = wob[:, kc * 1024 + mc * 128: kc * 1024 + (mc + 1) * 128]
                ok = onb[:, kc * T + s * TS: kc * T + (s + 1) * TS]
                mt = em.t(lambda t, wk=wk, ok=ok, kc=kc: t.matmul(pwo[:], wk, ok,
                                                                  start=(kc == 0), stop=(kc == NCD - 1)),
                          tok=(kc == NCD - 1))
            em.wait("vector", mt, em.ch_all(f"l0my{jb}"))
            yv = em.v(lambda v, jb=jb: v.tensor_copy(ystg[jb][:], pwo[:]))
            pwo_free = yv
            em.wait("sync", yv)
            dst = dram["y_l0attn"][mc * 128:(mc + 1) * 128, s * TS:(s + 1) * TS]
            em.dma(lambda e, dst=dst, jb=jb: e.dma_start(out=dst, in_=ystg[jb][:]), ch=f"l0my{jb}")
    em.wait("sync", em.ch_all("l0my0"), em.ch_all("l0my1"))


def stage_l0moe(nc, st, em, dram, E=32):
    NI = 6
    sb = lambda nm, sh, dt: st.enter_context(nc.sbuf_tensor(f"l0e_{nm}", sh, dt))
    xnb = sb("xn", [128, NCD * T], BF16)
    acc = sb("acc", [128, NCD * T], F32)
    gwf = sb("gwf", [32, 1024], F32)
    gwT = sb("gwT", [128, NCD * 32], BF16)
    csb = sb("cst", [128, CST_W], F32)
    selb = sb("sel", [32, CSTB_W], BF16)
    idb = sb("idb", [128, 128], BF16)
    ebr = sb("ebr", [1, 32], F32)
    ebbc = sb("ebbc", [128, 32], F32)
    scb = sb("scb", [128, 32], F32)
    s0b = sb("s0b", [128, 32], F32)
    spb = sb("spb", [128, 32], F32)
    mcol = sb("mcol", [128, 1], F32)
    msku = sb("msku", [128, 32], U8)
    mskf = sb("mskf", [128, 32], F32)
    negb = sb("negb", [128, 32], F32)
    rzc = sb("rzc", [128, 1], F32)
    cmbt = sb("cmbt", [128, 32], BF16)
    cmbr = sb("cmbr", [32, T], BF16)
    w13 = [sb(f"w13_{i}", [128, NCD * 768 * 2], BF16) for i in range(2)]
    w2b = sb("w2b", [128, NI * 1024], BF16)
    silu_t = sb("silu", [128, TS], F32)
    t2 = sb("t2", [128, TS], F32)
    asb = sb("asb", [128, NI * TS], BF16)
    ystg = sb("ystg", [128, TS], F32)

    pr = st.enter_context(nc.psum_tensor("l0e_pr", [128, 32], F32))
    prt = st.enter_context(nc.psum_tensor("l0e_prt", [32, 128], BF16))
    peb = st.enter_context(nc.psum_tensor("l0e_peb", [128, 32], F32))
    pg = st.enter_context(nc.psum_tensor("l0e_pg", [128, TS], F32))
    pu = st.enter_context(nc.psum_tensor("l0e_pu", [128, TS], F32))
    pcmb = st.enter_context(nc.psum_tensor("l0e_pcmb", [128, TS], F32))
    py = [st.enter_context(nc.psum_tensor(f"l0e_py{i}", [128, TS], F32)) for i in range(2)]

    load_fm(em, xnb[:], dram["xn"], ch="l0e")
    em.dma(lambda e: e.dma_start(out=gwf[:], in_=dram["L0_ffn_gate_w"]), ch="l0e")
    em.dma(lambda e: e.dma_start(out=csb[:], in_=dram["cst_f32"]), ch="l0e")
    em.dma(lambda e: e.dma_start(out=selb[:], in_=dram["cst_bf16"]), ch="l0e")
    em.dma(lambda e: e.dma_start(out=ebr[:], in_=dram["L0_ffn_e_bias"].rearrange("(o f) -> o f", o=1)), ch="l0e")
    ld = em.ch_all("l0e")
    em.wait("vector", ld)
    idt = em.v(lambda v: v.tensor_copy(idb[:], cst(csb, "ident")))
    zacc = em.v(lambda v: v.memset(acc[:], 0.0))
    zneg = em.v(lambda v: v.memset(negb[:], -1e30))
    gt_free = None
    for c in range(NCD):
        em.wait("tensor", ld, gt_free)
        tg = em.t(lambda t, c=c: t.transpose(pr[:], gwf[0:32, c * 128:(c + 1) * 128],
                                             csb[0:32, CST["ident"][0]:CST["ident"][0] + 32]), tok=True)
        em.wait("vector", tg)
        gt_free = em.v(lambda v, c=c: v.tensor_copy(gwT[:, c * 32:(c + 1) * 32], pr[:, 0:32]))
    em.wait("tensor", gt_free)
    bt = em.t(lambda t: t.matmul(peb[:], onesrow(csb), ebr[:], start=True, stop=True), tok=True)
    em.wait("vector", bt)
    ebv = em.v(lambda v: v.tensor_copy(ebbc[:], peb[:]))
    pr_free = gt_free
    prt_free = None
    cm_fin = None
    for tt in range(16):
        em.wait("tensor", pr_free)
        mt = None
        for kc in range(NCD):
            xk = xnb[:, kc * T + tt * 128: kc * T + (tt + 1) * 128]
            gk = gwT[:, kc * 32:(kc + 1) * 32]
            mt = em.t(lambda t, xk=xk, gk=gk, kc=kc: t.matmul(pr[:], xk, gk,
                                                              start=(kc == 0), stop=(kc == NCD - 1)), tok=(kc == NCD - 1))
        em.wait("scalar", mt)
        sg = em.s(lambda sc: sc.activation(out=scb[:], in_=pr[:], func=AF.Sigmoid))
        pr_free = sg
        em.wait("vector", sg, ebv, zneg)
        em.v(lambda v: v.tensor_add(s0b[:], scb[:], ebbc[:]))
        em.v(lambda v: v.tensor_copy(spb[:], s0b[:]))
        for r in range(4):
            em.v(lambda v: v.reduce_max(mcol[:], spb[:], AX.X))
            if r < 3:
                em.v(lambda v: v.tensor_single_scalar(mskf[:], spb[:], mcol[:, 0:1], ALU.is_ge))
                em.v(lambda v: v.scalar_tensor_tensor(spb[:], mskf[:], -1e30, spb[:], ALU.mult, ALU.add))
        em.v(lambda v: v.tensor_single_scalar(mskf[:], s0b[:], mcol[:, 0:1], ALU.is_ge))
        em.v(lambda v: v.tensor_mul(mskf[:], mskf[:], scb[:]))
        em.v(lambda v: v.reduce_sum(rzc[:], mskf[:], AX.X))
        em.v(lambda v: v.reciprocal(rzc[:], rzc[:]))
        cmv = em.v(lambda v: v.tensor_scalar_mul(cmbt[:], mskf[:], rzc[:, 0:1]))
        em.wait("tensor", cmv, idt, prt_free)
        ct = em.t(lambda t: t.transpose(prt[:], cmbt[:], idb[:]), tok=True)
        em.wait("vector", ct)
        cm_fin = em.v(lambda v, tt=tt: v.tensor_copy(cmbr[:, tt * 128:(tt + 1) * 128], prt[:]))
        prt_free = cm_fin
    def ldw13(e):
        i = e % 2
        em.dma(lambda ee, e=e, i=i: ee.dma_start(
            out=w13[i][:, 0:NCD * 768].rearrange("p (c m) -> p c m", c=NCD),
            in_=dram["L0_ffn_w1"][e].rearrange("(c p) m -> p c m", p=128)), q="gpsimd", ch=f"l0ew{i}")
        em.dma(lambda ee, e=e, i=i: ee.dma_start(
            out=w13[i][:, NCD * 768:].rearrange("p (c m) -> p c m", c=NCD),
            in_=dram["L0_ffn_w3"][e].rearrange("(c p) m -> p c m", p=128)), q="gpsimd", ch=f"l0ew{i}")
        return em.ch_all(f"l0ew{i}")

    wtok = {0: ldw13(0)}
    last_w13_user = [None, None]
    w2_user = None
    pg_free, pu_free, pcmb_free = None, None, None
    py_free = [None, None]
    afin = None
    for e in range(E):
        i = e % 2
        if e + 1 < E:
            em.wait("gpsimd", last_w13_user[(e + 1) % 2])
            wtok[e + 1] = ldw13(e + 1)
        em.wait("gpsimd", w2_user)
        em.dma(lambda ee, e=e: ee.dma_start(
            out=w2b[:].rearrange("p (c m) -> p c m", c=NI),
            in_=dram["L0_ffn_w2"][e].rearrange("(c p) m -> p c m", p=128)), q="gpsimd", ch="l0ew2")
        w2tok = em.ch_all("l0ew2")
        wt = wtok.pop(e)
        for s in range(NSTRIP):
            em.wait("tensor", cm_fin, pcmb_free)
            sl = selb[0:32, e * 128:(e + 1) * 128]
            cr = cmbr[0:32, s * TS:(s + 1) * TS]
            cbt = em.t(lambda t, sl=sl, cr=cr: t.matmul(pcmb[:], sl, cr, start=True, stop=True), tok=True)
            for ic in range(NI):
                em.wait("tensor", wt, ld, pg_free, pu_free)
                for kc in range(NCD):
                    w1k = w13[i][:, kc * 768 + ic * 128: kc * 768 + (ic + 1) * 128]
                    xk = xnb[:, kc * T + s * TS: kc * T + (s + 1) * TS]
                    em.t(lambda t, w1k=w1k, xk=xk, kc=kc: t.matmul(pg[:], w1k, xk,
                                                                   start=(kc == 0), stop=(kc == NCD - 1)))
                gmt = None
                for kc in range(NCD):
                    w3k = w13[i][:, NCD * 768 + kc * 768 + ic * 128: NCD * 768 + kc * 768 + (ic + 1) * 128]
                    xk = xnb[:, kc * T + s * TS: kc * T + (s + 1) * TS]
                    gmt = em.t(lambda t, w3k=w3k, xk=xk, kc=kc: t.matmul(pu[:], w3k, xk,
                                                                         start=(kc == 0), stop=(kc == NCD - 1)),
                               tok=(kc == NCD - 1))
                em.wait("scalar", gmt)
                sil = em.s(lambda sc: sc.activation(out=silu_t[:], in_=pg[:], func=AF.Silu))
                pg_free = sil
                em.wait("vector", sil, cbt)
                pu_free = em.v(lambda v: v.tensor_mul(t2[:], silu_t[:], pu[:]))
                av = em.v(lambda v, ic=ic: v.tensor_mul(asb[:, ic * TS:(ic + 1) * TS], t2[:], pcmb[:]))
            pcmb_free = av
            for oc in range(NCD):
                jb = oc % 2
                em.wait("tensor", av, w2tok, py_free[jb])
                ymt = None
                for ic in range(NI):
                    w2k = w2b[:, ic * 1024 + oc * 128: ic * 1024 + (oc + 1) * 128]
                    ak = asb[:, ic * TS:(ic + 1) * TS]
                    ymt = em.t(lambda t, w2k=w2k, ak=ak, ic=ic, jb=jb: t.matmul(py[jb][:], w2k, ak,
                                                                                start=(ic == 0), stop=(ic == NI - 1)),
                               tok=(ic == NI - 1))
                w2_user = ymt
                last_w13_user[i] = ymt
                em.wait("vector", ymt, zacc)
                ac = acc[:, oc * T + s * TS: oc * T + (s + 1) * TS]
                afin = em.v(lambda v, ac=ac, jb=jb: v.tensor_add(ac, ac, py[jb][:]))
                py_free[jb] = afin
    em.wait("sync", afin)
    store_fm(em, dram["y_l0ffn"], acc[:], ch="l0est")
    em.wait("sync", em.ch_all("l0est"))


def stage_add3(nc, st, em, dram, na, nb, ncc, nout, uniq, dta=F32):
    sb = lambda nm, sh, dt: st.enter_context(nc.sbuf_tensor(f"{uniq}_{nm}", sh, dt))
    a = sb("a", [128, NCD * T], F32)
    b = sb("b", [128, NCD * T], F32)
    load_fm(em, a[:], dram[na], ch=uniq)
    load_fm(em, b[:], dram[nb], ch=uniq)
    ld = em.ch_all(uniq)
    em.wait("vector", ld)
    fin = None
    for i in range(2 * NCD):
        sl = slice(i * 1024, (i + 1) * 1024)
        fin = em.v(lambda v, sl=sl: v.tensor_add(a[:, sl], a[:, sl], b[:, sl]))
    em.wait("sync", fin)
    load_fm(em, b[:], dram[ncc], ch=uniq + "c")
    ld2 = em.ch_all(uniq + "c")
    em.wait("vector", ld2)
    for i in range(2 * NCD):
        sl = slice(i * 1024, (i + 1) * 1024)
        fin = em.v(lambda v, sl=sl: v.tensor_add(a[:, sl], a[:, sl], b[:, sl]))
    em.wait("sync", fin)
    store_fm(em, dram[nout], a[:], ch=uniq + "st")
    em.wait("sync", em.ch_all(uniq + "st"))


def stage_mm(nc, st, em, dram, name_in, wname, bname, name_out, Mout, uniq, out_dt=BF16):
    NMO = Mout // 128
    sb = lambda nm, sh, dt: st.enter_context(nc.sbuf_tensor(f"{uniq}_{nm}", sh, dt))
    x = sb("x", [128, NCD * T], BF16)
    w = sb("w", [128, NCD * Mout], BF16)
    bb = sb("b", [128, NMO], F32)
    stg = [sb(f"stg{i}", [128, TS], out_dt) for i in range(2)]
    ps = [st.enter_context(nc.psum_tensor(f"{uniq}_p{i}", [128, TS], F32)) for i in range(2)]
    load_fm(em, x[:], dram[name_in], ch=uniq)
    load_pp(em, bb[:], dram[bname], ch=uniq)
    em.dma(lambda e: e.dma_start(out=w[:].rearrange("p (c m) -> p c m", c=NCD),
                                 in_=dram[wname].rearrange("(c p) m -> p c m", p=128)), q="gpsimd", ch=uniq + "w")
    ld = em.ch_all(uniq)
    ldw = em.ch_all(uniq + "w")
    ps_free = [None, None]
    for mc in range(NMO):
        for s in range(NSTRIP):
            ib = (mc * NSTRIP + s) % 2
            em.wait("tensor", ld, ldw, ps_free[ib])
            mt = None
            for kc in range(NCD):
                wk = w[:, kc * Mout + mc * 128: kc * Mout + (mc + 1) * 128]
                xk = x[:, kc * T + s * TS: kc * T + (s + 1) * TS]
                mt = em.t(lambda t, wk=wk, xk=xk, kc=kc, ib=ib: t.matmul(
                    ps[ib][:], wk, xk, start=(kc == 0), stop=(kc == NCD - 1)), tok=(kc == NCD - 1))
            em.wait("scalar", mt, em.ch_all(f"{uniq}s{ib}"))
            bcol = bb[:, mc: mc + 1]
            ev = em.s(lambda sc, ib=ib, bcol=bcol: sc.add(stg[ib][:], ps[ib][:], bcol))
            ps_free[ib] = ev
            em.wait("sync", ev)
            dst = dram[name_out][mc * 128:(mc + 1) * 128, s * TS:(s + 1) * TS]
            em.dma(lambda e, dst=dst, ib=ib: e.dma_start(out=dst, in_=stg[ib][:]), ch=f"{uniq}s{ib}")
    em.wait("sync", em.ch_all(f"{uniq}s0"), em.ch_all(f"{uniq}s1"))


TPAD = 4


def stage_l1attn(nc, st, em, dram):
    NG = 4
    sb = lambda nm, sh, dt: st.enter_context(nc.sbuf_tensor(f"l1a_{nm}", sh, dt))
    xnb = sb("xn", [128, NCD * T], BF16)
    wq = sb("wq", [128, NCD * 768], BF16)
    bq = sb("bq", [128, 24], F32)
    CWD = T + 2 * TPAD
    qkv = sb("qkv", [128, 6 * CWD], BF16)
    bias = sb("biasb", [128, 16 * 128], F32)
    csb = sb("cst", [128, CST_W], F32)
    idb = sb("idb", [128, 128], BF16)
    s2 = sb("s2", [128, 128], F32)
    nmax = sb("nmax", [128, 1], F32)
    zs = sb("zs", [128, 1], F32)
    rz = sb("rz", [128, 1], F32)
    eb = sb("eb", [128, 128], BF16)
    ebt = sb("ebt", [128, 128], BF16)
    vt = sb("vt", [128, 64], BF16)
    obuf = sb("obuf", [128, NCD * CWD], BF16)
    wp = sb("wp", [128, NCD * 1024], BF16)
    bp = sb("bp", [128, NCD], F32)
    ystg = [sb(f"ystg{i}", [128, TS], F32) for i in range(2)]

    pS = st.enter_context(nc.psum_tensor("l1a_pS", [128, 128], F32))
    pt1 = st.enter_context(nc.psum_tensor("l1a_pt1", [128, 128], BF16))
    pt2 = st.enter_context(nc.psum_tensor("l1a_pt2", [128, 64], BF16))
    pO = st.enter_context(nc.psum_tensor("l1a_pO", [64, 128], F32))
    pY = [st.enter_context(nc.psum_tensor(f"l1a_pY{i}", [128, TS], F32)) for i in range(2)]

    load_fm(em, xnb[:], dram["xn1"], ch="l1a")
    em.dma(lambda e: e.dma_start(out=csb[:], in_=dram["cst_f32"]), ch="l1a")
    em.dma(lambda e: e.dma_start(out=bias[:].rearrange("p (h j) -> p h j", h=16),
                                 in_=dram["l1a_bias"].rearrange("h p j -> p h j")), ch="l1a")
    em.dma(lambda e: e.dma_start(out=wp[:].rearrange("p (c m) -> p c m", c=NCD),
                                 in_=dram["L1_attn_w_proj"].rearrange("(c p) m -> p c m", p=128)),
           q="gpsimd", ch="l1apw")
    load_pp(em, bp[:], dram["L1_attn_b_proj"], ch="l1a")
    load_pp(em, bq[:], dram["L1_attn_b_qkv"], ch="l1a")
    ld = em.ch_all("l1a")
    ldpw = em.ch_all("l1apw")
    em.wait("vector", ld)
    idt = em.v(lambda v: v.tensor_copy(idb[:], cst(csb, "ident")))

    pq_free = None
    o_done = None
    for g in range(NG):
        em.wait("gpsimd", pq_free)
        for j, off in enumerate([g * 256, 1024 + g * 256, 2048 + g * 256]):
            em.dma(lambda e, j=j, off=off: e.dma_start(
                out=wq[:, j * NCD * 256:(j + 1) * NCD * 256].rearrange("p (c m) -> p c m", c=NCD),
                in_=dram["L1_attn_w_qkv"][:, off:off + 256].rearrange("(c p) m -> p c m", p=128)),
                q="gpsimd", ch=f"l1aw{g % 2}")
        ldw = em.ch_all(f"l1aw{g % 2}")
        ldwb = ld
        mt = None
        for mc in range(6):
            for s in range(NSTRIP):
                jb = (mc * NSTRIP + s) % 2
                em.wait("tensor", ld, ldw, pq_free)
                em.wait("scalar", ldwb)
                for kc in range(NCD):
                    wk = wq[:, kc * 256 + (mc // 2) * NCD * 256 + (mc % 2) * 128:
                            kc * 256 + (mc // 2) * NCD * 256 + (mc % 2) * 128 + 128]
                    xk = xnb[:, kc * T + s * TS: kc * T + (s + 1) * TS]
                    mt = em.t(lambda t, wk=wk, xk=xk, kc=kc, jb=jb: t.matmul(
                        pY[jb][:], wk, xk, start=(kc == 0), stop=(kc == NCD - 1)), tok=(kc == NCD - 1))
                em.wait("scalar", mt)
                bcol = bq[:, (mc // 2) * 8 + g * 2 + (mc % 2): (mc // 2) * 8 + g * 2 + (mc % 2) + 1]
                dstq = qkv[:, mc * CWD + TPAD + s * TS: mc * CWD + TPAD + (s + 1) * TS]
                ev = em.s(lambda sc, jb=jb, bcol=bcol, dstq=dstq: sc.add(dstq, pY[jb][:], bcol))
                pq_free = ev
        wr = None
        em.wait("vector", ev)
        for mc in range(6):
            wr = em.v(lambda v, mc=mc: v.tensor_copy(
                qkv[:, mc * CWD + T + TPAD: mc * CWD + T + 2 * TPAD],
                qkv[:, mc * CWD + TPAD: mc * CWD + 2 * TPAD]))
        pS_free = None
        pt1_free = None
        pt2_free = None
        pO_free = None
        for hh in range(4):
            h = 4 * g + hh
            qoff = (hh // 2) * CWD + 0 * NCD * CWD
            koff = 2 * CWD + (hh // 2) * CWD
            voff = 4 * CWD + (hh // 2) * CWD
            pb = 64 * (hh % 2)
            for tt in range(16):
                em.wait("tensor", ev, wr, pS_free)
                ro = 2 * TPAD + 128 * tt
                qs_ = qkv[pb:pb + 64, qoff + ro: qoff + ro + 128]
                ks_ = qkv[pb:pb + 64, koff + ro: koff + ro + 128]
                st_ = em.t(lambda t, qs_=qs_, ks_=ks_: t.matmul(pS[:], qs_, ks_, start=True, stop=True), tok=True)
                em.wait("vector", st_)
                bsl = bias[:, h * 128:(h + 1) * 128]
                em.v(lambda v, bsl=bsl: v.scalar_tensor_tensor(s2[:], pS[:], 0.125, bsl, ALU.mult, ALU.add))
                mx = em.v(lambda v: v.reduce_max(nmax[:], s2[:], AX.X, negate=True))
                em.wait("scalar", mx)
                et = em.s(lambda sc: sc.activation(out=eb[:], in_=s2[:], func=AF.Exp, bias=nmax[:, 0:1],
                                                   accum_out=zs[:, 0:1]))
                em.wait("vector", et)
                em.v(lambda v: v.reciprocal(rz[:], zs[:]))
                pv = em.v(lambda v: v.tensor_scalar_mul(eb[:], eb[:], rz[:, 0:1]))
                pS_free = pv
                em.wait("tensor", pv, idt, pt1_free)
                t1 = em.t(lambda t: t.transpose(pt1[:], eb[:], idb[:]), tok=True)
                vs_ = qkv[pb:pb + 64, voff + ro: voff + ro + 128]
                em.wait("tensor", pt2_free)
                t2_ = em.t(lambda t, vs_=vs_, pb=pb: t.transpose(pt2[:], vs_, idb[pb:pb + 64, pb:pb + 64]), tok=True)
                em.wait("vector", t1, t2_)
                e1 = em.v(lambda v: v.tensor_copy(ebt[:], pt1[:]))
                pt1_free = e1
                e2 = em.v(lambda v: v.tensor_copy(vt[:], pt2[:, 0:64]))
                pt2_free = e2
                em.wait("tensor", e1, e2, pO_free)
                ot = em.t(lambda t: t.matmul(pO[:], vt[:], ebt[:], start=True, stop=True), tok=True)
                em.wait("vector", ot)
                dsto = obuf[pb:pb + 64, (hh // 2 + 2 * g) * CWD + ro: (hh // 2 + 2 * g) * CWD + ro + 128]
                pO_free = em.v(lambda v, dsto=dsto: v.tensor_copy(dsto, pO[:]))
            o_done = pO_free
    uw = None
    for mc in range(NCD):
        em.wait("vector", o_done)
        uw = em.v(lambda v, mc=mc: v.tensor_copy(obuf[:, mc * CWD + TPAD: mc * CWD + 2 * TPAD],
                                                 obuf[:, mc * CWD + T + TPAD: mc * CWD + T + 2 * TPAD]))
    pY_free = [None, None]
    for mc in range(NCD):
        for s in range(NSTRIP):
            jb = (mc * NSTRIP + s) % 2
            em.wait("tensor", uw, ldpw, pY_free[jb])
            mt = None
            for kc in range(NCD):
                wk = wp[:, kc * 1024 + mc * 128: kc * 1024 + (mc + 1) * 128]
                ok = obuf[:, kc * CWD + TPAD + s * TS: kc * CWD + TPAD + (s + 1) * TS]
                mt = em.t(lambda t, wk=wk, ok=ok, kc=kc, jb=jb: t.matmul(
                    pY[jb][:], wk, ok, start=(kc == 0), stop=(kc == NCD - 1)), tok=(kc == NCD - 1))
            em.wait("scalar", mt, em.ch_all(f"l1ay{jb}"))
            bcol = bp[:, mc: mc + 1]
            ev = em.s(lambda sc, jb=jb, bcol=bcol: sc.add(ystg[jb][:], pY[jb][:], bcol))
            pY_free[jb] = ev
            em.wait("sync", ev)
            dst = dram["y_l1attn"][mc * 128:(mc + 1) * 128, s * TS:(s + 1) * TS]
            em.dma(lambda e, dst=dst, jb=jb: e.dma_start(out=dst, in_=ystg[jb][:]), ch=f"l1ay{jb}")
    em.wait("sync", em.ch_all("l1ay0"), em.ch_all("l1ay1"))


def build_l1a_bias(bias_table):
    WS = 8
    coords = np.stack(np.meshgrid(np.arange(WS), np.arange(WS), indexing="ij"))
    cf = coords.reshape(2, -1)
    rel = cf[:, :, None] - cf[:, None, :]
    rel = rel.transpose(1, 2, 0).astype(np.int64)
    rel[..., 0] += WS - 1
    rel[..., 1] += WS - 1
    rel[..., 0] *= 2 * WS - 1
    idx = rel.sum(-1)[:WS, :WS]
    bias = np.asarray(bias_table)[idx.reshape(-1)].reshape(WS, WS, 16).transpose(2, 0, 1)
    out = np.full((16, 128, 128), -1e30, dtype=np.float32)
    for w in range(16):
        out[:, 8 * w:8 * w + 8, 8 * w:8 * w + 8] = bias
    return out


def stage_l1fattn(nc, st, em, dram):
    NHE, DH = 16, 64
    QB = 512
    NQB = T // QB
    sb = lambda nm, sh, dt: st.enter_context(nc.sbuf_tensor(f"l1f_{nm}", sh, dt))
    qh = [sb(f"qh{i}", [64, T], BF16) for i in range(2)]
    kh = [sb(f"kh{i}", [64, T], BF16) for i in range(2)]
    vh = [sb(f"vh{i}", [64, T], BF16) for i in range(2)]
    vt = [sb(f"vt{i}", [128, 16 * 65], BF16) for i in range(2)]
    ET = [sb(f"ET{i}", [128, 16 * QB], BF16) for i in range(2)]
    rz = [sb(f"rz{i}", [1, QB], F32) for i in range(2)]
    rzb = [sb(f"rzb{i}", [64, QB], F32) for i in range(2)]
    negc = sb("negc", [128, 1], F32)
    ctx = sb("ctx", [128, NCD * T], BF16)
    csb = sb("cst", [128, CST_W], F32)
    idb = sb("idb", [128, 128], BF16)

    pS = [st.enter_context(nc.psum_tensor(f"l1f_pS{i}", [128, QB], F32)) for i in range(2)]
    pO = [st.enter_context(nc.psum_tensor(f"l1f_pO{i}", [65, QB], F32)) for i in range(2)]
    pB = [st.enter_context(nc.psum_tensor(f"l1f_pB{i}", [64, QB], F32)) for i in range(2)]
    pT = st.enter_context(nc.psum_tensor("l1f_pT", [128, 128], BF16))

    em.dma(lambda e: e.dma_start(out=csb[:], in_=dram["cst_f32"]), ch="l1f")
    ld = em.ch_all("l1f")
    em.wait("vector", ld)
    idt = em.v(lambda v: v.tensor_copy(idb[:], cst(csb, "ident")))
    em.v(lambda v: v.memset(negc[:], -20.0))
    for i in range(2):
        em.v(lambda v, i=i: v.memset(vt[i][:], 1.0))

    def ldhead(h):
        i = h % 2
        for buf, row in ((qh, 0), (kh, 1024), (vh, 2048)):
            em.dma(lambda e, buf=buf, row=row, h=h, i=i: e.dma_start(
                out=buf[i][:], in_=dram["qkv2"][row + h * DH: row + (h + 1) * DH, :]), ch=f"l1fh{i}")
        return em.ch_all(f"l1fh{i}")

    htok = {0: ldhead(0)}
    qk_user = [None, None]
    v_user = [None, None]
    vt_user = [None, None]
    et_read = [None, None]
    pS_user = [None, None]
    pO_user = [None, None]
    pB_user = [None, None]
    rz_user = [None, None]
    pT_free = None
    pend = None
    gi = 0
    for h in range(NHE):
        i = h % 2
        ltok = htok.pop(h)
        if h + 1 < NHE:
            em.wait("sync", qk_user[(h + 1) % 2], v_user[(h + 1) % 2])
            htok[h + 1] = ldhead(h + 1)
        vt_copy = None
        for c2 in range(8):
            trt = None
            for j in range(2):
                c = 2 * c2 + j
                src = vh[i][0:64, c * 128:(c + 1) * 128]
                em.wait("tensor", ltok, idt, pT_free)
                trt = em.t(lambda t, src=src, j=j: t.transpose(pT[:, j * 64:(j + 1) * 64],
                                                               src, idb[0:64, 0:64]), tok=(j == 1))
            v_user[i] = trt
            em.wait("vector", trt, vt_user[i])
            dstv = vt[i][:].rearrange("p (c w) -> p c w", w=65)[:, 2 * c2:2 * c2 + 2, 0:64]
            srcv = pT[:].rearrange("p (c w) -> p c w", w=64)
            vt_copy = em.v(lambda v, dstv=dstv, srcv=srcv: v.tensor_copy(dstv, srcv))
            pT_free = vt_copy
        for qb in range(NQB):
            jb = gi % 2
            last_exp = None
            for kt in range(16):
                ip = kt % 2
                em.wait("tensor", ltok, pS_user[ip])
                ks_ = kh[i][0:64, kt * 128:(kt + 1) * 128]
                qs_ = qh[i][0:64, qb * QB:(qb + 1) * QB]
                mm = em.t(lambda t, ks_=ks_, qs_=qs_, ip=ip: t.matmul(pS[ip][:], ks_, qs_,
                                                                      start=True, stop=True), tok=True)
                qk_user[i] = mm
                em.wait("scalar", mm, et_read[jb] if kt == 0 else None)
                et = em.s(lambda sc, ip=ip, jb=jb, kt=kt: sc.activation(
                    out=ET[jb][:, kt * QB:(kt + 1) * QB], in_=pS[ip][:],
                    func=AF.Exp, scale=0.125, bias=negc[:, 0:1]))
                pS_user[ip] = et
                last_exp = et
            if pend is not None:
                pend()
            cm = None
            for kt in range(16):
                em.wait("tensor", last_exp if kt == 0 else None,
                        pO_user[jb] if kt == 0 else None,
                        vt_copy if kt == 0 else None)
                vk = vt[i][:, kt * 65:(kt + 1) * 65]
                ek = ET[jb][:, kt * QB:(kt + 1) * QB]
                cm = em.t(lambda t, vk=vk, ek=ek, kt=kt, jb=jb: t.matmul(
                    pO[jb][:], vk, ek, start=(kt == 0), stop=(kt == 15)), tok=(kt == 15))
            vt_user[i] = cm
            et_read[jb] = cm
            em.wait("vector", cm, rz_user[jb])
            rc = em.v(lambda v, jb=jb: v.reciprocal(rz[jb][:], pO[jb][64:65, :]))

            def mk_evac(h=h, qb=qb, jb=jb, rc=rc):
                def go():
                    em.wait("tensor", rc, pB_user[jb])
                    bm = em.t(lambda t, jb=jb: t.matmul(pB[jb][:], onesrow(csb)[0:1, 0:64],
                                                        rz[jb][:], start=True, stop=True), tok=True)
                    rz_user[jb] = bm
                    em.wait("vector", bm)
                    cp = em.v(lambda v, jb=jb: v.tensor_copy(rzb[jb][:], pB[jb][:]))
                    pB_user[jb] = cp
                    dstc = ctx[64 * (h % 2):64 * (h % 2) + 64,
                               (h // 2) * T + qb * QB: (h // 2) * T + (qb + 1) * QB]
                    mu = em.v(lambda v, jb=jb, dstc=dstc: v.tensor_mul(dstc, pO[jb][0:64, :], rzb[jb][:]))
                    pO_user[jb] = mu
                return go
            pend = mk_evac()
            gi += 1
    pend()
    em.wait("sync", pO_user[0], pO_user[1])
    store_fm(em, dram["ctx"], ctx[:], ch="l1fst")
    em.wait("sync", em.ch_all("l1fst"))


def stage_l1h(nc, st, em, dram):
    sb = lambda nm, sh, dt: st.enter_context(nc.sbuf_tensor(f"l1h_{nm}", sh, dt))
    cb = sb("ctx", [128, NCD * T], BF16)
    xb = sb("x", [128, NCD * T], F32)
    wb = sb("w", [128, NCD * 1024], BF16)
    bo = sb("bo", [128, NCD], F32)
    stg = [sb(f"stg{i}", [128, TS], F32) for i in range(2)]
    ps = [st.enter_context(nc.psum_tensor(f"l1h_p{i}", [128, TS], F32)) for i in range(2)]
    load_fm(em, cb[:], dram["ctx"], ch="l1h")
    load_fm(em, xb[:], dram["xn1_32"], ch="l1h")
    load_pp(em, bo[:], dram["L1_ffn_b_o"], ch="l1h")
    em.dma(lambda e: e.dma_start(out=wb[:].rearrange("p (c m) -> p c m", c=NCD),
                                 in_=dram["L1_ffn_w_o"].rearrange("(c p) m -> p c m", p=128)), q="gpsimd", ch="l1hw")
    ld = em.ch_all("l1h")
    ldw = em.ch_all("l1hw")
    ps_free = [None, None]
    for mc in range(NCD):
        for s in range(NSTRIP):
            ib = (mc * NSTRIP + s) % 2
            em.wait("tensor", ld, ldw, ps_free[ib])
            mt = None
            for kc in range(NCD):
                wk = wb[:, kc * 1024 + mc * 128: kc * 1024 + (mc + 1) * 128]
                ck = cb[:, kc * T + s * TS: kc * T + (s + 1) * TS]
                mt = em.t(lambda t, wk=wk, ck=ck, kc=kc, ib=ib: t.matmul(
                    ps[ib][:], wk, ck, start=(kc == 0), stop=(kc == NCD - 1)), tok=(kc == NCD - 1))
            em.wait("vector", mt, ld, em.ch_all(f"l1hs{ib}"))
            bcol = bo[:, mc: mc + 1]
            xk = xb[:, mc * T + s * TS: mc * T + (s + 1) * TS]
            ev = em.v(lambda v, ib=ib, bcol=bcol, xk=xk: v.scalar_tensor_tensor(
                stg[ib][:], ps[ib][:], bcol, xk, ALU.add, ALU.add))
            ps_free[ib] = ev
            em.wait("sync", ev)
            dst = dram["h_ffn"][mc * 128:(mc + 1) * 128, s * TS:(s + 1) * TS]
            em.dma(lambda e, dst=dst, ib=ib: e.dma_start(out=dst, in_=stg[ib][:]), ch=f"l1hs{ib}")
    em.wait("sync", em.ch_all("l1hs0"), em.ch_all("l1hs1"))


def stage_l1moe(nc, st, em, dram, E=8):
    NI2 = 32
    sb = lambda nm, sh, dt: st.enter_context(nc.sbuf_tensor(f"l1m_{nm}", sh, dt))
    hbuf = sb("hb", [128, NCD * T], F32)
    xs = sb("xs", [128, NCD * TS], BF16)
    hx = sb("hx", [128, NI2 * TS], BF16)
    wbuf = [sb(f"wb{i}", [128, 16 * 1024], BF16) for i in range(2)]
    gwb = sb("gw", [128, NCD * 8], BF16)
    b1b = sb("b1", [128, E * NI2], F32)
    b2b = sb("b2", [128, E * NCD], F32)
    web = sb("web", [128, TS], BF16)
    wer = sb("wer", [8, T], BF16)
    selb = sb("sel", [32, CSTB_W], BF16)
    csb = sb("cst", [128, CST_W], F32)
    idb = sb("idb", [128, 128], BF16)
    prb = sb("prb", [128, 8], F32)
    spb = sb("spb", [128, 8], F32)
    mcol = sb("mcol", [128, 1], F32)
    mskf = sb("mskf", [128, 8], F32)
    nmax = sb("nmax", [128, 1], F32)
    zs = sb("zs", [128, 1], F32)
    rz = sb("rz", [128, 1], F32)
    wet = sb("wet", [128, 8], F32)
    tb = sb("tb", [128, TS], F32)

    pp = [st.enter_context(nc.psum_tensor(f"l1m_p{i}", [128, TS], F32)) for i in range(8)]

    load_fm(em, hbuf[:], dram["h_ffn"], ch="l1m")
    em.dma(lambda e: e.dma_start(out=gwb[:].rearrange("p (c m) -> p c m", c=NCD),
                                 in_=dram["L1_ffn_w_gate"].rearrange("(c p) m -> p c m", p=128)), q="gpsimd", ch="l1mg")
    em.dma(lambda e: e.dma_start(out=selb[:], in_=dram["cst_bf16"]), ch="l1m")
    em.dma(lambda e: e.dma_start(out=csb[:], in_=dram["cst_f32"]), ch="l1m")
    em.dma(lambda e: e.dma_start(out=b1b[:].rearrange("p (e c) -> p e c", e=E),
                                 in_=dram["L1_ffn_b1"].rearrange("e (c p) -> p e c", p=128),
                                 allow_slow_non_contiguous=True), ch="l1m")
    em.dma(lambda e: e.dma_start(out=b2b[:].rearrange("p (e c) -> p e c", e=E),
                                 in_=dram["L1_ffn_b2"].rearrange("e (c p) -> p e c", p=128),
                                 allow_slow_non_contiguous=True), ch="l1m")
    ld = em.ch_all("l1m")
    ldg = em.ch_all("l1mg")
    em.wait("vector", ld)
    idt = em.v(lambda v: v.tensor_copy(idb[:], cst(csb, "ident")))

    def ldw(name, e, hf, slot, guard):
        em.wait("gpsimd", guard)
        if name == "L1_ffn_w1":
            srcap = dram[name][e][:, hf * 2048:(hf + 1) * 2048].rearrange("(c p) m -> p c m", p=128)
            dstap = wbuf[slot][:].rearrange("p (c m) -> p c m", c=NCD)
        else:
            srcap = dram[name][e][hf * 2048:(hf + 1) * 2048, :].rearrange("(c p) m -> p c m", p=128)
            dstap = wbuf[slot][:].rearrange("p (c m) -> p c m", c=16)
        em.dma(lambda ee, srcap=srcap, dstap=dstap: ee.dma_start(out=dstap, in_=srcap),
               q="gpsimd", ch=f"l1mw{slot}")
        return em.ch_all(f"l1mw{slot}")

    pr_free = None
    prt_free = None
    xl_user = None
    acc_fin = None
    wslot_user = [None, None]
    for s in range(NSTRIP):
        em.wait("sync", xl_user)
        em.dma(lambda e, s=s: e.dma_start(
            out=xs[:].rearrange("p (c t) -> p c t", c=NCD),
            in_=dram["xln2"].rearrange("(c p) t -> p c t", p=128)[:, :, s * TS:(s + 1) * TS]), ch="l1mx")
        lx = em.ch_all("l1mx")
        wrt = None
        for tt in range(4):
            em.wait("tensor", lx, ldg, pr_free)
            mt = None
            for kc in range(NCD):
                xk = xs[:, kc * TS + tt * 128: kc * TS + (tt + 1) * 128]
                gk = gwb[:, kc * 8:(kc + 1) * 8]
                mt = em.t(lambda t, xk=xk, gk=gk, kc=kc: t.matmul(pp[0][:, 0:8], xk, gk,
                                                                  start=(kc == 0), stop=(kc == NCD - 1)),
                          tok=(kc == NCD - 1))
            em.wait("vector", mt)
            em.v(lambda v: v.reduce_max(nmax[:], pp[0][:, 0:8], AX.X, negate=True))
            mxt = em.v(lambda v: v.engine_nop())
            em.wait("scalar", mxt)
            et = em.s(lambda sc: sc.activation(out=prb[:], in_=pp[0][:, 0:8], func=AF.Exp, bias=nmax[:, 0:1],
                                               accum_out=zs[:, 0:1]))
            pr_free = et
            em.wait("vector", et)
            em.v(lambda v: v.reciprocal(rz[:], zs[:]))
            em.v(lambda v: v.tensor_scalar_mul(prb[:], prb[:], rz[:, 0:1]))
            em.v(lambda v: v.tensor_copy(spb[:], prb[:]))
            em.v(lambda v: v.reduce_max(mcol[:], spb[:], AX.X))
            em.v(lambda v: v.tensor_single_scalar(mskf[:], spb[:], mcol[:, 0:1], ALU.is_ge))
            em.v(lambda v: v.scalar_tensor_tensor(spb[:], mskf[:], -1e30, spb[:], ALU.mult, ALU.add))
            em.v(lambda v: v.reduce_max(mcol[:], spb[:], AX.X))
            em.v(lambda v: v.tensor_single_scalar(mskf[:], prb[:], mcol[:, 0:1], ALU.is_ge))
            em.v(lambda v: v.tensor_mul(mskf[:], mskf[:], prb[:]))
            em.v(lambda v: v.reduce_sum(zs[:], mskf[:], AX.X))
            em.v(lambda v: v.reciprocal(rz[:], zs[:]))
            wev = em.v(lambda v: v.tensor_scalar_mul(wet[:], mskf[:], rz[:, 0:1]))
            em.wait("tensor", wev, idt, prt_free)
            trt = em.t(lambda t: t.transpose(pp[1][0:8, 0:128], wet[:, 0:8], cst(csb, "ident")), tok=True)
            em.wait("vector", trt)
            wrt = em.v(lambda v, s=s, tt=tt: v.tensor_copy(wer[0:8, s * TS + tt * 128: s * TS + (tt + 1) * 128],
                                                           pp[1][0:8, 0:128]))
            prt_free = wrt
        for e in range(E):
            w1t0 = ldw("L1_ffn_w1", e, 0, 0, wslot_user[0])
            w1t1 = ldw("L1_ffn_w1", e, 1, 1, wslot_user[1])
            gt = None
            gel_user = [None, None]
            for hf in range(2):
                wt = w1t0 if hf == 0 else w1t1
                for ic in range(16):
                    gic = hf * 16 + ic
                    jb = gic % 2
                    em.wait("tensor", lx, wt, wrt, gel_user[jb], pr_free if gic == 0 else None)
                    mt = None
                    for kc in range(NCD):
                        wk = wbuf[hf][:, kc * 2048 + ic * 128: kc * 2048 + (ic + 1) * 128]
                        xk = xs[:, kc * TS:(kc + 1) * TS]
                        mt = em.t(lambda t, wk=wk, xk=xk, kc=kc, jb=jb: t.matmul(
                            pp[jb][:], wk, xk, start=(kc == 0), stop=(kc == NCD - 1)), tok=(kc == NCD - 1))
                    em.wait("scalar", mt, ld)
                    bcol = b1b[:, e * NI2 // 4 + gic // 4: e * NI2 // 4 + gic // 4 + 1] if False else \
                        b1b[:, e * 8 + gic // 4: e * 8 + gic // 4 + 1]
                    gt = em.s(lambda sc, jb=jb, bcol=bcol, gic=gic: sc.activation(
                        out=hx[:, gic * TS:(gic + 1) * TS], in_=pp[jb][:], func=AF.Gelu_apprx_tanh, bias=bcol))
                    gel_user[jb] = gt
                wslot_user[hf] = gt
            em.wait("tensor", wrt, gt)
            sl = selb[0:8, e * 128:(e + 1) * 128]
            cr = wer[0:8, s * TS:(s + 1) * TS]
            cbt = em.t(lambda t, sl=sl, cr=cr: t.matmul(pp[0][:], sl, cr, start=True, stop=True), tok=True)
            em.wait("vector", cbt)
            webv = em.v(lambda v: v.tensor_copy(web[:], pp[0][:]))
            w2t0 = ldw("L1_ffn_w2", e, 0, 0, wslot_user[0])
            ymt = None
            for hf in range(2):
                w2t = w2t0 if hf == 0 else ldw("L1_ffn_w2", e, 1, 1, wslot_user[1])
                for oc in range(NCD):
                    em.wait("tensor", w2t, gt, webv if hf == 0 else None)
                    for ic in range(16):
                        gic = hf * 16 + ic
                        wk = wbuf[hf][:, ic * 1024 + oc * 128: ic * 1024 + (oc + 1) * 128]
                        hk = hx[:, gic * TS:(gic + 1) * TS]
                        ymt = em.t(lambda t, wk=wk, hk=hk, hf=hf, ic=ic, oc=oc: t.matmul(
                            pp[oc][:], wk, hk, start=(hf == 0 and ic == 0), stop=(hf == 1 and ic == 15)),
                            tok=(ic == 15))
                wslot_user[hf] = ymt
            afin = None
            for oc in range(NCD):
                em.wait("vector", ymt, webv, ld)
                b2c = b2b[:, e * NCD + oc: e * NCD + oc + 1]
                em.v(lambda v, oc=oc, b2c=b2c: v.scalar_tensor_tensor(tb[:], pp[oc][:], b2c, web[:],
                                                                      ALU.add, ALU.mult))
                hk = hbuf[:, oc * T + s * TS: oc * T + (s + 1) * TS]
                afin = em.v(lambda v, hk=hk: v.tensor_add(hk, hk, tb[:]))
            acc_fin = afin
            pr_free = afin
            prt_free = afin
        xl_user = acc_fin
    em.wait("sync", acc_fin)
    store_fm(em, dram["y_l1ffn"], hbuf[:], ch="l1mst")
    em.wait("sync", em.ch_all("l1mst"))


SCR = {
    "xn": ([D, T], BF16), "q_s": ([D, T], BF16), "k_s": ([D, T], BF16), "v_s": ([D, T], BF16),
    "beta_tm": ([T, 4], F32), "q_n": ([D, T], BF16), "k_n": ([D, T], BF16),
    "k_tm": ([NH * T, DK], BF16), "v_tm": ([NH * T, DV], BF16), "delta_tm": ([NH * T, DV], BF16),
    "ls_fm": ([D, T], BF16), "ll_fm": ([D, T], BF16), "stats_f": ([16, NH * T], BF16),
    "probs_rows": ([4, NH * T], BF16), "y_l0attn": ([D, T], F32), "y_l0ffn": ([D, T], F32),
    "x1": ([D, T], F32), "xn1": ([D, T], BF16), "xn1_32": ([D, T], F32),
    "y_l1attn": ([D, T], F32), "xln1": ([D, T], BF16), "qkv2": ([3 * D, T], BF16),
    "ctx": ([D, T], BF16), "h_ffn": ([D, T], F32), "xln2": ([D, T], BF16),
    "y_l1ffn": ([D, T], F32), "x2out": ([D, T], F32),
}
CONSTS = {"cst_f32": ([128, CST_W], F32), "cst_bf16": ([32, CSTB_W], BF16)}
MODIN = {
    "xcore": ([D, T], F32),
    "L0_attn_Wq": ([D, D], F32), "L0_attn_Wk": ([D, D], F32), "L0_attn_Wv": ([D, D], F32),
    "L0_attn_Wb": ([D, 4], F32),
    "L0_attn_conv_q_w": ([D, 4], F32), "L0_attn_conv_k_w": ([D, 4], F32), "L0_attn_conv_v_w": ([D, 4], F32),
    "L0_attn_fir_short_w": ([D, 5], F32), "L0_attn_fir_long_w": ([D, 64], F32),
    "L0_attn_mlp_w1": ([1040, 1024], F32), "L0_attn_mlp_b1": ([1024], F32),
    "L0_attn_mlp_w2": ([1024, 4], F32), "L0_attn_mlp_b2": ([4], F32),
    "L0_attn_log_temp": ([4, 1], F32), "L0_attn_o_norm_w": ([256], F32), "L0_attn_Wo": ([D, D], F32),
    "L0_ffn_gate_w": ([32, D], F32), "L0_ffn_w1": ([32, D, 768], F32), "L0_ffn_w3": ([32, D, 768], F32),
    "L0_ffn_w2": ([32, 768, D], F32), "L0_ffn_e_bias": ([32], F32),
    "l1a_bias": ([16, 128, 128], F32),
    "L1_attn_w_qkv": ([D, 3 * D], F32), "L1_attn_b_qkv": ([3 * D], F32),
    "L1_attn_w_proj": ([D, D], F32), "L1_attn_b_proj": ([D], F32),
    "L1_ffn_ln1_w": ([D], F32), "L1_ffn_ln1_b": ([D], F32),
    "L1_ffn_ln2_w": ([D], F32), "L1_ffn_ln2_b": ([D], F32),
    "L1_ffn_w_qkv": ([D, 3 * D], F32), "L1_ffn_b_qkv": ([3 * D], F32),
    "L1_ffn_w_o": ([D, D], F32), "L1_ffn_b_o": ([D], F32),
    "L1_ffn_w_gate": ([D, 8], F32), "L1_ffn_w1": ([8, D, 4096], F32), "L1_ffn_b1": ([8, 4096], F32),
    "L1_ffn_w2": ([8, 4096, D], F32), "L1_ffn_b2": ([8, D], F32),
}

STAGES = [
    ("ln0", lambda nc, st, em, dram: stage_ln(nc, st, em, dram, "xcore", "xn", uniq="ln0"),
     ["xcore"], ["xn"]),
    ("l0qkv", stage_l0qkv,
     ["xn", "L0_attn_Wq", "L0_attn_Wk", "L0_attn_Wv", "L0_attn_Wb",
      "L0_attn_conv_q_w", "L0_attn_conv_k_w", "L0_attn_conv_v_w"],
     ["q_s", "k_s", "v_s", "beta_tm"]),
    ("l0prep", stage_l0prep, ["q_s", "k_s", "v_s"], ["q_n", "k_n", "k_tm", "v_tm"]),
    ("delta", stage_delta, ["q_n", "k_n", "k_tm", "v_tm", "beta_tm"], ["delta_tm"]),
    ("l0fir", stage_l0fir,
     ["v_s", "delta_tm", "v_tm", "L0_attn_fir_short_w", "L0_attn_fir_long_w"],
     ["ls_fm", "ll_fm", "stats_f"]),
    ("l0gate", stage_l0gate,
     ["xn", "stats_f", "L0_attn_mlp_w1", "L0_attn_mlp_b1", "L0_attn_mlp_w2",
      "L0_attn_mlp_b2", "L0_attn_log_temp"], ["probs_rows"]),
    ("l0mix", stage_l0mix,
     ["ls_fm", "ll_fm", "v_s", "delta_tm", "probs_rows", "L0_attn_o_norm_w", "L0_attn_Wo"],
     ["y_l0attn"]),
    ("l0moe", stage_l0moe,
     ["xn", "L0_ffn_gate_w", "L0_ffn_e_bias", "L0_ffn_w1", "L0_ffn_w3", "L0_ffn_w2"],
     ["y_l0ffn"]),
    ("mid", lambda nc, st, em, dram: stage_add3(nc, st, em, dram, "xcore", "y_l0attn", "y_l0ffn", "x1", "m1"),
     ["xcore", "y_l0attn", "y_l0ffn"], ["x1"]),
    ("ln1", lambda nc, st, em, dram: stage_ln(nc, st, em, dram, "x1", "xn1", also_f32="xn1_32", uniq="ln1"),
     ["x1"], ["xn1", "xn1_32"]),
    ("l1attn", stage_l1attn,
     ["xn1", "l1a_bias", "L1_attn_w_qkv", "L1_attn_b_qkv", "L1_attn_w_proj", "L1_attn_b_proj"],
     ["y_l1attn"]),
    ("lnf1", lambda nc, st, em, dram: stage_ln(nc, st, em, dram, "xn1", "xln1", gamma="L1_ffn_ln1_w",
                                              beta="L1_ffn_ln1_b", in_dt=BF16, uniq="lnf1"),
     ["xn1", "L1_ffn_ln1_w", "L1_ffn_ln1_b"], ["xln1"]),
    ("qkv2", lambda nc, st, em, dram: stage_mm(nc, st, em, dram, "xln1", "L1_ffn_w_qkv", "L1_ffn_b_qkv",
                                              "qkv2", 3 * D, "qk2"),
     ["xln1", "L1_ffn_w_qkv", "L1_ffn_b_qkv"], ["qkv2"]),
    ("l1fattn", stage_l1fattn, ["qkv2"], ["ctx"]),
    ("l1h", stage_l1h, ["ctx", "xn1_32", "L1_ffn_w_o", "L1_ffn_b_o"], ["h_ffn"]),
    ("lnf2", lambda nc, st, em, dram: stage_ln(nc, st, em, dram, "h_ffn", "xln2", gamma="L1_ffn_ln2_w",
                                              beta="L1_ffn_ln2_b", uniq="lnf2"),
     ["h_ffn", "L1_ffn_ln2_w", "L1_ffn_ln2_b"], ["xln2"]),
    ("l1moe", stage_l1moe,
     ["h_ffn", "xln2", "L1_ffn_w_gate", "L1_ffn_w1", "L1_ffn_b1", "L1_ffn_w2", "L1_ffn_b2"],
     ["y_l1ffn"]),
    ("final", lambda nc, st, em, dram: stage_add3(nc, st, em, dram, "x1", "y_l1attn", "y_l1ffn", "x2out", "fin"),
     ["x1", "y_l1attn", "y_l1ffn"], ["x2out"]),
]


def prep_host_inputs(inputs):
    hs = inputs["L0_attn_hidden_states"]
    shared = {}
    for nm in MODIN:
        if nm in ("xcore", "l1a_bias"):
            continue
        if nm == "L0_attn_fir_short_w":
            shared[nm] = np.ascontiguousarray(inputs[nm].reshape(1024, 5))
        elif nm == "L0_attn_fir_long_w":
            shared[nm] = np.ascontiguousarray(inputs[nm].reshape(1024, 64))
        else:
            shared[nm] = np.ascontiguousarray(inputs[nm])
    shared["l1a_bias"] = build_l1a_bias(inputs["L1_attn_bias_table"])
    shared["cst_f32"] = build_const_f32()
    shared["cst_bf16"] = build_const_bf16()
    xcores = [np.ascontiguousarray(hs[c].T).astype(np.float32, copy=False) for c in range(NCORES)]
    return xcores, shared


def kernel(**inputs):
    xcores, shared = prep_host_inputs(inputs)
    per_core = [dict(xcore=xcores[c]) for c in range(NCORES)]
    for name, fn, ins_, outs_ in STAGES:
        isp = {}
        for nm in ins_ + ["cst_f32", "cst_bf16"]:
            if nm in SCR:
                isp[nm] = SCR[nm]
            elif nm in MODIN:
                isp[nm] = MODIN[nm]
            else:
                isp[nm] = CONSTS[nm]
        osp = {nm: SCR[nm] for nm in outs_}
        in_maps = []
        for c in range(NCORES):
            m = {}
            for nm in isp:
                if nm in shared:
                    m[nm] = shared[nm]
                else:
                    m[nm] = per_core[c][nm]
            in_maps.append(m)
        res = run_stages([fn], in_maps, isp, osp)
        for c in range(NCORES):
            for nm in outs_:
                per_core[c][nm] = np.asarray(res.results[c][nm])
    out = np.stack([per_core[c]["x2out"].T for c in range(NCORES)], axis=0)
    return np.ascontiguousarray(out.astype(np.float32, copy=False))


def kernel_unfused(**inputs):
    xcores, shared = prep_host_inputs(inputs)
    per_core = [dict(xcore=xcores[c]) for c in range(NCORES)]
    for name, fn, ins_, outs_ in STAGES:
        isp = {}
        for nm in ins_ + ["cst_f32", "cst_bf16"]:
            isp[nm] = SCR.get(nm) or MODIN.get(nm) or CONSTS.get(nm)
        osp = {nm: SCR[nm] for nm in outs_}
        in_maps = []
        for c in range(NCORES):
            in_maps.append({nm: (shared[nm] if nm in shared else per_core[c][nm]) for nm in isp})
        res = run_stages([fn], in_maps, isp, osp)
        for c in range(NCORES):
            for nm in outs_:
                per_core[c][nm] = np.asarray(res.results[c][nm])
    out = np.stack([per_core[c]["x2out"].T for c in range(NCORES)], axis=0)
    return np.ascontiguousarray(out.astype(np.float32, copy=False))


def build_fused():
    nc = bass.Bass("TRN2", target_bir_lowering=False, debug=False)
    dram = {}
    for nm, (sh, dt) in {**MODIN, **CONSTS}.items():
        dram[nm] = nc.dram_tensor(nm, list(sh), dt, kind="ExternalInput").ap()
    for nm, (sh, dt) in SCR.items():
        kind = "ExternalOutput" if nm == "x2out" else "Internal"
        dram[nm] = nc.dram_tensor(nm, list(sh), dt, kind=kind).ap()
    with ExitStack() as st_outer:
        em = Em(nc, st_outer)
        for name, fn, ins_, outs_ in STAGES:
            with ExitStack() as st_stage:
                fn(nc, st_stage, em, dram)
            em.barrier()
        em.finalize()
    return nc


def kernel(**inputs):
    xcores, shared = prep_host_inputs(inputs)
    nc = build_fused()
    in_maps = [{**shared, "xcore": xcores[c]} for c in range(NCORES)]
    res = run_bass_kernel_spmd(nc, in_maps, core_ids=list(range(NCORES)))
    out = np.stack([np.asarray(res.results[c]["x2out"]).T for c in range(NCORES)], axis=0)
    return np.ascontiguousarray(out.astype(np.float32, copy=False))


# version 11
# speedup vs baseline: 1.2831x; 1.0641x over previous
import numpy as np
import ml_dtypes
from contextlib import ExitStack
import concourse.bass as bass
import concourse.mybir as mybir
from concourse.bass_utils import run_bass_kernel_spmd

F32 = mybir.dt.float32
BF16 = mybir.dt.bfloat16
U8 = mybir.dt.uint8
AF = mybir.ActivationFunctionType
ALU = mybir.AluOpType
AX = mybir.AxisListType

NCORES = 8
T = 2048
D = 1024
NCD = D // 128
TS = 512
NSTRIP = T // TS
NH, DK, DV = 4, 256, 256

ENG = ("sync", "tensor", "vector", "scalar", "gpsimd")


class Em:
    def __init__(self, nc, st):
        self.nc, self.st = nc, st
        self.prog = {e: [] for e in ENG}
        self.sem_h = {}
        self.sem_c = {}
        self.last_wait = {e: {} for e in ENG}

    def _sem(self, name):
        if name not in self.sem_h:
            self.sem_h[name] = self.st.enter_context(self.nc.semaphore(name))
            self.sem_c[name] = 0
        return self.sem_h[name]

    AUTO_CHAIN = ("vector", "scalar", "gpsimd")

    def _op(self, engine, fn, tok, dma):
        if engine in self.AUTO_CHAIN and not dma:
            name = "c_" + engine
            h = self._sem(name)
            before = self.sem_c[name]
            self.wait(engine, (name, before))
            self.sem_c[name] = before + 1
            self.prog[engine].append(lambda eng: fn(eng).then_inc(h, 1))
            return (name, before + 1)
        if not tok:
            self.prog[engine].append(fn)
            return None
        name = ("d_" if dma else "c_") + engine
        h = self._sem(name)
        inc = 16 if dma else 1
        self.sem_c[name] += inc
        val = self.sem_c[name]
        self.prog[engine].append(lambda eng: fn(eng).then_inc(h, inc))
        return (name, val)

    def t(self, fn, tok=False):
        return self._op("tensor", fn, tok, False)

    def v(self, fn, tok=False):
        return self._op("vector", fn, tok, False)

    def s(self, fn, tok=False):
        return self._op("scalar", fn, tok, False)

    def g(self, fn, tok=False):
        return self._op("gpsimd", fn, tok, False)

    def tfence(self):
        return self._op("tensor", lambda t: t.nop(), True, False)

    def vfence(self):
        return self._op("vector", lambda v: v.engine_nop(), True, False)

    def gfence(self):
        return self._op("gpsimd", lambda g: g.engine_nop(), True, False)

    def dma(self, fn, q="sync", ch=None):
        ch = "dch_" + (ch or q)
        h = self._sem(ch)
        self.sem_c[ch] += 16
        val = self.sem_c[ch]
        self.prog[q].append(lambda eng: fn(eng).then_inc(h, 16))
        return (ch, val)

    def ch_all(self, ch):
        ch = "dch_" + ch
        return (ch, self.sem_c.get(ch, 0))

    def wait(self, engine, *toks):
        for tk in toks:
            if tk is None:
                continue
            name, val = tk
            if val <= self.last_wait[engine].get(name, 0):
                continue
            self.last_wait[engine][name] = val
            h = self.sem_h[name]
            self.prog[engine].append(lambda eng, h=h, v=val: eng.wait_ge(h, v))

    def barrier(self):
        waits = [(nm, cnt) for nm, cnt in self.sem_c.items() if cnt > 0]
        for e in ENG:
            if self.prog[e] or True:
                self.wait(e, *waits)

    def finalize(self):
        with self.nc.Block() as block:
            for e in ENG:
                ops = self.prog[e]
                if ops:
                    getattr(block, e)(lambda eng, ops=ops: [op(eng) for op in ops])


class Banks:

    def __init__(self, em, tiles):
        self.em = em
        self.tiles = tiles
        self.free_tok = [None] * len(tiles)
        self.i = 0

    def take(self):
        i = self.i % len(self.tiles)
        self.i += 1
        return i, self.tiles[i], self.free_tok[i]

    def consumed(self, i, tok):
        self.free_tok[i] = tok


CST = {}
_w = 0
def _c(name, width):
    global _w
    CST[name] = (_w, width)
    _w += width
_c("ones", 1)
_c("ident", 128)
_c("strict_low", 128)
_c("incl_up", 128)
_c("eps_ln", 1)
_c("eps_l2", 1)
CST_W = _w


def build_const_f32():
    c = np.zeros((128, CST_W), dtype=np.float32)
    c[:, CST["ones"][0]] = 1.0
    o, _ = CST["ident"]
    c[:, o:o + 128] = np.eye(128, dtype=np.float32)
    o, _ = CST["strict_low"]
    c[:, o:o + 128] = np.tril(np.ones((128, 128), np.float32), -1)
    o, _ = CST["incl_up"]
    c[:, o:o + 128] = np.triu(np.ones((128, 128), np.float32), 0)
    c[:, CST["eps_ln"][0]] = 1e-5
    c[:, CST["eps_l2"][0]] = 1e-6
    return c


CSTB_W = 32 * 128


def build_const_bf16():
    import ml_dtypes as _md
    c = np.zeros((32, CSTB_W), dtype=_md.bfloat16)
    for r in range(32):
        c[r, r * 128:(r + 1) * 128] = 1.0
    return c


def cst(csb, name):
    o, w = CST[name]
    return csb[:, o:o + w]


def onesrow(csb):
    o, _ = CST["incl_up"]
    return csb[0:1, o:o + 128]


def load_pp(em, dst_sb, dram_ap, ch=None):
    return em.dma(lambda e: e.dma_start(out=dst_sb, in_=dram_ap.rearrange("(c p) -> p c", p=128),
                                        allow_slow_non_contiguous=True), ch=ch)


def load_fm(em, dst_sb, dram_ap, q="sync", ch=None):
    c = dram_ap.shape[0] // 128
    return em.dma(lambda e: e.dma_start(out=dst_sb.rearrange("p (c t) -> p c t", c=c),
                                        in_=dram_ap.rearrange("(c p) t -> p c t", p=128)), q=q, ch=ch)


def store_fm(em, dram_ap, src_sb, q="sync", ch=None):
    c = dram_ap.shape[0] // 128
    return em.dma(lambda e: e.dma_start(out=dram_ap.rearrange("(c p) t -> p c t", p=128),
                                        in_=src_sb.rearrange("p (c t) -> p c t", c=c)), q=q, ch=ch)


def stage_ln(nc, st, em, dram, name_in, name_out, gamma=None, beta=None, also_f32=None,
             in_dt=F32, uniq=None):
    u = uniq or f"ln_{name_in}"
    sb = lambda nm, sh, dt: st.enter_context(nc.sbuf_tensor(f"{u}_{nm}", sh, dt))
    xs = sb("xs", [128, NCD * T], in_dt)
    xnb = sb("xnb", [128, NCD * T], BF16)
    xn32 = sb("xn32", [128, NCD * T], F32) if also_f32 else None
    sq = sb("sq", [128, TS], F32)
    rows = sb("rows", [1, 6 * TS], F32)
    csb = sb("cst", [128, CST_W], F32)
    gsb = sb("g", [128, NCD], F32) if gamma else None
    bsb = sb("b", [128, NCD], F32) if beta else None
    tmp = sb("tmp", [128, TS], F32)

    pm = st.enter_context(nc.psum_tensor(f"{u}_pm", [1, TS], F32))
    pv = st.enter_context(nc.psum_tensor(f"{u}_pv", [1, TS], F32))
    pbm = st.enter_context(nc.psum_tensor(f"{u}_pbm", [128, TS], F32))
    pbr = st.enter_context(nc.psum_tensor(f"{u}_pbr", [128, TS], F32))

    onesb = sb("onesb", [128, 1], BF16) if in_dt == BF16 else None
    load_fm(em, xs[:], dram[name_in], ch=u)
    em.dma(lambda e: e.dma_start(out=csb[:], in_=dram["cst_f32"]), ch=u)
    if gamma:
        load_pp(em, gsb[:], dram[gamma], ch=u)
    if beta:
        load_pp(em, bsb[:], dram[beta], ch=u)
    ld = ldc = ldg = ldb = em.ch_all(u)
    ones_ap = cst(csb, "ones")
    ocast = None
    if onesb is not None:
        em.wait("vector", ld)
        ocast = em.v(lambda v: v.tensor_copy(onesb[:], cst(csb, "ones")))
        ones_ap = onesb[:]

    stat_free = None
    bc_free = None
    for s in range(NSTRIP):
        xstrip = lambda c: xs[:, c * T + s * TS: c * T + (s + 1) * TS]
        em.wait("tensor", ld, ldc, stat_free, ocast)
        for c in range(NCD):
            xc = xstrip(c)
            em.t(lambda t, xc=xc, c=c: t.matmul(pm[:], ones_ap, xc,
                                                start=(c == 0), stop=(c == NCD - 1)))
        prev_mm = None
        for c in range(NCD):
            xc = xstrip(c)
            em.wait("scalar", ld, stat_free, prev_mm)
            sqt = em.s(lambda sc, xc=xc: sc.activation(out=sq[:], in_=xc, func=AF.Square), tok=True)
            em.wait("tensor", sqt)
            prev_mm = em.t(lambda t, c=c: t.matmul(pv[:], cst(csb, "ones"), sq[:],
                                                   start=(c == 0), stop=(c == NCD - 1)), tok=True)
        tok_mm = prev_mm
        em.wait("vector", tok_mm)
        em.v(lambda v: v.tensor_scalar_mul(rows[:, 0:TS], pm[:], 1.0 / D))
        em.v(lambda v: v.tensor_scalar_mul(rows[:, TS:2 * TS], pv[:], 1.0 / D))
        em.v(lambda v: v.tensor_mul(rows[:, 2 * TS:3 * TS], rows[:, 0:TS], rows[:, 0:TS]))
        stat_free = em.v(lambda v: v.tensor_sub(rows[:, 3 * TS:4 * TS], rows[:, TS:2 * TS], rows[:, 2 * TS:3 * TS]), tok=True)
        em.wait("scalar", stat_free)
        epsap = csb[0:1, CST["eps_ln"][0]:CST["eps_ln"][0] + 1]
        sqt2 = em.s(lambda sc, epsap=epsap: sc.activation(out=rows[:, 4 * TS:5 * TS], in_=rows[:, 3 * TS:4 * TS],
                                             func=AF.Sqrt, bias=epsap), tok=True)
        em.wait("vector", sqt2)
        rt = em.v(lambda v: v.reciprocal(rows[:, 5 * TS:6 * TS], rows[:, 4 * TS:5 * TS]), tok=True)
        em.wait("tensor", rt, bc_free)
        em.t(lambda t: t.matmul(pbm[:], onesrow(csb), rows[0:1, 0:TS], start=True, stop=True))
        bt = em.t(lambda t: t.matmul(pbr[:], onesrow(csb), rows[0:1, 5 * TS:6 * TS], start=True, stop=True), tok=True)
        em.wait("vector", bt, ldg, ldb)
        for c in range(NCD):
            xc = xstrip(c)
            oc = xnb[:, c * T + s * TS: c * T + (s + 1) * TS]
            oc32 = xn32[:, c * T + s * TS: c * T + (s + 1) * TS] if also_f32 else None
            em.v(lambda v, xc=xc: v.tensor_sub(tmp[:], xc, pbm[:]))
            if gamma:
                em.v(lambda v: v.tensor_mul(tmp[:], tmp[:], pbr[:]))
                gcol = gsb[:, c:c + 1]
                bcol = bsb[:, c:c + 1]
                em.v(lambda v, oc=oc, gcol=gcol, bcol=bcol: v.tensor_scalar(oc, tmp[:], gcol, bcol, ALU.mult, ALU.add))
                if also_f32:
                    em.v(lambda v, oc32=oc32, gcol=gcol, bcol=bcol: v.tensor_scalar(oc32, tmp[:], gcol, bcol, ALU.mult, ALU.add))
            else:
                em.v(lambda v, oc=oc: v.tensor_mul(oc, tmp[:], pbr[:]))
                if also_f32:
                    em.v(lambda v, oc32=oc32: v.tensor_mul(oc32, tmp[:], pbr[:]))
        bc_free = em.vfence()
    em.wait("sync", bc_free)
    store_fm(em, dram[name_out], xnb[:], ch=u + "st")
    if also_f32:
        store_fm(em, dram[also_f32], xn32[:], ch=u + "st")
    em.wait("sync", em.ch_all(u + "st"))


def stage_l0qkv(nc, st, em, dram, parts=('qkv', 'beta')):
    sb = lambda nm, sh, dt: st.enter_context(nc.sbuf_tensor(f"l0qkv_{nm}", sh, dt))
    xnb = sb("xn", [128, NCD * T], BF16)
    wsb = sb("w", [128, NCD * 1024], BF16)
    wbb = sb("wb", [128, NCD * 4], BF16)
    cwsb = sb("cw", [128, NCD * 4 * 3], F32)
    pre = sb("pre", [128, 3 + T], F32)
    conv = sb("conv", [128, T], F32)
    outb = sb("out", [128, NCD * T], BF16)
    beta_sb = sb("beta", [128, 16 * 4], F32)

    pb = [st.enter_context(nc.psum_tensor(f"l0qkv_p{i}", [128, TS], F32)) for i in range(4)]
    pbeta = st.enter_context(nc.psum_tensor("l0qkv_pbeta", [128, 4], F32))

    load_fm(em, xnb[:], dram["xn"], ch="l0qkv")
    for i, nm in enumerate(["L0_attn_conv_q_w", "L0_attn_conv_k_w", "L0_attn_conv_v_w"]):
        em.dma(lambda e, nm=nm, i=i: e.dma_start(
            out=cwsb[:, i * NCD * 4:(i + 1) * NCD * 4].rearrange("p (c j) -> p c j", c=NCD),
            in_=dram[nm].rearrange("(c p) j -> p c j", p=128)), ch="l0qkv")
    em.dma(lambda e: e.dma_start(out=wbb[:].rearrange("p (c m) -> p c m", c=NCD),
                                 in_=dram["L0_attn_Wb"].rearrange("(c p) m -> p c m", p=128)), q="gpsimd", ch="l0qkvb")
    ldx = em.ch_all("l0qkv")
    ldcw = [ldx, ldx, ldx]
    ldwb = em.ch_all("l0qkvb")

    banks = Banks(em, pb)
    zpad = em.v(lambda v: v.memset(pre[:, 0:3], 0.0), tok=True)
    last_mm = None
    tensors = [("L0_attn_Wq", "q_s"), ("L0_attn_Wk", "k_s"), ("L0_attn_Wv", "v_s")] if 'qkv' in parts else []
    for ti, (wname, oname) in enumerate(tensors):
        em.wait("gpsimd", last_mm)
        em.dma(lambda e, wname=wname: e.dma_start(
            out=wsb[:].rearrange("p (c m) -> p c m", c=NCD),
            in_=dram[wname].rearrange("(c p) m -> p c m", p=128)), q="gpsimd", ch=f"l0qkv_w{ti}")
        ldw = em.ch_all(f"l0qkv_w{ti}")
        for mc in range(NCD):
            for s in range(NSTRIP):
                bi, pt, ftok = banks.take()
                em.wait("tensor", ldx, ldw, ftok)
                mt = None
                for kc in range(NCD):
                    wk = wsb[:, kc * 1024 + mc * 128: kc * 1024 + (mc + 1) * 128]
                    xk = xnb[:, kc * T + s * TS: kc * T + (s + 1) * TS]
                    mt = em.t(lambda t, wk=wk, xk=xk, kc=kc, pt=pt: t.matmul(
                        pt[:], wk, xk, start=(kc == 0), stop=(kc == NCD - 1)), tok=(kc == NCD - 1))
                last_mm = mt
                em.wait("vector", mt, zpad)
                et = em.v(lambda v, s=s, pt=pt: v.tensor_copy(pre[:, 3 + s * TS: 3 + (s + 1) * TS], pt[:]), tok=True)
                banks.consumed(bi, et)
            em.wait("vector", ldcw[ti])
            cw = [cwsb[:, ti * NCD * 4 + mc * 4 + j: ti * NCD * 4 + mc * 4 + j + 1] for j in range(4)]
            em.v(lambda v, cw0=cw[0]: v.tensor_scalar_mul(conv[:], pre[:, 0:T], cw0))
            for j in range(1, 4):
                em.v(lambda v, j=j, cwj=cw[j]: v.scalar_tensor_tensor(conv[:], pre[:, j:j + T], cwj, conv[:], ALU.mult, ALU.add))
            cvt = em.vfence()
            em.wait("scalar", cvt)
            oc = outb[:, mc * T:(mc + 1) * T]
            stok = em.s(lambda sc, oc=oc: sc.activation(out=oc, in_=conv[:], func=AF.Silu), tok=True)
            em.wait("vector", stok)
        em.wait("sync", stok)
        store_fm(em, dram[oname], outb[:], ch="l0qkvst")
        em.wait("scalar", em.ch_all("l0qkvst"))
    if 'beta' not in parts:
        return
    em.wait("tensor", ldwb, ldx)
    sg = None
    for tt in range(16):
        em.wait("tensor", sg)
        mt = None
        for kc in range(NCD):
            xk = xnb[:, kc * T + tt * 128: kc * T + (tt + 1) * 128]
            wk = wbb[:, kc * 4:(kc + 1) * 4]
            mt = em.t(lambda t, xk=xk, wk=wk, kc=kc: t.matmul(
                pbeta[:], xk, wk, start=(kc == 0), stop=(kc == NCD - 1)), tok=(kc == NCD - 1))
        em.wait("scalar", mt)
        ob = beta_sb[:, tt * 4:(tt + 1) * 4]
        sg = em.s(lambda sc, ob=ob: sc.activation(out=ob, in_=pbeta[:], func=AF.Sigmoid), tok=True)
    em.wait("sync", sg)
    em.dma(lambda e: e.dma_start(out=dram["beta_tm"].rearrange("(tt p) h -> p tt h", p=128),
                                 in_=beta_sb[:].rearrange("p (tt h) -> p tt h", tt=16)), ch="l0qkvst")
    em.wait("sync", em.ch_all("l0qkvst"))


def np_dt(dt):
    return {F32: np.float32, BF16: ml_dtypes.bfloat16, U8: np.uint8}[dt]


def run_stages(stage_fns, inputs_per_core, input_specs, output_specs, trace=False):
    nc = bass.Bass("TRN2", target_bir_lowering=False, debug=False)
    dram = {}
    for nm, (sh, dt) in input_specs.items():
        dram[nm] = nc.dram_tensor(nm, list(sh), dt, kind="ExternalInput").ap()
    for nm, (sh, dt) in output_specs.items():
        dram[nm] = nc.dram_tensor(nm, list(sh), dt, kind="ExternalOutput").ap()
    with ExitStack() as st:
        em = Em(nc, st)
        for f in stage_fns:
            f(nc, st, em, dram)
        em.finalize()
    res = run_bass_kernel_spmd(nc, inputs_per_core, core_ids=list(range(len(inputs_per_core))), trace=trace)
    return res





def sim_stages(stage_fns, inputs, input_specs, output_specs):
    from concourse.bass_interp import CoreSim
    nc = bass.Bass("TRN2", target_bir_lowering=False, debug=False)
    dram = {}
    for nm, (sh, dt) in input_specs.items():
        dram[nm] = nc.dram_tensor(nm, list(sh), dt, kind="ExternalInput").ap()
    for nm, (sh, dt) in output_specs.items():
        dram[nm] = nc.dram_tensor(nm, list(sh), dt, kind="ExternalOutput").ap()
    with ExitStack() as st:
        em = Em(nc, st)
        for f in stage_fns:
            f(nc, st, em, dram)
        em.finalize()
    sim = CoreSim(nc, trace=False)
    for nm in input_specs:
        sim.tensor(nm)[:] = inputs[nm]
    sim.simulate(check_with_hw=False)
    return {nm: np.array(sim.tensor(nm)) for nm in output_specs}


def stage_l0prep(nc, st, em, dram):
    sb = lambda nm, sh, dt: st.enter_context(nc.sbuf_tensor(f"l0pr_{nm}", sh, dt))
    qs = sb("qs", [128, NCD * T], BF16)
    ks = sb("ks", [128, NCD * T], BF16)
    vs = sb("vs", [128, NCD * T], BF16)
    csb = sb("cst", [128, CST_W], F32)
    idb = sb("idb", [128, 128], BF16)
    sq = sb("sq", [128, TS], F32)
    row = sb("row", [1, 2 * TS], F32)
    stg = [sb(f"stg{i}", [128, 256], BF16) for i in range(2)]

    prow = st.enter_context(nc.psum_tensor("l0pr_prow", [1, TS], F32))
    pbr = st.enter_context(nc.psum_tensor("l0pr_pbr", [128, TS], F32))
    ptr = [st.enter_context(nc.psum_tensor(f"l0pr_ptr{i}", [128, 256], BF16)) for i in range(2)]

    load_fm(em, qs[:], dram["q_s"], ch="l0pr")
    load_fm(em, ks[:], dram["k_s"], ch="l0pr")
    load_fm(em, vs[:], dram["v_s"], ch="l0pr")
    em.dma(lambda e: e.dma_start(out=csb[:], in_=dram["cst_f32"]), ch="l0pr")
    ld = em.ch_all("l0pr")
    em.wait("vector", ld)
    idt = em.v(lambda v: v.tensor_copy(idb[:], cst(csb, "ident")))

    eps2 = csb[0:1, CST["eps_l2"][0]:CST["eps_l2"][0] + 1]
    last_norm = None
    prow_free = None
    pbr_free = None
    for ti, src_sb in enumerate([qs, ks]):
        for h in range(NH):
            for s in range(NSTRIP):
                mmt = None
                for i, c in enumerate([2 * h, 2 * h + 1]):
                    xc = src_sb[:, c * T + s * TS: c * T + (s + 1) * TS]
                    em.wait("scalar", ld, prow_free, mmt)
                    sqt = em.s(lambda sc, xc=xc: sc.activation(out=sq[:], in_=xc, func=AF.Square))
                    em.wait("tensor", sqt, prow_free)
                    mmt = em.t(lambda t, i=i: t.matmul(prow[:], cst(csb, "ones"), sq[:],
                                                       start=(i == 0), stop=(i == 1)), tok=True)
                em.wait("scalar", mmt)
                rsq = em.s(lambda sc: sc.activation(out=row[:, 0:TS], in_=prow[:], func=AF.Sqrt, bias=eps2))
                prow_free = rsq
                em.wait("vector", rsq)
                rec = em.v(lambda v: v.reciprocal(row[:, TS:2 * TS], row[:, 0:TS]))
                em.wait("tensor", rec, pbr_free)
                bct = em.t(lambda t: t.matmul(pbr[:], onesrow(csb), row[0:1, TS:2 * TS], start=True, stop=True), tok=True)
                em.wait("vector", bct)
                for c in (2 * h, 2 * h + 1):
                    xc = src_sb[:, c * T + s * TS: c * T + (s + 1) * TS]
                    last_norm = em.v(lambda v, xc=xc: v.tensor_mul(xc, xc, pbr[:]))
                pbr_free = last_norm
        em.wait("sync", last_norm)
        store_fm(em, dram["q_n" if ti == 0 else "k_n"], src_sb[:], ch="l0prst")
    ev_tok = {0: None, 1: None}
    mm_free = {0: None, 1: None}
    for src_sb, oname in [(ks, "k_tm"), (vs, "v_tm")]:
        for h in range(NH):
            for tt in range(16):
                i = tt % 2
                em.wait("tensor", idt, last_norm, mm_free[i])
                trt = None
                for dc in range(2):
                    xc = src_sb[:, (2 * h + dc) * T + tt * 128: (2 * h + dc) * T + (tt + 1) * 128]
                    trt = em.t(lambda t, xc=xc, dc=dc, i=i: t.transpose(ptr[i][:, dc * 128:(dc + 1) * 128], xc, idb[:]),
                               tok=(dc == 1))
                em.wait("vector", trt, em.ch_all(f"l0prtt{i}"))
                ev = em.v(lambda v, i=i: v.tensor_copy(stg[i][:], ptr[i][:]))
                mm_free[i] = ev
                em.wait("sync", ev)
                dst = dram[oname][h * T + tt * 128: h * T + (tt + 1) * 128, :]
                em.dma(lambda e, dst=dst, i=i: e.dma_start(out=dst, in_=stg[i][:]), ch=f"l0prtt{i}")
    em.wait("sync", em.ch_all("l0prtt0"), em.ch_all("l0prtt1"), em.ch_all("l0prst"))


def stage_delta(nc, st, em, dram):
    C = 128
    NCHUNK = T // C
    sb = lambda nm, sh, dt: st.enter_context(nc.sbuf_tensor(f"dl_{nm}", sh, dt))
    qn = sb("qn", [128, NCD * T], BF16)
    kn = sb("kn", [128, NCD * T], BF16)
    bt = sb("bt", [128, 16 * 4], F32)
    csb = sb("cst", [128, CST_W], F32)
    idb = sb("idb", [128, 128], BF16)
    ktm = [sb(f"ktm{i}", [128, 256], BF16) for i in range(2)]
    vtm = [sb(f"vtm{i}", [128, 256], BF16) for i in range(2)]
    Ab = sb("A", [128, 128], BF16)
    BT = sb("BT", [128, 128], BF16)
    Xb = sb("X", [128, 512], BF16)
    wfm = sb("wfm", [128, 256], BF16)
    u2 = sb("u2", [128, 256], BF16)
    al = sb("al", [128, 128], BF16)
    s32 = sb("s32", [128, 512], F32)
    sbf = sb("sbf", [128, 512], BF16)
    dstg = [sb(f"dstg{i}", [128, 256], BF16) for i in range(2)]

    pG = st.enter_context(nc.psum_tensor("dl_pG", [128, 128], F32))
    pX = st.enter_context(nc.psum_tensor("dl_pX", [128, 512], F32))
    pB = st.enter_context(nc.psum_tensor("dl_pB", [128, 128], F32))
    pT = st.enter_context(nc.psum_tensor("dl_pT", [128, 256], BF16))
    pt1 = st.enter_context(nc.psum_tensor("dl_pt1", [128, 256], F32))
    pS = st.enter_context(nc.psum_tensor("dl_pS", [128, 512], F32))
    pO = st.enter_context(nc.psum_tensor("dl_pO", [128, 256], F32))

    load_fm(em, qn[:], dram["q_n"], ch="dl")
    load_fm(em, kn[:], dram["k_n"], ch="dl")
    em.dma(lambda e: e.dma_start(out=bt[:].rearrange("p (tt h) -> p tt h", tt=16),
                                 in_=dram["beta_tm"].rearrange("(tt p) h -> p tt h", p=128)), ch="dl")
    em.dma(lambda e: e.dma_start(out=csb[:], in_=dram["cst_f32"]), ch="dl")
    ld = em.ch_all("dl")
    em.wait("vector", ld)
    idt = em.v(lambda v: v.tensor_copy(idb[:], cst(csb, "ident")))

    def ldtile(h, n, wait_tok=None):
        i = n % 2
        em.wait("sync", wait_tok)
        em.dma(lambda e, h=h, n=n, i=i: e.dma_start(out=ktm[i][:], in_=dram["k_tm"][h * T + n * C: h * T + (n + 1) * C, :]), ch=f"dlt{i}")
        em.dma(lambda e, h=h, n=n, i=i: e.dma_start(out=vtm[i][:], in_=dram["v_tm"][h * T + n * C: h * T + (n + 1) * C, :]), ch=f"dlt{i}")
        return em.ch_all(f"dlt{i}")

    tok = ldtile(0, 0)
    tile_toks = {(0, 0): tok}
    last_store = {0: None, 1: None}
    for h in range(NH):
        zs = em.v(lambda v: v.memset(s32[:], 0.0))
        zb = em.v(lambda v: v.memset(sbf[:], 0.0))
        for n in range(NCHUNK):
            i = n % 2
            ltok = tile_toks.pop((h, n))
            em.wait("tensor", zb)
            nh, nn = (h, n + 1) if n + 1 < NCHUNK else (h + 1, 0)
            if nh < NH:
                tile_toks[(nh, nn)] = ldtile(nh, nn, wait_tok=zb)
            bcol = bt[:, n * 4 + h: n * 4 + h + 1]
            em.wait("tensor", ld)
            gt = None
            for dc in range(2):
                kc = kn[:, (2 * h + dc) * T + n * C: (2 * h + dc) * T + (n + 1) * C]
                gt = em.t(lambda t, kc=kc, dc=dc: t.matmul(pG[:], kc, kc, start=(dc == 0), stop=(dc == 1)),
                          tok=(dc == 1))
            em.wait("vector", gt, ltok)
            a1 = em.v(lambda v: v.tensor_mul(Ab[:], pG[:], cst(csb, "strict_low")))
            a2 = em.v(lambda v, bcol=bcol: v.tensor_scalar(Ab[:], Ab[:], bcol, -1.0, ALU.mult, ALU.mult))
            x1 = em.v(lambda v, i=i, bcol=bcol: v.tensor_scalar_mul(Xb[:, 0:256], vtm[i][:], bcol))
            x2 = em.v(lambda v, i=i, bcol=bcol: v.tensor_scalar_mul(Xb[:, 256:512], ktm[i][:], bcol))
            prevBT = None
            prevX = x2
            prevB = a2
            for p in range(7):
                em.wait("tensor", prevB, prevX, idt, prevBT)
                trt = em.t(lambda t: t.transpose(pT[:, 0:128], Ab[:], idb[:]), tok=True)
                em.wait("vector", trt)
                prevBT = em.v(lambda v: v.tensor_copy(BT[:], pT[:, 0:128]))
                em.wait("tensor", prevBT, prevX)
                em.t(lambda t: t.matmul(pX[:], idb[:], Xb[:], start=True, stop=False))
                xt = em.t(lambda t: t.matmul(pX[:], BT[:], Xb[:], start=False, stop=True), tok=True)
                if p < 6:
                    bsq = em.t(lambda t: t.matmul(pB[:], BT[:], Ab[:], start=True, stop=True), tok=True)
                em.wait("vector", xt)
                prevX = em.v(lambda v: v.tensor_copy(Xb[:], pX[:]))
                if p < 6:
                    em.wait("vector", bsq)
                    prevB = em.v(lambda v: v.tensor_copy(Ab[:], pB[:]))
            em.wait("tensor", prevX, prevBT)
            trt = None
            for dc in range(2):
                trt = em.t(lambda t, dc=dc: t.transpose(pT[:, 128 * dc:128 * (dc + 1)], Xb[:, 256 + 128 * dc: 256 + 128 * (dc + 1)], idb[:]), tok=(dc == 1))
            em.wait("vector", trt)
            wv = em.v(lambda v: v.tensor_copy(wfm[:], pT[:]))
            em.wait("tensor", wv, zb)
            tt1 = None
            for dc in range(2):
                tt1 = em.t(lambda t, dc=dc: t.matmul(pt1[:], wfm[:, 128 * dc:128 * (dc + 1)], sbf[:, 256 * dc:256 * (dc + 1)],
                                                     start=(dc == 0), stop=(dc == 1)), tok=(dc == 1))
            em.wait("vector", tt1)
            ut = em.v(lambda v: v.tensor_sub(u2[:], Xb[:, 0:256], pt1[:]))
            altt = None
            for dc in range(2):
                kc = kn[:, (2 * h + dc) * T + n * C: (2 * h + dc) * T + (n + 1) * C]
                qc = qn[:, (2 * h + dc) * T + n * C: (2 * h + dc) * T + (n + 1) * C]
                altt = em.t(lambda t, kc=kc, qc=qc, dc=dc: t.matmul(pG[:], kc, qc, start=(dc == 0), stop=(dc == 1)),
                            tok=(dc == 1))
            em.wait("vector", altt)
            alv = em.v(lambda v: v.tensor_mul(al[:], pG[:], cst(csb, "incl_up")))
            em.wait("tensor", alv, ut)
            for dc in range(2):
                qc = qn[:, (2 * h + dc) * T + n * C: (2 * h + dc) * T + (n + 1) * C]
                em.t(lambda t, qc=qc, dc=dc: t.matmul(pO[:], qc, sbf[:, 256 * dc:256 * (dc + 1)], start=(dc == 0), stop=False))
            ot = em.t(lambda t: t.matmul(pO[:], al[:], u2[:], start=False, stop=True), tok=True)
            stt = None
            for dc in range(2):
                stt = em.t(lambda t, dc=dc, i=i: t.matmul(pS[:, 256 * dc:256 * (dc + 1)], ktm[i][:, 128 * dc:128 * (dc + 1)], u2[:],
                                                          start=True, stop=True), tok=(dc == 1))
            em.wait("vector", ot)
            j = n % 2
            em.wait("vector", em.ch_all(f"dlst{j}"))
            ov = em.v(lambda v, j=j: v.tensor_copy(dstg[j][:], pO[:]))
            em.wait("sync", ov)
            dst = dram["delta_tm"][h * T + n * C: h * T + (n + 1) * C, :]
            em.dma(lambda e, dst=dst, j=j: e.dma_start(out=dst, in_=dstg[j][:]), ch=f"dlst{j}")
            em.wait("vector", stt)
            sv = em.v(lambda v: v.tensor_add(s32[:], s32[:], pS[:]))
            zb = em.v(lambda v: v.tensor_copy(sbf[:], s32[:]))
    em.wait("sync", em.ch_all("dlst0"), em.ch_all("dlst1"))


PAD = 63
CW = PAD + T


def stage_l0fir(nc, st, em, dram):
    HW = 512
    sb = lambda nm, sh, dt: st.enter_context(nc.sbuf_tensor(f"l0f_{nm}", sh, dt))
    vpad = sb("vpad", [128, NCD * CW], BF16)
    lsb = sb("ls", [128, NCD * T], BF16)
    llb = sb("ll", [128, NCD * T], BF16)
    wsw = sb("wsw", [128, NCD * 5], F32)
    wlw = sb("wlw", [128, NCD * 64], F32)
    csb = sb("cst", [128, CST_W], F32)
    idb = sb("idb", [128, 128], BF16)
    diag = [sb(f"diag{i}", [128, 69 * 128], BF16) for i in range(2)]
    tile = [sb(f"tile{i}", [128, 256], BF16) for i in range(2)]
    sqb = sb("sqb", [128, 256], F32)
    raw = sb("raw", [128, 12], F32)
    st16 = sb("st16", [128, 16], BF16)
    mtmp = sb("mtmp", [128, 4], F32)
    sstg = [sb(f"sstg{i}", [16, 128], BF16) for i in range(2)]

    pcv = [st.enter_context(nc.psum_tensor(f"l0f_pcv{i}", [128, HW], F32)) for i in range(2)]
    ptr = st.enter_context(nc.psum_tensor("l0f_ptr", [128, 256], BF16))
    pst = st.enter_context(nc.psum_tensor("l0f_pst", [16, 128], BF16))

    zp = em.v(lambda v: v.memset(vpad[:], 0.0))
    em.wait("sync", zp)
    em.dma(lambda e: e.dma_start(
        out=vpad[:].rearrange("p (c t) -> p c t", c=NCD)[:, :, PAD:CW],
        in_=dram["v_s"].rearrange("(c p) t -> p c t", p=128)), ch="l0f")
    em.dma(lambda e: e.dma_start(out=wsw[:].rearrange("p (c j) -> p c j", c=NCD),
                                 in_=dram["L0_attn_fir_short_w"].rearrange("(c p) j -> p c j", p=128)), ch="l0f")
    em.dma(lambda e: e.dma_start(out=wlw[:].rearrange("p (c j) -> p c j", c=NCD),
                                 in_=dram["L0_attn_fir_long_w"].rearrange("(c p) j -> p c j", p=128)), ch="l0f")
    em.dma(lambda e: e.dma_start(out=csb[:], in_=dram["cst_f32"]), ch="l0f")
    ld = em.ch_all("l0f")
    em.wait("vector", ld)
    idt = em.v(lambda v: v.tensor_copy(idb[:], cst(csb, "ident")))

    diag_user = [None, None]
    pcv_user = [None, None]
    evac_tok = {}
    rr = 0
    for c in range(NCD):
        i = c % 2
        em.wait("vector", diag_user[i], ld)
        dbuild = None
        for j in range(64):
            wj = wlw[:, c * 64 + j: c * 64 + j + 1]
            dbuild = em.v(lambda v, j=j, wj=wj, i=i: v.tensor_scalar_mul(
                diag[i][:, j * 128:(j + 1) * 128], idb[:], wj))
        for j in range(5):
            wj = wsw[:, c * 5 + j: c * 5 + j + 1]
            dbuild = em.v(lambda v, j=j, wj=wj, i=i: v.tensor_scalar_mul(
                diag[i][:, (64 + j) * 128:(64 + j + 1) * 128], idb[:], wj))
        base = c * CW
        for hh in range(4):
            for kind in ("L", "S"):
                pp = rr % 2
                rr += 1
                em.wait("tensor", dbuild, ld, pcv_user[pp])
                mm = None
                taps = range(64) if kind == "L" else range(5)
                for jj, j in enumerate(taps):
                    off = j if kind == "L" else 59 + j
                    dslot = j if kind == "L" else 64 + j
                    xs_ = vpad[:, base + off + hh * HW: base + off + hh * HW + HW]
                    dg = diag[i][:, dslot * 128:(dslot + 1) * 128]
                    last = (jj == (63 if kind == "L" else 4))
                    mm = em.t(lambda t, dg=dg, xs_=xs_, jj=jj, pp=pp, last=last: t.matmul(
                        pcv[pp][:], dg, xs_, start=(jj == 0), stop=last), tok=last)
                diag_user[i] = mm
                em.wait("scalar", mm)
                dstb = (llb if kind == "L" else lsb)[:, c * T + hh * HW: c * T + hh * HW + HW]
                ev = em.s(lambda sc, dstb=dstb, pp=pp: sc.activation(out=dstb, in_=pcv[pp][:], func=AF.Copy))
                pcv_user[pp] = ev
                evac_tok[(kind, c, hh)] = ev

    ptr_free = None
    pst_free = None
    sqb_user = None
    tile_free = [None, None]
    for h in range(NH):
        cvdone = [evac_tok[(k, 2 * h + dc, hh)] for k in ("L", "S") for dc in (0, 1) for hh in range(4)]
        for tt in range(16):
            ib = tt % 2
            for q in range(4):
                if q < 2:
                    srcb = lsb if q == 0 else llb
                    em.wait("tensor", idt, ptr_free, *cvdone)
                    trt = None
                    for dc in range(2):
                        xc = srcb[:, (2 * h + dc) * T + tt * 128: (2 * h + dc) * T + (tt + 1) * 128]
                        trt = em.t(lambda t, xc=xc, dc=dc: t.transpose(ptr[:, dc * 128:(dc + 1) * 128], xc, idb[:]),
                                   tok=(dc == 1))
                    em.wait("vector", trt, tile_free[ib] if q == 0 else None)
                    tv = em.v(lambda v, ib=ib: v.tensor_copy(tile[ib][:], ptr[:]))
                    ptr_free = tv
                    src_ap = tile[ib][:]
                    ltok = tv
                else:
                    nm = "delta_tm" if q == 2 else "v_tm"
                    em.wait("sync", tile_free[ib])
                    em.dma(lambda e, nm=nm, h=h, tt=tt, ib=ib: e.dma_start(
                        out=tile[ib][:], in_=dram[nm][h * T + tt * 128: h * T + (tt + 1) * 128, :]), ch=f"l0ft{ib}")
                    ltok = em.ch_all(f"l0ft{ib}")
                    em.wait("vector", ltok)
                    src_ap = tile[ib][:]
                em.v(lambda v, src_ap=src_ap, q=q: v.reduce_sum(raw[:, 3 * q: 3 * q + 1], src_ap, AX.X))
                em.wait("scalar", ltok, sqb_user)
                sqt = em.s(lambda sc, src_ap=src_ap: sc.activation(out=sqb[:], in_=src_ap, func=AF.Square))
                em.wait("vector", sqt)
                sqb_user = em.v(lambda v, q=q: v.reduce_sum(raw[:, 3 * q + 1: 3 * q + 2], sqb[:], AX.X))
                lastr = em.v(lambda v, src_ap=src_ap, q=q: v.reduce_sum(raw[:, 3 * q + 2: 3 * q + 3], src_ap, AX.X,
                                                                       apply_absolute_value=True))
                tile_free[ib] = lastr
            s16v = lambda i: st16[:].rearrange("p (q s) -> p q s", s=4)[:, :, i:i + 1]
            rawv = lambda i: raw[:].rearrange("p (q s) -> p q s", s=3)[:, :, i:i + 1]
            em.v(lambda v: v.tensor_scalar_mul(s16v(0), rawv(0), 1.0 / 256))
            em.v(lambda v: v.tensor_scalar_mul(mtmp[:].rearrange("p (q s) -> p q s", s=1), rawv(0), 1.0 / 256))
            em.v(lambda v: v.tensor_mul(mtmp[:], mtmp[:], mtmp[:]))
            em.v(lambda v: v.scalar_tensor_tensor(s16v(1), rawv(1), 1.0 / 256,
                                                  mtmp[:].rearrange("p (q s) -> p q s", s=1), ALU.mult, ALU.subtract))
            fin = em.v(lambda v: v.tensor_scalar_mul(s16v(2), rawv(2), 1.0 / 256))
            em.wait("scalar", lastr, fin)
            sqr = em.s(lambda sc: sc.activation(out=s16v(3), in_=rawv(1), func=AF.Sqrt))
            em.wait("tensor", sqr, fin, pst_free)
            tst = em.t(lambda t: t.transpose(pst[:], st16[:], idb[:]), tok=True)
            jb = tt % 2
            em.wait("vector", tst, em.ch_all(f"l0fs{jb}"))
            sv = em.v(lambda v, jb=jb: v.tensor_copy(sstg[jb][:], pst[:]))
            pst_free = sv
            em.wait("sync", sv)
            dst = dram["stats_f"][0:16, h * T + tt * 128: h * T + (tt + 1) * 128]
            em.dma(lambda e, dst=dst, jb=jb: e.dma_start(out=dst, in_=sstg[jb][:]), ch=f"l0fs{jb}")
    st1 = store_fm(em, dram["ls_fm"], lsb[:], ch="l0fst")
    st2 = store_fm(em, dram["ll_fm"], llb[:], ch="l0fst")
    em.wait("sync", em.ch_all("l0fst"), em.ch_all("l0fs0"), em.ch_all("l0fs1"))


def stage_l0gate(nc, st, em, dram):
    sb = lambda nm, sh, dt: st.enter_context(nc.sbuf_tensor(f"l0g_{nm}", sh, dt))
    xnb = sb("xn", [128, NCD * T], BF16)
    w1a = sb("w1a", [128, NCD * 1024], BF16)
    w1b = sb("w1b", [16, 1024], BF16)
    b1 = sb("b1", [128, NCD], F32)
    w2 = sb("w2", [128, NCD * 4], BF16)
    b2r = sb("b2r", [1, 4], F32)
    ltr = sb("ltr", [1, 4], F32)
    rtr = sb("rtr", [1, 4], F32)
    stf = sb("stf", [16, NH * T], BF16)
    csb = sb("cst", [128, CST_W], F32)
    idb = sb("idb", [128, 128], BF16)
    h1s = sb("h1s", [128, NCD * TS], BF16)
    b2bc = sb("b2bc", [128, 4], F32)
    rtc = sb("rtc", [128, 4], F32)
    tsb = sb("tsb", [128, 4], F32)
    nmax = sb("nmax", [128, 1], F32)
    zsum = sb("zsum", [128, 1], F32)
    rz = sb("rz", [128, 1], F32)
    eb = sb("eb", [128, 4], F32)
    prt = sb("prt", [128, 4], BF16)
    pstg = [sb(f"pstg{i}", [4, 128], BF16) for i in range(2)]

    ph1 = [st.enter_context(nc.psum_tensor(f"l0g_ph1_{i}", [128, TS], F32)) for i in range(2)]
    plog = st.enter_context(nc.psum_tensor("l0g_plog", [128, 4], F32))
    pbc = st.enter_context(nc.psum_tensor("l0g_pbc", [128, 4], F32))
    ppr = st.enter_context(nc.psum_tensor("l0g_ppr", [4, 128], BF16))

    load_fm(em, xnb[:], dram["xn"], ch="l0g")
    em.dma(lambda e: e.dma_start(out=csb[:], in_=dram["cst_f32"]), ch="l0g")
    em.dma(lambda e: e.dma_start(out=stf[:], in_=dram["stats_f"]), ch="l0g")
    load_pp(em, b1[:], dram["L0_attn_mlp_b1"], ch="l0g")
    em.dma(lambda e: e.dma_start(out=b2r[:], in_=dram["L0_attn_mlp_b2"].rearrange("(o f) -> o f", o=1)), ch="l0g")
    em.dma(lambda e: e.dma_start(out=ltr[:], in_=dram["L0_attn_log_temp"].rearrange("(a h) one -> a (h one)", a=1)), ch="l0g")
    em.dma(lambda e: e.dma_start(out=w1a[:].rearrange("p (c m) -> p c m", c=NCD),
                                 in_=dram["L0_attn_mlp_w1"][0:1024, :].rearrange("(c p) m -> p c m", p=128)),
           q="gpsimd", ch="l0gw")
    em.dma(lambda e: e.dma_start(out=w1b[:], in_=dram["L0_attn_mlp_w1"][1024:1040, :]), q="gpsimd", ch="l0gw")
    em.dma(lambda e: e.dma_start(out=w2[:].rearrange("p (c m) -> p c m", c=NCD),
                                 in_=dram["L0_attn_mlp_w2"].rearrange("(c p) m -> p c m", p=128)),
           q="gpsimd", ch="l0gw")
    ld = em.ch_all("l0g")
    ldw = em.ch_all("l0gw")
    em.wait("vector", ld)
    idt = em.v(lambda v: v.tensor_copy(idb[:], cst(csb, "ident")))
    em.wait("scalar", ld)
    ext = em.s(lambda sc: sc.activation(out=rtr[:], in_=ltr[:], func=AF.Exp))
    em.wait("vector", ext)
    rct = em.v(lambda v: v.reciprocal(rtr[:], rtr[:]))
    em.wait("tensor", rct)
    bt1 = em.t(lambda t: t.matmul(pbc[:], onesrow(csb), b2r[:], start=True, stop=True), tok=True)
    em.wait("vector", bt1)
    bv1 = em.v(lambda v: v.tensor_copy(b2bc[:], pbc[:]))
    bv2 = bv1
    for h in range(NH):
        em.wait("tensor", bv2)
        bt2 = em.t(lambda t, h=h: t.matmul(pbc[:, 0:1], onesrow(csb), rtr[0:1, h:h + 1], start=True, stop=True), tok=True)
        em.wait("vector", bt2)
        bv2 = em.v(lambda v, h=h: v.tensor_copy(rtc[:, h:h + 1], pbc[:, 0:1]))

    ph_free = [None, None]
    plog_free = None
    ppr_free = None
    for h in range(NH):
        for s in range(NSTRIP):
            for mc in range(NCD):
                ib = mc % 2
                em.wait("tensor", ld, ldw, ph_free[ib])
                for kc in range(NCD):
                    wk = w1a[:, kc * 1024 + mc * 128: kc * 1024 + (mc + 1) * 128]
                    xk = xnb[:, kc * T + s * TS: kc * T + (s + 1) * TS]
                    em.t(lambda t, wk=wk, xk=xk, kc=kc, ib=ib: t.matmul(
                        ph1[ib][:], wk, xk, start=(kc == 0), stop=False))
                sk = stf[0:16, h * T + s * TS: h * T + (s + 1) * TS]
                w1bk = w1b[0:16, mc * 128: (mc + 1) * 128]
                mt = em.t(lambda t, sk=sk, w1bk=w1bk, ib=ib: t.matmul(
                    ph1[ib][:], w1bk, sk, start=False, stop=True), tok=True)
                em.wait("scalar", mt, ldw)
                b1c = b1[:, mc: mc + 1]
                gl = em.s(lambda sc, ib=ib, b1c=b1c, mc=mc: sc.activation(
                    out=h1s[:, mc * TS:(mc + 1) * TS], in_=ph1[ib][:], func=AF.Gelu, bias=b1c))
                ph_free[ib] = gl
            for tt in range(4):
                gtt = s * 4 + tt
                jb = gtt % 2
                em.wait("tensor", gl, plog_free)
                lt = None
                for mc in range(NCD):
                    hk = h1s[:, mc * TS + tt * 128: mc * TS + (tt + 1) * 128]
                    wk2 = w2[:, mc * 4: (mc + 1) * 4]
                    lt = em.t(lambda t, hk=hk, wk2=wk2, mc=mc: t.matmul(
                        plog[:], hk, wk2, start=(mc == 0), stop=(mc == NCD - 1)), tok=(mc == NCD - 1))
                em.wait("vector", lt, bv1, bv2)
                em.v(lambda v: v.tensor_add(tsb[:], plog[:], b2bc[:]))
                em.v(lambda v, h=h: v.tensor_scalar_mul(tsb[:], tsb[:], rtc[:, h:h + 1]))
                mx = em.v(lambda v: v.reduce_max(nmax[:], tsb[:], AX.X, negate=True))
                em.wait("scalar", mx)
                et = em.s(lambda sc: sc.activation(out=eb[:], in_=tsb[:], func=AF.Exp, bias=nmax[:, 0:1],
                                                   accum_out=zsum[:, 0:1]))
                em.wait("vector", et)
                plog_free = em.v(lambda v: v.reciprocal(rz[:], zsum[:]))
                em.v(lambda v: v.tensor_scalar_mul(rz[:], rz[:], 0.8))
                pv = em.v(lambda v: v.tensor_scalar(prt[:], eb[:], rz[:, 0:1], 0.05, ALU.mult, ALU.add))
                em.wait("tensor", pv, idt, ppr_free)
                ptk = em.t(lambda t: t.transpose(ppr[:], prt[:], idb[:]), tok=True)
                em.wait("vector", ptk, em.ch_all(f"l0gp{jb}"))
                sv = em.v(lambda v, jb=jb: v.tensor_copy(pstg[jb][:], ppr[:]))
                ppr_free = sv
                em.wait("sync", sv)
                dst = dram["probs_rows"][0:4, h * T + gtt * 128: h * T + (gtt + 1) * 128]
                em.dma(lambda e, dst=dst, jb=jb: e.dma_start(out=dst, in_=pstg[jb][:]), ch=f"l0gp{jb}")
    em.wait("sync", em.ch_all("l0gp0"), em.ch_all("l0gp1"))


def stage_l0mix(nc, st, em, dram):
    sb = lambda nm, sh, dt: st.enter_context(nc.sbuf_tensor(f"l0m_{nm}", sh, dt))
    lsb = sb("ls", [128, NCD * T], BF16)
    llb = sb("ll", [128, NCD * T], BF16)
    vsb = sb("vs", [128, NCD * T], BF16)
    onb = sb("on", [128, NCD * T], BF16)
    wob = sb("wo", [128, NCD * 1024], BF16)
    prw = sb("prw", [4, NH * T], BF16)
    selb = sb("sel", [32, CSTB_W], BF16)
    csb = sb("cst", [128, CST_W], F32)
    idb = sb("idb", [128, 128], BF16)
    onw = sb("onw", [128, 2], F32)
    dtile = [sb(f"dt{i}", [128, 4 * 256], BF16) for i in range(2)]
    dstg = sb("dstg", [128, 2 * TS], BF16)
    accb = sb("acc", [128, 2 * TS], F32)
    tmpb = sb("tmp", [128, TS], F32)
    sqb = sb("sq", [128, TS], F32)
    rowb = sb("row", [1, 2 * TS], F32)
    ystg = [sb(f"ystg{i}", [128, TS], F32) for i in range(2)]

    pbc = [st.enter_context(nc.psum_tensor(f"l0m_pbc{i}", [128, TS], F32)) for i in range(4)]
    pd = st.enter_context(nc.psum_tensor("l0m_pd", [128, TS], BF16))
    prow = st.enter_context(nc.psum_tensor("l0m_prow", [1, TS], F32))
    pbr = st.enter_context(nc.psum_tensor("l0m_pbr", [128, TS], F32))
    pwo = st.enter_context(nc.psum_tensor("l0m_pwo", [128, TS], F32))

    load_fm(em, lsb[:], dram["ls_fm"], ch="l0m")
    load_fm(em, llb[:], dram["ll_fm"], ch="l0m")
    load_fm(em, vsb[:], dram["v_s"], ch="l0m")
    em.dma(lambda e: e.dma_start(out=prw[:], in_=dram["probs_rows"]), ch="l0m")
    em.dma(lambda e: e.dma_start(out=selb[:], in_=dram["cst_bf16"]), ch="l0m")
    em.dma(lambda e: e.dma_start(out=csb[:], in_=dram["cst_f32"]), ch="l0m")
    load_pp(em, onw[:], dram["L0_attn_o_norm_w"], ch="l0m")
    em.dma(lambda e: e.dma_start(out=wob[:].rearrange("p (c m) -> p c m", c=NCD),
                                 in_=dram["L0_attn_Wo"].rearrange("(c p) m -> p c m", p=128)), q="gpsimd", ch="l0mw")
    ld = em.ch_all("l0m")
    ldw = em.ch_all("l0mw")
    em.wait("vector", ld)
    idt = em.v(lambda v: v.tensor_copy(idb[:], cst(csb, "ident")))

    eps1 = csb[0:1, CST["eps_ln"][0]:CST["eps_ln"][0] + 1]
    pd_free = None
    pbc_free = [None] * 4
    prow_free = None
    pbr_free = None
    acc_free = None
    dt_free = [None, None]
    for h in range(NH):
        for s in range(NSTRIP):
            ib = (h * NSTRIP + s) % 2
            em.wait("sync", dt_free[ib])
            em.dma(lambda e, h=h, s=s, ib=ib: e.dma_start(
                out=dtile[ib][:].rearrange("p (j d) -> p j d", j=4),
                in_=dram["delta_tm"][h * T + s * TS: h * T + (s + 1) * TS, :].rearrange("(j p) d -> p j d", p=128)),
                ch=f"l0md{ib}")
            dl = em.ch_all(f"l0md{ib}")
            bts = []
            for i in range(4):
                em.wait("tensor", ld, pbc_free[i])
                pr = prw[0:4, h * T + s * TS: h * T + (s + 1) * TS]
                sl = selb[0:4, i * 128: (i + 1) * 128]
                bts.append(em.t(lambda t, pr=pr, sl=sl, i=i: t.matmul(pbc[i][:], sl, pr, start=True, stop=True), tok=True))
            for dc in range(2):
                em.wait("tensor", idt, dl, pd_free)
                trt = None
                for j in range(4):
                    xc = dtile[ib][:, j * 256 + dc * 128: j * 256 + (dc + 1) * 128]
                    trt = em.t(lambda t, xc=xc, j=j: t.transpose(pd[:, j * 128:(j + 1) * 128], xc, idb[:]),
                               tok=(j == 3))
                dt_free[ib] = trt
                em.wait("vector", trt)
                pd_free = em.v(lambda v, dc=dc: v.tensor_copy(dstg[:, dc * TS:(dc + 1) * TS], pd[:]))
                c = 2 * h + dc
                em.wait("vector", bts[0], bts[1], bts[2], bts[3], acc_free)
                adst = accb[:, dc * TS:(dc + 1) * TS]
                em.v(lambda v, c=c, s=s, adst=adst: v.tensor_mul(
                    adst, lsb[:, c * T + s * TS: c * T + (s + 1) * TS], pbc[0][:]))
                for i, srcb in ((1, llb), (3, vsb)):
                    em.v(lambda v, c=c, s=s, srcb=srcb, i=i: v.tensor_mul(
                        tmpb[:], srcb[:, c * T + s * TS: c * T + (s + 1) * TS], pbc[i][:]))
                    em.v(lambda v, adst=adst: v.tensor_add(adst, adst, tmpb[:]))
                em.v(lambda v, dc=dc: v.tensor_mul(tmpb[:], dstg[:, dc * TS:(dc + 1) * TS], pbc[2][:]))
                mfin = em.v(lambda v, adst=adst: v.tensor_add(adst, adst, tmpb[:]))
                for i in range(4):
                    pbc_free[i] = mfin
            mmt = None
            for dc in range(2):
                em.wait("scalar", mfin, prow_free, mmt)
                sqt = em.s(lambda sc, dc=dc: sc.activation(out=sqb[:], in_=accb[:, dc * TS:(dc + 1) * TS], func=AF.Square))
                em.wait("tensor", sqt, prow_free)
                mmt = em.t(lambda t, dc=dc: t.matmul(prow[:], cst(csb, "ones"), sqb[:],
                                                     start=(dc == 0), stop=(dc == 1)), tok=True)
            em.wait("vector", mmt)
            em.v(lambda v: v.tensor_scalar_mul(rowb[:, 0:TS], prow[:], 1.0 / 256))
            rv = em.v(lambda v: v.engine_nop())
            prow_free = rv
            em.wait("scalar", rv)
            sq2 = em.s(lambda sc: sc.activation(out=rowb[:, TS:2 * TS], in_=rowb[:, 0:TS], func=AF.Sqrt, bias=eps1))
            em.wait("vector", sq2)
            rc = em.v(lambda v: v.reciprocal(rowb[:, 0:TS], rowb[:, TS:2 * TS]))
            em.wait("tensor", rc, pbr_free)
            bct = em.t(lambda t: t.matmul(pbr[:], onesrow(csb), rowb[0:1, 0:TS], start=True, stop=True), tok=True)
            em.wait("vector", bct)
            af = None
            for dc in range(2):
                c = 2 * h + dc
                oc = onb[:, c * T + s * TS: c * T + (s + 1) * TS]
                wcol = onw[:, dc: dc + 1]
                af = em.v(lambda v, oc=oc, wcol=wcol, dc=dc: v.scalar_tensor_tensor(
                    oc, accb[:, dc * TS:(dc + 1) * TS], wcol, pbr[:], ALU.mult, ALU.mult))
            pbr_free = af
            acc_free = af
    onfin = af
    pwo_free = None
    for mc in range(NCD):
        for s in range(NSTRIP):
            jb = (mc * NSTRIP + s) % 2
            em.wait("tensor", onfin, ldw, pwo_free)
            mt = None
            for kc in range(NCD):
                wk = wob[:, kc * 1024 + mc * 128: kc * 1024 + (mc + 1) * 128]
                ok = onb[:, kc * T + s * TS: kc * T + (s + 1) * TS]
                mt = em.t(lambda t, wk=wk, ok=ok, kc=kc: t.matmul(pwo[:], wk, ok,
                                                                  start=(kc == 0), stop=(kc == NCD - 1)),
                          tok=(kc == NCD - 1))
            em.wait("vector", mt, em.ch_all(f"l0my{jb}"))
            yv = em.v(lambda v, jb=jb: v.tensor_copy(ystg[jb][:], pwo[:]))
            pwo_free = yv
            em.wait("sync", yv)
            dst = dram["y_l0attn"][mc * 128:(mc + 1) * 128, s * TS:(s + 1) * TS]
            em.dma(lambda e, dst=dst, jb=jb: e.dma_start(out=dst, in_=ystg[jb][:]), ch=f"l0my{jb}")
    em.wait("sync", em.ch_all("l0my0"), em.ch_all("l0my1"))


def stage_l0moe(nc, st, em, dram, E=32):
    NI = 6
    sb = lambda nm, sh, dt: st.enter_context(nc.sbuf_tensor(f"l0e_{nm}", sh, dt))
    xnb = sb("xn", [128, NCD * T], BF16)
    acc = sb("acc", [128, NCD * T], F32)
    gwf = sb("gwf", [32, 1024], F32)
    gwT = sb("gwT", [128, NCD * 32], BF16)
    csb = sb("cst", [128, CST_W], F32)
    selb = sb("sel", [32, CSTB_W], BF16)
    idb = sb("idb", [128, 128], BF16)
    ebr = sb("ebr", [1, 32], F32)
    ebbc = sb("ebbc", [128, 32], F32)
    scb = sb("scb", [128, 32], F32)
    s0b = sb("s0b", [128, 32], F32)
    spb = sb("spb", [128, 32], F32)
    mcol = sb("mcol", [128, 1], F32)
    msku = sb("msku", [128, 32], U8)
    mskf = sb("mskf", [128, 32], F32)
    negb = sb("negb", [128, 32], F32)
    rzc = sb("rzc", [128, 1], F32)
    cmbt = sb("cmbt", [128, 32], BF16)
    cmbr = sb("cmbr", [32, T], BF16)
    w13 = [sb(f"w13_{i}", [128, NCD * 768 * 2], BF16) for i in range(2)]
    w2b = sb("w2b", [128, NI * 1024], BF16)
    silu_t = sb("silu", [128, TS], F32)
    t2 = sb("t2", [128, TS], F32)
    asb = sb("asb", [128, NI * TS], BF16)
    ystg = sb("ystg", [128, TS], F32)

    pcmb = st.enter_context(nc.psum_tensor("l0e_pcmb", [128, TS], F32))
    py = [st.enter_context(nc.psum_tensor(f"l0e_py{i}", [128, TS], F32)) for i in range(2)]
    rst = ExitStack()
    pr = rst.enter_context(nc.psum_tensor("l0e_pr", [128, 32], F32))
    prt = rst.enter_context(nc.psum_tensor("l0e_prt", [32, 128], BF16))
    peb = rst.enter_context(nc.psum_tensor("l0e_peb", [128, 32], F32))

    load_fm(em, xnb[:], dram["xn"], ch="l0e")
    em.dma(lambda e: e.dma_start(out=gwf[:], in_=dram["L0_ffn_gate_w"]), ch="l0e")
    em.dma(lambda e: e.dma_start(out=csb[:], in_=dram["cst_f32"]), ch="l0e")
    em.dma(lambda e: e.dma_start(out=selb[:], in_=dram["cst_bf16"]), ch="l0e")
    em.dma(lambda e: e.dma_start(out=ebr[:], in_=dram["L0_ffn_e_bias"].rearrange("(o f) -> o f", o=1)), ch="l0e")
    ld = em.ch_all("l0e")
    em.wait("vector", ld)
    idt = em.v(lambda v: v.tensor_copy(idb[:], cst(csb, "ident")))
    zacc = em.v(lambda v: v.memset(acc[:], 0.0))
    zneg = em.v(lambda v: v.memset(negb[:], -1e30))
    gt_free = None
    for c in range(NCD):
        em.wait("tensor", ld, gt_free)
        tg = em.t(lambda t, c=c: t.transpose(pr[:], gwf[0:32, c * 128:(c + 1) * 128],
                                             csb[0:32, CST["ident"][0]:CST["ident"][0] + 32]), tok=True)
        em.wait("vector", tg)
        gt_free = em.v(lambda v, c=c: v.tensor_copy(gwT[:, c * 32:(c + 1) * 32], pr[:, 0:32]))
    em.wait("tensor", gt_free)
    bt = em.t(lambda t: t.matmul(peb[:], onesrow(csb), ebr[:], start=True, stop=True), tok=True)
    em.wait("vector", bt)
    ebv = em.v(lambda v: v.tensor_copy(ebbc[:], peb[:]))
    pr_free = gt_free
    prt_free = None
    cm_fin = None
    for tt in range(16):
        em.wait("tensor", pr_free)
        mt = None
        for kc in range(NCD):
            xk = xnb[:, kc * T + tt * 128: kc * T + (tt + 1) * 128]
            gk = gwT[:, kc * 32:(kc + 1) * 32]
            mt = em.t(lambda t, xk=xk, gk=gk, kc=kc: t.matmul(pr[:], xk, gk,
                                                              start=(kc == 0), stop=(kc == NCD - 1)), tok=(kc == NCD - 1))
        em.wait("scalar", mt)
        sg = em.s(lambda sc: sc.activation(out=scb[:], in_=pr[:], func=AF.Sigmoid))
        pr_free = sg
        em.wait("vector", sg, ebv, zneg)
        em.v(lambda v: v.tensor_add(s0b[:], scb[:], ebbc[:]))
        em.v(lambda v: v.tensor_copy(spb[:], s0b[:]))
        for r in range(4):
            em.v(lambda v: v.reduce_max(mcol[:], spb[:], AX.X))
            if r < 3:
                em.v(lambda v: v.tensor_single_scalar(mskf[:], spb[:], mcol[:, 0:1], ALU.is_ge))
                em.v(lambda v: v.scalar_tensor_tensor(spb[:], mskf[:], -1e30, spb[:], ALU.mult, ALU.add))
        em.v(lambda v: v.tensor_single_scalar(mskf[:], s0b[:], mcol[:, 0:1], ALU.is_ge))
        em.v(lambda v: v.tensor_mul(mskf[:], mskf[:], scb[:]))
        em.v(lambda v: v.reduce_sum(rzc[:], mskf[:], AX.X))
        em.v(lambda v: v.reciprocal(rzc[:], rzc[:]))
        cmv = em.v(lambda v: v.tensor_scalar_mul(cmbt[:], mskf[:], rzc[:, 0:1]))
        em.wait("tensor", cmv, idt, prt_free)
        ct = em.t(lambda t: t.transpose(prt[:], cmbt[:], idb[:]), tok=True)
        em.wait("vector", ct)
        cm_fin = em.v(lambda v, tt=tt: v.tensor_copy(cmbr[:, tt * 128:(tt + 1) * 128], prt[:]))
        prt_free = cm_fin
    rst.close()
    pg = [st.enter_context(nc.psum_tensor(f"l0e_pg{i}", [128, TS], F32)) for i in range(2)]
    pu = [st.enter_context(nc.psum_tensor(f"l0e_pu{i}", [128, TS], F32)) for i in range(2)]
    def ldw13(e):
        i = e % 2
        em.dma(lambda ee, e=e, i=i: ee.dma_start(
            out=w13[i][:, 0:NCD * 768].rearrange("p (c m) -> p c m", c=NCD),
            in_=dram["L0_ffn_w1"][e].rearrange("(c p) m -> p c m", p=128)), q="gpsimd", ch=f"l0ew{i}")
        em.dma(lambda ee, e=e, i=i: ee.dma_start(
            out=w13[i][:, NCD * 768:].rearrange("p (c m) -> p c m", c=NCD),
            in_=dram["L0_ffn_w3"][e].rearrange("(c p) m -> p c m", p=128)), q="gpsimd", ch=f"l0ew{i}")
        return em.ch_all(f"l0ew{i}")

    wtok = {0: ldw13(0)}
    last_w13_user = [None, None]
    w2_user = None
    pg_free, pu_free = [None, None], [None, None]
    pcmb_free = None
    t2_user = None
    py_free = [None, None]
    afin = None
    for e in range(E):
        i = e % 2
        if e + 1 < E:
            em.wait("gpsimd", last_w13_user[(e + 1) % 2])
            wtok[e + 1] = ldw13(e + 1)
        em.wait("gpsimd", w2_user)
        em.dma(lambda ee, e=e: ee.dma_start(
            out=w2b[:].rearrange("p (c m) -> p c m", c=NI),
            in_=dram["L0_ffn_w2"][e].rearrange("(c p) m -> p c m", p=128)), q="gpsimd", ch="l0ew2")
        w2tok = em.ch_all("l0ew2")
        wt = wtok.pop(e)
        for s in range(NSTRIP):
            em.wait("tensor", cm_fin, pcmb_free)
            sl = selb[0:32, e * 128:(e + 1) * 128]
            cr = cmbr[0:32, s * TS:(s + 1) * TS]
            cbt = em.t(lambda t, sl=sl, cr=cr: t.matmul(pcmb[:], sl, cr, start=True, stop=True), tok=True)
            for ic in range(NI):
                jc = ic % 2
                em.wait("tensor", wt, ld, cm_fin, pg_free[jc], pu_free[jc])
                for kc in range(NCD):
                    w1k = w13[i][:, kc * 768 + ic * 128: kc * 768 + (ic + 1) * 128]
                    xk = xnb[:, kc * T + s * TS: kc * T + (s + 1) * TS]
                    em.t(lambda t, w1k=w1k, xk=xk, kc=kc, jc=jc: t.matmul(pg[jc][:], w1k, xk,
                                                                          start=(kc == 0), stop=(kc == NCD - 1)))
                gmt = None
                for kc in range(NCD):
                    w3k = w13[i][:, NCD * 768 + kc * 768 + ic * 128: NCD * 768 + kc * 768 + (ic + 1) * 128]
                    xk = xnb[:, kc * T + s * TS: kc * T + (s + 1) * TS]
                    gmt = em.t(lambda t, w3k=w3k, xk=xk, kc=kc, jc=jc: t.matmul(pu[jc][:], w3k, xk,
                                                                               start=(kc == 0), stop=(kc == NCD - 1)),
                               tok=(kc == NCD - 1))
                em.wait("scalar", gmt, t2_user if ic >= 1 else None)
                sil = em.s(lambda sc, jc=jc: sc.activation(out=silu_t[:], in_=pg[jc][:], func=AF.Silu))
                pg_free[jc] = sil
                em.wait("vector", sil, cbt)
                t2v = em.v(lambda v, jc=jc: v.tensor_mul(t2[:], silu_t[:], pu[jc][:]))
                pu_free[jc] = t2v
                av = em.v(lambda v, ic=ic: v.tensor_mul(asb[:, ic * TS:(ic + 1) * TS], t2[:], pcmb[:]))
                t2_user = av
            pcmb_free = av
            for oc in range(NCD):
                jb = oc % 2
                em.wait("tensor", av, w2tok, py_free[jb])
                ymt = None
                for ic in range(NI):
                    w2k = w2b[:, ic * 1024 + oc * 128: ic * 1024 + (oc + 1) * 128]
                    ak = asb[:, ic * TS:(ic + 1) * TS]
                    ymt = em.t(lambda t, w2k=w2k, ak=ak, ic=ic, jb=jb: t.matmul(py[jb][:], w2k, ak,
                                                                                start=(ic == 0), stop=(ic == NI - 1)),
                               tok=(ic == NI - 1))
                w2_user = ymt
                last_w13_user[i] = ymt
                em.wait("vector", ymt, zacc)
                ac = acc[:, oc * T + s * TS: oc * T + (s + 1) * TS]
                afin = em.v(lambda v, ac=ac, jb=jb: v.tensor_add(ac, ac, py[jb][:]))
                py_free[jb] = afin
    em.wait("sync", afin)
    store_fm(em, dram["y_l0ffn"], acc[:], ch="l0est")
    em.wait("sync", em.ch_all("l0est"))


def stage_add3(nc, st, em, dram, na, nb, ncc, nout, uniq, dta=F32):
    sb = lambda nm, sh, dt: st.enter_context(nc.sbuf_tensor(f"{uniq}_{nm}", sh, dt))
    a = sb("a", [128, NCD * T], F32)
    b = sb("b", [128, NCD * T], F32)
    load_fm(em, a[:], dram[na], ch=uniq)
    load_fm(em, b[:], dram[nb], ch=uniq)
    ld = em.ch_all(uniq)
    em.wait("vector", ld)
    fin = None
    for i in range(2 * NCD):
        sl = slice(i * 1024, (i + 1) * 1024)
        fin = em.v(lambda v, sl=sl: v.tensor_add(a[:, sl], a[:, sl], b[:, sl]))
    em.wait("sync", fin)
    load_fm(em, b[:], dram[ncc], ch=uniq + "c")
    ld2 = em.ch_all(uniq + "c")
    em.wait("vector", ld2)
    for i in range(2 * NCD):
        sl = slice(i * 1024, (i + 1) * 1024)
        fin = em.v(lambda v, sl=sl: v.tensor_add(a[:, sl], a[:, sl], b[:, sl]))
    em.wait("sync", fin)
    store_fm(em, dram[nout], a[:], ch=uniq + "st")
    em.wait("sync", em.ch_all(uniq + "st"))


def stage_mm(nc, st, em, dram, name_in, wname, bname, name_out, Mout, uniq, out_dt=BF16):
    NMO = Mout // 128
    sb = lambda nm, sh, dt: st.enter_context(nc.sbuf_tensor(f"{uniq}_{nm}", sh, dt))
    x = sb("x", [128, NCD * T], BF16)
    w = sb("w", [128, NCD * Mout], BF16)
    bb = sb("b", [128, NMO], F32)
    stg = [sb(f"stg{i}", [128, TS], out_dt) for i in range(2)]
    ps = [st.enter_context(nc.psum_tensor(f"{uniq}_p{i}", [128, TS], F32)) for i in range(2)]
    load_fm(em, x[:], dram[name_in], ch=uniq)
    load_pp(em, bb[:], dram[bname], ch=uniq)
    em.dma(lambda e: e.dma_start(out=w[:].rearrange("p (c m) -> p c m", c=NCD),
                                 in_=dram[wname].rearrange("(c p) m -> p c m", p=128)), q="gpsimd", ch=uniq + "w")
    ld = em.ch_all(uniq)
    ldw = em.ch_all(uniq + "w")
    ps_free = [None, None]
    for mc in range(NMO):
        for s in range(NSTRIP):
            ib = (mc * NSTRIP + s) % 2
            em.wait("tensor", ld, ldw, ps_free[ib])
            mt = None
            for kc in range(NCD):
                wk = w[:, kc * Mout + mc * 128: kc * Mout + (mc + 1) * 128]
                xk = x[:, kc * T + s * TS: kc * T + (s + 1) * TS]
                mt = em.t(lambda t, wk=wk, xk=xk, kc=kc, ib=ib: t.matmul(
                    ps[ib][:], wk, xk, start=(kc == 0), stop=(kc == NCD - 1)), tok=(kc == NCD - 1))
            em.wait("scalar", mt, em.ch_all(f"{uniq}s{ib}"))
            bcol = bb[:, mc: mc + 1]
            ev = em.s(lambda sc, ib=ib, bcol=bcol: sc.add(stg[ib][:], ps[ib][:], bcol))
            ps_free[ib] = ev
            em.wait("sync", ev)
            dst = dram[name_out][mc * 128:(mc + 1) * 128, s * TS:(s + 1) * TS]
            em.dma(lambda e, dst=dst, ib=ib: e.dma_start(out=dst, in_=stg[ib][:]), ch=f"{uniq}s{ib}")
    em.wait("sync", em.ch_all(f"{uniq}s0"), em.ch_all(f"{uniq}s1"))


TPAD = 4


def stage_l1attn(nc, st, em, dram):
    NG = 4
    sb = lambda nm, sh, dt: st.enter_context(nc.sbuf_tensor(f"l1a_{nm}", sh, dt))
    xnb = sb("xn", [128, NCD * T], BF16)
    wq = sb("wq", [128, NCD * 768], BF16)
    bq = sb("bq", [128, 24], F32)
    CWD = T + 2 * TPAD
    qkv = sb("qkv", [128, 6 * CWD], BF16)
    bias = sb("biasb", [128, 16 * 128], F32)
    csb = sb("cst", [128, CST_W], F32)
    idb = sb("idb", [128, 128], BF16)
    s2 = sb("s2", [128, 128], F32)
    nmax = sb("nmax", [128, 1], F32)
    zs = sb("zs", [128, 1], F32)
    rz = sb("rz", [128, 1], F32)
    eb = sb("eb", [128, 128], BF16)
    ebt = sb("ebt", [128, 128], BF16)
    vt = sb("vt", [128, 64], BF16)
    obuf = sb("obuf", [128, NCD * CWD], BF16)
    wp = sb("wp", [128, NCD * 1024], BF16)
    bp = sb("bp", [128, NCD], F32)
    ystg = [sb(f"ystg{i}", [128, TS], F32) for i in range(2)]

    pS = st.enter_context(nc.psum_tensor("l1a_pS", [128, 128], F32))
    pt1 = st.enter_context(nc.psum_tensor("l1a_pt1", [128, 128], BF16))
    pt2 = st.enter_context(nc.psum_tensor("l1a_pt2", [128, 64], BF16))
    pO = st.enter_context(nc.psum_tensor("l1a_pO", [64, 128], F32))
    pY = [st.enter_context(nc.psum_tensor(f"l1a_pY{i}", [128, TS], F32)) for i in range(2)]

    load_fm(em, xnb[:], dram["xn1"], ch="l1a")
    em.dma(lambda e: e.dma_start(out=csb[:], in_=dram["cst_f32"]), ch="l1a")
    em.dma(lambda e: e.dma_start(out=bias[:].rearrange("p (h j) -> p h j", h=16),
                                 in_=dram["l1a_bias"].rearrange("h p j -> p h j")), ch="l1a")
    em.dma(lambda e: e.dma_start(out=wp[:].rearrange("p (c m) -> p c m", c=NCD),
                                 in_=dram["L1_attn_w_proj"].rearrange("(c p) m -> p c m", p=128)),
           q="gpsimd", ch="l1apw")
    load_pp(em, bp[:], dram["L1_attn_b_proj"], ch="l1a")
    load_pp(em, bq[:], dram["L1_attn_b_qkv"], ch="l1a")
    ld = em.ch_all("l1a")
    ldpw = em.ch_all("l1apw")
    em.wait("vector", ld)
    idt = em.v(lambda v: v.tensor_copy(idb[:], cst(csb, "ident")))

    pq_free = None
    o_done = None
    for g in range(NG):
        em.wait("gpsimd", pq_free)
        for j, off in enumerate([g * 256, 1024 + g * 256, 2048 + g * 256]):
            em.dma(lambda e, j=j, off=off: e.dma_start(
                out=wq[:, j * NCD * 256:(j + 1) * NCD * 256].rearrange("p (c m) -> p c m", c=NCD),
                in_=dram["L1_attn_w_qkv"][:, off:off + 256].rearrange("(c p) m -> p c m", p=128)),
                q="gpsimd", ch=f"l1aw{g % 2}")
        ldw = em.ch_all(f"l1aw{g % 2}")
        ldwb = ld
        mt = None
        for mc in range(6):
            for s in range(NSTRIP):
                jb = (mc * NSTRIP + s) % 2
                em.wait("tensor", ld, ldw, pq_free)
                em.wait("scalar", ldwb)
                for kc in range(NCD):
                    wk = wq[:, kc * 256 + (mc // 2) * NCD * 256 + (mc % 2) * 128:
                            kc * 256 + (mc // 2) * NCD * 256 + (mc % 2) * 128 + 128]
                    xk = xnb[:, kc * T + s * TS: kc * T + (s + 1) * TS]
                    mt = em.t(lambda t, wk=wk, xk=xk, kc=kc, jb=jb: t.matmul(
                        pY[jb][:], wk, xk, start=(kc == 0), stop=(kc == NCD - 1)), tok=(kc == NCD - 1))
                em.wait("scalar", mt)
                bcol = bq[:, (mc // 2) * 8 + g * 2 + (mc % 2): (mc // 2) * 8 + g * 2 + (mc % 2) + 1]
                dstq = qkv[:, mc * CWD + TPAD + s * TS: mc * CWD + TPAD + (s + 1) * TS]
                ev = em.s(lambda sc, jb=jb, bcol=bcol, dstq=dstq: sc.add(dstq, pY[jb][:], bcol))
                pq_free = ev
        wr = None
        em.wait("vector", ev)
        for mc in range(6):
            wr = em.v(lambda v, mc=mc: v.tensor_copy(
                qkv[:, mc * CWD + T + TPAD: mc * CWD + T + 2 * TPAD],
                qkv[:, mc * CWD + TPAD: mc * CWD + 2 * TPAD]))
        pS_free = None
        pt1_free = None
        pt2_free = None
        pO_free = None
        for hh in range(4):
            h = 4 * g + hh
            qoff = (hh // 2) * CWD + 0 * NCD * CWD
            koff = 2 * CWD + (hh // 2) * CWD
            voff = 4 * CWD + (hh // 2) * CWD
            pb = 64 * (hh % 2)
            for tt in range(16):
                em.wait("tensor", ev, wr, pS_free)
                ro = 2 * TPAD + 128 * tt
                qs_ = qkv[pb:pb + 64, qoff + ro: qoff + ro + 128]
                ks_ = qkv[pb:pb + 64, koff + ro: koff + ro + 128]
                st_ = em.t(lambda t, qs_=qs_, ks_=ks_: t.matmul(pS[:], qs_, ks_, start=True, stop=True), tok=True)
                em.wait("vector", st_)
                bsl = bias[:, h * 128:(h + 1) * 128]
                em.v(lambda v, bsl=bsl: v.scalar_tensor_tensor(s2[:], pS[:], 0.125, bsl, ALU.mult, ALU.add))
                mx = em.v(lambda v: v.reduce_max(nmax[:], s2[:], AX.X, negate=True))
                em.wait("scalar", mx)
                et = em.s(lambda sc: sc.activation(out=eb[:], in_=s2[:], func=AF.Exp, bias=nmax[:, 0:1],
                                                   accum_out=zs[:, 0:1]))
                em.wait("vector", et)
                em.v(lambda v: v.reciprocal(rz[:], zs[:]))
                pv = em.v(lambda v: v.tensor_scalar_mul(eb[:], eb[:], rz[:, 0:1]))
                pS_free = pv
                em.wait("tensor", pv, idt, pt1_free)
                t1 = em.t(lambda t: t.transpose(pt1[:], eb[:], idb[:]), tok=True)
                vs_ = qkv[pb:pb + 64, voff + ro: voff + ro + 128]
                em.wait("tensor", pt2_free)
                t2_ = em.t(lambda t, vs_=vs_, pb=pb: t.transpose(pt2[:], vs_, idb[pb:pb + 64, pb:pb + 64]), tok=True)
                em.wait("vector", t1, t2_)
                e1 = em.v(lambda v: v.tensor_copy(ebt[:], pt1[:]))
                pt1_free = e1
                e2 = em.v(lambda v: v.tensor_copy(vt[:], pt2[:, 0:64]))
                pt2_free = e2
                em.wait("tensor", e1, e2, pO_free)
                ot = em.t(lambda t: t.matmul(pO[:], vt[:], ebt[:], start=True, stop=True), tok=True)
                em.wait("vector", ot)
                dsto = obuf[pb:pb + 64, (hh // 2 + 2 * g) * CWD + ro: (hh // 2 + 2 * g) * CWD + ro + 128]
                pO_free = em.v(lambda v, dsto=dsto: v.tensor_copy(dsto, pO[:]))
            o_done = pO_free
    uw = None
    for mc in range(NCD):
        em.wait("vector", o_done)
        uw = em.v(lambda v, mc=mc: v.tensor_copy(obuf[:, mc * CWD + TPAD: mc * CWD + 2 * TPAD],
                                                 obuf[:, mc * CWD + T + TPAD: mc * CWD + T + 2 * TPAD]))
    pY_free = [None, None]
    for mc in range(NCD):
        for s in range(NSTRIP):
            jb = (mc * NSTRIP + s) % 2
            em.wait("tensor", uw, ldpw, pY_free[jb])
            mt = None
            for kc in range(NCD):
                wk = wp[:, kc * 1024 + mc * 128: kc * 1024 + (mc + 1) * 128]
                ok = obuf[:, kc * CWD + TPAD + s * TS: kc * CWD + TPAD + (s + 1) * TS]
                mt = em.t(lambda t, wk=wk, ok=ok, kc=kc, jb=jb: t.matmul(
                    pY[jb][:], wk, ok, start=(kc == 0), stop=(kc == NCD - 1)), tok=(kc == NCD - 1))
            em.wait("scalar", mt, em.ch_all(f"l1ay{jb}"))
            bcol = bp[:, mc: mc + 1]
            ev = em.s(lambda sc, jb=jb, bcol=bcol: sc.add(ystg[jb][:], pY[jb][:], bcol))
            pY_free[jb] = ev
            em.wait("sync", ev)
            dst = dram["y_l1attn"][mc * 128:(mc + 1) * 128, s * TS:(s + 1) * TS]
            em.dma(lambda e, dst=dst, jb=jb: e.dma_start(out=dst, in_=ystg[jb][:]), ch=f"l1ay{jb}")
    em.wait("sync", em.ch_all("l1ay0"), em.ch_all("l1ay1"))


def build_l1a_bias(bias_table):
    WS = 8
    coords = np.stack(np.meshgrid(np.arange(WS), np.arange(WS), indexing="ij"))
    cf = coords.reshape(2, -1)
    rel = cf[:, :, None] - cf[:, None, :]
    rel = rel.transpose(1, 2, 0).astype(np.int64)
    rel[..., 0] += WS - 1
    rel[..., 1] += WS - 1
    rel[..., 0] *= 2 * WS - 1
    idx = rel.sum(-1)[:WS, :WS]
    bias = np.asarray(bias_table)[idx.reshape(-1)].reshape(WS, WS, 16).transpose(2, 0, 1)
    out = np.full((16, 128, 128), -1e30, dtype=np.float32)
    for w in range(16):
        out[:, 8 * w:8 * w + 8, 8 * w:8 * w + 8] = bias
    return out


def stage_l1fattn(nc, st, em, dram):
    NHE, DH = 16, 64
    QB = 512
    NQB = T // QB
    sb = lambda nm, sh, dt: st.enter_context(nc.sbuf_tensor(f"l1f_{nm}", sh, dt))
    qh = [sb(f"qh{i}", [64, T], BF16) for i in range(2)]
    kh = [sb(f"kh{i}", [64, T], BF16) for i in range(2)]
    vh = [sb(f"vh{i}", [64, T], BF16) for i in range(2)]
    vt = [sb(f"vt{i}", [128, 16 * 65], BF16) for i in range(2)]
    ET = [sb(f"ET{i}", [128, 16 * QB], BF16) for i in range(2)]
    rz = [sb(f"rz{i}", [1, QB], F32) for i in range(2)]
    rzb = [sb(f"rzb{i}", [64, QB], F32) for i in range(2)]
    negc = sb("negc", [128, 1], F32)
    ctx = sb("ctx", [128, NCD * T], BF16)
    csb = sb("cst", [128, CST_W], F32)
    idb = sb("idb", [128, 128], BF16)

    pS = [st.enter_context(nc.psum_tensor(f"l1f_pS{i}", [128, QB], F32)) for i in range(2)]
    pO = [st.enter_context(nc.psum_tensor(f"l1f_pO{i}", [65, QB], F32)) for i in range(2)]
    pB = [st.enter_context(nc.psum_tensor(f"l1f_pB{i}", [64, QB], F32)) for i in range(2)]
    pT = st.enter_context(nc.psum_tensor("l1f_pT", [128, 128], BF16))

    em.dma(lambda e: e.dma_start(out=csb[:], in_=dram["cst_f32"]), ch="l1f")
    ld = em.ch_all("l1f")
    em.wait("vector", ld)
    idt = em.v(lambda v: v.tensor_copy(idb[:], cst(csb, "ident")))
    em.v(lambda v: v.memset(negc[:], -20.0))
    for i in range(2):
        em.v(lambda v, i=i: v.memset(vt[i][:], 1.0))

    def ldhead(h):
        i = h % 2
        for buf, row in ((qh, 0), (kh, 1024), (vh, 2048)):
            em.dma(lambda e, buf=buf, row=row, h=h, i=i: e.dma_start(
                out=buf[i][:], in_=dram["qkv2"][row + h * DH: row + (h + 1) * DH, :]), ch=f"l1fh{i}")
        return em.ch_all(f"l1fh{i}")

    htok = {0: ldhead(0)}
    qk_user = [None, None]
    v_user = [None, None]
    vt_user = [None, None]
    et_read = [None, None]
    pS_user = [None, None]
    pO_user = [None, None]
    pB_user = [None, None]
    rz_user = [None, None]
    pT_free = None
    pend = None
    gi = 0
    for h in range(NHE):
        i = h % 2
        ltok = htok.pop(h)
        if h + 1 < NHE:
            em.wait("sync", qk_user[(h + 1) % 2], v_user[(h + 1) % 2])
            htok[h + 1] = ldhead(h + 1)
        vt_copy = None
        for c2 in range(8):
            trt = None
            for j in range(2):
                c = 2 * c2 + j
                src = vh[i][0:64, c * 128:(c + 1) * 128]
                em.wait("tensor", ltok, idt, pT_free)
                trt = em.t(lambda t, src=src, j=j: t.transpose(pT[:, j * 64:(j + 1) * 64],
                                                               src, idb[0:64, 0:64]), tok=(j == 1))
            v_user[i] = trt
            em.wait("vector", trt, vt_user[i])
            dstv = vt[i][:].rearrange("p (c w) -> p c w", w=65)[:, 2 * c2:2 * c2 + 2, 0:64]
            srcv = pT[:].rearrange("p (c w) -> p c w", w=64)
            vt_copy = em.v(lambda v, dstv=dstv, srcv=srcv: v.tensor_copy(dstv, srcv))
            pT_free = vt_copy
        for qb in range(NQB):
            jb = gi % 2
            last_exp = None
            for kt in range(16):
                ip = kt % 2
                em.wait("tensor", ltok, pS_user[ip])
                ks_ = kh[i][0:64, kt * 128:(kt + 1) * 128]
                qs_ = qh[i][0:64, qb * QB:(qb + 1) * QB]
                mm = em.t(lambda t, ks_=ks_, qs_=qs_, ip=ip: t.matmul(pS[ip][:], ks_, qs_,
                                                                      start=True, stop=True), tok=True)
                qk_user[i] = mm
                em.wait("scalar", mm, et_read[jb] if kt == 0 else None)
                et = em.s(lambda sc, ip=ip, jb=jb, kt=kt: sc.activation(
                    out=ET[jb][:, kt * QB:(kt + 1) * QB], in_=pS[ip][:],
                    func=AF.Exp, scale=0.125, bias=negc[:, 0:1]))
                pS_user[ip] = et
                last_exp = et
            if pend is not None:
                pend()
            cm = None
            for kt in range(16):
                em.wait("tensor", last_exp if kt == 0 else None,
                        pO_user[jb] if kt == 0 else None,
                        vt_copy if kt == 0 else None)
                vk = vt[i][:, kt * 65:(kt + 1) * 65]
                ek = ET[jb][:, kt * QB:(kt + 1) * QB]
                cm = em.t(lambda t, vk=vk, ek=ek, kt=kt, jb=jb: t.matmul(
                    pO[jb][:], vk, ek, start=(kt == 0), stop=(kt == 15)), tok=(kt == 15))
            vt_user[i] = cm
            et_read[jb] = cm
            em.wait("vector", cm, rz_user[jb])
            rc = em.v(lambda v, jb=jb: v.reciprocal(rz[jb][:], pO[jb][64:65, :]))

            def mk_evac(h=h, qb=qb, jb=jb, rc=rc):
                def go():
                    em.wait("tensor", rc, pB_user[jb])
                    bm = em.t(lambda t, jb=jb: t.matmul(pB[jb][:], onesrow(csb)[0:1, 0:64],
                                                        rz[jb][:], start=True, stop=True), tok=True)
                    rz_user[jb] = bm
                    em.wait("vector", bm)
                    cp = em.v(lambda v, jb=jb: v.tensor_copy(rzb[jb][:], pB[jb][:]))
                    pB_user[jb] = cp
                    dstc = ctx[64 * (h % 2):64 * (h % 2) + 64,
                               (h // 2) * T + qb * QB: (h // 2) * T + (qb + 1) * QB]
                    mu = em.v(lambda v, jb=jb, dstc=dstc: v.tensor_mul(dstc, pO[jb][0:64, :], rzb[jb][:]))
                    pO_user[jb] = mu
                return go
            pend = mk_evac()
            gi += 1
    pend()
    em.wait("sync", pO_user[0], pO_user[1])
    store_fm(em, dram["ctx"], ctx[:], ch="l1fst")
    em.wait("sync", em.ch_all("l1fst"))


def stage_l1h(nc, st, em, dram):
    sb = lambda nm, sh, dt: st.enter_context(nc.sbuf_tensor(f"l1h_{nm}", sh, dt))
    cb = sb("ctx", [128, NCD * T], BF16)
    xb = sb("x", [128, NCD * T], F32)
    wb = sb("w", [128, NCD * 1024], BF16)
    bo = sb("bo", [128, NCD], F32)
    stg = [sb(f"stg{i}", [128, TS], F32) for i in range(2)]
    ps = [st.enter_context(nc.psum_tensor(f"l1h_p{i}", [128, TS], F32)) for i in range(2)]
    load_fm(em, cb[:], dram["ctx"], ch="l1h")
    load_fm(em, xb[:], dram["xn1_32"], ch="l1h")
    load_pp(em, bo[:], dram["L1_ffn_b_o"], ch="l1h")
    em.dma(lambda e: e.dma_start(out=wb[:].rearrange("p (c m) -> p c m", c=NCD),
                                 in_=dram["L1_ffn_w_o"].rearrange("(c p) m -> p c m", p=128)), q="gpsimd", ch="l1hw")
    ld = em.ch_all("l1h")
    ldw = em.ch_all("l1hw")
    ps_free = [None, None]
    for mc in range(NCD):
        for s in range(NSTRIP):
            ib = (mc * NSTRIP + s) % 2
            em.wait("tensor", ld, ldw, ps_free[ib])
            mt = None
            for kc in range(NCD):
                wk = wb[:, kc * 1024 + mc * 128: kc * 1024 + (mc + 1) * 128]
                ck = cb[:, kc * T + s * TS: kc * T + (s + 1) * TS]
                mt = em.t(lambda t, wk=wk, ck=ck, kc=kc, ib=ib: t.matmul(
                    ps[ib][:], wk, ck, start=(kc == 0), stop=(kc == NCD - 1)), tok=(kc == NCD - 1))
            em.wait("vector", mt, ld, em.ch_all(f"l1hs{ib}"))
            bcol = bo[:, mc: mc + 1]
            xk = xb[:, mc * T + s * TS: mc * T + (s + 1) * TS]
            ev = em.v(lambda v, ib=ib, bcol=bcol, xk=xk: v.scalar_tensor_tensor(
                stg[ib][:], ps[ib][:], bcol, xk, ALU.add, ALU.add))
            ps_free[ib] = ev
            em.wait("sync", ev)
            dst = dram["h_ffn"][mc * 128:(mc + 1) * 128, s * TS:(s + 1) * TS]
            em.dma(lambda e, dst=dst, ib=ib: e.dma_start(out=dst, in_=stg[ib][:]), ch=f"l1hs{ib}")
    em.wait("sync", em.ch_all("l1hs0"), em.ch_all("l1hs1"))


def stage_l1moe(nc, st, em, dram, E=8):
    NI2 = 32
    sb = lambda nm, sh, dt: st.enter_context(nc.sbuf_tensor(f"l1m_{nm}", sh, dt))
    hbuf = sb("hb", [128, NCD * T], F32)
    xs = sb("xs", [128, NCD * TS], BF16)
    hx = sb("hx", [128, NI2 * TS], BF16)
    wbuf = [sb(f"wb{i}", [128, 16 * 1024], BF16) for i in range(2)]
    gwb = sb("gw", [128, NCD * 8], BF16)
    b1b = sb("b1", [128, E * NI2], F32)
    b2b = sb("b2", [128, E * NCD], F32)
    web = sb("web", [128, TS], BF16)
    wer = sb("wer", [8, T], BF16)
    selb = sb("sel", [32, CSTB_W], BF16)
    csb = sb("cst", [128, CST_W], F32)
    idb = sb("idb", [128, 128], BF16)
    prb = sb("prb", [128, 8], F32)
    spb = sb("spb", [128, 8], F32)
    mcol = sb("mcol", [128, 1], F32)
    mskf = sb("mskf", [128, 8], F32)
    nmax = sb("nmax", [128, 1], F32)
    zs = sb("zs", [128, 1], F32)
    rz = sb("rz", [128, 1], F32)
    wet = sb("wet", [128, 8], F32)
    tb = sb("tb", [128, TS], F32)

    pp = [st.enter_context(nc.psum_tensor(f"l1m_p{i}", [128, TS], F32)) for i in range(8)]

    load_fm(em, hbuf[:], dram["h_ffn"], ch="l1m")
    em.dma(lambda e: e.dma_start(out=gwb[:].rearrange("p (c m) -> p c m", c=NCD),
                                 in_=dram["L1_ffn_w_gate"].rearrange("(c p) m -> p c m", p=128)), q="gpsimd", ch="l1mg")
    em.dma(lambda e: e.dma_start(out=selb[:], in_=dram["cst_bf16"]), ch="l1m")
    em.dma(lambda e: e.dma_start(out=csb[:], in_=dram["cst_f32"]), ch="l1m")
    em.dma(lambda e: e.dma_start(out=b1b[:].rearrange("p (e c) -> p e c", e=E),
                                 in_=dram["L1_ffn_b1"].rearrange("e (c p) -> p e c", p=128),
                                 allow_slow_non_contiguous=True), ch="l1m")
    em.dma(lambda e: e.dma_start(out=b2b[:].rearrange("p (e c) -> p e c", e=E),
                                 in_=dram["L1_ffn_b2"].rearrange("e (c p) -> p e c", p=128),
                                 allow_slow_non_contiguous=True), ch="l1m")
    ld = em.ch_all("l1m")
    ldg = em.ch_all("l1mg")
    em.wait("vector", ld)
    idt = em.v(lambda v: v.tensor_copy(idb[:], cst(csb, "ident")))

    def ldw(name, e, hf, slot, guard):
        em.wait("gpsimd", guard)
        if name == "L1_ffn_w1":
            srcap = dram[name][e][:, hf * 2048:(hf + 1) * 2048].rearrange("(c p) m -> p c m", p=128)
            dstap = wbuf[slot][:].rearrange("p (c m) -> p c m", c=NCD)
        else:
            srcap = dram[name][e][hf * 2048:(hf + 1) * 2048, :].rearrange("(c p) m -> p c m", p=128)
            dstap = wbuf[slot][:].rearrange("p (c m) -> p c m", c=16)
        em.dma(lambda ee, srcap=srcap, dstap=dstap: ee.dma_start(out=dstap, in_=srcap),
               q="gpsimd", ch=f"l1mw{slot}")
        return em.ch_all(f"l1mw{slot}")

    pr_free = None
    prt_free = None
    xl_user = None
    acc_fin = None
    wslot_user = [None, None]
    for s in range(NSTRIP):
        em.wait("sync", xl_user)
        em.dma(lambda e, s=s: e.dma_start(
            out=xs[:].rearrange("p (c t) -> p c t", c=NCD),
            in_=dram["xln2"].rearrange("(c p) t -> p c t", p=128)[:, :, s * TS:(s + 1) * TS]), ch="l1mx")
        lx = em.ch_all("l1mx")
        wrt = None
        for tt in range(4):
            em.wait("tensor", lx, ldg, pr_free)
            mt = None
            for kc in range(NCD):
                xk = xs[:, kc * TS + tt * 128: kc * TS + (tt + 1) * 128]
                gk = gwb[:, kc * 8:(kc + 1) * 8]
                mt = em.t(lambda t, xk=xk, gk=gk, kc=kc: t.matmul(pp[0][:, 0:8], xk, gk,
                                                                  start=(kc == 0), stop=(kc == NCD - 1)),
                          tok=(kc == NCD - 1))
            em.wait("vector", mt)
            em.v(lambda v: v.reduce_max(nmax[:], pp[0][:, 0:8], AX.X, negate=True))
            mxt = em.v(lambda v: v.engine_nop())
            em.wait("scalar", mxt)
            et = em.s(lambda sc: sc.activation(out=prb[:], in_=pp[0][:, 0:8], func=AF.Exp, bias=nmax[:, 0:1],
                                               accum_out=zs[:, 0:1]))
            pr_free = et
            em.wait("vector", et)
            em.v(lambda v: v.reciprocal(rz[:], zs[:]))
            em.v(lambda v: v.tensor_scalar_mul(prb[:], prb[:], rz[:, 0:1]))
            em.v(lambda v: v.tensor_copy(spb[:], prb[:]))
            em.v(lambda v: v.reduce_max(mcol[:], spb[:], AX.X))
            em.v(lambda v: v.tensor_single_scalar(mskf[:], spb[:], mcol[:, 0:1], ALU.is_ge))
            em.v(lambda v: v.scalar_tensor_tensor(spb[:], mskf[:], -1e30, spb[:], ALU.mult, ALU.add))
            em.v(lambda v: v.reduce_max(mcol[:], spb[:], AX.X))
            em.v(lambda v: v.tensor_single_scalar(mskf[:], prb[:], mcol[:, 0:1], ALU.is_ge))
            em.v(lambda v: v.tensor_mul(mskf[:], mskf[:], prb[:]))
            em.v(lambda v: v.reduce_sum(zs[:], mskf[:], AX.X))
            em.v(lambda v: v.reciprocal(rz[:], zs[:]))
            wev = em.v(lambda v: v.tensor_scalar_mul(wet[:], mskf[:], rz[:, 0:1]))
            em.wait("tensor", wev, idt, prt_free)
            trt = em.t(lambda t: t.transpose(pp[1][0:8, 0:128], wet[:, 0:8], cst(csb, "ident")), tok=True)
            em.wait("vector", trt)
            wrt = em.v(lambda v, s=s, tt=tt: v.tensor_copy(wer[0:8, s * TS + tt * 128: s * TS + (tt + 1) * 128],
                                                           pp[1][0:8, 0:128]))
            prt_free = wrt
        for e in range(E):
            w1t0 = ldw("L1_ffn_w1", e, 0, 0, wslot_user[0])
            w1t1 = ldw("L1_ffn_w1", e, 1, 1, wslot_user[1])
            gt = None
            gel_user = [None, None]
            for hf in range(2):
                wt = w1t0 if hf == 0 else w1t1
                for ic in range(16):
                    gic = hf * 16 + ic
                    jb = gic % 2
                    em.wait("tensor", lx, wt, wrt, gel_user[jb], pr_free if gic == 0 else None)
                    mt = None
                    for kc in range(NCD):
                        wk = wbuf[hf][:, kc * 2048 + ic * 128: kc * 2048 + (ic + 1) * 128]
                        xk = xs[:, kc * TS:(kc + 1) * TS]
                        mt = em.t(lambda t, wk=wk, xk=xk, kc=kc, jb=jb: t.matmul(
                            pp[jb][:], wk, xk, start=(kc == 0), stop=(kc == NCD - 1)), tok=(kc == NCD - 1))
                    em.wait("scalar", mt, ld)
                    bcol = b1b[:, e * NI2 // 4 + gic // 4: e * NI2 // 4 + gic // 4 + 1] if False else \
                        b1b[:, e * 8 + gic // 4: e * 8 + gic // 4 + 1]
                    gt = em.s(lambda sc, jb=jb, bcol=bcol, gic=gic: sc.activation(
                        out=hx[:, gic * TS:(gic + 1) * TS], in_=pp[jb][:], func=AF.Gelu_apprx_tanh, bias=bcol))
                    gel_user[jb] = gt
                wslot_user[hf] = gt
            em.wait("tensor", wrt, gt)
            sl = selb[0:8, e * 128:(e + 1) * 128]
            cr = wer[0:8, s * TS:(s + 1) * TS]
            cbt = em.t(lambda t, sl=sl, cr=cr: t.matmul(pp[0][:], sl, cr, start=True, stop=True), tok=True)
            em.wait("vector", cbt)
            webv = em.v(lambda v: v.tensor_copy(web[:], pp[0][:]))
            w2t0 = ldw("L1_ffn_w2", e, 0, 0, wslot_user[0])
            ymt = None
            for hf in range(2):
                w2t = w2t0 if hf == 0 else ldw("L1_ffn_w2", e, 1, 1, wslot_user[1])
                for oc in range(NCD):
                    em.wait("tensor", w2t, gt, webv if hf == 0 else None)
                    for ic in range(16):
                        gic = hf * 16 + ic
                        wk = wbuf[hf][:, ic * 1024 + oc * 128: ic * 1024 + (oc + 1) * 128]
                        hk = hx[:, gic * TS:(gic + 1) * TS]
                        ymt = em.t(lambda t, wk=wk, hk=hk, hf=hf, ic=ic, oc=oc: t.matmul(
                            pp[oc][:], wk, hk, start=(hf == 0 and ic == 0), stop=(hf == 1 and ic == 15)),
                            tok=(ic == 15))
                wslot_user[hf] = ymt
            afin = None
            for oc in range(NCD):
                em.wait("vector", ymt, webv, ld)
                b2c = b2b[:, e * NCD + oc: e * NCD + oc + 1]
                em.v(lambda v, oc=oc, b2c=b2c: v.scalar_tensor_tensor(tb[:], pp[oc][:], b2c, web[:],
                                                                      ALU.add, ALU.mult))
                hk = hbuf[:, oc * T + s * TS: oc * T + (s + 1) * TS]
                afin = em.v(lambda v, hk=hk: v.tensor_add(hk, hk, tb[:]))
            acc_fin = afin
            pr_free = afin
            prt_free = afin
        xl_user = acc_fin
    em.wait("sync", acc_fin)
    store_fm(em, dram["y_l1ffn"], hbuf[:], ch="l1mst")
    em.wait("sync", em.ch_all("l1mst"))


SCR = {
    "xn": ([D, T], BF16), "q_s": ([D, T], BF16), "k_s": ([D, T], BF16), "v_s": ([D, T], BF16),
    "beta_tm": ([T, 4], F32), "q_n": ([D, T], BF16), "k_n": ([D, T], BF16),
    "k_tm": ([NH * T, DK], BF16), "v_tm": ([NH * T, DV], BF16), "delta_tm": ([NH * T, DV], BF16),
    "ls_fm": ([D, T], BF16), "ll_fm": ([D, T], BF16), "stats_f": ([16, NH * T], BF16),
    "probs_rows": ([4, NH * T], BF16), "y_l0attn": ([D, T], F32), "y_l0ffn": ([D, T], F32),
    "x1": ([D, T], F32), "xn1": ([D, T], BF16), "xn1_32": ([D, T], F32),
    "y_l1attn": ([D, T], F32), "xln1": ([D, T], BF16), "qkv2": ([3 * D, T], BF16),
    "ctx": ([D, T], BF16), "h_ffn": ([D, T], F32), "xln2": ([D, T], BF16),
    "y_l1ffn": ([D, T], F32), "x2out": ([D, T], F32),
}
CONSTS = {"cst_f32": ([128, CST_W], F32), "cst_bf16": ([32, CSTB_W], BF16)}
MODIN = {
    "xcore": ([D, T], F32),
    "L0_attn_Wq": ([D, D], F32), "L0_attn_Wk": ([D, D], F32), "L0_attn_Wv": ([D, D], F32),
    "L0_attn_Wb": ([D, 4], F32),
    "L0_attn_conv_q_w": ([D, 4], F32), "L0_attn_conv_k_w": ([D, 4], F32), "L0_attn_conv_v_w": ([D, 4], F32),
    "L0_attn_fir_short_w": ([D, 5], F32), "L0_attn_fir_long_w": ([D, 64], F32),
    "L0_attn_mlp_w1": ([1040, 1024], F32), "L0_attn_mlp_b1": ([1024], F32),
    "L0_attn_mlp_w2": ([1024, 4], F32), "L0_attn_mlp_b2": ([4], F32),
    "L0_attn_log_temp": ([4, 1], F32), "L0_attn_o_norm_w": ([256], F32), "L0_attn_Wo": ([D, D], F32),
    "L0_ffn_gate_w": ([32, D], F32), "L0_ffn_w1": ([32, D, 768], F32), "L0_ffn_w3": ([32, D, 768], F32),
    "L0_ffn_w2": ([32, 768, D], F32), "L0_ffn_e_bias": ([32], F32),
    "l1a_bias": ([16, 128, 128], F32),
    "L1_attn_w_qkv": ([D, 3 * D], F32), "L1_attn_b_qkv": ([3 * D], F32),
    "L1_attn_w_proj": ([D, D], F32), "L1_attn_b_proj": ([D], F32),
    "L1_ffn_ln1_w": ([D], F32), "L1_ffn_ln1_b": ([D], F32),
    "L1_ffn_ln2_w": ([D], F32), "L1_ffn_ln2_b": ([D], F32),
    "L1_ffn_w_qkv": ([D, 3 * D], F32), "L1_ffn_b_qkv": ([3 * D], F32),
    "L1_ffn_w_o": ([D, D], F32), "L1_ffn_b_o": ([D], F32),
    "L1_ffn_w_gate": ([D, 8], F32), "L1_ffn_w1": ([8, D, 4096], F32), "L1_ffn_b1": ([8, 4096], F32),
    "L1_ffn_w2": ([8, 4096, D], F32), "L1_ffn_b2": ([8, D], F32),
}

STAGES = [
    ("ln0", lambda nc, st, em, dram: stage_ln(nc, st, em, dram, "xcore", "xn", uniq="ln0"),
     ["xcore"], ["xn"]),
    ("l0qkv", stage_l0qkv,
     ["xn", "L0_attn_Wq", "L0_attn_Wk", "L0_attn_Wv", "L0_attn_Wb",
      "L0_attn_conv_q_w", "L0_attn_conv_k_w", "L0_attn_conv_v_w"],
     ["q_s", "k_s", "v_s", "beta_tm"]),
    ("l0prep", stage_l0prep, ["q_s", "k_s", "v_s"], ["q_n", "k_n", "k_tm", "v_tm"]),
    ("delta", stage_delta, ["q_n", "k_n", "k_tm", "v_tm", "beta_tm"], ["delta_tm"]),
    ("l0fir", stage_l0fir,
     ["v_s", "delta_tm", "v_tm", "L0_attn_fir_short_w", "L0_attn_fir_long_w"],
     ["ls_fm", "ll_fm", "stats_f"]),
    ("l0gate", stage_l0gate,
     ["xn", "stats_f", "L0_attn_mlp_w1", "L0_attn_mlp_b1", "L0_attn_mlp_w2",
      "L0_attn_mlp_b2", "L0_attn_log_temp"], ["probs_rows"]),
    ("l0mix", stage_l0mix,
     ["ls_fm", "ll_fm", "v_s", "delta_tm", "probs_rows", "L0_attn_o_norm_w", "L0_attn_Wo"],
     ["y_l0attn"]),
    ("l0moe", stage_l0moe,
     ["xn", "L0_ffn_gate_w", "L0_ffn_e_bias", "L0_ffn_w1", "L0_ffn_w3", "L0_ffn_w2"],
     ["y_l0ffn"]),
    ("mid", lambda nc, st, em, dram: stage_add3(nc, st, em, dram, "xcore", "y_l0attn", "y_l0ffn", "x1", "m1"),
     ["xcore", "y_l0attn", "y_l0ffn"], ["x1"]),
    ("ln1", lambda nc, st, em, dram: stage_ln(nc, st, em, dram, "x1", "xn1", also_f32="xn1_32", uniq="ln1"),
     ["x1"], ["xn1", "xn1_32"]),
    ("l1attn", stage_l1attn,
     ["xn1", "l1a_bias", "L1_attn_w_qkv", "L1_attn_b_qkv", "L1_attn_w_proj", "L1_attn_b_proj"],
     ["y_l1attn"]),
    ("lnf1", lambda nc, st, em, dram: stage_ln(nc, st, em, dram, "xn1", "xln1", gamma="L1_ffn_ln1_w",
                                              beta="L1_ffn_ln1_b", in_dt=BF16, uniq="lnf1"),
     ["xn1", "L1_ffn_ln1_w", "L1_ffn_ln1_b"], ["xln1"]),
    ("qkv2", lambda nc, st, em, dram: stage_mm(nc, st, em, dram, "xln1", "L1_ffn_w_qkv", "L1_ffn_b_qkv",
                                              "qkv2", 3 * D, "qk2"),
     ["xln1", "L1_ffn_w_qkv", "L1_ffn_b_qkv"], ["qkv2"]),
    ("l1fattn", stage_l1fattn, ["qkv2"], ["ctx"]),
    ("l1h", stage_l1h, ["ctx", "xn1_32", "L1_ffn_w_o", "L1_ffn_b_o"], ["h_ffn"]),
    ("lnf2", lambda nc, st, em, dram: stage_ln(nc, st, em, dram, "h_ffn", "xln2", gamma="L1_ffn_ln2_w",
                                              beta="L1_ffn_ln2_b", uniq="lnf2"),
     ["h_ffn", "L1_ffn_ln2_w", "L1_ffn_ln2_b"], ["xln2"]),
    ("l1moe", stage_l1moe,
     ["h_ffn", "xln2", "L1_ffn_w_gate", "L1_ffn_w1", "L1_ffn_b1", "L1_ffn_w2", "L1_ffn_b2"],
     ["y_l1ffn"]),
    ("final", lambda nc, st, em, dram: stage_add3(nc, st, em, dram, "x1", "y_l1attn", "y_l1ffn", "x2out", "fin"),
     ["x1", "y_l1attn", "y_l1ffn"], ["x2out"]),
]


def prep_host_inputs(inputs):
    hs = inputs["L0_attn_hidden_states"]
    shared = {}
    for nm in MODIN:
        if nm in ("xcore", "l1a_bias"):
            continue
        if nm == "L0_attn_fir_short_w":
            shared[nm] = np.ascontiguousarray(inputs[nm].reshape(1024, 5))
        elif nm == "L0_attn_fir_long_w":
            shared[nm] = np.ascontiguousarray(inputs[nm].reshape(1024, 64))
        else:
            shared[nm] = np.ascontiguousarray(inputs[nm])
    shared["l1a_bias"] = build_l1a_bias(inputs["L1_attn_bias_table"])
    shared["cst_f32"] = build_const_f32()
    shared["cst_bf16"] = build_const_bf16()
    xcores = [np.ascontiguousarray(hs[c].T).astype(np.float32, copy=False) for c in range(NCORES)]
    return xcores, shared


def kernel(**inputs):
    xcores, shared = prep_host_inputs(inputs)
    per_core = [dict(xcore=xcores[c]) for c in range(NCORES)]
    for name, fn, ins_, outs_ in STAGES:
        isp = {}
        for nm in ins_ + ["cst_f32", "cst_bf16"]:
            if nm in SCR:
                isp[nm] = SCR[nm]
            elif nm in MODIN:
                isp[nm] = MODIN[nm]
            else:
                isp[nm] = CONSTS[nm]
        osp = {nm: SCR[nm] for nm in outs_}
        in_maps = []
        for c in range(NCORES):
            m = {}
            for nm in isp:
                if nm in shared:
                    m[nm] = shared[nm]
                else:
                    m[nm] = per_core[c][nm]
            in_maps.append(m)
        res = run_stages([fn], in_maps, isp, osp)
        for c in range(NCORES):
            for nm in outs_:
                per_core[c][nm] = np.asarray(res.results[c][nm])
    out = np.stack([per_core[c]["x2out"].T for c in range(NCORES)], axis=0)
    return np.ascontiguousarray(out.astype(np.float32, copy=False))


def kernel_unfused(**inputs):
    xcores, shared = prep_host_inputs(inputs)
    per_core = [dict(xcore=xcores[c]) for c in range(NCORES)]
    for name, fn, ins_, outs_ in STAGES:
        isp = {}
        for nm in ins_ + ["cst_f32", "cst_bf16"]:
            isp[nm] = SCR.get(nm) or MODIN.get(nm) or CONSTS.get(nm)
        osp = {nm: SCR[nm] for nm in outs_}
        in_maps = []
        for c in range(NCORES):
            in_maps.append({nm: (shared[nm] if nm in shared else per_core[c][nm]) for nm in isp})
        res = run_stages([fn], in_maps, isp, osp)
        for c in range(NCORES):
            for nm in outs_:
                per_core[c][nm] = np.asarray(res.results[c][nm])
    out = np.stack([per_core[c]["x2out"].T for c in range(NCORES)], axis=0)
    return np.ascontiguousarray(out.astype(np.float32, copy=False))


def build_fused():
    nc = bass.Bass("TRN2", target_bir_lowering=False, debug=False)
    dram = {}
    for nm, (sh, dt) in {**MODIN, **CONSTS}.items():
        dram[nm] = nc.dram_tensor(nm, list(sh), dt, kind="ExternalInput").ap()
    for nm, (sh, dt) in SCR.items():
        kind = "ExternalOutput" if nm == "x2out" else "Internal"
        dram[nm] = nc.dram_tensor(nm, list(sh), dt, kind=kind).ap()
    with ExitStack() as st_outer:
        em = Em(nc, st_outer)
        for name, fn, ins_, outs_ in STAGES:
            with ExitStack() as st_stage:
                fn(nc, st_stage, em, dram)
            em.barrier()
        em.finalize()
    return nc


def kernel(**inputs):
    xcores, shared = prep_host_inputs(inputs)
    nc = build_fused()
    in_maps = [{**shared, "xcore": xcores[c]} for c in range(NCORES)]
    res = run_bass_kernel_spmd(nc, in_maps, core_ids=list(range(NCORES)))
    out = np.stack([np.asarray(res.results[c]["x2out"]).T for c in range(NCORES)], axis=0)
    return np.ascontiguousarray(out.astype(np.float32, copy=False))
